# version 09
# baseline (speedup 1.0000x reference)
import numpy as np
import ml_dtypes
import concourse.bass as bass
import concourse.mybir as mybir
import concourse.dve_ops as dve_ops
from concourse.bass_utils import run_bass_kernel_spmd


def recip_acc(e, out, in_, scratch):
    return e.reciprocal(out, in_)

F32 = mybir.dt.float32
BF16 = mybir.dt.bfloat16
I32 = mybir.dt.int32
AX = mybir.AxisListType
ALU = mybir.AluOpType
ACTF = mybir.ActivationFunctionType

D = 1024
NDC = 8
H = 16
DH = 64
FF = 4096
NFC = 32
NB = 2

PHYS = {"pe": "pe", "act": "act", "dve": "dve", "sp": "sp", "pl": "pl",
        "sp_dma": "sp", "pl_dma": "pl", "pl_cc": "pl"}
N_DMA_SLOTS = 6


class Op:
    __slots__ = ("i", "eng", "fn", "reads", "writes", "is_dma", "deps",
                 "sem_key", "sem_val", "marked", "vc", "tag")

    def __init__(self, i, eng, fn, reads, writes, is_dma, tag):
        self.i, self.eng, self.fn = i, eng, fn
        self.reads, self.writes, self.is_dma = tuple(reads), tuple(writes), is_dma
        self.deps = set()
        self.sem_key = None
        self.sem_val = None
        self.marked = False
        self.vc = None
        self.tag = tag


class Prog:
    NO_COALESCE = False

    def __init__(self, nc, sim_strict=False):
        self.nc = nc
        self.ops = []
        self.res_w = {}
        self.res_r = {}
        self.sim_strict = sim_strict

    def op(self, eng, fn, reads=(), writes=(), tag=""):
        o = Op(len(self.ops), eng, fn, reads, writes,
               eng in ("sp_dma", "pl_dma"), tag)
        for r in o.reads:
            if r in self.res_w:
                o.deps.add(self.res_w[r])
        for w in o.writes:
            if w in self.res_w:
                o.deps.add(self.res_w[w])
            for rd in self.res_r.get(w, ()):
                o.deps.add(rd)
        for r in o.reads:
            self.res_r.setdefault(r, []).append(o.i)
        for w in o.writes:
            self.res_w[w] = o.i
            self.res_r[w] = []
        o.deps.discard(o.i)
        if len(o.deps) > 1:
            keep = {}
            extra = []
            for di in o.deps:
                pp = self.ops[di]
                if pp.is_dma or pp.eng == "pl_cc":
                    extra.append(di)
                else:
                    e = PHYS[pp.eng]
                    if keep.get(e, -1) < di:
                        keep[e] = di
            o.deps = set(extra) | set(keep.values())
        self.ops.append(o)
        return o

    def pe(self, fn, reads=(), writes=(), tag=""):
        return self.op("pe", fn, reads, writes, tag)

    def act(self, fn, reads=(), writes=(), tag=""):
        return self.op("act", fn, reads, writes, tag)

    def dve(self, fn, reads=(), writes=(), tag=""):
        return self.op("dve", fn, reads, writes, tag)

    def sp(self, fn, reads=(), writes=(), tag=""):
        return self.op("sp", fn, reads, writes, tag)

    def pl(self, fn, reads=(), writes=(), tag=""):
        return self.op("pl", fn, reads, writes, tag)

    def dma(self, queue, out_ap, in_ap, reads=(), writes=(), tag=""):
        return self.op(queue, lambda e, o=out_ap, i=in_ap: e.dma_start(o, i),
                       reads, writes, tag)

    def collective(self, alu, rg, in_ap, out_ap, reads=(), writes=(), tag=""):
        def fn(e):
            return e.collective_compute("AllReduce", alu, replica_groups=rg,
                                        ins=[in_ap], outs=[out_ap])
        return self.op("pl_cc", fn, reads, writes, tag)

    def fence(self, reads, writes, tag="fence"):
        return self.op("sp", lambda e: None, reads, writes, tag)

    def barrier(self, tag="barrier"):
        for eng in ("pe", "act", "dve", "sp", "pl"):
            self.op(eng, lambda e: None, ("BARRIER",), ("BARRIER",), tag + "." + eng)

    def emit(self):
        nc = self.nc

        def relaxed(pp, o):
            if pp.is_dma or pp.eng == "pl_cc" or PHYS[pp.eng] != PHYS[o.eng]:
                return False
            return (not self.sim_strict) or PHYS[pp.eng] == "pe"

        for o in self.ops:
            for di in o.deps:
                pp = self.ops[di]
                if not relaxed(pp, o):
                    pp.marked = True
        sems = {e: nc.alloc_semaphore(name=f"sem_{e}")
                for e in ("pe", "act", "dve", "sp", "pl")}
        dma_sems = {q: [nc.alloc_semaphore(name=f"{q}{i}")
                        for i in range(N_DMA_SLOTS)]
                    for q in ("sp_dma", "pl_dma")}
        cc_sem = nc.alloc_semaphore(name="sem_cc")

        counts = {e: 0 for e in ("pe", "act", "dve", "sp", "pl")}
        dma_counts = {q: [0] * N_DMA_SLOTS for q in ("sp_dma", "pl_dma")}
        dma_rr = {q: 0 for q in ("sp_dma", "pl_dma")}
        ccn = 0
        for o in self.ops:
            if o.is_dma:
                sl = dma_rr[o.eng] % N_DMA_SLOTS
                dma_rr[o.eng] += 1
                dma_counts[o.eng][sl] += 1
                o.sem_key = (o.eng, sl)
                o.sem_val = 16 * dma_counts[o.eng][sl]
                o.marked = True
            elif o.eng == "pl_cc":
                ccn += 1
                o.sem_key = ("cc",)
                o.sem_val = ccn
                o.marked = True
            elif o.marked:
                counts[o.eng] += 1
                o.sem_key = (o.eng,)
                o.sem_val = counts[o.eng]
        for e, n in counts.items():
            assert n < 60000, (e, n)
        for q in dma_counts:
            for n in dma_counts[q]:
                assert 16 * n < 60000, (q, n)

        def sem_of(key):
            if key[0] in ("sp_dma", "pl_dma"):
                return dma_sems[key[0]][key[1]]
            if key[0] == "cc":
                return cc_sem
            return sems[key[0]]

        per_eng = {e: [] for e in ("pe", "act", "dve", "sp", "pl")}
        for o in self.ops:
            per_eng[PHYS[o.eng]].append(o)
        vclocks = {e: {} for e in per_eng}

        def body(eng, name):
            vc = vclocks[name]
            for o in per_eng[name]:
                for di in sorted(o.deps):
                    pp = self.ops[di]
                    if relaxed(pp, o):
                        if pp.vc:
                            for kk, vv in pp.vc.items():
                                if vc.get(kk, 0) < vv:
                                    vc[kk] = vv
                        continue
                    if pp.sem_key is None:
                        continue
                    if vc.get(pp.sem_key, 0) < pp.sem_val or Prog.NO_COALESCE:
                        eng.wait_ge(sem_of(pp.sem_key), pp.sem_val)
                        vc[pp.sem_key] = max(vc.get(pp.sem_key, 0), pp.sem_val)
                    if (not Prog.NO_COALESCE) and pp.vc:
                        for kk, vv in pp.vc.items():
                            if vc.get(kk, 0) < vv:
                                vc[kk] = vv
                if o.is_dma:
                    prev = o.sem_val - 16
                    if prev > 0 and vc.get(o.sem_key, 0) < prev:
                        eng.wait_ge(sem_of(o.sem_key), prev)
                        vc[o.sem_key] = prev
                ins = o.fn(eng)
                if o.sem_key is not None:
                    if ins is None:
                        ins = eng.nop()
                    ins.then_inc(sem_of(o.sem_key), 16 if o.is_dma else 1)
                snap = dict(vc)
                if o.sem_key is not None:
                    snap[o.sem_key] = o.sem_val
                o.vc = snap

        with nc.Block("main") as blk:
            @blk.tensor
            def _(t):
                body(t, "pe")

            @blk.scalar
            def _(sc):
                body(sc, "act")

            @blk.vector
            def _(v):
                body(v, "dve")

            @blk.sync
            def _(sy):
                body(sy, "sp")

            @blk.gpsimd
            def _(g):
                body(g, "pl")




class K:
    B_L0QKV, B_L0PROJ, B_B1, B_B2 = 0, 24, 32, 64
    B_LNW, B_LNB, B_L1QKV, B_L1PROJ, B_BASE = 72, 80, 88, 112, 120
    S_MXC, S_MNC = 0, 8
    S_CINV, S_CZP, S_CS, S_CNZS = 16, 24, 32, 40
    S_WMN, S_WMX, S_WINV, S_WZP, S_WS, S_WNZS = 48, 49, 50, 51, 52, 53
    S_RMX, S_RMN = 54, 57
    S_OMX, S_OMN = 60, 61
    S_T0 = 62
    QP_INV, QP_ZP, QP_S, QP_NZS = 0, 3, 6, 9
    QP_OINV, QP_OZP, QP_OS, QP_ONZS = 12, 13, 14, 15

    DEBUG_OUT = ()

    def __init__(self, T=2048, n_cores=8, sim_strict=False):
        assert T % 512 == 0 and (2 * (T // 512)) <= 8
        self.T = T
        self.NTC = T // 512
        self.NT128 = T // 128
        self.LC = min(1024, T)
        self.n_cores = n_cores
        self.nc = bass.Bass("TRN2", target_bir_lowering=False,
                            detect_race_conditions=sim_strict)
        self.p = Prog(self.nc, sim_strict=sim_strict)
        self.d = {}
        self.s = {}
        self.psum = {}
        self._declare()

    def dram(self, name, shape, dt, kind="Internal"):
        if kind == "Internal" and name in K.DEBUG_OUT:
            kind = "ExternalOutput"
        self.d[name] = self.nc.dram_tensor(name, list(shape), dt, kind=kind)
        return self.d[name]

    def sbuf(self, name, shape, dt):
        self.s[name] = self.nc.alloc_sbuf_tensor(name, list(shape), dt)
        return self.s[name]

    def sbuf_at(self, name, shape, dt, off):
        self.s[name] = self.nc.alloc_sbuf_tensor_at(name, list(shape), dt,
                                                    offset=off)
        return self.s[name]

    def _declare(self):
        T, NTC, NT128 = self.T, self.NTC, self.NT128
        dr = self.dram
        dr("x_fm", [NB, NDC, 128, T], F32, "ExternalInput")
        dr("l0_qkv_w", [3 * D, D], F32, "ExternalInput")
        dr("l0_qkv_b", [128, 24], F32, "ExternalInput")
        dr("l0_qkv_bv_row", [1, D], F32, "ExternalInput")
        dr("l0_proj_w", [D, D], F32, "ExternalInput")
        dr("l0_proj_b", [128, 8], F32, "ExternalInput")
        dr("l0_w1", [D, FF], F32, "ExternalInput")
        dr("l0_b1", [128, NFC], F32, "ExternalInput")
        dr("l0_w2", [FF, D], F32, "ExternalInput")
        dr("l0_b2", [128, NDC], F32, "ExternalInput")
        dr("l1_ln_w", [128, NDC], F32, "ExternalInput")
        dr("l1_ln_b", [128, NDC], F32, "ExternalInput")
        dr("l1_qkv_w", [D, 3 * D], F32, "ExternalInput")
        dr("l1_qkv_b", [128, 24], F32, "ExternalInput")
        dr("l1_qkv_bv_row", [1, D], F32, "ExternalInput")
        dr("l1_proj_w", [D, D], F32, "ExternalInput")
        dr("l1_proj_b", [128, 8], F32, "ExternalInput")
        dr("l1_base_wT", [D, D], F32, "ExternalInput")
        dr("l1_base_b", [128, 8], F32, "ExternalInput")
        dr("l1_lora_A", [D, 128], F32, "ExternalInput")
        dr("l1_lora_B", [128, D], F32, "ExternalInput")
        dr("l1_router_wT", [D, 8], F32, "ExternalInput")
        dr("l1_router_b", [8, 1], F32, "ExternalInput")
        dr("c_idn", [128, 128], F32, "ExternalInput")
        dr("c_headind", [128, NDC, 16], F32, "ExternalInput")
        dr("c_expind2", [8, 128], F32, "ExternalInput")
        dr("out_fm", [NB, NDC, 128, T], F32, "ExternalOutput")
        dr("xn32", [NB, NDC, 128, T], F32)
        dr("g0", [NB, NDC, 128, T], F32)
        dr("xmid", [NB, NDC, 128, T], F32)
        dr("xn132", [NB, NDC, 128, T], F32)
        dr("g1", [NB, NDC, 128, T], F32)
        dr("qkraw", [NB, 16, NTC, 128, 512], F32)
        dr("vraw", [NB, NT128, 2, 128, 512], F32)
        dr("oraw", [NB, H, NTC, 64, 512], F32)
        dr("wqT_qk", [16, NDC, 128, 128], BF16)
        dr("w1b", [128, NDC, NFC * 128], BF16)
        dr("w2b", [128, NFC, NDC * 128], BF16)
        dr("wqT_v", [NDC, 128, D], BF16)
        dr("pwT", [NDC, NDC, 128, 128], BF16)
        dr("ar1_in", [128, 16], F32)
        dr("ar1_out", [128, 16], F32)
        dr("ar2_in", [8, 1], F32)
        dr("ar2_out", [8, 1], F32)
        dr("ar3_in", [2, 1], F32)
        dr("dbg_lgt", [NB, 8, T], F32)
        dr("dbg_wrt", [NB, 8, T], F32)
        dr("dbg_axw", [NB, 128, T], F32)
        dr("dbg_wt", [128, NDC, 128], F32)
        dr("dbg_den", [8, 4, 2, 512], F32)
        dr("dbg_cb", [128, 16], F32)
        dr("dbg_nrm", [16, 2, 2048], F32)
        dr("dbg_sm", [16, 4], F32)
        dr("dbg_hind", [128, NDC, 16], F32)
        dr("dbg_s", [2, 128, 512], F32)
        dr("dbg_ka", [128, NDC, T], F32)
        dr("dbg_qa", [128, NDC, T], F32)
        dr("ar3_out", [2, 1], F32)

        sb = self.sbuf
        sb("act_a", [128, NDC, T], BF16)
        sb("act_b", [128, NDC, T], BF16)
        sb("fx", [128, 2, max(self.LC, 1024)], F32)
        sb("fy", [128, 2, 512], F32)
        sb("sq", [128, 2, self.LC], BF16)
        sb("ev", [128, 4, 512], F32)
        sb("qi", [128, 2, 512], I32)
        sb("ptile", [128, 2, 1024], BF16)
        sb("wtile", [128, 2, NDC, 128], BF16)
        sb("wst", [128, 8, 128], BF16)
        sb("mur", [128, 2, 512], F32)
        sb("rr", [128, 2, 512], F32)
        sb("den", [128, 512], F32)
        sb("rec", [128, 512], F32)
        sb("rsc", [128, 512], F32)
        sb("lA", [128, NDC, 128], BF16)
        sb("lB", [128, D], BF16)
        sb("lR", [128, NDC, 8], BF16)
        sb("bias", [128, 128], F32)
        sb("brow", [1, 2, D], BF16)
        sb("ones_b", [128, 128], BF16)
        sb("ones_f", [1, 128], F32)
        sb("onesf64", [66, 64], F32)
        sb("idn_f", [128, 128], F32)
        sb("idn_bt", [128, 128], BF16)
        sb("hind", [128, NDC, 16], BF16)
        sb("eind", [8, 128], BF16)
        sb("tmp8", [128, 96], F32)
        sb("small", [128, 48], F32)
        sb("qp", [128, 16], F32)
        sb("cbias", [128, 16], F32)
        sb("stat", [128, 64], F32)
        sb("cst", [128, 4], F32)

        base = (self.nc.sbuf_base + 31) & ~31
        top = self.nc.sbuf_top
        avail = top - base
        need = max(NFC * self.LC * 2 + 2 * NDC * 128 * 2 + NFC * 128 * 2,
                   NT128 * H * 66 * 2 + NDC * T * 2
                   + max(T * 4, NDC * 512 * 2, 2 * T * 2) + T * 2)
        assert need <= avail, (need, avail, base, top)
        A = base
        self.sbuf_at("hbuf", [128, NFC, self.LC], BF16, A)
        o1 = A + NFC * self.LC * 2
        self.sbuf_at("w1t", [128, 2, NDC, 128], BF16, o1)
        o2 = o1 + 2 * NDC * 128 * 2
        self.sbuf_at("w2t", [128, NFC, 128], BF16, o2)
        self.sbuf_at("wvb0", [128, NDC, D], BF16, o1)
        self.sbuf_at("vaug", [128, NT128, H, 66], BF16, A)
        ov = A + NT128 * H * 66 * 2
        self.sbuf_at("nrm0", [16, 2 * T], BF16, ov)
        self.sbuf_at("kst", [128, NDC, T], BF16, ov)
        oc = ov + NDC * T * 2
        self.sbuf_at("lgt", [8, T], F32, oc)
        self.sbuf_at("wvb", [128, NDC, 512], BF16, oc)
        self.sbuf_at("nrm1", [16, 2 * T], BF16, oc)
        self.sbuf_at("axw", [128, T], BF16, oc)
        oc2 = oc + max(T * 4, NDC * 512 * 2, 2 * T * 2)
        self.sbuf_at("wrt", [8, T], BF16, oc2)
        assert oc2 + T * 2 <= top, (oc2 + T * 2, top)

        for i in range(8):
            self.psum[f"pb{i}"] = self.nc.alloc_psum_tensor(f"pb{i}", [128, 512], F32)
        pbase = self.nc.psum_base
        self.nc.psum_base = 0
        for i in range(2):
            self.psum[f"pbt{i}"] = self.nc.alloc_psum_tensor(f"pbt{i}", [128, 1024], BF16)
        for b in (2, 4, 6):
            self.nc.psum_base = b
            self.psum[f"pb{b}{b + 1}"] = self.nc.alloc_psum_tensor(
                f"pb{b}{b + 1}", [128, 1024], F32)
        self.nc.psum_base = pbase




def quant_chain(k, src_ap, dst_ap, inv_ap, zp_ap, s_ap, nzs_ap,
                reads, writes, sl, tag="", np_=128):
    p, s = k.p, k.s
    n = src_ap.shape[-1]
    assert n <= 512

    def f1(e):
        return e.activation(s["ev"][0:np_, 2 + sl, 0:n], src_ap, ACTF.Relu,
                            bias=zp_ap, scale=inv_ap)
    p.act(f1, reads=reads, writes=(f"ev{2 + sl}",), tag=tag + ".r")

    def f2(e):
        return e.tensor_scalar(s["qi"][0:np_, sl, 0:n], s["ev"][0:np_, 2 + sl, 0:n],
                               255.0, None, ALU.min)
    p.dve(f2, reads=(f"ev{2 + sl}",), writes=(f"qi{sl}",), tag=tag + ".q")

    def f3(e):
        return e.tensor_scalar(dst_ap, s["qi"][0:np_, sl, 0:n],
                               s_ap, nzs_ap, ALU.mult, ALU.add)
    p.dve(f3, reads=(f"qi{sl}", reads[-1] if reads else "qp"), writes=writes,
          tag=tag + ".d")


def qparams_perpart(k, mn_ap, mx_ap, inv_ap, zp_ap, s_ap, nzs_ap,
                    res, reads, t0_ap, tag=""):
    p, s = k.p, k.s
    w = mn_ap.ap[-1][1]
    qi = s["qi"]

    def f1(e):
        e.tensor_scalar(mn_ap, mn_ap, 0.0, None, ALU.min)
        e.drain()
        e.tensor_scalar(mx_ap, mx_ap, 0.0, None, ALU.max)
        e.drain()
        e.tensor_sub(t0_ap, mx_ap, mn_ap)
        e.drain()
        e.tensor_scalar(s_ap, t0_ap, 1.0 / 255.0, 1e-8, ALU.mult, ALU.max)
        e.drain()
        recip_acc(e, inv_ap, s_ap, t0_ap)
        e.drain()
        e.tensor_mul(t0_ap, mn_ap, inv_ap)
        e.drain()
        e.tensor_scalar(qi[:, 0, 0:w], t0_ap, -1.0, None, ALU.mult)
        e.drain()
        e.tensor_copy(zp_ap, qi[:, 0, 0:w])
        e.drain()
        e.tensor_mul(t0_ap, zp_ap, s_ap)
        e.drain()
        return e.tensor_scalar(nzs_ap, t0_ap, -1.0, None, ALU.mult)
    p.dve(f1, reads=reads, writes=(res, "qi0"), tag=tag)


def w1_view(k):
    return k.d["l0_w1"][:, :].rearrange("(a p) (b j) -> p a b j", p=128, j=128)


def w2_view(k):
    return k.d["l0_w2"][:, :].rearrange("(b p) (a j) -> p b a j", p=128, j=128)


def qkt_view(k, name):
    return k.d[name][:, :, :, :]




def emit_setup(k):
    p, s, d = k.p, k.s, k.d
    for dst, src, res in [
        (s["bias"][:, K.B_L0QKV:K.B_L0QKV + 24], d["l0_qkv_b"][:, :], "bias"),
        (s["bias"][:, K.B_L0PROJ:K.B_L0PROJ + 8], d["l0_proj_b"][:, :], "bias"),
        (s["bias"][:, K.B_B1:K.B_B1 + NFC], d["l0_b1"][:, :], "bias"),
        (s["bias"][:, K.B_B2:K.B_B2 + NDC], d["l0_b2"][:, :], "bias"),
        (s["bias"][:, K.B_LNW:K.B_LNW + 8], d["l1_ln_w"][:, :], "bias"),
        (s["bias"][:, K.B_LNB:K.B_LNB + 8], d["l1_ln_b"][:, :], "bias"),
        (s["bias"][:, K.B_L1QKV:K.B_L1QKV + 24], d["l1_qkv_b"][:, :], "bias"),
        (s["bias"][:, K.B_L1PROJ:K.B_L1PROJ + 8], d["l1_proj_b"][:, :], "bias"),
        (s["bias"][:, K.B_BASE:K.B_BASE + 8], d["l1_base_b"][:, :], "bias"),
        (s["idn_f"][:, :], d["c_idn"][:, :], "idn_f"),
        (s["small"][0:8, 30:31], d["l1_router_b"][:, :], "rb"),
    ]:
        p.dma("sp_dma", dst, src, writes=(res,), tag="setup")
    p.dma("pl_dma", s["hind"][:, :, :], d["c_headind"][:, :, :], writes=("hind",))
    p.dma("pl_dma", s["eind"][:, :], d["c_expind2"][:, :], writes=("eind",))
    p.dma("pl_dma", s["brow"][0:1, 0, :], d["l0_qkv_bv_row"][:, :], writes=("brow0",))
    p.dma("pl_dma", s["brow"][0:1, 1, :], d["l1_qkv_bv_row"][:, :], writes=("brow1",))
    p.dma("pl_dma", s["lA"][:, :, :],
          d["l1_lora_A"][:, :].rearrange("(a p) j -> p a j", p=128), writes=("lA",))
    p.dma("pl_dma", s["lB"][:, :], d["l1_lora_B"][:, :], writes=("lB",))
    p.dma("pl_dma", s["lR"][:, :, :],
          d["l1_router_wT"][:, :].rearrange("(a p) j -> p a j", p=128), writes=("lR",))

    st = s["stat"]

    def consts(e):
        e.memset(s["ones_b"][:, :], 1.0)
        e.drain()
        e.memset(s["ones_f"][:, :], 1.0)
        e.drain()
        e.memset(s["onesf64"][:, :], 4.503599627370496e15)
        e.drain()
        e.memset(st[:, :], -1e30)
        e.drain()
        e.memset(st[:, K.S_MNC:K.S_MNC + 8], 1e30)
        e.drain()
        e.memset(st[:, K.S_RMN:K.S_RMN + 3], 1e30)
        e.drain()
        e.memset(st[:, K.S_OMN:K.S_OMN + 1], 1e30)
        e.drain()
        e.memset(s["cst"][:, 0:1], 1e-5)
        e.drain()
        e.memset(s["cst"][:, 1:2], 0.125)
        e.drain()
        e.memset(s["cst"][:, 2:3], -1.0)
        e.drain()
        return e.memset(s["cst"][:, 3:4], 36.04365338911715)
    p.dve(consts, writes=("ones_b", "ones_f", "onesf64", "chanstat",
                          "qkvstat", "ostat", "wstat", "cst"),
          tag="setup.c")

    def idnb(e):
        return e.tensor_copy(s["idn_bt"][:, :], s["idn_f"][:, :])
    p.dve(idnb, reads=("idn_f",), writes=("idn_bt",), tag="setup.idnb")


def emit_wq_preproc(k, wname, nfc):
    p, s, d = k.p, k.s, k.d
    st = s["stat"]
    for fc in range(nfc):
        sl = fc % 2
        p.dma("sp_dma", s["fx"][:, sl, 0:1024], d[wname][fc * 128:(fc + 1) * 128, :],
              writes=(f"fx{sl}",), tag=f"wq.{wname}.{fc}")

        def mnmx(e, sl=sl):
            e.tensor_reduce(st[:, K.S_WMN:K.S_WMN + 1], s["fx"][:, sl, 0:1024],
                            AX.X, ALU.min)
            e.drain()
            return e.tensor_reduce(st[:, K.S_WMX:K.S_WMX + 1], s["fx"][:, sl, 0:1024],
                                   AX.X, ALU.max)
        p.dve(mnmx, reads=(f"fx{sl}",), writes=("wstat",), tag=f"wq.mm{fc}")
        qparams_perpart(k, st[:, K.S_WMN:K.S_WMN + 1], st[:, K.S_WMX:K.S_WMX + 1],
                        st[:, K.S_WINV:K.S_WINV + 1], st[:, K.S_WZP:K.S_WZP + 1],
                        st[:, K.S_WS:K.S_WS + 1], st[:, K.S_WNZS:K.S_WNZS + 1],
                        "wpar", ("wstat",), st[:, K.S_T0:K.S_T0 + 1], tag=f"wq.p{fc}")
        for h2 in range(2):
            quant_chain(k, s["fx"][:, sl, h2 * 512:(h2 + 1) * 512],
                        s["ptile"][:, 0, h2 * 512:(h2 + 1) * 512],
                        st[:, K.S_WINV:K.S_WINV + 1], st[:, K.S_WZP:K.S_WZP + 1],
                        st[:, K.S_WS:K.S_WS + 1], st[:, K.S_WNZS:K.S_WNZS + 1],
                        reads=(f"fx{sl}", "wpar"), writes=(f"ptile{h2}",),
                        sl=h2, tag=f"wq.q{fc}.{h2}")
        for dc in range(NDC):
            pb = k.psum[f"pbt{dc % 2}"]

            def tp(e, dc=dc, pb=pb):
                return e.transpose(pb[:, 0:128],
                                   s["ptile"][:, 0, dc * 128:(dc + 1) * 128],
                                   s["idn_bt"][:, :])
            p.pe(tp, reads=(f"ptile{dc // 4}", "idn_bt"), writes=(f"pt{dc % 2}",),
                 tag=f"wq.t{fc}.{dc}")

            def ev(e, dc=dc, pb=pb):
                return e.tensor_copy(s["wst"][:, dc, :], pb[:, 0:128])
            p.dve(ev, reads=(f"pt{dc % 2}",), writes=(f"wst{dc}",),
                  tag=f"wq.e{fc}.{dc}")
            if wname == "l0_proj_w":
                dst = d["pwT"][fc, dc, :, :]
                res = f"pw.{fc}.{dc}"
            elif fc >= 16:
                dst = d["wqT_v"][dc, :, (fc - 16) * 128:(fc - 15) * 128]
                res = f"wv.{dc}.{fc - 16}"
            else:
                dst = d["wqT_qk"][fc, dc, :, :]
                res = f"wqk.{fc}.{dc}"
            p.dma("sp_dma", dst, s["wst"][:, dc, :], reads=(f"wst{dc}",),
                  writes=(res,), tag=f"wq.s{fc}.{dc}")


def emit_wcast(k):
    p, s, d = k.p, k.s, k.d
    CW = k.T
    w1f = d["l0_w1"][:, :].rearrange("(a p) f -> p a f", p=128)
    w2f = d["l0_w2"][:, :].rearrange("(b p) f -> p b f", p=128)
    for dc in range(NDC):
        for r in range(FF // CW):
            p.dma("pl_dma", s["act_b"][:, dc, :],
                  w1f[:, dc, r * CW:(r + 1) * CW],
                  writes=(f"act_b.{dc}",), tag=f"wc1i.{r}.{dc}")
            p.dma("sp_dma", d["w1b"][:, dc, r * CW:(r + 1) * CW],
                  s["act_b"][:, dc, :], reads=(f"act_b.{dc}",),
                  writes=(f"w1b.{r}.{dc}",), tag=f"wc1o.{r}.{dc}")
    cc = min(CW, D)
    g = 0
    for b in range(NFC):
        for r in range(D // cc):
            dc = g % NDC
            g += 1
            p.dma("pl_dma", s["act_b"][:, dc, 0:cc],
                  w2f[:, b, r * cc:(r + 1) * cc],
                  writes=(f"act_b.{dc}",), tag=f"wc2i.{b}.{r}")
            p.dma("sp_dma", d["w2b"][:, b, r * cc:(r + 1) * cc],
                  s["act_b"][:, dc, 0:cc], reads=(f"act_b.{dc}",),
                  writes=(f"w2b.{b}.{r}",), tag=f"wc2o.{b}.{r}")


def emit_l0_stats(k, bi):
    p, s, d = k.p, k.s, k.d
    T, NTC, LC = k.T, k.NTC, k.LC
    st = s["stat"]
    nl = T // LC
    for dc in range(NDC):
        for li in range(nl):
            sl = (dc * nl + li) % 2
            p.dma("sp_dma", s["fx"][:, sl, 0:LC], d["x_fm"][bi, dc, :, li * LC:(li + 1) * LC],
                  writes=(f"fx{sl}",), tag=f"s{bi}.x{dc}.{li}")

            def sqc(e, sl=sl, dc=dc, li=li):
                e.activation(s["sq"][:, sl, :], s["fx"][:, sl, 0:LC], ACTF.Square)
                e.drain()
                return e.activation(s["act_a"][:, dc, li * LC:(li + 1) * LC],
                                    s["fx"][:, sl, 0:LC], ACTF.Identity)
            p.act(sqc, reads=(f"fx{sl}",), writes=(f"sq{sl}", f"act_a.{dc}"),
                  tag=f"s{bi}.sq{dc}.{li}")
            for q5 in range(LC // 512):
                tc = li * (LC // 512) + q5

                def mm1(e, dc=dc, li=li, q5=q5, tc=tc):
                    return e.matmul(k.psum[f"pb{tc}"][:, :], s["ones_b"][:, :],
                                    s["act_a"][:, dc, (li * LC + q5 * 512):(li * LC + (q5 + 1) * 512)],
                                    start=(dc == 0), stop=(dc == NDC - 1))
                p.pe(mm1, reads=(f"act_a.{dc}", "ones_b"), writes=(f"pt{tc}",),
                     tag=f"s{bi}.mS{dc}.{tc}")

                def mm2(e, sl=sl, q5=q5, dc=dc, tc=tc):
                    return e.matmul(k.psum[f"pb{NTC + tc}"][:, :], s["ones_b"][:, :],
                                    s["sq"][:, sl, q5 * 512:(q5 + 1) * 512],
                                    start=(dc == 0), stop=(dc == NDC - 1))
                p.pe(mm2, reads=(f"sq{sl}", "ones_b"), writes=(f"pt{NTC + tc}",),
                     tag=f"s{bi}.mQ{dc}.{tc}")
    for tc in range(NTC):
        def stats(e, tc=tc):
            e.tensor_scalar(s["mur"][:, 0, :], k.psum[f"pb{tc}"][:, :],
                            1.0 / 1024.0, None, ALU.mult)
            e.drain()
            e.tensor_scalar(s["rr"][:, 0, :], k.psum[f"pb{NTC + tc}"][:, :],
                            1.0 / 1024.0, None, ALU.mult)
            e.drain()
            e.scalar_tensor_tensor(s["rsc"][:, 0:512], s["mur"][:, 0, :], 1.0,
                                   s["mur"][:, 0, :], ALU.mult, ALU.mult)
            e.drain()
            return e.tensor_sub(s["rsc"][:, 0:512], s["rr"][:, 0, :], s["rsc"][:, 0:512])
        p.dve(stats, reads=(f"pt{tc}", f"pt{NTC + tc}"), writes=("rsc", "mur0", "rr0"),
              tag=f"s{bi}.st{tc}")

        def sqr(e):
            return e.activation(s["rsc"][:, 0:512], s["rsc"][:, 0:512], ACTF.Sqrt, bias=s["cst"][:, 0:1])
        p.act(sqr, reads=("rsc",), writes=("rsc",), tag=f"s{bi}.sd{tc}")

        def rcp(e):
            recip_acc(e, s["rr"][:, 1, :], s["rsc"][:, 0:512], s["rec"][:, 0:512])
            e.drain()
            return e.tensor_mul(s["mur"][:, 1, :], s["mur"][:, 0, :], s["rr"][:, 1, :])
        p.dve(rcp, reads=("rsc", "mur0"), writes=("rr1", "mur1", "rec"),
              tag=f"s{bi}.rc{tc}")
        for dc in range(NDC):
            sl = (tc * NDC + dc) % 2
            p.dma("sp_dma", s["fx"][:, sl, 0:512],
                  d["x_fm"][bi, dc, :, tc * 512:(tc + 1) * 512],
                  writes=(f"fx{sl}",), tag=f"s{bi}.x2{dc}.{tc}")

            def app(e, sl=sl, dc=dc):
                e.tensor_mul(s["ev"][:, 3, :], s["fx"][:, sl, 0:512], s["rr"][:, 1, :])
                e.drain()
                e.tensor_sub(s["fy"][:, sl, :], s["ev"][:, 3, :], s["mur"][:, 1, :])
                e.drain()
                e.tensor_reduce(s["small"][:, 0:1], s["fy"][:, sl, :], AX.X, ALU.max)
                e.drain()
                e.tensor_reduce(s["small"][:, 1:2], s["fy"][:, sl, :], AX.X, ALU.min)
                e.drain()
                e.tensor_tensor(st[:, K.S_MXC + dc:K.S_MXC + dc + 1],
                                st[:, K.S_MXC + dc:K.S_MXC + dc + 1],
                                s["small"][:, 0:1], ALU.max)
                e.drain()
                return e.tensor_tensor(st[:, K.S_MNC + dc:K.S_MNC + dc + 1],
                                       st[:, K.S_MNC + dc:K.S_MNC + dc + 1],
                                       s["small"][:, 1:2], ALU.min)
            p.dve(app, reads=(f"fx{sl}", "rr1", "mur1"),
                  writes=(f"fy{sl}", "chanstat", "ev3", "small"),
                  tag=f"s{bi}.ap{dc}.{tc}")
            p.dma("sp_dma", d["xn32"][bi, dc, :, tc * 512:(tc + 1) * 512],
                  s["fy"][:, sl, :], reads=(f"fy{sl}",),
                  writes=(f"xn32.{bi}.{dc}.{tc}",), tag=f"s{bi}.w{dc}.{tc}")


def emit_ar1(k):
    p, s, d = k.p, k.s, k.d
    st = s["stat"]

    def pack(e):
        e.tensor_copy(s["tmp8"][:, 0:8], st[:, K.S_MXC:K.S_MXC + 8])
        e.drain()
        return e.tensor_scalar(s["tmp8"][:, 8:16], st[:, K.S_MNC:K.S_MNC + 8],
                               -1.0, None, ALU.mult)
    p.dve(pack, reads=("chanstat",), writes=("tmp8a",), tag="ar1.pk")
    p.dma("pl_dma", d["ar1_in"][:, :], s["tmp8"][:, 0:16], reads=("tmp8a",),
          writes=("ar1_in",), tag="ar1.st")
    p.collective(ALU.max, [list(range(k.n_cores))],
                 d["ar1_in"].ap().opt(), d["ar1_out"].ap().opt(),
                 reads=("ar1_in",), writes=("ar1_out",), tag="ar1.cc")
    p.dma("pl_dma", s["tmp8"][:, 16:32], d["ar1_out"][:, :], reads=("ar1_out",),
          writes=("tmp8b",), tag="ar1.ld")

    def unpack(e):
        e.tensor_copy(st[:, K.S_MXC:K.S_MXC + 8], s["tmp8"][:, 16:24])
        e.drain()
        return e.tensor_scalar(st[:, K.S_MNC:K.S_MNC + 8], s["tmp8"][:, 24:32],
                               -1.0, None, ALU.mult)
    p.dve(unpack, reads=("tmp8b",), writes=("chanstat",), tag="ar1.un")
    qparams_perpart(k, st[:, K.S_MNC:K.S_MNC + 8], st[:, K.S_MXC:K.S_MXC + 8],
                    st[:, K.S_CINV:K.S_CINV + 8], st[:, K.S_CZP:K.S_CZP + 8],
                    st[:, K.S_CS:K.S_CS + 8], st[:, K.S_CNZS:K.S_CNZS + 8],
                    "chanpar", ("chanstat",), s["tmp8"][:, 88:96], tag="ar1.par")


def emit_ffn(k, bi):
    p, s, d = k.p, k.s, k.d
    T, NTC, LC = k.T, k.NTC, k.LC
    if bi == 1:
        p.fence(reads=(), writes=("wvb0", "w1t0", "w1t1", "w2t0", "w2t1"), tag="f.ffn1")
    for dc in range(NDC):
        p.dma("pl_dma", s["act_a"][:, dc, :], d["xn32"][bi, dc, :, :],
              reads=tuple(f"xn32.{bi}.{dc}.{t}" for t in range(NTC)),
              writes=(f"act_a.{dc}",), tag=f"fn{bi}.xn{dc}")
    for tcb in range(T // LC):
        for fc in range(NFC):
            wsl = fc % 2
            p.dma("pl_dma", s["w1t"][:, wsl, :, :],
                  d["w1b"][:, :, fc * 128:(fc + 1) * 128],
                  reads=tuple(f"w1b.{fc * 128 // k.T}.{dcc}"
                              for dcc in range(NDC)),
                  writes=(f"w1t{wsl}",), tag=f"fn{bi}.w1.{tcb}.{fc}")
            for h2 in range(LC // 512):
                pb = k.psum[f"pb{h2}"]
                tsl = slice(tcb * LC + h2 * 512, tcb * LC + (h2 + 1) * 512)
                for dc in range(NDC):
                    def mm(e, pb=pb, wsl=wsl, dc=dc, tsl=tsl):
                        return e.matmul(pb[:, :], s["w1t"][:, wsl, dc, :],
                                        s["act_a"][:, dc, tsl],
                                        start=(dc == 0), stop=(dc == NDC - 1))
                    p.pe(mm, reads=(f"w1t{wsl}", f"act_a.{dc}"), writes=(f"pt{h2}",),
                         tag=f"fn{bi}.h{tcb}.{fc}.{h2}.{dc}")

                def gelu(e, pb=pb, fc=fc, h2=h2):
                    return e.activation(s["hbuf"][:, fc, h2 * 512:(h2 + 1) * 512],
                                        pb[:, :], ACTF.Gelu_apprx_tanh,
                                        bias=s["bias"][:, K.B_B1 + fc:K.B_B1 + fc + 1])
                p.act(gelu, reads=(f"pt{h2}", "bias"), writes=(f"hbuf.{fc}",),
                      tag=f"fn{bi}.g{tcb}.{fc}.{h2}")
        for dc in range(NDC):
            for hf in range(2):
                p.dma("pl_dma" if hf else "sp_dma",
                      s["w2t"][:, hf * (NFC // 2):(hf + 1) * (NFC // 2), :],
                      d["w2b"][:, hf * (NFC // 2):(hf + 1) * (NFC // 2),
                               dc * 128:(dc + 1) * 128],
                      reads=tuple(f"w2b.{b}.{dc * 128 // min(k.T, D)}"
                                  for b in range(hf * (NFC // 2),
                                                 (hf + 1) * (NFC // 2))),
                      writes=(f"w2t{hf}",), tag=f"fn{bi}.w2.{tcb}.{dc}.{hf}")
            for h2 in range(LC // 512):
                pb = k.psum[f"pb{2 + h2}"]
                for fc in range(NFC):
                    def mm(e, pb=pb, fc=fc, h2=h2):
                        return e.matmul(pb[:, :], s["w2t"][:, fc, :],
                                        s["hbuf"][:, fc, h2 * 512:(h2 + 1) * 512],
                                        start=(fc == 0), stop=(fc == NFC - 1))
                    p.pe(mm, reads=(f"w2t{fc // (NFC // 2)}", f"hbuf.{fc}"),
                         writes=(f"pt{2 + h2}",),
                         tag=f"fn{bi}.y{tcb}.{dc}.{h2}.{fc}")

                def silu(e, pb=pb, dc=dc, h2=h2):
                    return e.activation(s["ev"][:, h2, :], pb[:, :], ACTF.Silu,
                                        bias=s["bias"][:, K.B_B2 + dc:K.B_B2 + dc + 1])
                p.act(silu, reads=(f"pt{2 + h2}", "bias"), writes=(f"ev{h2}",),
                      tag=f"fn{bi}.s{tcb}.{dc}.{h2}")
                tc = tcb * (LC // 512) + h2
                p.dma("sp_dma", d["g0"][bi, dc, :, tc * 512:(tc + 1) * 512],
                      s["ev"][:, h2, :], reads=(f"ev{h2}",),
                      writes=(f"g0.{bi}.{dc}.{tc}",), tag=f"fn{bi}.gs{tcb}.{dc}.{h2}")


def emit_xq_qkv(k, bi):
    p, s, d = k.p, k.s, k.d
    T, NTC = k.T, k.NTC
    st = s["stat"]
    for dc in range(NDC):
        for tc in range(NTC):
            sl = (dc * NTC + tc) % 2
            p.dma("sp_dma", s["fx"][:, sl, 0:512],
                  d["xn32"][bi, dc, :, tc * 512:(tc + 1) * 512],
                  reads=(f"xn32.{bi}.{dc}.{tc}",), writes=(f"fx{sl}",),
                  tag=f"xq{bi}.{dc}.{tc}")
            quant_chain(k, s["fx"][:, sl, 0:512],
                        s["act_a"][:, dc, tc * 512:(tc + 1) * 512],
                        st[:, K.S_CINV + dc:K.S_CINV + dc + 1],
                        st[:, K.S_CZP + dc:K.S_CZP + dc + 1],
                        st[:, K.S_CS + dc:K.S_CS + dc + 1],
                        st[:, K.S_CNZS + dc:K.S_CNZS + dc + 1],
                        reads=(f"fx{sl}", "chanpar"), writes=(f"act_a.{dc}",),
                        sl=sl, tag=f"xq{bi}.{dc}.{tc}")
    p.fence(reads=(), writes=("w1t0", "w1t1", "w2t0", "w2t1", "wvb0"),
            tag=f"f.v{bi}")
    for dc in range(NDC):
        p.dma("pl_dma", s["wvb0"][:, dc, :], d["wqT_v"][dc, :, :],
              reads=tuple(f"wv.{dc}.{j}" for j in range(8)),
              writes=("wvb0",), tag=f"qv{bi}.w{dc}")
    for fc in range(16):
        wsl = fc % 2
        p.dma("sp_dma", s["wtile"][:, wsl, :, :],
              d["wqT_qk"][fc, :, :, :].rearrange("a p j -> p a j"),
              reads=tuple(f"wqk.{fc}.{dcc}" for dcc in range(NDC)),
              writes=(f"wtile{wsl}",), tag=f"qk{bi}.w{fc}")
        for dc in range(NDC):
            for tc in range(NTC):
                def mm(e, wsl=wsl, dc=dc, tc=tc):
                    return e.matmul(k.psum[f"pb{tc}"][:, :], s["wtile"][:, wsl, dc, :],
                                    s["act_a"][:, dc, tc * 512:(tc + 1) * 512],
                                    start=(dc == 0), stop=(dc == NDC - 1))
                p.pe(mm, reads=(f"wtile{wsl}", f"act_a.{dc}"), writes=(f"pt{tc}",),
                     tag=f"qk{bi}.m{fc}.{dc}.{tc}")
        for tc in range(NTC):
            def evac(e, fc=fc, tc=tc):
                return e.activation(s["ev"][:, tc % 2, :], k.psum[f"pb{tc}"][:, :],
                                    ACTF.Identity,
                                    bias=s["bias"][:, K.B_L0QKV + fc:K.B_L0QKV + fc + 1])
            p.act(evac, reads=(f"pt{tc}", "bias"), writes=(f"ev{tc % 2}",),
                  tag=f"qk{bi}.e{fc}.{tc}")

            def mnmx(e, fc=fc, tc=tc):
                cx = K.S_RMX + (0 if fc < 8 else 1)
                cn = K.S_RMN + (0 if fc < 8 else 1)
                e.tensor_reduce(s["small"][:, 2:3], s["ev"][:, tc % 2, :], AX.X, ALU.max)
                e.drain()
                e.tensor_reduce(s["small"][:, 3:4], s["ev"][:, tc % 2, :], AX.X, ALU.min)
                e.drain()
                e.tensor_tensor(st[:, cx:cx + 1], st[:, cx:cx + 1],
                                s["small"][:, 2:3], ALU.max)
                e.drain()
                return e.tensor_tensor(st[:, cn:cn + 1], st[:, cn:cn + 1],
                                       s["small"][:, 3:4], ALU.min)
            p.dve(mnmx, reads=(f"ev{tc % 2}",), writes=("qkvstat", "small"),
                  tag=f"qk{bi}.n{fc}.{tc}")
            p.dma("sp_dma", d["qkraw"][bi, fc, tc, :, :], s["ev"][:, tc % 2, :],
                  reads=(f"ev{tc % 2}",), writes=(f"qkraw.{bi}.{fc}.{tc}",),
                  tag=f"qk{bi}.s{fc}.{tc}")
    for t128 in range(k.NT128):
        for fv in range(2):
            pq = 4 + (2 * t128 + fv) % 2
            pb = k.psum[f"pb{pq}"]
            for dc in range(NDC):
                def mm(e, pb=pb, t128=t128, fv=fv, dc=dc):
                    return e.matmul(pb[:, :],
                                    s["act_a"][:, dc, t128 * 128:(t128 + 1) * 128],
                                    s["wvb0"][:, dc, fv * 512:(fv + 1) * 512],
                                    start=(dc == 0), stop=False)
                p.pe(mm, reads=("wvb0", f"act_a.{dc}"), writes=(f"pt{pq}",),
                     tag=f"qv{bi}.m{t128}.{fv}.{dc}")

            def mmb(e, pb=pb, fv=fv):
                return e.matmul(pb[:, :], s["ones_b"][0:1, 0:128],
                                s["brow"][0:1, 0, fv * 512:(fv + 1) * 512],
                                start=False, stop=True)
            p.pe(mmb, reads=("brow0", "ones_b"), writes=(f"pt{pq}",),
                 tag=f"qv{bi}.b{t128}.{fv}")

            def evac(e, pb=pb, fv=fv):
                return e.tensor_copy(s["ev"][:, fv, :], pb[:, :])
            p.dve(evac, reads=(f"pt{pq}",), writes=(f"ev{fv}",),
                  tag=f"qv{bi}.e{t128}.{fv}")

            def mnmx(e, fv=fv):
                e.tensor_reduce(s["small"][:, 4:5], s["ev"][:, fv, :], AX.X, ALU.max)
                e.drain()
                e.tensor_reduce(s["small"][:, 5:6], s["ev"][:, fv, :], AX.X, ALU.min)
                e.drain()
                cx, cn = K.S_RMX + 2, K.S_RMN + 2
                e.tensor_tensor(st[:, cx:cx + 1], st[:, cx:cx + 1],
                                s["small"][:, 4:5], ALU.max)
                e.drain()
                return e.tensor_tensor(st[:, cn:cn + 1], st[:, cn:cn + 1],
                                       s["small"][:, 5:6], ALU.min)
            p.dve(mnmx, reads=(f"ev{fv}",), writes=("qkvstat", "small"),
                  tag=f"qv{bi}.n{t128}.{fv}")
            p.dma("sp_dma", d["vraw"][bi, t128, fv, :, :], s["ev"][:, fv, :],
                  reads=(f"ev{fv}",), writes=(f"vraw.{bi}.{t128}.{fv}",),
                  tag=f"qv{bi}.s{t128}.{fv}")


def bcast_params(k, pbname, src_res, dst_fn, tag):
    pass


def emit_ar2(k):
    p, s, d = k.p, k.s, k.d
    st = s["stat"]

    def pack(e):
        e.memset(s["tmp8"][:, 38:40], 0.0)
        e.drain()
        e.tensor_copy(s["tmp8"][:, 32:35], st[:, K.S_RMX:K.S_RMX + 3])
        e.drain()
        return e.tensor_scalar(s["tmp8"][:, 35:38], st[:, K.S_RMN:K.S_RMN + 3],
                               -1.0, None, ALU.mult)
    p.dve(pack, reads=("qkvstat",), writes=("tmp8c",), tag="ar2.pk")

    def tp(e):
        return e.transpose(k.psum["pb6"][0:8, 0:128], s["tmp8"][:, 32:40],
                           s["idn_f"][:, :])
    p.pe(tp, reads=("tmp8c", "idn_f"), writes=("pt6",), tag="ar2.tp")

    def red(e):
        return e.tensor_reduce(s["small"][0:8, 6:7], k.psum["pb6"][0:8, 0:128],
                               AX.X, ALU.max)
    p.dve(red, reads=("pt6",), writes=("small",), tag="ar2.rd")
    p.dma("pl_dma", d["ar2_in"][:, :], s["small"][0:8, 6:7], reads=("small",),
          writes=("ar2_in",), tag="ar2.st")
    p.collective(ALU.max, [list(range(k.n_cores))],
                 d["ar2_in"].ap().opt(), d["ar2_out"].ap().opt(),
                 reads=("ar2_in",), writes=("ar2_out",), tag="ar2.cc")
    p.dma("pl_dma", s["small"][0:8, 7:8], d["ar2_out"][:, :], reads=("ar2_out",),
          writes=("small78",), tag="ar2.ld")

    def tp2(e):
        return e.transpose(k.psum["pb7"][0:1, 0:8], s["small"][0:8, 7:8],
                           s["idn_f"][0:8, 0:8])
    p.pe(tp2, reads=("small78", "idn_f"), writes=("pt7",), tag="ar2.tp2")

    def ev7(e):
        return e.tensor_copy(s["small"][0:1, 8:16], k.psum["pb7"][0:1, 0:8])
    p.dve(ev7, reads=("pt7",), writes=("smallrow",), tag="ar2.e7")

    def bc(e):
        return e.matmul(k.psum["pb7"][:, 0:8], s["ones_f"][0:1, :],
                        s["small"][0:1, 8:16], start=True, stop=True)
    p.pe(bc, reads=("smallrow", "ones_f"), writes=("pt7",), tag="ar2.bc")

    def par(e):
        qp = s["qp"]
        pbv = k.psum["pb7"]
        sm = s["small"]
        e.tensor_scalar(sm[:, 9:12], pbv[:, 3:6], -1.0, None, ALU.mult)
        e.drain()
        e.tensor_scalar(sm[:, 9:12], sm[:, 9:12], 0.0, None, ALU.min)
        e.drain()
        e.tensor_scalar(sm[:, 12:15], pbv[:, 0:3], 0.0, None, ALU.max)
        e.drain()
        e.tensor_sub(sm[:, 15:18], sm[:, 12:15], sm[:, 9:12])
        e.drain()
        e.tensor_scalar(qp[:, K.QP_S:K.QP_S + 3], sm[:, 15:18], 1.0 / 255.0, 1e-8,
                        ALU.mult, ALU.max)
        e.drain()
        recip_acc(e, qp[:, K.QP_INV:K.QP_INV + 3],
                                     qp[:, K.QP_S:K.QP_S + 3], sm[:, 15:18])
        e.drain()
        e.tensor_mul(sm[:, 15:18], sm[:, 9:12], qp[:, K.QP_INV:K.QP_INV + 3])
        e.drain()
        e.tensor_scalar(s["qi"][:, 0, 0:3], sm[:, 15:18], -1.0, None, ALU.mult)
        e.drain()
        e.tensor_copy(qp[:, K.QP_ZP:K.QP_ZP + 3], s["qi"][:, 0, 0:3])
        e.drain()
        e.tensor_mul(sm[:, 15:18], qp[:, K.QP_ZP:K.QP_ZP + 3], qp[:, K.QP_S:K.QP_S + 3])
        e.drain()
        return e.tensor_scalar(qp[:, K.QP_NZS:K.QP_NZS + 3], sm[:, 15:18],
                               -1.0, None, ALU.mult)
    p.dve(par, reads=("pt7",), writes=("qpar", "small", "qi0"), tag="ar2.par")


def emit_l0_attention(k, bi):
    p, s, d = k.p, k.s, k.d
    T, NTC, NT128 = k.T, k.NTC, k.NT128
    qp = s["qp"]
    if bi == 0:
        p.fence(reads=(), writes=("wvb0", "w1t0", "w1t1", "w2t",
                       *(f"hbuf.{fc}" for fc in range(NFC)),
                       "vaug.ones", "nrmX"), tag="f.attn0")

        def vm(e):
            return e.memset(s["vaug"][:, :, :, :], 1.0)
        p.dve(vm, reads=("vaug.ones",), writes=("vaug.ones",), tag="vmemset")
    for sec, dst, base, ti in (("k", "act_a", 8, 1), ("q", "act_b", 0, 0)):
        for fcl in range(8):
            fc = base + fcl
            for tc in range(NTC):
                sl = (fcl * NTC + tc) % 2
                p.dma("sp_dma", s["fx"][:, sl, 0:512], d["qkraw"][bi, fc, tc, :, :],
                      reads=(f"qkraw.{bi}.{fc}.{tc}",), writes=(f"fx{sl}",),
                      tag=f"a{bi}.l{sec}{fcl}.{tc}")
                quant_chain(k, s["fx"][:, sl, 0:512],
                            s[dst][:, fcl, tc * 512:(tc + 1) * 512],
                            qp[:, K.QP_INV + ti:K.QP_INV + ti + 1],
                            qp[:, K.QP_ZP + ti:K.QP_ZP + ti + 1],
                            qp[:, K.QP_S + ti:K.QP_S + ti + 1],
                            qp[:, K.QP_NZS + ti:K.QP_NZS + ti + 1],
                            reads=(f"fx{sl}", "qpar"), writes=(f"{dst}.{fcl}",),
                            sl=sl, tag=f"a{bi}.q{sec}{fcl}.{tc}")
    for t128 in range(NT128):
        for fv in range(2):
            sl = (t128 * 2 + fv) % 2
            p.dma("sp_dma", s["fx"][:, sl, 0:512], d["vraw"][bi, t128, fv, :, :],
                  reads=(f"vraw.{bi}.{t128}.{fv}",), writes=(f"fx{sl}",),
                  tag=f"a{bi}.lv{t128}.{fv}")
            quant_chain(k, s["fx"][:, sl, 0:512],
                        s["vaug"][:, t128, fv * 8:(fv + 1) * 8, 0:64],
                        qp[:, K.QP_INV + 2:K.QP_INV + 3],
                        qp[:, K.QP_ZP + 2:K.QP_ZP + 3],
                        qp[:, K.QP_S + 2:K.QP_S + 3],
                        qp[:, K.QP_NZS + 2:K.QP_NZS + 3],
                        reads=(f"fx{sl}", "qpar"), writes=(f"vg.{t128 % 4}",),
                        sl=sl, tag=f"a{bi}.qv{t128}.{fv}")
    if bi == 0 and "dbg_ka" in K.DEBUG_OUT:
        for dc in range(NDC):
            p.dma("pl_dma", d["dbg_ka"][:, dc, :], s["act_a"][:, dc, :],
                  reads=(f"act_a.{dc}",), writes=("dbgka",), tag=f"dbg.ka{dc}")
            p.dma("pl_dma", d["dbg_qa"][:, dc, :], s["act_b"][:, dc, :],
                  reads=(f"act_b.{dc}",), writes=("dbgqa",), tag=f"dbg.qa{dc}")
    attention_core(k, bi, layer=0, Ks=s["act_a"], Kres="act_a",
                   Qs=s["act_b"], Qres="act_b", nrm=s["nrm0"], nres="nrmX")


def attention_core(k, bi, layer, Ks, Kres, Qs, Qres, nrm, nres):
    p, s, d = k.p, k.s, k.d
    T, NTC, NT128 = k.T, k.NTC, k.NT128
    st = s["stat"]
    for idx, (src, sres) in enumerate(((Qs, Qres), (Ks, Kres))):
        off = 0 if idx == 0 else T
        for tc in range(NTC):
            for dc in range(NDC):
                def sqr(e, src=src, dc=dc, tc=tc):
                    return e.activation(s["sq"][:, 0, 0:512],
                                        src[:, dc, tc * 512:(tc + 1) * 512],
                                        ACTF.Square)
                p.act(sqr, reads=(f"{sres}.{dc}",), writes=("sq0",),
                      tag=f"n{layer}{bi}.s{idx}.{tc}.{dc}")

                def mm(e, dc=dc):
                    return e.matmul(k.psum["pb6"][0:16, :], s["hind"][:, dc, :],
                                    s["sq"][:, 0, 0:512],
                                    start=(dc == 0), stop=(dc == NDC - 1))
                p.pe(mm, reads=("sq0", "hind"), writes=("pt6",),
                     tag=f"n{layer}{bi}.m{idx}.{tc}.{dc}")

            def rt(e, off=off, tc=tc):
                return e.activation(nrm[0:16, off + tc * 512:off + (tc + 1) * 512],
                                    k.psum["pb6"][0:16, :], ACTF.Sqrt)
            p.act(rt, reads=("pt6",), writes=(nres,), tag=f"n{layer}{bi}.r{idx}.{tc}")

    if layer == 0 and bi == 0 and "dbg_hind" in K.DEBUG_OUT:
        p.dma("pl_dma", d["dbg_hind"][:, :, :], s["hind"][:, :, :],
              reads=("hind",), writes=("hind",), tag="dbg.hind")
    if layer == 0 and bi == 0 and "dbg_nrm" in K.DEBUG_OUT:
        p.dma("pl_dma", d["dbg_nrm"][:, 0, 0:T], nrm[0:16, 0:T],
              reads=(nres,), writes=(nres,), tag="dbg.nrmq")
        p.dma("pl_dma", d["dbg_nrm"][:, 1, 0:T], nrm[0:16, T:2 * T],
              reads=(nres,), writes=(nres,), tag="dbg.nrmk")

    def cmax(e):
        e.tensor_reduce(s["small"][0:16, 20:21], nrm[0:16, 0:T], AX.X, ALU.max)
        e.drain()
        e.tensor_reduce(s["small"][0:16, 21:22], nrm[0:16, T:2 * T], AX.X, ALU.max)
        e.drain()
        e.tensor_mul(s["small"][0:16, 22:23], s["small"][0:16, 20:21],
                     s["small"][0:16, 21:22])
        e.drain()
        return e.tensor_scalar(s["small"][0:16, 23:24], s["small"][0:16, 22:23],
                               -0.125, None, ALU.mult)
    p.dve(cmax, reads=(nres,), writes=("small",), tag=f"c{layer}{bi}.mx")

    if layer == 0 and bi == 0 and "dbg_sm" in K.DEBUG_OUT:
        p.dma("sp_dma", d["dbg_sm"][:, :], s["small"][0:16, 20:24],
              reads=("small",), writes=("small",), tag="dbg.sm")

    def ctp(e):
        return e.transpose(k.psum["pb7"][0:1, 0:16], s["small"][0:16, 23:24],
                           s["idn_f"][0:16, 0:16])
    p.pe(ctp, reads=("small", "idn_f"), writes=("pt7",), tag=f"c{layer}{bi}.tp")

    def cev(e):
        return e.tensor_copy(s["small"][0:1, 32:48], k.psum["pb7"][0:1, 0:16])
    p.dve(cev, reads=("pt7",), writes=("small",), tag=f"c{layer}{bi}.ev")

    def cbc(e):
        return e.matmul(k.psum["pb7"][:, 0:16], s["ones_f"][0:1, :],
                        s["small"][0:1, 32:48], start=True, stop=True)
    p.pe(cbc, reads=("small", "ones_f"), writes=("pt7",), tag=f"c{layer}{bi}.bc")

    def cbe(e):
        return e.tensor_copy(s["cbias"][:, :], k.psum["pb7"][:, 0:16])
    p.dve(cbe, reads=("pt7",), writes=("cbias",), tag=f"c{layer}{bi}.eb")
    if layer == 0 and bi == 0 and "dbg_cb" in K.DEBUG_OUT:
        p.dma("sp_dma", d["dbg_cb"][:, :], s["cbias"][:, :],
              reads=("cbias",), writes=("cbias",), tag="dbg.cb")

    for hp in range(8):
        for qc in range(NTC):
            qsl = slice(qc * 512, (qc + 1) * 512)
            NJ = NT128 // 2

            def emit_S(j, sub):
                h = 2 * hp + sub
                po = 64 * sub
                for half in range(2):
                    kc = 2 * j + half

                    def mmS(e, hp=hp, po=po, kc=kc, qsl=qsl,
                            sub=sub, half=half):
                        return e.matmul(
                            k.psum[f"pb{2 + 2 * sub}{3 + 2 * sub}"][
                                :, 512 * half:512 * (half + 1)],
                            Ks[po:po + 64, hp, kc * 128:(kc + 1) * 128],
                            Qs[po:po + 64, hp, qsl],
                            start=True, stop=True)
                    p.pe(mmS, reads=(f"{Kres}.{hp}", f"{Qres}.{hp}"),
                         writes=(f"pt{2 + 2 * sub + half}",),
                         tag=f"A{layer}{bi}.S{hp}.{sub}.{qc}.{kc}")

            def emit_E(j, sub):
                h = 2 * hp + sub

                def expv(e, h=h, sub=sub):
                    return e.activation(s["ptile"][:, sub, :],
                                        k.psum[f"pb{2 + 2 * sub}{3 + 2 * sub}"][:, :],
                                        ACTF.Exp, scale=s["cst"][:, 1:2],
                                        bias=s["cbias"][:, h:h + 1])
                p.act(expv, reads=(f"pt{2 + 2 * sub}", f"pt{3 + 2 * sub}",
                                   "cbias"),
                      writes=(f"ptile{2 * sub}", f"ptile{2 * sub + 1}"),
                      tag=f"A{layer}{bi}.E{hp}.{sub}.{qc}.{j}")

            def emit_PV(j, sub):
                h = 2 * hp + sub
                pacc = k.psum[f"pb{sub}"]
                for half in range(2):
                    kc = 2 * j + half

                    def mmPV(e, h=h, kc=kc, sub=sub, half=half, pacc=pacc):
                        return e.matmul(pacc[0:66, :],
                                        s["vaug"][:, kc, h, 0:66],
                                        s["ptile"][:, sub, 512 * half:512 * (half + 1)],
                                        start=(kc == 0), stop=(kc == NT128 - 1))
                    p.pe(mmPV, reads=(f"ptile{2 * sub + half}", f"vg.{kc % 4}",
                                      "vaug.ones"),
                         writes=(f"pt{sub}",),
                         tag=f"A{layer}{bi}.P{hp}.{sub}.{qc}.{kc}")

            emit_S(0, 0)
            emit_S(0, 1)
            for j in range(NJ):
                for sub in range(2):
                    emit_E(j, sub)
                    if j + 1 < NJ:
                        emit_S(j + 1, sub)
                    emit_PV(j, sub)

            def dcp0(e):
                return e.tensor_copy(s["den"][64:65, :512], k.psum["pb0"][64:65, :])
            p.dve(dcp0, reads=("pt0",), writes=("den",),
                  tag=f"A{layer}{bi}.D{hp}.0.{qc}")

            def dcp1(e):
                return e.tensor_copy(s["rsc"][64:65, 0:512], k.psum["pb1"][64:65, :])
            p.dve(dcp1, reads=("pt1",), writes=("rsc",),
                  tag=f"A{layer}{bi}.D{hp}.1.{qc}")

            def dbc0(e):
                return e.matmul(k.psum["pb67"][0:64, 0:512],
                                s["onesf64"][64:65, 0:64],
                                s["den"][64:65, :512], start=True, stop=True)
            p.pe(dbc0, reads=("den", "onesf64"), writes=("pt6",),
                 tag=f"A{layer}{bi}.B{hp}.0.{qc}")

            def dbc1(e):
                return e.matmul(k.psum["pb67"][0:64, 512:1024],
                                s["onesf64"][64:65, 0:64],
                                s["rsc"][64:65, 0:512], start=True, stop=True)
            p.pe(dbc1, reads=("rsc", "onesf64"), writes=("pt7",),
                 tag=f"A{layer}{bi}.B{hp}.1.{qc}")

            def rcd(e):
                e.activation(s["ev"][0:64, 2:4, :], k.psum["pb67"][0:64, :],
                             ACTF.Ln)
                e.drain()
                e.activation(s["rec"][0:64, 0:512], s["ev"][0:64, 2, :],
                             ACTF.Exp, scale=s["cst"][0:64, 2:3],
                             bias=s["cst"][0:64, 3:4])
                return e.activation(s["rsc"][0:64, 0:512], s["ev"][0:64, 3, :],
                                    ACTF.Exp, scale=s["cst"][0:64, 2:3],
                                    bias=s["cst"][0:64, 3:4])
            p.act(rcd, reads=("pt6", "pt7", "cst"),
                  writes=("rec", "rsc", "ev2", "ev3"),
                  tag=f"A{layer}{bi}.R{hp}.{qc}")

            for sub in range(2):
                h = 2 * hp + sub
                po = 64 * sub
                pacc = k.psum[f"pb{sub}"]
                rtile = "rec" if sub == 0 else "rsc"
                if layer == 0:
                    def div(e, pacc=pacc, sub=sub, rtile=rtile):
                        return e.tensor_mul(s["ev"][0:64, sub, :], pacc[0:64, :],
                                            s[rtile][0:64, 0:512])
                    p.dve(div, reads=(f"pt{sub}", rtile), writes=(f"ev{sub}",),
                          tag=f"A{layer}{bi}.V{hp}.{sub}.{qc}")

                    def omn(e, sub=sub):
                        e.tensor_reduce(s["small"][0:64, 24:25], s["ev"][0:64, sub, :],
                                        AX.X, ALU.max)
                        e.drain()
                        e.tensor_reduce(s["small"][0:64, 25:26], s["ev"][0:64, sub, :],
                                        AX.X, ALU.min)
                        e.drain()
                        e.tensor_tensor(st[0:64, K.S_OMX:K.S_OMX + 1],
                                        st[0:64, K.S_OMX:K.S_OMX + 1],
                                        s["small"][0:64, 24:25], ALU.max)
                        e.drain()
                        return e.tensor_tensor(st[0:64, K.S_OMN:K.S_OMN + 1],
                                               st[0:64, K.S_OMN:K.S_OMN + 1],
                                               s["small"][0:64, 25:26], ALU.min)
                    p.dve(omn, reads=(f"ev{sub}",), writes=("ostat", "small"),
                          tag=f"A{layer}{bi}.O{hp}.{sub}.{qc}")
                    p.dma("sp_dma", d["oraw"][bi, h, qc, :, :], s["ev"][0:64, sub, :],
                          reads=(f"ev{sub}",), writes=(f"oraw.{bi}.{h}.{qc}",),
                          tag=f"A{layer}{bi}.W{hp}.{sub}.{qc}")
                else:
                    def divb(e, pacc=pacc, po=po, hp=hp, qsl=qsl, rtile=rtile):
                        return e.tensor_mul(s["act_a"][po:po + 64, hp, qsl],
                                            pacc[0:64, :], s[rtile][0:64, 0:512])
                    p.dve(divb, reads=(f"pt{sub}", rtile), writes=(f"act_a.{hp}",),
                          tag=f"A{layer}{bi}.V{hp}.{sub}.{qc}")


def emit_ar3(k):
    p, s, d = k.p, k.s, k.d
    st = s["stat"]

    def pack(e):
        e.tensor_copy(s["tmp8"][0:64, 40:41], st[0:64, K.S_OMX:K.S_OMX + 1])
        e.drain()
        return e.tensor_scalar(s["tmp8"][0:64, 41:42], st[0:64, K.S_OMN:K.S_OMN + 1],
                               -1.0, None, ALU.mult)
    p.dve(pack, reads=("ostat",), writes=("tmp8d",), tag="ar3.pk")

    def tp(e):
        return e.transpose(k.psum["pb6"][0:2, 0:64], s["tmp8"][0:64, 40:42],
                           s["idn_f"][0:64, 0:64])
    p.pe(tp, reads=("tmp8d", "idn_f"), writes=("pt6",), tag="ar3.tp")

    def red(e):
        return e.tensor_reduce(s["small"][0:2, 28:29], k.psum["pb6"][0:2, 0:64],
                               AX.X, ALU.max)
    p.dve(red, reads=("pt6",), writes=("small",), tag="ar3.rd")
    p.dma("pl_dma", d["ar3_in"][:, :], s["small"][0:2, 28:29], reads=("small",),
          writes=("ar3_in",), tag="ar3.st")
    p.collective(ALU.max, [list(range(k.n_cores))],
                 d["ar3_in"].ap().opt(), d["ar3_out"].ap().opt(),
                 reads=("ar3_in",), writes=("ar3_out",), tag="ar3.cc")
    p.dma("pl_dma", s["small"][0:2, 29:30], d["ar3_out"][:, :], reads=("ar3_out",),
          writes=("small29",), tag="ar3.ld")

    def tp2(e):
        return e.transpose(k.psum["pb7"][0:1, 0:2], s["small"][0:2, 29:30],
                           s["idn_f"][0:2, 0:2])
    p.pe(tp2, reads=("small29", "idn_f"), writes=("pt7",), tag="ar3.tp2")

    def ev(e):
        return e.tensor_copy(s["small"][0:1, 26:28], k.psum["pb7"][0:1, 0:2])
    p.dve(ev, reads=("pt7",), writes=("small",), tag="ar3.ev")

    def bc(e):
        return e.matmul(k.psum["pb7"][:, 0:2], s["ones_f"][0:1, :],
                        s["small"][0:1, 26:28], start=True, stop=True)
    p.pe(bc, reads=("small", "ones_f"), writes=("pt7",), tag="ar3.bc")

    def par(e):
        qp = s["qp"]
        pbv = k.psum["pb7"]
        sm = s["small"]
        e.tensor_scalar(sm[:, 9:10], pbv[:, 1:2], -1.0, None, ALU.mult)
        e.drain()
        e.tensor_scalar(sm[:, 9:10], sm[:, 9:10], 0.0, None, ALU.min)
        e.drain()
        e.tensor_scalar(sm[:, 10:11], pbv[:, 0:1], 0.0, None, ALU.max)
        e.drain()
        e.tensor_sub(sm[:, 11:12], sm[:, 10:11], sm[:, 9:10])
        e.drain()
        e.tensor_scalar(qp[:, K.QP_OS:K.QP_OS + 1], sm[:, 11:12], 1.0 / 255.0, 1e-8,
                        ALU.mult, ALU.max)
        e.drain()
        recip_acc(e, qp[:, K.QP_OINV:K.QP_OINV + 1],
                                     qp[:, K.QP_OS:K.QP_OS + 1], sm[:, 11:12])
        e.drain()
        e.tensor_mul(sm[:, 11:12], sm[:, 9:10], qp[:, K.QP_OINV:K.QP_OINV + 1])
        e.drain()
        e.tensor_scalar(s["qi"][:, 0, 0:1], sm[:, 11:12], -1.0, None, ALU.mult)
        e.drain()
        e.tensor_copy(qp[:, K.QP_OZP:K.QP_OZP + 1], s["qi"][:, 0, 0:1])
        e.drain()
        e.tensor_mul(sm[:, 11:12], qp[:, K.QP_OZP:K.QP_OZP + 1],
                     qp[:, K.QP_OS:K.QP_OS + 1])
        e.drain()
        return e.tensor_scalar(qp[:, K.QP_ONZS:K.QP_ONZS + 1], sm[:, 11:12],
                               -1.0, None, ALU.mult)
    p.dve(par, reads=("pt7",), writes=("opar", "small", "qi0"), tag="ar3.par")


def emit_l0_proj_update(k, bi):
    p, s, d = k.p, k.s, k.d
    T, NTC = k.T, k.NTC
    qp = s["qp"]
    for hh in range(H):
        dc, po = hh // 2, 64 * (hh % 2)
        for qc in range(NTC):
            sl = (hh * NTC + qc) % 2
            p.dma("sp_dma" if sl == 0 else "pl_dma",
                  s["fx"][0:64, sl, 0:512], d["oraw"][bi, hh, qc, :, :],
                  reads=(f"oraw.{bi}.{hh}.{qc}",), writes=(f"fx{sl}",),
                  tag=f"o{bi}.{hh}.{qc}")
            quant_chain(k, s["fx"][0:64, sl, 0:512],
                        s["act_a"][po:po + 64, dc, qc * 512:(qc + 1) * 512],
                        qp[0:64, K.QP_OINV:K.QP_OINV + 1],
                        qp[0:64, K.QP_OZP:K.QP_OZP + 1],
                        qp[0:64, K.QP_OS:K.QP_OS + 1],
                        qp[0:64, K.QP_ONZS:K.QP_ONZS + 1],
                        reads=(f"fx{sl}", "opar"), writes=(f"act_a.{dc}",),
                        sl=sl, tag=f"o{bi}.{hh}.{qc}", np_=64)
    for fc in range(NDC):
        wsl = fc % 2
        p.dma("sp_dma", s["wtile"][:, wsl, :, :],
              d["pwT"][fc, :, :, :].rearrange("a p j -> p a j"),
              reads=tuple(f"pw.{fc}.{dcc}" for dcc in range(NDC)),
              writes=(f"wtile{wsl}",), tag=f"pj{bi}.w{fc}")
        for dc in range(NDC):
            for tc in range(NTC):
                def mm(e, wsl=wsl, dc=dc, tc=tc):
                    return e.matmul(k.psum[f"pb{tc}"][:, :], s["wtile"][:, wsl, dc, :],
                                    s["act_a"][:, dc, tc * 512:(tc + 1) * 512],
                                    start=(dc == 0), stop=(dc == NDC - 1))
                p.pe(mm, reads=(f"wtile{wsl}", f"act_a.{dc}"), writes=(f"pt{tc}",),
                     tag=f"pj{bi}.m{fc}.{dc}.{tc}")
        for tc in range(NTC):
            def evac(e, fc=fc, tc=tc):
                return e.activation(s["ev"][:, tc % 2, :], k.psum[f"pb{tc}"][:, :],
                                    ACTF.Identity,
                                    bias=s["bias"][:, K.B_L0PROJ + fc:K.B_L0PROJ + fc + 1])
            p.act(evac, reads=(f"pt{tc}", "bias"), writes=(f"ev{tc % 2}",),
                  tag=f"pj{bi}.e{fc}.{tc}")
            sl = tc % 2
            p.dma("sp_dma" if sl == 0 else "pl_dma", s["fx"][:, sl, 0:512],
                  d["x_fm"][bi, fc, :, tc * 512:(tc + 1) * 512],
                  writes=(f"fx{sl}",), tag=f"pj{bi}.x{fc}.{tc}")
            p.dma("pl_dma" if sl == 0 else "sp_dma", s["fy"][:, sl, :],
                  d["g0"][bi, fc, :, tc * 512:(tc + 1) * 512],
                  reads=(f"g0.{bi}.{fc}.{tc}",), writes=(f"fy{sl}",),
                  tag=f"pj{bi}.g{fc}.{tc}")

            def upd(e, sl=sl):
                e.tensor_mul(s["ev"][:, sl, :], s["ev"][:, sl, :], s["fy"][:, sl, :])
                e.drain()
                return e.tensor_add(s["ev"][:, sl, :], s["ev"][:, sl, :],
                                    s["fx"][:, sl, 0:512])
            p.dve(upd, reads=(f"ev{sl}", f"fx{sl}", f"fy{sl}"), writes=(f"ev{sl}",),
                  tag=f"pj{bi}.u{fc}.{tc}")
            p.dma("sp_dma", d["xmid"][bi, fc, :, tc * 512:(tc + 1) * 512],
                  s["ev"][:, sl, :], reads=(f"ev{sl}",),
                  writes=(f"xmid.{bi}.{fc}.{tc}",), tag=f"pj{bi}.s{fc}.{tc}")


def emit_l1_stats(k, bi):
    p, s, d = k.p, k.s, k.d
    T, NTC, LC = k.T, k.NTC, k.LC
    nl = T // LC
    for dc in range(NDC):
        for li in range(nl):
            sl = (dc * nl + li) % 2
            p.dma("sp_dma", s["fx"][:, sl, 0:LC],
                  d["xmid"][bi, dc, :, li * LC:(li + 1) * LC],
                  reads=tuple(f"xmid.{bi}.{dc}.{li * (LC // 512) + j}"
                              for j in range(LC // 512)),
                  writes=(f"fx{sl}",), tag=f"t{bi}.x{dc}.{li}")

            def sqc(e, sl=sl, dc=dc, li=li):
                e.activation(s["sq"][:, sl, :], s["fx"][:, sl, 0:LC], ACTF.Square)
                e.drain()
                return e.activation(s["act_b"][:, dc, li * LC:(li + 1) * LC],
                                    s["fx"][:, sl, 0:LC], ACTF.Identity)
            p.act(sqc, reads=(f"fx{sl}",), writes=(f"sq{sl}", f"act_b.{dc}"),
                  tag=f"t{bi}.q{dc}.{li}")
            for q5 in range(LC // 512):
                tc = li * (LC // 512) + q5

                def mm1(e, dc=dc, li=li, q5=q5, tc=tc):
                    return e.matmul(k.psum[f"pb{tc}"][:, :], s["ones_b"][:, :],
                                    s["act_b"][:, dc, li * LC + q5 * 512:li * LC + (q5 + 1) * 512],
                                    start=(dc == 0), stop=(dc == NDC - 1))
                p.pe(mm1, reads=(f"act_b.{dc}", "ones_b"), writes=(f"pt{tc}",),
                     tag=f"t{bi}.mS{dc}.{tc}")

                def mm2(e, sl=sl, q5=q5, dc=dc, tc=tc):
                    return e.matmul(k.psum[f"pb{NTC + tc}"][:, :], s["ones_b"][:, :],
                                    s["sq"][:, sl, q5 * 512:(q5 + 1) * 512],
                                    start=(dc == 0), stop=(dc == NDC - 1))
                p.pe(mm2, reads=(f"sq{sl}", "ones_b"), writes=(f"pt{NTC + tc}",),
                     tag=f"t{bi}.mQ{dc}.{tc}")
    for tc in range(NTC):
        def stats(e, tc=tc):
            e.tensor_scalar(s["mur"][:, 0, :], k.psum[f"pb{tc}"][:, :],
                            1.0 / 1024.0, None, ALU.mult)
            e.drain()
            e.tensor_scalar(s["rr"][:, 0, :], k.psum[f"pb{NTC + tc}"][:, :],
                            1.0 / 1024.0, None, ALU.mult)
            e.drain()
            e.scalar_tensor_tensor(s["rsc"][:, 0:512], s["mur"][:, 0, :], 1.0,
                                   s["mur"][:, 0, :], ALU.mult, ALU.mult)
            e.drain()
            return e.tensor_sub(s["rsc"][:, 0:512], s["rr"][:, 0, :], s["rsc"][:, 0:512])
        p.dve(stats, reads=(f"pt{tc}", f"pt{NTC + tc}"), writes=("rsc", "mur0", "rr0"),
              tag=f"t{bi}.st{tc}")
        p.act(lambda e: e.activation(s["rsc"][:, 0:512], s["rsc"][:, 0:512], ACTF.Sqrt,
                                     bias=s["cst"][:, 0:1]),
              reads=("rsc", "cst"), writes=("rsc",), tag=f"t{bi}.sd{tc}")

        def rcp(e):
            recip_acc(e, s["rr"][:, 1, :], s["rsc"][:, 0:512], s["rec"][:, 0:512])
            e.drain()
            return e.tensor_mul(s["mur"][:, 1, :], s["mur"][:, 0, :], s["rr"][:, 1, :])
        p.dve(rcp, reads=("rsc", "mur0"), writes=("rr1", "mur1", "rec"),
              tag=f"t{bi}.rc{tc}")
        for dc in range(NDC):
            sl = (tc * NDC + dc) % 2
            p.dma("sp_dma", s["fx"][:, sl, 0:512],
                  d["xmid"][bi, dc, :, tc * 512:(tc + 1) * 512],
                  reads=(f"xmid.{bi}.{dc}.{tc}",), writes=(f"fx{sl}",),
                  tag=f"t{bi}.x2{dc}.{tc}")

            def app(e, sl=sl):
                e.tensor_mul(s["ev"][:, 3, :], s["fx"][:, sl, 0:512], s["rr"][:, 1, :])
                e.drain()
                return e.tensor_sub(s["fy"][:, sl, :], s["ev"][:, 3, :],
                                    s["mur"][:, 1, :])
            p.dve(app, reads=(f"fx{sl}", "rr1", "mur1"), writes=(f"fy{sl}", "ev3"),
                  tag=f"t{bi}.ap{dc}.{tc}")
            p.dma("sp_dma", d["xn132"][bi, dc, :, tc * 512:(tc + 1) * 512],
                  s["fy"][:, sl, :], reads=(f"fy{sl}",),
                  writes=(f"xn132.{bi}.{dc}.{tc}",), tag=f"t{bi}.w{dc}.{tc}")
    for dc in range(NDC):
        for li in range(nl):
            sl = (dc * nl + li) % 2
            p.dma("sp_dma", s["fx"][:, sl, 0:LC],
                  d["xn132"][bi, dc, :, li * LC:(li + 1) * LC],
                  reads=tuple(f"xn132.{bi}.{dc}.{li * (LC // 512) + j}"
                              for j in range(LC // 512)),
                  writes=(f"fx{sl}",), tag=f"u{bi}.x{dc}.{li}")

            def sqc(e, sl=sl, dc=dc, li=li):
                e.activation(s["sq"][:, sl, :], s["fx"][:, sl, 0:LC], ACTF.Square)
                e.drain()
                return e.activation(s["act_b"][:, dc, li * LC:(li + 1) * LC],
                                    s["fx"][:, sl, 0:LC], ACTF.Identity)
            p.act(sqc, reads=(f"fx{sl}",), writes=(f"sq{sl}", f"act_b.{dc}"),
                  tag=f"u{bi}.q{dc}.{li}")
            for q5 in range(LC // 512):
                tc = li * (LC // 512) + q5

                def mm1(e, dc=dc, li=li, q5=q5, tc=tc):
                    return e.matmul(k.psum[f"pb{tc}"][:, :], s["ones_b"][:, :],
                                    s["act_b"][:, dc, li * LC + q5 * 512:li * LC + (q5 + 1) * 512],
                                    start=(dc == 0), stop=(dc == NDC - 1))
                p.pe(mm1, reads=(f"act_b.{dc}", "ones_b"), writes=(f"pt{tc}",),
                     tag=f"u{bi}.mS{dc}.{tc}")

                def mm2(e, sl=sl, q5=q5, dc=dc, tc=tc):
                    return e.matmul(k.psum[f"pb{NTC + tc}"][:, :], s["ones_b"][:, :],
                                    s["sq"][:, sl, q5 * 512:(q5 + 1) * 512],
                                    start=(dc == 0), stop=(dc == NDC - 1))
                p.pe(mm2, reads=(f"sq{sl}", "ones_b"), writes=(f"pt{NTC + tc}",),
                     tag=f"u{bi}.mQ{dc}.{tc}")
    for tc in range(NTC):
        def stats(e, tc=tc):
            e.tensor_scalar(s["mur"][:, 0, :], k.psum[f"pb{tc}"][:, :],
                            1.0 / 1024.0, None, ALU.mult)
            e.drain()
            e.tensor_scalar(s["rr"][:, 0, :], k.psum[f"pb{NTC + tc}"][:, :],
                            1.0 / 1024.0, None, ALU.mult)
            e.drain()
            e.scalar_tensor_tensor(s["rsc"][:, 0:512], s["mur"][:, 0, :], 1.0,
                                   s["mur"][:, 0, :], ALU.mult, ALU.mult)
            e.drain()
            return e.tensor_sub(s["rsc"][:, 0:512], s["rr"][:, 0, :], s["rsc"][:, 0:512])
        p.dve(stats, reads=(f"pt{tc}", f"pt{NTC + tc}"), writes=("rsc", "mur0", "rr0"),
              tag=f"u{bi}.st{tc}")
        p.act(lambda e: e.activation(s["rsc"][:, 0:512], s["rsc"][:, 0:512], ACTF.Sqrt,
                                     bias=s["cst"][:, 0:1]),
              reads=("rsc", "cst"), writes=("rsc",), tag=f"u{bi}.sd{tc}")

        def rcp(e):
            recip_acc(e, s["rr"][:, 1, :], s["rsc"][:, 0:512], s["rec"][:, 0:512])
            e.drain()
            return e.tensor_mul(s["mur"][:, 1, :], s["mur"][:, 0, :], s["rr"][:, 1, :])
        p.dve(rcp, reads=("rsc", "mur0"), writes=("rr1", "mur1", "rec"),
              tag=f"u{bi}.rc{tc}")
        for dc in range(NDC):
            sl = (tc * NDC + dc) % 2
            p.dma("sp_dma", s["fx"][:, sl, 0:512],
                  d["xn132"][bi, dc, :, tc * 512:(tc + 1) * 512],
                  reads=(f"xn132.{bi}.{dc}.{tc}",), writes=(f"fx{sl}",),
                  tag=f"u{bi}.x2{dc}.{tc}")

            def app(e, sl=sl):
                e.tensor_mul(s["ev"][:, 3, :], s["fx"][:, sl, 0:512], s["rr"][:, 1, :])
                e.drain()
                return e.tensor_sub(s["ev"][:, 3, :], s["ev"][:, 3, :],
                                    s["mur"][:, 1, :])
            p.dve(app, reads=(f"fx{sl}", "rr1", "mur1"), writes=("ev3",),
                  tag=f"u{bi}.a2{dc}.{tc}")

            def aff(e, dc=dc, tc=tc):
                return e.activation(s["act_a"][:, dc, tc * 512:(tc + 1) * 512],
                                    s["ev"][:, 3, :], ACTF.Identity,
                                    bias=s["bias"][:, K.B_LNB + dc:K.B_LNB + dc + 1],
                                    scale=s["bias"][:, K.B_LNW + dc:K.B_LNW + dc + 1])
            p.act(aff, reads=("ev3", "bias"), writes=(f"act_a.{dc}",),
                  tag=f"u{bi}.af{dc}.{tc}")


def l1w_view(k, name):
    return k.d[name][:, :].rearrange("(a p) (b j) -> p a b j", p=128, j=128)


def emit_l1_qkv(k, bi):
    p, s, d = k.p, k.s, k.d
    T, NTC, NT128 = k.T, k.NTC, k.NT128
    if bi == 0:
        p.fence(reads=(), writes=("nrmX",) + tuple(f"kst.{j}" for j in range(NDC)),
                tag="f.kst")
    p.fence(reads=(), writes=("lgt", "wrt", "axw", "nrmY", "wvb"), tag=f"f.wvb{bi}")
    for fc in range(16):
        wsl = fc % 2
        p.dma("pl_dma", s["wtile"][:, wsl, :, :], l1w_view(k, "l1_qkv_w")[:, :, fc, :],
              writes=(f"wtile{wsl}",), tag=f"q1{bi}.w{fc}")
        for dc in range(NDC):
            for tc in range(NTC):
                def mm(e, wsl=wsl, dc=dc, tc=tc):
                    return e.matmul(k.psum[f"pb{tc}"][:, :], s["wtile"][:, wsl, dc, :],
                                    s["act_a"][:, dc, tc * 512:(tc + 1) * 512],
                                    start=(dc == 0), stop=(dc == NDC - 1))
                p.pe(mm, reads=(f"wtile{wsl}", f"act_a.{dc}"), writes=(f"pt{tc}",),
                     tag=f"q1{bi}.m{fc}.{dc}.{tc}")
        for tc in range(NTC):
            if fc < 8:
                dst = s["act_b"][:, fc, tc * 512:(tc + 1) * 512]
                res = f"act_b.{fc}"
            else:
                dst = s["kst"][:, fc - 8, tc * 512:(tc + 1) * 512]
                res = f"kst.{fc - 8}"

            def evac(e, fc=fc, tc=tc, dst=dst):
                return e.activation(dst, k.psum[f"pb{tc}"][:, :], ACTF.Identity,
                                    bias=s["bias"][:, K.B_L1QKV + fc:K.B_L1QKV + fc + 1])
            p.act(evac, reads=(f"pt{tc}", "bias"), writes=(res,),
                  tag=f"q1{bi}.e{fc}.{tc}")
    for fv in range(2):
        p.dma("pl_dma", s["wvb"][:, :, :],
              d["l1_qkv_w"][:, 2048 + fv * 512:2048 + (fv + 1) * 512]
              .rearrange("(a p) j -> p a j", p=128),
              writes=("wvb",), tag=f"v1{bi}.w{fv}")
        for t128 in range(NT128):
            pq = 4 + t128 % 2
            pb = k.psum[f"pb{pq}"]
            for dc in range(NDC):
                def mm(e, pb=pb, t128=t128, dc=dc):
                    return e.matmul(pb[:, :],
                                    s["act_a"][:, dc, t128 * 128:(t128 + 1) * 128],
                                    s["wvb"][:, dc, :],
                                    start=(dc == 0), stop=False)
                p.pe(mm, reads=("wvb", f"act_a.{dc}"), writes=(f"pt{pq}",),
                     tag=f"v1{bi}.m{fv}.{t128}.{dc}")

            def mmb(e, pb=pb, fv=fv):
                return e.matmul(pb[:, :], s["ones_b"][0:1, 0:128],
                                s["brow"][0:1, 1, fv * 512:(fv + 1) * 512],
                                start=False, stop=True)
            p.pe(mmb, reads=("brow1", "ones_b"), writes=(f"pt{pq}",),
                 tag=f"v1{bi}.b{fv}.{t128}")

            def evac(e, pb=pb, t128=t128, fv=fv):
                return e.tensor_copy(s["vaug"][:, t128, fv * 8:(fv + 1) * 8, 0:64],
                                     pb[:, :])
            p.dve(evac, reads=(f"pt{pq}", "vaug.ones"), writes=(f"vg.{t128 % 4}",),
                  tag=f"v1{bi}.e{fv}.{t128}")


def emit_l1_ffn(k, bi):
    p, s, d = k.p, k.s, k.d
    T, NTC, NT128 = k.T, k.NTC, k.NT128
    p.fence(reads=(), writes=("wvb", "lgt", "wrt", "axw"), tag=f"f.lgt{bi}")
    for dc in range(NDC):
        p.dma("pl_dma", s["act_a"][:, dc, :], d["xn132"][bi, dc, :, :],
              reads=tuple(f"xn132.{bi}.{dc}.{t}" for t in range(NTC)),
              writes=(f"act_a.{dc}",), tag=f"ff{bi}.xn{dc}")
    for tc in range(NTC):
        for dc in range(NDC):
            def mm(e, dc=dc, tc=tc):
                return e.matmul(k.psum["pb6"][0:8, :], s["lR"][:, dc, :],
                                s["act_a"][:, dc, tc * 512:(tc + 1) * 512],
                                start=(dc == 0), stop=(dc == NDC - 1))
            p.pe(mm, reads=("lR", f"act_a.{dc}"), writes=("pt6",),
                 tag=f"rt{bi}.m{tc}.{dc}")

        def evac(e, tc=tc):
            return e.activation(s["lgt"][0:8, tc * 512:(tc + 1) * 512],
                                k.psum["pb6"][0:8, :], ACTF.Identity,
                                bias=s["small"][0:8, 30:31])
        p.act(evac, reads=("pt6", "rb"), writes=("lgt",), tag=f"rt{bi}.e{tc}")
    if "dbg_lgt" in K.DEBUG_OUT:
        p.dma("sp_dma", d["dbg_lgt"][bi, :, :], s["lgt"][0:8, :],
              reads=("lgt",), writes=("dbgl",), tag=f"dbg.lgt{bi}")
    for t128 in range(NT128):
        def tp(e, t128=t128):
            return e.transpose(k.psum["pb7"][0:128, 0:8],
                               s["lgt"][0:8, t128 * 128:(t128 + 1) * 128],
                               s["idn_f"][0:8, 0:8])
        p.pe(tp, reads=("lgt", "idn_f"), writes=("pt7",), tag=f"rt{bi}.t{t128}")

        def chain(e):
            t8 = s["tmp8"]
            e.tensor_copy(t8[:, 50:58], k.psum["pb7"][0:128, 0:8])
            e.drain()
            e.tensor_reduce(t8[:, 58:59], t8[:, 50:58], AX.X, ALU.max)
            e.drain()
            e.tensor_scalar(t8[:, 59:60], t8[:, 58:59], -1.0, None, ALU.mult)
            e.drain()
            e.tensor_scalar(t8[:, 64:72], t8[:, 50:58], t8[:, 58:59], None,
                            ALU.is_equal)
            e.drain()
            e.scalar_tensor_tensor(t8[:, 64:72], t8[:, 64:72], -1e30, t8[:, 50:58],
                                   ALU.mult, ALU.add)
            e.drain()
            e.tensor_reduce(t8[:, 60:61], t8[:, 64:72], AX.X, ALU.max)
            e.drain()
            return e.tensor_scalar(t8[:, 64:72], t8[:, 50:58], t8[:, 60:61], None,
                                   ALU.is_ge)
        p.dve(chain, reads=("pt7",), writes=("t8r",), tag=f"rt{bi}.c{t128}")

        def ex(e):
            return e.activation(s["tmp8"][:, 72:80], s["tmp8"][:, 50:58], ACTF.Exp,
                                bias=s["tmp8"][:, 59:60])
        p.act(ex, reads=("t8r",), writes=("t8x",), tag=f"rt{bi}.x{t128}")

        def fin(e):
            t8 = s["tmp8"]
            e.tensor_mul(t8[:, 72:80], t8[:, 72:80], t8[:, 64:72])
            e.drain()
            e.tensor_reduce(t8[:, 61:62], t8[:, 72:80], AX.X, ALU.add)
            e.drain()
            recip_acc(e, t8[:, 62:63], t8[:, 61:62], t8[:, 63:64])
            e.drain()
            return e.tensor_scalar(t8[:, 72:80], t8[:, 72:80], t8[:, 62:63], None,
                                   ALU.mult)
        p.dve(fin, reads=("t8x", "t8r"), writes=("t8x",), tag=f"rt{bi}.f{t128}")

        def tpb(e):
            return e.transpose(k.psum["pb7"][0:8, 0:128], s["tmp8"][:, 72:80],
                               s["idn_f"][:, :])
        p.pe(tpb, reads=("t8x", "idn_f"), writes=("pt7",), tag=f"rt{bi}.b{t128}")

        def wev(e, t128=t128):
            return e.tensor_copy(s["wrt"][0:8, t128 * 128:(t128 + 1) * 128],
                                 k.psum["pb7"][0:8, 0:128])
        p.dve(wev, reads=("pt7",), writes=("wrt",), tag=f"rt{bi}.w{t128}")
    if "dbg_wrt" in K.DEBUG_OUT:
        def wrtc(e):
            return e.tensor_copy(s["rr"][:, 0, 0:8].rearrange("p a -> p a") if False else s["mur"][0:8, 0, 0:512], s["wrt"][0:8, 0:512])
        p.dma("pl_dma", d["dbg_wrt"][bi, :, :], s["wrt"][0:8, :],
              reads=("wrt",), writes=("dbgw",), tag=f"dbg.wrt{bi}")
    p.fence(reads=(), writes=("lgt", "axw"), tag=f"f.axw{bi}")
    for tc in range(NTC):
        pb2 = k.psum[f"pb{tc % 2}"]
        for dc in range(NDC):
            def mm(e, pb2=pb2, dc=dc, tc=tc):
                return e.matmul(pb2[:, :], s["lA"][:, dc, :],
                                s["act_a"][:, dc, tc * 512:(tc + 1) * 512],
                                start=(dc == 0), stop=(dc == NDC - 1))
            p.pe(mm, reads=("lA", f"act_a.{dc}"), writes=(f"pt{tc % 2}",),
                 tag=f"ax{bi}.m{tc}.{dc}")
        pb3 = k.psum[f"pb{2 + tc % 2}"]

        def mmw(e, pb3=pb3, tc=tc):
            return e.matmul(pb3[:, :], s["eind"][:, :],
                            s["wrt"][0:8, tc * 512:(tc + 1) * 512],
                            start=True, stop=True)
        p.pe(mmw, reads=("eind", "wrt"), writes=(f"pt{2 + tc % 2}",),
             tag=f"ax{bi}.w{tc}")

        def axw(e, pb2=pb2, pb3=pb3, tc=tc):
            e.tensor_copy(s["rsc"][:, 0:512], pb3[:, :])
            e.drain()
            return e.tensor_mul(s["axw"][:, tc * 512:(tc + 1) * 512],
                                pb2[:, :], s["rsc"][:, 0:512])
        p.dve(axw, reads=(f"pt{tc % 2}", f"pt{2 + tc % 2}"), writes=("axw", "rsc"),
              tag=f"ax{bi}.s{tc}")
    if "dbg_axw" in K.DEBUG_OUT:
        p.dma("pl_dma", d["dbg_axw"][bi, :, :], s["axw"][:, :],
              reads=("axw",), writes=("dbga",), tag=f"dbg.axw{bi}")
    for fc in range(NDC):
        wsl = fc % 2
        p.dma("pl_dma", s["wtile"][:, wsl, :, :], l1w_view(k, "l1_base_wT")[:, :, fc, :],
              writes=(f"wtile{wsl}",), tag=f"bs{bi}.w{fc}")
        if fc == 0 and bi == 0 and "dbg_wt" in K.DEBUG_OUT:
            p.dma("pl_dma", d["dbg_wt"][:, :, :], s["wtile"][:, wsl, :, :],
                  reads=(f"wtile{wsl}",), writes=("dbgwt",), tag="dbg.wt")
        for dc in range(NDC):
            for tc in range(NTC):
                pq = 4 + tc

                def mm(e, wsl=wsl, dc=dc, tc=tc, pq=pq):
                    return e.matmul(k.psum[f"pb{pq}"][:, :], s["wtile"][:, wsl, dc, :],
                                    s["act_a"][:, dc, tc * 512:(tc + 1) * 512],
                                    start=(dc == 0), stop=False)
                p.pe(mm, reads=(f"wtile{wsl}", f"act_a.{dc}"), writes=(f"pt{pq}",),
                     tag=f"bs{bi}.m{fc}.{dc}.{tc}")
        for tc in range(NTC):
            pq = 4 + tc

            def mml(e, fc=fc, tc=tc, pq=pq):
                return e.matmul(k.psum[f"pb{pq}"][:, :],
                                s["lB"][:, fc * 128:(fc + 1) * 128],
                                s["axw"][:, tc * 512:(tc + 1) * 512],
                                start=False, stop=True)
            p.pe(mml, reads=("lB", "axw"), writes=(f"pt{pq}",),
                 tag=f"bs{bi}.l{fc}.{tc}")

            def silu(e, fc=fc, tc=tc, pq=pq):
                return e.activation(s["ev"][:, tc % 2, :], k.psum[f"pb{pq}"][:, :],
                                    ACTF.Silu,
                                    bias=s["bias"][:, K.B_BASE + fc:K.B_BASE + fc + 1])
            p.act(silu, reads=(f"pt{pq}", "bias"), writes=(f"ev{tc % 2}",),
                  tag=f"bs{bi}.s{fc}.{tc}")
            p.dma("sp_dma", d["g1"][bi, fc, :, tc * 512:(tc + 1) * 512],
                  s["ev"][:, tc % 2, :], reads=(f"ev{tc % 2}",),
                  writes=(f"g1.{bi}.{fc}.{tc}",), tag=f"bs{bi}.g{fc}.{tc}")


def emit_l1_attention(k, bi):
    p, s = k.p, k.s
    p.fence(reads=(), writes=("lgt", "wrt", "axw", "wvb", "nrmY"), tag=f"f.n1{bi}")
    attention_core(k, bi, layer=1, Ks=s["kst"], Kres="kst",
                   Qs=s["act_b"], Qres="act_b", nrm=s["nrm1"], nres="nrmY")


def emit_l1_proj_out(k, bi):
    p, s, d = k.p, k.s, k.d
    T, NTC = k.T, k.NTC
    for fc in range(NDC):
        wsl = fc % 2
        p.dma("pl_dma", s["wtile"][:, wsl, :, :], l1w_view(k, "l1_proj_w")[:, :, fc, :],
              writes=(f"wtile{wsl}",), tag=f"p1{bi}.w{fc}")
        for dc in range(NDC):
            for tc in range(NTC):
                def mm(e, wsl=wsl, dc=dc, tc=tc):
                    return e.matmul(k.psum[f"pb{tc}"][:, :], s["wtile"][:, wsl, dc, :],
                                    s["act_a"][:, dc, tc * 512:(tc + 1) * 512],
                                    start=(dc == 0), stop=(dc == NDC - 1))
                p.pe(mm, reads=(f"wtile{wsl}", f"act_a.{dc}"), writes=(f"pt{tc}",),
                     tag=f"p1{bi}.m{fc}.{dc}.{tc}")
        for tc in range(NTC):
            def evac(e, fc=fc, tc=tc):
                return e.activation(s["ev"][:, 2 + tc % 2, :], k.psum[f"pb{tc}"][:, :],
                                    ACTF.Identity,
                                    bias=s["bias"][:, K.B_L1PROJ + fc:K.B_L1PROJ + fc + 1])
            p.act(evac, reads=(f"pt{tc}", "bias"), writes=(f"ev{2 + tc % 2}",),
                  tag=f"p1{bi}.e{fc}.{tc}")
            sl = tc % 2
            p.dma("sp_dma", s["fx"][:, sl, 0:512],
                  d["xn132"][bi, fc, :, tc * 512:(tc + 1) * 512],
                  reads=(f"xn132.{bi}.{fc}.{tc}",), writes=(f"fx{sl}",),
                  tag=f"p1{bi}.n{fc}.{tc}")
            p.dma("sp_dma", s["fy"][:, sl, :],
                  d["g1"][bi, fc, :, tc * 512:(tc + 1) * 512],
                  reads=(f"g1.{bi}.{fc}.{tc}",), writes=(f"fy{sl}",),
                  tag=f"p1{bi}.g{fc}.{tc}")
            p.dma("sp_dma", s["mur"][:, sl, :],
                  d["xmid"][bi, fc, :, tc * 512:(tc + 1) * 512],
                  reads=(f"xmid.{bi}.{fc}.{tc}",), writes=(f"mur{sl}",),
                  tag=f"p1{bi}.xm{fc}.{tc}")

            def upd(e, sl=sl, tc=tc):
                e.tensor_add(s["ev"][:, 2 + sl, :], s["ev"][:, 2 + sl, :],
                             s["fx"][:, sl, 0:512])
                e.drain()
                e.tensor_mul(s["ev"][:, 2 + sl, :], s["ev"][:, 2 + sl, :],
                             s["fy"][:, sl, :])
                e.drain()
                return e.tensor_add(s["ev"][:, 2 + sl, :], s["ev"][:, 2 + sl, :],
                                    s["mur"][:, sl, :])
            p.dve(upd, reads=(f"ev{2 + sl}", f"fx{sl}", f"fy{sl}", f"mur{sl}"),
                  writes=(f"ev{2 + sl}",), tag=f"p1{bi}.u{fc}.{tc}")
            p.dma("sp_dma", d["out_fm"][bi, fc, :, tc * 512:(tc + 1) * 512],
                  s["ev"][:, 2 + sl, :], reads=(f"ev{2 + sl}",),
                  writes=(f"OUT.{bi}.{fc}.{tc}",), tag=f"p1{bi}.o{fc}.{tc}")


def emit_final(k):
    outs = tuple(f"OUT.{bi}.{fc}.{tc}" for bi in range(NB) for fc in range(NDC)
                 for tc in range(k.NTC))
    k.p.fence(reads=outs, writes=(), tag="final")


def build(T=2048, n_cores=8, sim_strict=False, debug_out=()):
    K.DEBUG_OUT = tuple(debug_out)
    k = K(T=T, n_cores=n_cores, sim_strict=sim_strict)
    emit_setup(k)
    emit_wq_preproc(k, "l0_qkv_w", 24)
    emit_wq_preproc(k, "l0_proj_w", 8)
    emit_wcast(k)
    for bi in range(NB):
        emit_l0_stats(k, bi)
    emit_ar1(k)
    for bi in range(NB):
        emit_ffn(k, bi)
        emit_xq_qkv(k, bi)
    emit_ar2(k)
    for bi in range(NB):
        emit_l0_attention(k, bi)
    emit_ar3(k)
    for bi in range(NB):
        emit_l0_proj_update(k, bi)
    for bi in range(NB):
        emit_l1_stats(k, bi)
        emit_l1_qkv(k, bi)
        emit_l1_ffn(k, bi)
        emit_l1_attention(k, bi)
        emit_l1_proj_out(k, bi)
    emit_final(k)
    k.p.emit()
    return k




def make_inmaps(inputs, n_cores=8, T=2048):
    x = np.asarray(inputs["L0_attn_x"], np.float32)
    Bt, N, C = x.shape
    f32 = lambda a: np.ascontiguousarray(np.asarray(a, np.float32))

    def col(a, w):
        return np.ascontiguousarray(np.asarray(a, np.float32).reshape(w, 128).T)

    qkvb = np.asarray(inputs["L0_attn_qkv_b"], np.float32)
    l1qkvb = np.asarray(inputs["L1_attn_b_qkv"], np.float32)
    hi = np.zeros((128, NDC, 16), np.float32)
    for dc in range(NDC):
        for pp in range(128):
            hi[pp, dc, (dc * 128 + pp) // 64] = 1.0
    ei = np.zeros((8, 128), np.float32)
    for e in range(8):
        ei[e, e * 16:(e + 1) * 16] = 2.0

    shared = {
        "l0_qkv_w": f32(inputs["L0_attn_qkv_w"]),
        "l0_qkv_b": col(qkvb, 24),
        "l0_qkv_bv_row": f32(qkvb[2048:3072]).reshape(1, D),
        "l0_proj_w": f32(inputs["L0_attn_proj_w"]),
        "l0_proj_b": col(inputs["L0_attn_proj_b"], 8),
        "l1_ln_w": col(inputs["L1_attn_ln_w"], 8),
        "l1_ln_b": col(inputs["L1_attn_ln_b"], 8),
        "l1_qkv_w": f32(inputs["L1_attn_W_qkv"]),
        "l1_qkv_b": col(l1qkvb, 24),
        "l1_qkv_bv_row": f32(l1qkvb[2048:3072]).reshape(1, D),
        "l1_proj_w": f32(inputs["L1_attn_W_proj"]),
        "l1_proj_b": col(inputs["L1_attn_b_proj"], 8),
        "l1_base_wT": f32(np.asarray(inputs["L1_ffn_base_w"], np.float32).T),
        "l1_base_b": col(inputs["L1_ffn_base_b"], 8),
        "l1_lora_A": f32(np.concatenate(
            [np.asarray(inputs["L1_ffn_lora_A"], np.float32)[e] for e in range(8)],
            axis=1)),
        "l1_lora_B": f32(np.asarray(inputs["L1_ffn_lora_B"], np.float32)
                         .reshape(128, D)),
        "l1_router_wT": f32(np.asarray(inputs["L1_ffn_router_w"], np.float32).T),
        "l1_router_b": f32(np.asarray(inputs["L1_ffn_router_b"], np.float32)
                           .reshape(8, 1)),
        "c_idn": np.eye(128, dtype=np.float32),
        "c_headind": hi,
        "c_expind2": ei,
    }
    w1 = np.asarray(inputs["L0_ffn_w1"], np.float32)
    b1 = np.asarray(inputs["L0_ffn_b1"], np.float32)
    w2 = np.asarray(inputs["L0_ffn_w2"], np.float32)
    b2 = np.asarray(inputs["L0_ffn_b2"], np.float32)
    in_maps = []
    for c in range(n_cores):
        m = dict(shared)
        xs = x[2 * c:2 * c + 2]
        m["x_fm"] = np.ascontiguousarray(
            xs.reshape(2, N, NDC, 128).transpose(0, 2, 3, 1))
        m["l0_w1"] = f32(w1[c])
        m["l0_b1"] = col(b1[c], NFC)
        m["l0_w2"] = f32(w2[c])
        m["l0_b2"] = col(b2[c], NDC)
        in_maps.append(m)
    return in_maps


def kernel(**inputs):
    n_cores, T = 8, 2048
    k = build(T=T, n_cores=n_cores, sim_strict=True)
    in_maps = make_inmaps(inputs, n_cores=n_cores, T=T)
    res = run_bass_kernel_spmd(k.nc, in_maps, core_ids=list(range(n_cores)))
    outs = []
    for c in range(n_cores):
        o = np.asarray(res.results[c]["out_fm"], np.float32)
        outs.append(o.transpose(0, 3, 1, 2).reshape(2, T, D))
    return np.concatenate(outs, axis=0).astype(np.float32)


# version 10
# speedup vs baseline: 1.0107x; 1.0107x over previous
import numpy as np
import ml_dtypes
import concourse.bass as bass
import concourse.mybir as mybir
import concourse.dve_ops as dve_ops
from concourse.bass_utils import run_bass_kernel_spmd


def recip_acc(e, out, in_, scratch):
    return e.reciprocal(out, in_)

F32 = mybir.dt.float32
BF16 = mybir.dt.bfloat16
I32 = mybir.dt.int32
AX = mybir.AxisListType
ALU = mybir.AluOpType
ACTF = mybir.ActivationFunctionType

D = 1024
NDC = 8
H = 16
DH = 64
FF = 4096
NFC = 32
NB = 2

PHYS = {"pe": "pe", "act": "act", "dve": "dve", "sp": "sp", "pl": "pl",
        "sp_dma": "sp", "pl_dma": "pl", "pl_cc": "pl"}
N_DMA_SLOTS = 6


class Op:
    __slots__ = ("i", "eng", "fn", "reads", "writes", "is_dma", "deps",
                 "sem_key", "sem_val", "marked", "vc", "tag")

    def __init__(self, i, eng, fn, reads, writes, is_dma, tag):
        self.i, self.eng, self.fn = i, eng, fn
        self.reads, self.writes, self.is_dma = tuple(reads), tuple(writes), is_dma
        self.deps = set()
        self.sem_key = None
        self.sem_val = None
        self.marked = False
        self.vc = None
        self.tag = tag


class Prog:
    NO_COALESCE = False

    def __init__(self, nc, sim_strict=False):
        self.nc = nc
        self.ops = []
        self.res_w = {}
        self.res_r = {}
        self.sim_strict = sim_strict

    def op(self, eng, fn, reads=(), writes=(), tag=""):
        o = Op(len(self.ops), eng, fn, reads, writes,
               eng in ("sp_dma", "pl_dma"), tag)
        for r in o.reads:
            if r in self.res_w:
                o.deps.add(self.res_w[r])
        for w in o.writes:
            if w in self.res_w:
                o.deps.add(self.res_w[w])
            for rd in self.res_r.get(w, ()):
                o.deps.add(rd)
        for r in o.reads:
            self.res_r.setdefault(r, []).append(o.i)
        for w in o.writes:
            self.res_w[w] = o.i
            self.res_r[w] = []
        o.deps.discard(o.i)
        if len(o.deps) > 1:
            keep = {}
            extra = []
            for di in o.deps:
                pp = self.ops[di]
                if pp.is_dma or pp.eng == "pl_cc":
                    extra.append(di)
                else:
                    e = PHYS[pp.eng]
                    if keep.get(e, -1) < di:
                        keep[e] = di
            o.deps = set(extra) | set(keep.values())
        self.ops.append(o)
        return o

    def pe(self, fn, reads=(), writes=(), tag=""):
        return self.op("pe", fn, reads, writes, tag)

    def act(self, fn, reads=(), writes=(), tag=""):
        return self.op("act", fn, reads, writes, tag)

    def dve(self, fn, reads=(), writes=(), tag=""):
        return self.op("dve", fn, reads, writes, tag)

    def sp(self, fn, reads=(), writes=(), tag=""):
        return self.op("sp", fn, reads, writes, tag)

    def pl(self, fn, reads=(), writes=(), tag=""):
        return self.op("pl", fn, reads, writes, tag)

    def dma(self, queue, out_ap, in_ap, reads=(), writes=(), tag=""):
        return self.op(queue, lambda e, o=out_ap, i=in_ap: e.dma_start(o, i),
                       reads, writes, tag)

    def collective(self, alu, rg, in_ap, out_ap, reads=(), writes=(), tag=""):
        def fn(e):
            return e.collective_compute("AllReduce", alu, replica_groups=rg,
                                        ins=[in_ap], outs=[out_ap])
        return self.op("pl_cc", fn, reads, writes, tag)

    def fence(self, reads, writes, tag="fence"):
        return self.op("sp", lambda e: None, reads, writes, tag)

    def barrier(self, tag="barrier"):
        for eng in ("pe", "act", "dve", "sp", "pl"):
            self.op(eng, lambda e: None, ("BARRIER",), ("BARRIER",), tag + "." + eng)

    def emit(self):
        nc = self.nc

        def relaxed(pp, o):
            if pp.is_dma or pp.eng == "pl_cc" or PHYS[pp.eng] != PHYS[o.eng]:
                return False
            return (not self.sim_strict) or PHYS[pp.eng] == "pe"

        for o in self.ops:
            for di in o.deps:
                pp = self.ops[di]
                if not relaxed(pp, o):
                    pp.marked = True
        sems = {e: nc.alloc_semaphore(name=f"sem_{e}")
                for e in ("pe", "act", "dve", "sp", "pl")}
        dma_sems = {q: [nc.alloc_semaphore(name=f"{q}{i}")
                        for i in range(N_DMA_SLOTS)]
                    for q in ("sp_dma", "pl_dma")}
        cc_sem = nc.alloc_semaphore(name="sem_cc")

        counts = {e: 0 for e in ("pe", "act", "dve", "sp", "pl")}
        dma_counts = {q: [0] * N_DMA_SLOTS for q in ("sp_dma", "pl_dma")}
        dma_rr = {q: 0 for q in ("sp_dma", "pl_dma")}
        ccn = 0
        for o in self.ops:
            if o.is_dma:
                sl = dma_rr[o.eng] % N_DMA_SLOTS
                dma_rr[o.eng] += 1
                dma_counts[o.eng][sl] += 1
                o.sem_key = (o.eng, sl)
                o.sem_val = 16 * dma_counts[o.eng][sl]
                o.marked = True
            elif o.eng == "pl_cc":
                ccn += 1
                o.sem_key = ("cc",)
                o.sem_val = ccn
                o.marked = True
            elif o.marked:
                counts[o.eng] += 1
                o.sem_key = (o.eng,)
                o.sem_val = counts[o.eng]
        for e, n in counts.items():
            assert n < 60000, (e, n)
        for q in dma_counts:
            for n in dma_counts[q]:
                assert 16 * n < 60000, (q, n)

        def sem_of(key):
            if key[0] in ("sp_dma", "pl_dma"):
                return dma_sems[key[0]][key[1]]
            if key[0] == "cc":
                return cc_sem
            return sems[key[0]]

        per_eng = {e: [] for e in ("pe", "act", "dve", "sp", "pl")}
        for o in self.ops:
            per_eng[PHYS[o.eng]].append(o)
        vclocks = {e: {} for e in per_eng}

        def body(eng, name):
            vc = vclocks[name]
            for o in per_eng[name]:
                for di in sorted(o.deps):
                    pp = self.ops[di]
                    if relaxed(pp, o):
                        if pp.vc:
                            for kk, vv in pp.vc.items():
                                if vc.get(kk, 0) < vv:
                                    vc[kk] = vv
                        continue
                    if pp.sem_key is None:
                        continue
                    if vc.get(pp.sem_key, 0) < pp.sem_val or Prog.NO_COALESCE:
                        eng.wait_ge(sem_of(pp.sem_key), pp.sem_val)
                        vc[pp.sem_key] = max(vc.get(pp.sem_key, 0), pp.sem_val)
                    if (not Prog.NO_COALESCE) and pp.vc:
                        for kk, vv in pp.vc.items():
                            if vc.get(kk, 0) < vv:
                                vc[kk] = vv
                if o.is_dma:
                    prev = o.sem_val - 16
                    if prev > 0 and vc.get(o.sem_key, 0) < prev:
                        eng.wait_ge(sem_of(o.sem_key), prev)
                        vc[o.sem_key] = prev
                ins = o.fn(eng)
                if o.sem_key is not None:
                    if ins is None:
                        ins = eng.nop()
                    ins.then_inc(sem_of(o.sem_key), 16 if o.is_dma else 1)
                snap = dict(vc)
                if o.sem_key is not None:
                    snap[o.sem_key] = o.sem_val
                o.vc = snap

        with nc.Block("main") as blk:
            @blk.tensor
            def _(t):
                body(t, "pe")

            @blk.scalar
            def _(sc):
                body(sc, "act")

            @blk.vector
            def _(v):
                body(v, "dve")

            @blk.sync
            def _(sy):
                body(sy, "sp")

            @blk.gpsimd
            def _(g):
                body(g, "pl")




class K:
    B_L0QKV, B_L0PROJ, B_B1, B_B2 = 0, 24, 32, 64
    B_LNW, B_LNB, B_L1QKV, B_L1PROJ, B_BASE = 72, 80, 88, 112, 120
    S_MXC, S_MNC = 0, 8
    S_CINV, S_CZP, S_CS, S_CNZS = 16, 24, 32, 40
    S_WMN, S_WMX, S_WINV, S_WZP, S_WS, S_WNZS = 48, 49, 50, 51, 52, 53
    S_RMX, S_RMN = 54, 57
    S_OMX, S_OMN = 60, 61
    S_T0 = 62
    QP_INV, QP_ZP, QP_S, QP_NZS = 0, 3, 6, 9
    QP_OINV, QP_OZP, QP_OS, QP_ONZS = 12, 13, 14, 15

    DEBUG_OUT = ()

    def __init__(self, T=2048, n_cores=8, sim_strict=False):
        assert T % 512 == 0 and (2 * (T // 512)) <= 8
        self.T = T
        self.NTC = T // 512
        self.NT128 = T // 128
        self.LC = min(1024, T)
        self.n_cores = n_cores
        self.nc = bass.Bass("TRN2", target_bir_lowering=False,
                            detect_race_conditions=sim_strict)
        self.p = Prog(self.nc, sim_strict=sim_strict)
        self.d = {}
        self.s = {}
        self.psum = {}
        self._declare()

    def dram(self, name, shape, dt, kind="Internal"):
        if kind == "Internal" and name in K.DEBUG_OUT:
            kind = "ExternalOutput"
        self.d[name] = self.nc.dram_tensor(name, list(shape), dt, kind=kind)
        return self.d[name]

    def sbuf(self, name, shape, dt):
        self.s[name] = self.nc.alloc_sbuf_tensor(name, list(shape), dt)
        return self.s[name]

    def sbuf_at(self, name, shape, dt, off):
        self.s[name] = self.nc.alloc_sbuf_tensor_at(name, list(shape), dt,
                                                    offset=off)
        return self.s[name]

    def _declare(self):
        T, NTC, NT128 = self.T, self.NTC, self.NT128
        dr = self.dram
        dr("x_fm", [NB, NDC, 128, T], F32, "ExternalInput")
        dr("l0_qkv_w", [3 * D, D], F32, "ExternalInput")
        dr("l0_qkv_b", [128, 24], F32, "ExternalInput")
        dr("l0_qkv_bv_row", [1, D], F32, "ExternalInput")
        dr("l0_proj_w", [D, D], F32, "ExternalInput")
        dr("l0_proj_b", [128, 8], F32, "ExternalInput")
        dr("l0_w1", [D, FF], F32, "ExternalInput")
        dr("l0_b1", [128, NFC], F32, "ExternalInput")
        dr("l0_w2", [FF, D], F32, "ExternalInput")
        dr("l0_b2", [128, NDC], F32, "ExternalInput")
        dr("l1_ln_w", [128, NDC], F32, "ExternalInput")
        dr("l1_ln_b", [128, NDC], F32, "ExternalInput")
        dr("l1_qkv_w", [D, 3 * D], F32, "ExternalInput")
        dr("l1_qkv_b", [128, 24], F32, "ExternalInput")
        dr("l1_qkv_bv_row", [1, D], F32, "ExternalInput")
        dr("l1_proj_w", [D, D], F32, "ExternalInput")
        dr("l1_proj_b", [128, 8], F32, "ExternalInput")
        dr("l1_base_wT", [D, D], F32, "ExternalInput")
        dr("l1_base_b", [128, 8], F32, "ExternalInput")
        dr("l1_lora_A", [D, 128], F32, "ExternalInput")
        dr("l1_lora_B", [128, D], F32, "ExternalInput")
        dr("l1_router_wT", [D, 8], F32, "ExternalInput")
        dr("l1_router_b", [8, 1], F32, "ExternalInput")
        dr("c_idn", [128, 128], F32, "ExternalInput")
        dr("c_headind", [128, NDC, 16], F32, "ExternalInput")
        dr("c_expind2", [8, 128], F32, "ExternalInput")
        dr("out_fm", [NB, NDC, 128, T], F32, "ExternalOutput")
        dr("xn32", [NB, NDC, 128, T], F32)
        dr("g0", [NB, NDC, 128, T], F32)
        dr("xmid", [NB, NDC, 128, T], F32)
        dr("xn132", [NB, NDC, 128, T], F32)
        dr("g1", [NB, NDC, 128, T], F32)
        dr("qkraw", [NB, 16, NTC, 128, 512], F32)
        dr("vraw", [NB, NT128, 2, 128, 512], F32)
        dr("oraw", [NB, H, NTC, 64, 512], F32)
        dr("wqT_qk", [16, NDC, 128, 128], BF16)
        dr("w1b", [128, NDC, NFC * 128], BF16)
        dr("w2b", [128, NFC, NDC * 128], BF16)
        dr("wqT_v", [NDC, 128, D], BF16)
        dr("pwT", [NDC, NDC, 128, 128], BF16)
        dr("ar1_in", [128, 16], F32)
        dr("ar1_out", [128, 16], F32)
        dr("ar2_in", [8, 1], F32)
        dr("ar2_out", [8, 1], F32)
        dr("ar3_in", [2, 1], F32)
        dr("dbg_lgt", [NB, 8, T], F32)
        dr("dbg_wrt", [NB, 8, T], F32)
        dr("dbg_axw", [NB, 128, T], F32)
        dr("dbg_wt", [128, NDC, 128], F32)
        dr("dbg_den", [8, 4, 2, 512], F32)
        dr("dbg_cb", [128, 16], F32)
        dr("dbg_nrm", [16, 2, 2048], F32)
        dr("dbg_sm", [16, 4], F32)
        dr("dbg_hind", [128, NDC, 16], F32)
        dr("dbg_s", [2, 128, 512], F32)
        dr("dbg_ka", [128, NDC, T], F32)
        dr("dbg_qa", [128, NDC, T], F32)
        dr("ar3_out", [2, 1], F32)

        sb = self.sbuf
        sb("act_a", [128, NDC, T], BF16)
        sb("act_b", [128, NDC, T], BF16)
        sb("fx", [128, 2, max(self.LC, 1024)], F32)
        sb("fy", [128, 2, 512], F32)
        sb("sq", [128, 2, self.LC], BF16)
        sb("ev", [128, 4, 512], F32)
        sb("qi", [128, 2, 512], I32)
        sb("ptile", [128, 2, 1024], BF16)
        sb("wtile", [128, 2, NDC, 128], BF16)
        sb("wst", [128, 8, 128], BF16)
        sb("mur", [128, 2, 512], F32)
        sb("rr", [128, 2, 512], F32)
        sb("den", [128, 512], F32)
        sb("rec", [128, 512], F32)
        sb("rsc", [128, 512], F32)
        sb("lA", [128, NDC, 128], BF16)
        sb("lB", [128, D], BF16)
        sb("lR", [128, NDC, 8], BF16)
        sb("bias", [128, 128], F32)
        sb("brow", [1, 2, D], BF16)
        sb("ones_b", [128, 128], BF16)
        sb("ones_f", [1, 128], F32)
        sb("onesf64", [66, 64], F32)
        sb("idn_f", [128, 128], F32)
        sb("idn_bt", [128, 128], BF16)
        sb("hind", [128, NDC, 16], BF16)
        sb("eind", [8, 128], BF16)
        sb("tmp8", [128, 96], F32)
        sb("small", [128, 48], F32)
        sb("qp", [128, 16], F32)
        sb("cbias", [128, 16], F32)
        sb("stat", [128, 64], F32)
        sb("cst", [128, 4], F32)

        base = (self.nc.sbuf_base + 31) & ~31
        top = self.nc.sbuf_top
        avail = top - base
        need = max(NFC * self.LC * 2 + 2 * NDC * 128 * 2 + NFC * 128 * 2,
                   NT128 * H * 66 * 2 + NDC * T * 2
                   + max(T * 4, NDC * 512 * 2, 2 * T * 2) + T * 2)
        assert need <= avail, (need, avail, base, top)
        A = base
        self.sbuf_at("hbuf", [128, NFC, self.LC], BF16, A)
        o1 = A + NFC * self.LC * 2
        self.sbuf_at("w1t", [128, 2, NDC, 128], BF16, o1)
        o2 = o1 + 2 * NDC * 128 * 2
        self.sbuf_at("w2t", [128, NFC, 128], BF16, o2)
        self.sbuf_at("wvb0", [128, NDC, D], BF16, o1)
        self.sbuf_at("vaug", [128, NT128, H, 66], BF16, A)
        ov = A + NT128 * H * 66 * 2
        self.sbuf_at("nrm0", [16, 2 * T], BF16, ov)
        self.sbuf_at("kst", [128, NDC, T], BF16, ov)
        oc = ov + NDC * T * 2
        self.sbuf_at("lgt", [8, T], F32, oc)
        self.sbuf_at("wvb", [128, NDC, 512], BF16, oc)
        self.sbuf_at("nrm1", [16, 2 * T], BF16, oc)
        self.sbuf_at("axw", [128, T], BF16, oc)
        oc2 = oc + max(T * 4, NDC * 512 * 2, 2 * T * 2)
        self.sbuf_at("wrt", [8, T], BF16, oc2)
        assert oc2 + T * 2 <= top, (oc2 + T * 2, top)

        for i in range(8):
            self.psum[f"pb{i}"] = self.nc.alloc_psum_tensor(f"pb{i}", [128, 512], F32)
        pbase = self.nc.psum_base
        self.nc.psum_base = 0
        for i in range(2):
            self.psum[f"pbt{i}"] = self.nc.alloc_psum_tensor(f"pbt{i}", [128, 1024], BF16)
        for b in (2, 4, 6):
            self.nc.psum_base = b
            self.psum[f"pb{b}{b + 1}"] = self.nc.alloc_psum_tensor(
                f"pb{b}{b + 1}", [128, 1024], F32)
        self.nc.psum_base = pbase




def quant_chain(k, src_ap, dst_ap, inv_ap, zp_ap, s_ap, nzs_ap,
                reads, writes, sl, tag="", np_=128):
    p, s = k.p, k.s
    n = src_ap.shape[-1]
    assert n <= 512

    def f1(e):
        return e.activation(s["ev"][0:np_, 2 + sl, 0:n], src_ap, ACTF.Relu,
                            bias=zp_ap, scale=inv_ap)
    p.act(f1, reads=reads, writes=(f"ev{2 + sl}",), tag=tag + ".r")

    def f2(e):
        return e.tensor_scalar(s["qi"][0:np_, sl, 0:n], s["ev"][0:np_, 2 + sl, 0:n],
                               255.0, None, ALU.min)
    p.dve(f2, reads=(f"ev{2 + sl}",), writes=(f"qi{sl}",), tag=tag + ".q")

    def f3(e):
        return e.tensor_scalar(dst_ap, s["qi"][0:np_, sl, 0:n],
                               s_ap, nzs_ap, ALU.mult, ALU.add)
    p.dve(f3, reads=(f"qi{sl}", reads[-1] if reads else "qp"), writes=writes,
          tag=tag + ".d")


def qparams_perpart(k, mn_ap, mx_ap, inv_ap, zp_ap, s_ap, nzs_ap,
                    res, reads, t0_ap, tag=""):
    p, s = k.p, k.s
    w = mn_ap.ap[-1][1]
    qi = s["qi"]

    def f1(e):
        e.tensor_scalar(mn_ap, mn_ap, 0.0, None, ALU.min)
        e.drain()
        e.tensor_scalar(mx_ap, mx_ap, 0.0, None, ALU.max)
        e.drain()
        e.tensor_sub(t0_ap, mx_ap, mn_ap)
        e.drain()
        e.tensor_scalar(s_ap, t0_ap, 1.0 / 255.0, 1e-8, ALU.mult, ALU.max)
        e.drain()
        recip_acc(e, inv_ap, s_ap, t0_ap)
        e.drain()
        e.tensor_mul(t0_ap, mn_ap, inv_ap)
        e.drain()
        e.tensor_scalar(qi[:, 0, 0:w], t0_ap, -1.0, None, ALU.mult)
        e.drain()
        e.tensor_copy(zp_ap, qi[:, 0, 0:w])
        e.drain()
        e.tensor_mul(t0_ap, zp_ap, s_ap)
        e.drain()
        return e.tensor_scalar(nzs_ap, t0_ap, -1.0, None, ALU.mult)
    p.dve(f1, reads=reads, writes=(res, "qi0"), tag=tag)


def w1_view(k):
    return k.d["l0_w1"][:, :].rearrange("(a p) (b j) -> p a b j", p=128, j=128)


def w2_view(k):
    return k.d["l0_w2"][:, :].rearrange("(b p) (a j) -> p b a j", p=128, j=128)


def qkt_view(k, name):
    return k.d[name][:, :, :, :]




def emit_setup(k):
    p, s, d = k.p, k.s, k.d
    for dst, src, res in [
        (s["bias"][:, K.B_L0QKV:K.B_L0QKV + 24], d["l0_qkv_b"][:, :], "bias"),
        (s["bias"][:, K.B_L0PROJ:K.B_L0PROJ + 8], d["l0_proj_b"][:, :], "bias"),
        (s["bias"][:, K.B_B1:K.B_B1 + NFC], d["l0_b1"][:, :], "bias"),
        (s["bias"][:, K.B_B2:K.B_B2 + NDC], d["l0_b2"][:, :], "bias"),
        (s["bias"][:, K.B_LNW:K.B_LNW + 8], d["l1_ln_w"][:, :], "bias"),
        (s["bias"][:, K.B_LNB:K.B_LNB + 8], d["l1_ln_b"][:, :], "bias"),
        (s["bias"][:, K.B_L1QKV:K.B_L1QKV + 24], d["l1_qkv_b"][:, :], "bias"),
        (s["bias"][:, K.B_L1PROJ:K.B_L1PROJ + 8], d["l1_proj_b"][:, :], "bias"),
        (s["bias"][:, K.B_BASE:K.B_BASE + 8], d["l1_base_b"][:, :], "bias"),
        (s["idn_f"][:, :], d["c_idn"][:, :], "idn_f"),
        (s["small"][0:8, 30:31], d["l1_router_b"][:, :], "rb"),
    ]:
        p.dma("sp_dma", dst, src, writes=(res,), tag="setup")
    p.dma("pl_dma", s["hind"][:, :, :], d["c_headind"][:, :, :], writes=("hind",))
    p.dma("pl_dma", s["eind"][:, :], d["c_expind2"][:, :], writes=("eind",))
    p.dma("pl_dma", s["brow"][0:1, 0, :], d["l0_qkv_bv_row"][:, :], writes=("brow0",))
    p.dma("pl_dma", s["brow"][0:1, 1, :], d["l1_qkv_bv_row"][:, :], writes=("brow1",))
    p.dma("pl_dma", s["lA"][:, :, :],
          d["l1_lora_A"][:, :].rearrange("(a p) j -> p a j", p=128), writes=("lA",))
    p.dma("pl_dma", s["lB"][:, :], d["l1_lora_B"][:, :], writes=("lB",))
    p.dma("pl_dma", s["lR"][:, :, :],
          d["l1_router_wT"][:, :].rearrange("(a p) j -> p a j", p=128), writes=("lR",))

    st = s["stat"]

    def consts(e):
        e.memset(s["ones_b"][:, :], 1.0)
        e.drain()
        e.memset(s["ones_f"][:, :], 1.0)
        e.drain()
        e.memset(s["onesf64"][:, :], 4.503599627370496e15)
        e.drain()
        e.memset(st[:, :], -1e30)
        e.drain()
        e.memset(st[:, K.S_MNC:K.S_MNC + 8], 1e30)
        e.drain()
        e.memset(st[:, K.S_RMN:K.S_RMN + 3], 1e30)
        e.drain()
        e.memset(st[:, K.S_OMN:K.S_OMN + 1], 1e30)
        e.drain()
        e.memset(s["cst"][:, 0:1], 1e-5)
        e.drain()
        e.memset(s["cst"][:, 1:2], 0.125)
        e.drain()
        e.memset(s["cst"][:, 2:3], -1.0)
        e.drain()
        return e.memset(s["cst"][:, 3:4], 36.04365338911715)
    p.dve(consts, writes=("ones_b", "ones_f", "onesf64", "chanstat",
                          "qkvstat", "ostat", "wstat", "cst"),
          tag="setup.c")

    def idnb(e):
        return e.tensor_copy(s["idn_bt"][:, :], s["idn_f"][:, :])
    p.dve(idnb, reads=("idn_f",), writes=("idn_bt",), tag="setup.idnb")


def emit_wq_preproc(k, wname, nfc):
    p, s, d = k.p, k.s, k.d
    st = s["stat"]
    for fc in range(nfc):
        sl = fc % 2
        p.dma("sp_dma", s["fx"][:, sl, 0:1024], d[wname][fc * 128:(fc + 1) * 128, :],
              writes=(f"fx{sl}",), tag=f"wq.{wname}.{fc}")

        def mnmx(e, sl=sl):
            e.tensor_reduce(st[:, K.S_WMN:K.S_WMN + 1], s["fx"][:, sl, 0:1024],
                            AX.X, ALU.min)
            e.drain()
            return e.tensor_reduce(st[:, K.S_WMX:K.S_WMX + 1], s["fx"][:, sl, 0:1024],
                                   AX.X, ALU.max)
        p.dve(mnmx, reads=(f"fx{sl}",), writes=("wstat",), tag=f"wq.mm{fc}")
        qparams_perpart(k, st[:, K.S_WMN:K.S_WMN + 1], st[:, K.S_WMX:K.S_WMX + 1],
                        st[:, K.S_WINV:K.S_WINV + 1], st[:, K.S_WZP:K.S_WZP + 1],
                        st[:, K.S_WS:K.S_WS + 1], st[:, K.S_WNZS:K.S_WNZS + 1],
                        "wpar", ("wstat",), st[:, K.S_T0:K.S_T0 + 1], tag=f"wq.p{fc}")
        for h2 in range(2):
            quant_chain(k, s["fx"][:, sl, h2 * 512:(h2 + 1) * 512],
                        s["ptile"][:, 0, h2 * 512:(h2 + 1) * 512],
                        st[:, K.S_WINV:K.S_WINV + 1], st[:, K.S_WZP:K.S_WZP + 1],
                        st[:, K.S_WS:K.S_WS + 1], st[:, K.S_WNZS:K.S_WNZS + 1],
                        reads=(f"fx{sl}", "wpar"), writes=(f"ptile{h2}",),
                        sl=h2, tag=f"wq.q{fc}.{h2}")
        for dc in range(NDC):
            pb = k.psum[f"pbt{dc % 2}"]

            def tp(e, dc=dc, pb=pb):
                return e.transpose(pb[:, 0:128],
                                   s["ptile"][:, 0, dc * 128:(dc + 1) * 128],
                                   s["idn_bt"][:, :])
            p.pe(tp, reads=(f"ptile{dc // 4}", "idn_bt"), writes=(f"pt{dc % 2}",),
                 tag=f"wq.t{fc}.{dc}")

            def ev(e, dc=dc, pb=pb):
                return e.tensor_copy(s["wst"][:, dc, :], pb[:, 0:128])
            p.dve(ev, reads=(f"pt{dc % 2}",), writes=(f"wst{dc}",),
                  tag=f"wq.e{fc}.{dc}")
            if wname == "l0_proj_w":
                dst = d["pwT"][fc, dc, :, :]
                res = f"pw.{fc}.{dc}"
            elif fc >= 16:
                dst = d["wqT_v"][dc, :, (fc - 16) * 128:(fc - 15) * 128]
                res = f"wv.{dc}.{fc - 16}"
            else:
                dst = d["wqT_qk"][fc, dc, :, :]
                res = f"wqk.{fc}.{dc}"
            p.dma("sp_dma", dst, s["wst"][:, dc, :], reads=(f"wst{dc}",),
                  writes=(res,), tag=f"wq.s{fc}.{dc}")


def emit_wcast(k):
    p, s, d = k.p, k.s, k.d
    CW = k.T
    w1f = d["l0_w1"][:, :].rearrange("(a p) f -> p a f", p=128)
    w2f = d["l0_w2"][:, :].rearrange("(b p) f -> p b f", p=128)
    for dc in range(NDC):
        for r in range(FF // CW):
            p.dma("pl_dma", s["act_b"][:, dc, :],
                  w1f[:, dc, r * CW:(r + 1) * CW],
                  writes=(f"act_b.{dc}",), tag=f"wc1i.{r}.{dc}")
            p.dma("pl_dma", d["w1b"][:, dc, r * CW:(r + 1) * CW],
                  s["act_b"][:, dc, :], reads=(f"act_b.{dc}",),
                  writes=(f"w1b.{r}.{dc}",), tag=f"wc1o.{r}.{dc}")
    cc = min(CW, D)
    g = 0
    for b in range(NFC):
        for r in range(D // cc):
            dc = g % NDC
            g += 1
            p.dma("pl_dma", s["act_b"][:, dc, 0:cc],
                  w2f[:, b, r * cc:(r + 1) * cc],
                  writes=(f"act_b.{dc}",), tag=f"wc2i.{b}.{r}")
            p.dma("pl_dma", d["w2b"][:, b, r * cc:(r + 1) * cc],
                  s["act_b"][:, dc, 0:cc], reads=(f"act_b.{dc}",),
                  writes=(f"w2b.{b}.{r}",), tag=f"wc2o.{b}.{r}")


def emit_l0_stats(k, bi):
    p, s, d = k.p, k.s, k.d
    T, NTC, LC = k.T, k.NTC, k.LC
    st = s["stat"]
    nl = T // LC
    for dc in range(NDC):
        for li in range(nl):
            sl = (dc * nl + li) % 2
            p.dma("sp_dma", s["fx"][:, sl, 0:LC], d["x_fm"][bi, dc, :, li * LC:(li + 1) * LC],
                  writes=(f"fx{sl}",), tag=f"s{bi}.x{dc}.{li}")

            def sqc(e, sl=sl, dc=dc, li=li):
                e.activation(s["sq"][:, sl, :], s["fx"][:, sl, 0:LC], ACTF.Square)
                e.drain()
                return e.activation(s["act_a"][:, dc, li * LC:(li + 1) * LC],
                                    s["fx"][:, sl, 0:LC], ACTF.Identity)
            p.act(sqc, reads=(f"fx{sl}",), writes=(f"sq{sl}", f"act_a.{dc}"),
                  tag=f"s{bi}.sq{dc}.{li}")
            for q5 in range(LC // 512):
                tc = li * (LC // 512) + q5

                def mm1(e, dc=dc, li=li, q5=q5, tc=tc):
                    return e.matmul(k.psum[f"pb{tc}"][:, :], s["ones_b"][:, :],
                                    s["act_a"][:, dc, (li * LC + q5 * 512):(li * LC + (q5 + 1) * 512)],
                                    start=(dc == 0), stop=(dc == NDC - 1))
                p.pe(mm1, reads=(f"act_a.{dc}", "ones_b"), writes=(f"pt{tc}",),
                     tag=f"s{bi}.mS{dc}.{tc}")

                def mm2(e, sl=sl, q5=q5, dc=dc, tc=tc):
                    return e.matmul(k.psum[f"pb{NTC + tc}"][:, :], s["ones_b"][:, :],
                                    s["sq"][:, sl, q5 * 512:(q5 + 1) * 512],
                                    start=(dc == 0), stop=(dc == NDC - 1))
                p.pe(mm2, reads=(f"sq{sl}", "ones_b"), writes=(f"pt{NTC + tc}",),
                     tag=f"s{bi}.mQ{dc}.{tc}")
    for tc in range(NTC):
        def stats(e, tc=tc):
            e.tensor_scalar(s["mur"][:, 0, :], k.psum[f"pb{tc}"][:, :],
                            1.0 / 1024.0, None, ALU.mult)
            e.drain()
            e.tensor_scalar(s["rr"][:, 0, :], k.psum[f"pb{NTC + tc}"][:, :],
                            1.0 / 1024.0, None, ALU.mult)
            e.drain()
            e.scalar_tensor_tensor(s["rsc"][:, 0:512], s["mur"][:, 0, :], 1.0,
                                   s["mur"][:, 0, :], ALU.mult, ALU.mult)
            e.drain()
            return e.tensor_sub(s["rsc"][:, 0:512], s["rr"][:, 0, :], s["rsc"][:, 0:512])
        p.dve(stats, reads=(f"pt{tc}", f"pt{NTC + tc}"), writes=("rsc", "mur0", "rr0"),
              tag=f"s{bi}.st{tc}")

        def sqr(e):
            return e.activation(s["rsc"][:, 0:512], s["rsc"][:, 0:512], ACTF.Sqrt, bias=s["cst"][:, 0:1])
        p.act(sqr, reads=("rsc",), writes=("rsc",), tag=f"s{bi}.sd{tc}")

        def rcp(e):
            recip_acc(e, s["rr"][:, 1, :], s["rsc"][:, 0:512], s["rec"][:, 0:512])
            e.drain()
            return e.tensor_mul(s["mur"][:, 1, :], s["mur"][:, 0, :], s["rr"][:, 1, :])
        p.dve(rcp, reads=("rsc", "mur0"), writes=("rr1", "mur1", "rec"),
              tag=f"s{bi}.rc{tc}")
        for dc in range(NDC):
            sl = (tc * NDC + dc) % 2
            p.dma("sp_dma", s["fx"][:, sl, 0:512],
                  d["x_fm"][bi, dc, :, tc * 512:(tc + 1) * 512],
                  writes=(f"fx{sl}",), tag=f"s{bi}.x2{dc}.{tc}")

            def app(e, sl=sl, dc=dc):
                e.tensor_mul(s["ev"][:, 3, :], s["fx"][:, sl, 0:512], s["rr"][:, 1, :])
                e.drain()
                e.tensor_sub(s["fy"][:, sl, :], s["ev"][:, 3, :], s["mur"][:, 1, :])
                e.drain()
                e.tensor_reduce(s["small"][:, 0:1], s["fy"][:, sl, :], AX.X, ALU.max)
                e.drain()
                e.tensor_reduce(s["small"][:, 1:2], s["fy"][:, sl, :], AX.X, ALU.min)
                e.drain()
                e.tensor_tensor(st[:, K.S_MXC + dc:K.S_MXC + dc + 1],
                                st[:, K.S_MXC + dc:K.S_MXC + dc + 1],
                                s["small"][:, 0:1], ALU.max)
                e.drain()
                return e.tensor_tensor(st[:, K.S_MNC + dc:K.S_MNC + dc + 1],
                                       st[:, K.S_MNC + dc:K.S_MNC + dc + 1],
                                       s["small"][:, 1:2], ALU.min)
            p.dve(app, reads=(f"fx{sl}", "rr1", "mur1"),
                  writes=(f"fy{sl}", "chanstat", "ev3", "small"),
                  tag=f"s{bi}.ap{dc}.{tc}")
            p.dma("sp_dma", d["xn32"][bi, dc, :, tc * 512:(tc + 1) * 512],
                  s["fy"][:, sl, :], reads=(f"fy{sl}",),
                  writes=(f"xn32.{bi}.{dc}.{tc}",), tag=f"s{bi}.w{dc}.{tc}")


def emit_ar1(k):
    p, s, d = k.p, k.s, k.d
    st = s["stat"]

    def pack(e):
        e.tensor_copy(s["tmp8"][:, 0:8], st[:, K.S_MXC:K.S_MXC + 8])
        e.drain()
        return e.tensor_scalar(s["tmp8"][:, 8:16], st[:, K.S_MNC:K.S_MNC + 8],
                               -1.0, None, ALU.mult)
    p.dve(pack, reads=("chanstat",), writes=("tmp8a",), tag="ar1.pk")
    p.dma("pl_dma", d["ar1_in"][:, :], s["tmp8"][:, 0:16], reads=("tmp8a",),
          writes=("ar1_in",), tag="ar1.st")
    p.collective(ALU.max, [list(range(k.n_cores))],
                 d["ar1_in"].ap().opt(), d["ar1_out"].ap().opt(),
                 reads=("ar1_in",), writes=("ar1_out",), tag="ar1.cc")
    p.dma("pl_dma", s["tmp8"][:, 16:32], d["ar1_out"][:, :], reads=("ar1_out",),
          writes=("tmp8b",), tag="ar1.ld")

    def unpack(e):
        e.tensor_copy(st[:, K.S_MXC:K.S_MXC + 8], s["tmp8"][:, 16:24])
        e.drain()
        return e.tensor_scalar(st[:, K.S_MNC:K.S_MNC + 8], s["tmp8"][:, 24:32],
                               -1.0, None, ALU.mult)
    p.dve(unpack, reads=("tmp8b",), writes=("chanstat",), tag="ar1.un")
    qparams_perpart(k, st[:, K.S_MNC:K.S_MNC + 8], st[:, K.S_MXC:K.S_MXC + 8],
                    st[:, K.S_CINV:K.S_CINV + 8], st[:, K.S_CZP:K.S_CZP + 8],
                    st[:, K.S_CS:K.S_CS + 8], st[:, K.S_CNZS:K.S_CNZS + 8],
                    "chanpar", ("chanstat",), s["tmp8"][:, 88:96], tag="ar1.par")


def emit_ffn(k, bi):
    p, s, d = k.p, k.s, k.d
    T, NTC, LC = k.T, k.NTC, k.LC
    if bi == 1:
        p.fence(reads=(), writes=("wvb0", "w1t0", "w1t1", "w2t0", "w2t1"), tag="f.ffn1")
    for dc in range(NDC):
        p.dma("pl_dma", s["act_a"][:, dc, :], d["xn32"][bi, dc, :, :],
              reads=tuple(f"xn32.{bi}.{dc}.{t}" for t in range(NTC)),
              writes=(f"act_a.{dc}",), tag=f"fn{bi}.xn{dc}")
    for tcb in range(T // LC):
        for fc in range(NFC):
            wsl = fc % 2
            p.dma("pl_dma", s["w1t"][:, wsl, :, :],
                  d["w1b"][:, :, fc * 128:(fc + 1) * 128],
                  reads=tuple(f"w1b.{fc * 128 // k.T}.{dcc}"
                              for dcc in range(NDC)),
                  writes=(f"w1t{wsl}",), tag=f"fn{bi}.w1.{tcb}.{fc}")
            for h2 in range(LC // 512):
                pb = k.psum[f"pb{h2}"]
                tsl = slice(tcb * LC + h2 * 512, tcb * LC + (h2 + 1) * 512)
                for dc in range(NDC):
                    def mm(e, pb=pb, wsl=wsl, dc=dc, tsl=tsl):
                        return e.matmul(pb[:, :], s["w1t"][:, wsl, dc, :],
                                        s["act_a"][:, dc, tsl],
                                        start=(dc == 0), stop=(dc == NDC - 1))
                    p.pe(mm, reads=(f"w1t{wsl}", f"act_a.{dc}"), writes=(f"pt{h2}",),
                         tag=f"fn{bi}.h{tcb}.{fc}.{h2}.{dc}")

                def gelu(e, pb=pb, fc=fc, h2=h2):
                    return e.activation(s["hbuf"][:, fc, h2 * 512:(h2 + 1) * 512],
                                        pb[:, :], ACTF.Gelu_apprx_tanh,
                                        bias=s["bias"][:, K.B_B1 + fc:K.B_B1 + fc + 1])
                p.act(gelu, reads=(f"pt{h2}", "bias"), writes=(f"hbuf.{fc}",),
                      tag=f"fn{bi}.g{tcb}.{fc}.{h2}")
        for dc in range(NDC):
            for hf in range(2):
                p.dma("pl_dma" if hf else "sp_dma",
                      s["w2t"][:, hf * (NFC // 2):(hf + 1) * (NFC // 2), :],
                      d["w2b"][:, hf * (NFC // 2):(hf + 1) * (NFC // 2),
                               dc * 128:(dc + 1) * 128],
                      reads=tuple(f"w2b.{b}.{dc * 128 // min(k.T, D)}"
                                  for b in range(hf * (NFC // 2),
                                                 (hf + 1) * (NFC // 2))),
                      writes=(f"w2t{hf}",), tag=f"fn{bi}.w2.{tcb}.{dc}.{hf}")
            for h2 in range(LC // 512):
                pb = k.psum[f"pb{2 + h2}"]
                for fc in range(NFC):
                    def mm(e, pb=pb, fc=fc, h2=h2):
                        return e.matmul(pb[:, :], s["w2t"][:, fc, :],
                                        s["hbuf"][:, fc, h2 * 512:(h2 + 1) * 512],
                                        start=(fc == 0), stop=(fc == NFC - 1))
                    p.pe(mm, reads=(f"w2t{fc // (NFC // 2)}", f"hbuf.{fc}"),
                         writes=(f"pt{2 + h2}",),
                         tag=f"fn{bi}.y{tcb}.{dc}.{h2}.{fc}")

                def silu(e, pb=pb, dc=dc, h2=h2):
                    return e.activation(s["ev"][:, h2, :], pb[:, :], ACTF.Silu,
                                        bias=s["bias"][:, K.B_B2 + dc:K.B_B2 + dc + 1])
                p.act(silu, reads=(f"pt{2 + h2}", "bias"), writes=(f"ev{h2}",),
                      tag=f"fn{bi}.s{tcb}.{dc}.{h2}")
                tc = tcb * (LC // 512) + h2
                p.dma("sp_dma", d["g0"][bi, dc, :, tc * 512:(tc + 1) * 512],
                      s["ev"][:, h2, :], reads=(f"ev{h2}",),
                      writes=(f"g0.{bi}.{dc}.{tc}",), tag=f"fn{bi}.gs{tcb}.{dc}.{h2}")


def emit_xq_qkv(k, bi):
    p, s, d = k.p, k.s, k.d
    T, NTC = k.T, k.NTC
    st = s["stat"]
    for dc in range(NDC):
        for tc in range(NTC):
            sl = (dc * NTC + tc) % 2
            p.dma("sp_dma", s["fx"][:, sl, 0:512],
                  d["xn32"][bi, dc, :, tc * 512:(tc + 1) * 512],
                  reads=(f"xn32.{bi}.{dc}.{tc}",), writes=(f"fx{sl}",),
                  tag=f"xq{bi}.{dc}.{tc}")
            quant_chain(k, s["fx"][:, sl, 0:512],
                        s["act_a"][:, dc, tc * 512:(tc + 1) * 512],
                        st[:, K.S_CINV + dc:K.S_CINV + dc + 1],
                        st[:, K.S_CZP + dc:K.S_CZP + dc + 1],
                        st[:, K.S_CS + dc:K.S_CS + dc + 1],
                        st[:, K.S_CNZS + dc:K.S_CNZS + dc + 1],
                        reads=(f"fx{sl}", "chanpar"), writes=(f"act_a.{dc}",),
                        sl=sl, tag=f"xq{bi}.{dc}.{tc}")
    p.fence(reads=(), writes=("w1t0", "w1t1", "w2t0", "w2t1", "wvb0"),
            tag=f"f.v{bi}")
    for dc in range(NDC):
        p.dma("pl_dma", s["wvb0"][:, dc, :], d["wqT_v"][dc, :, :],
              reads=tuple(f"wv.{dc}.{j}" for j in range(8)),
              writes=("wvb0",), tag=f"qv{bi}.w{dc}")
    for fc in range(16):
        wsl = fc % 2
        p.dma("sp_dma", s["wtile"][:, wsl, :, :],
              d["wqT_qk"][fc, :, :, :].rearrange("a p j -> p a j"),
              reads=tuple(f"wqk.{fc}.{dcc}" for dcc in range(NDC)),
              writes=(f"wtile{wsl}",), tag=f"qk{bi}.w{fc}")
        for dc in range(NDC):
            for tc in range(NTC):
                def mm(e, wsl=wsl, dc=dc, tc=tc):
                    return e.matmul(k.psum[f"pb{tc}"][:, :], s["wtile"][:, wsl, dc, :],
                                    s["act_a"][:, dc, tc * 512:(tc + 1) * 512],
                                    start=(dc == 0), stop=(dc == NDC - 1))
                p.pe(mm, reads=(f"wtile{wsl}", f"act_a.{dc}"), writes=(f"pt{tc}",),
                     tag=f"qk{bi}.m{fc}.{dc}.{tc}")
        for tc in range(NTC):
            def evac(e, fc=fc, tc=tc):
                return e.activation(s["ev"][:, tc % 2, :], k.psum[f"pb{tc}"][:, :],
                                    ACTF.Identity,
                                    bias=s["bias"][:, K.B_L0QKV + fc:K.B_L0QKV + fc + 1])
            p.act(evac, reads=(f"pt{tc}", "bias"), writes=(f"ev{tc % 2}",),
                  tag=f"qk{bi}.e{fc}.{tc}")

            def mnmx(e, fc=fc, tc=tc):
                cx = K.S_RMX + (0 if fc < 8 else 1)
                cn = K.S_RMN + (0 if fc < 8 else 1)
                e.tensor_reduce(s["small"][:, 2:3], s["ev"][:, tc % 2, :], AX.X, ALU.max)
                e.drain()
                e.tensor_reduce(s["small"][:, 3:4], s["ev"][:, tc % 2, :], AX.X, ALU.min)
                e.drain()
                e.tensor_tensor(st[:, cx:cx + 1], st[:, cx:cx + 1],
                                s["small"][:, 2:3], ALU.max)
                e.drain()
                return e.tensor_tensor(st[:, cn:cn + 1], st[:, cn:cn + 1],
                                       s["small"][:, 3:4], ALU.min)
            p.dve(mnmx, reads=(f"ev{tc % 2}",), writes=("qkvstat", "small"),
                  tag=f"qk{bi}.n{fc}.{tc}")
            p.dma("sp_dma" if (fc + tc) % 2 == 0 else "pl_dma",
                  d["qkraw"][bi, fc, tc, :, :], s["ev"][:, tc % 2, :],
                  reads=(f"ev{tc % 2}",), writes=(f"qkraw.{bi}.{fc}.{tc}",),
                  tag=f"qk{bi}.s{fc}.{tc}")
    for t128 in range(k.NT128):
        for fv in range(2):
            pq = 4 + (2 * t128 + fv) % 2
            pb = k.psum[f"pb{pq}"]
            for dc in range(NDC):
                def mm(e, pb=pb, t128=t128, fv=fv, dc=dc):
                    return e.matmul(pb[:, :],
                                    s["act_a"][:, dc, t128 * 128:(t128 + 1) * 128],
                                    s["wvb0"][:, dc, fv * 512:(fv + 1) * 512],
                                    start=(dc == 0), stop=False)
                p.pe(mm, reads=("wvb0", f"act_a.{dc}"), writes=(f"pt{pq}",),
                     tag=f"qv{bi}.m{t128}.{fv}.{dc}")

            def mmb(e, pb=pb, fv=fv):
                return e.matmul(pb[:, :], s["ones_b"][0:1, 0:128],
                                s["brow"][0:1, 0, fv * 512:(fv + 1) * 512],
                                start=False, stop=True)
            p.pe(mmb, reads=("brow0", "ones_b"), writes=(f"pt{pq}",),
                 tag=f"qv{bi}.b{t128}.{fv}")

            def evac(e, pb=pb, fv=fv):
                return e.tensor_copy(s["ev"][:, fv, :], pb[:, :])
            p.dve(evac, reads=(f"pt{pq}",), writes=(f"ev{fv}",),
                  tag=f"qv{bi}.e{t128}.{fv}")

            def mnmx(e, fv=fv):
                e.tensor_reduce(s["small"][:, 4:5], s["ev"][:, fv, :], AX.X, ALU.max)
                e.drain()
                e.tensor_reduce(s["small"][:, 5:6], s["ev"][:, fv, :], AX.X, ALU.min)
                e.drain()
                cx, cn = K.S_RMX + 2, K.S_RMN + 2
                e.tensor_tensor(st[:, cx:cx + 1], st[:, cx:cx + 1],
                                s["small"][:, 4:5], ALU.max)
                e.drain()
                return e.tensor_tensor(st[:, cn:cn + 1], st[:, cn:cn + 1],
                                       s["small"][:, 5:6], ALU.min)
            p.dve(mnmx, reads=(f"ev{fv}",), writes=("qkvstat", "small"),
                  tag=f"qv{bi}.n{t128}.{fv}")
            p.dma("sp_dma" if (t128 + fv) % 2 == 0 else "pl_dma",
                  d["vraw"][bi, t128, fv, :, :], s["ev"][:, fv, :],
                  reads=(f"ev{fv}",), writes=(f"vraw.{bi}.{t128}.{fv}",),
                  tag=f"qv{bi}.s{t128}.{fv}")


def bcast_params(k, pbname, src_res, dst_fn, tag):
    pass


def emit_ar2(k):
    p, s, d = k.p, k.s, k.d
    st = s["stat"]

    def pack(e):
        e.memset(s["tmp8"][:, 38:40], 0.0)
        e.drain()
        e.tensor_copy(s["tmp8"][:, 32:35], st[:, K.S_RMX:K.S_RMX + 3])
        e.drain()
        return e.tensor_scalar(s["tmp8"][:, 35:38], st[:, K.S_RMN:K.S_RMN + 3],
                               -1.0, None, ALU.mult)
    p.dve(pack, reads=("qkvstat",), writes=("tmp8c",), tag="ar2.pk")

    def tp(e):
        return e.transpose(k.psum["pb6"][0:8, 0:128], s["tmp8"][:, 32:40],
                           s["idn_f"][:, :])
    p.pe(tp, reads=("tmp8c", "idn_f"), writes=("pt6",), tag="ar2.tp")

    def red(e):
        return e.tensor_reduce(s["small"][0:8, 6:7], k.psum["pb6"][0:8, 0:128],
                               AX.X, ALU.max)
    p.dve(red, reads=("pt6",), writes=("small",), tag="ar2.rd")
    p.dma("pl_dma", d["ar2_in"][:, :], s["small"][0:8, 6:7], reads=("small",),
          writes=("ar2_in",), tag="ar2.st")
    p.collective(ALU.max, [list(range(k.n_cores))],
                 d["ar2_in"].ap().opt(), d["ar2_out"].ap().opt(),
                 reads=("ar2_in",), writes=("ar2_out",), tag="ar2.cc")
    p.dma("pl_dma", s["small"][0:8, 7:8], d["ar2_out"][:, :], reads=("ar2_out",),
          writes=("small78",), tag="ar2.ld")

    def tp2(e):
        return e.transpose(k.psum["pb7"][0:1, 0:8], s["small"][0:8, 7:8],
                           s["idn_f"][0:8, 0:8])
    p.pe(tp2, reads=("small78", "idn_f"), writes=("pt7",), tag="ar2.tp2")

    def ev7(e):
        return e.tensor_copy(s["small"][0:1, 8:16], k.psum["pb7"][0:1, 0:8])
    p.dve(ev7, reads=("pt7",), writes=("smallrow",), tag="ar2.e7")

    def bc(e):
        return e.matmul(k.psum["pb7"][:, 0:8], s["ones_f"][0:1, :],
                        s["small"][0:1, 8:16], start=True, stop=True)
    p.pe(bc, reads=("smallrow", "ones_f"), writes=("pt7",), tag="ar2.bc")

    def par(e):
        qp = s["qp"]
        pbv = k.psum["pb7"]
        sm = s["small"]
        e.tensor_scalar(sm[:, 9:12], pbv[:, 3:6], -1.0, None, ALU.mult)
        e.drain()
        e.tensor_scalar(sm[:, 9:12], sm[:, 9:12], 0.0, None, ALU.min)
        e.drain()
        e.tensor_scalar(sm[:, 12:15], pbv[:, 0:3], 0.0, None, ALU.max)
        e.drain()
        e.tensor_sub(sm[:, 15:18], sm[:, 12:15], sm[:, 9:12])
        e.drain()
        e.tensor_scalar(qp[:, K.QP_S:K.QP_S + 3], sm[:, 15:18], 1.0 / 255.0, 1e-8,
                        ALU.mult, ALU.max)
        e.drain()
        recip_acc(e, qp[:, K.QP_INV:K.QP_INV + 3],
                                     qp[:, K.QP_S:K.QP_S + 3], sm[:, 15:18])
        e.drain()
        e.tensor_mul(sm[:, 15:18], sm[:, 9:12], qp[:, K.QP_INV:K.QP_INV + 3])
        e.drain()
        e.tensor_scalar(s["qi"][:, 0, 0:3], sm[:, 15:18], -1.0, None, ALU.mult)
        e.drain()
        e.tensor_copy(qp[:, K.QP_ZP:K.QP_ZP + 3], s["qi"][:, 0, 0:3])
        e.drain()
        e.tensor_mul(sm[:, 15:18], qp[:, K.QP_ZP:K.QP_ZP + 3], qp[:, K.QP_S:K.QP_S + 3])
        e.drain()
        return e.tensor_scalar(qp[:, K.QP_NZS:K.QP_NZS + 3], sm[:, 15:18],
                               -1.0, None, ALU.mult)
    p.dve(par, reads=("pt7",), writes=("qpar", "small", "qi0"), tag="ar2.par")


def emit_l0_attention(k, bi):
    p, s, d = k.p, k.s, k.d
    T, NTC, NT128 = k.T, k.NTC, k.NT128
    qp = s["qp"]
    if bi == 0:
        p.fence(reads=(), writes=("wvb0", "w1t0", "w1t1", "w2t",
                       *(f"hbuf.{fc}" for fc in range(NFC)),
                       "vaug.ones", "nrmX"), tag="f.attn0")

        def vm(e):
            return e.memset(s["vaug"][:, :, :, :], 1.0)
        p.dve(vm, reads=("vaug.ones",), writes=("vaug.ones",), tag="vmemset")
    for sec, dst, base, ti in (("k", "act_a", 8, 1), ("q", "act_b", 0, 0)):
        for fcl in range(8):
            fc = base + fcl
            for tc in range(NTC):
                sl = (fcl * NTC + tc) % 2
                p.dma("sp_dma", s["fx"][:, sl, 0:512], d["qkraw"][bi, fc, tc, :, :],
                      reads=(f"qkraw.{bi}.{fc}.{tc}",), writes=(f"fx{sl}",),
                      tag=f"a{bi}.l{sec}{fcl}.{tc}")
                quant_chain(k, s["fx"][:, sl, 0:512],
                            s[dst][:, fcl, tc * 512:(tc + 1) * 512],
                            qp[:, K.QP_INV + ti:K.QP_INV + ti + 1],
                            qp[:, K.QP_ZP + ti:K.QP_ZP + ti + 1],
                            qp[:, K.QP_S + ti:K.QP_S + ti + 1],
                            qp[:, K.QP_NZS + ti:K.QP_NZS + ti + 1],
                            reads=(f"fx{sl}", "qpar"), writes=(f"{dst}.{fcl}",),
                            sl=sl, tag=f"a{bi}.q{sec}{fcl}.{tc}")
    for t128 in range(NT128):
        for fv in range(2):
            sl = (t128 * 2 + fv) % 2
            p.dma("sp_dma", s["fx"][:, sl, 0:512], d["vraw"][bi, t128, fv, :, :],
                  reads=(f"vraw.{bi}.{t128}.{fv}",), writes=(f"fx{sl}",),
                  tag=f"a{bi}.lv{t128}.{fv}")
            quant_chain(k, s["fx"][:, sl, 0:512],
                        s["vaug"][:, t128, fv * 8:(fv + 1) * 8, 0:64],
                        qp[:, K.QP_INV + 2:K.QP_INV + 3],
                        qp[:, K.QP_ZP + 2:K.QP_ZP + 3],
                        qp[:, K.QP_S + 2:K.QP_S + 3],
                        qp[:, K.QP_NZS + 2:K.QP_NZS + 3],
                        reads=(f"fx{sl}", "qpar"), writes=(f"vg.{t128 % 4}",),
                        sl=sl, tag=f"a{bi}.qv{t128}.{fv}")
    if bi == 0 and "dbg_ka" in K.DEBUG_OUT:
        for dc in range(NDC):
            p.dma("pl_dma", d["dbg_ka"][:, dc, :], s["act_a"][:, dc, :],
                  reads=(f"act_a.{dc}",), writes=("dbgka",), tag=f"dbg.ka{dc}")
            p.dma("pl_dma", d["dbg_qa"][:, dc, :], s["act_b"][:, dc, :],
                  reads=(f"act_b.{dc}",), writes=("dbgqa",), tag=f"dbg.qa{dc}")
    attention_core(k, bi, layer=0, Ks=s["act_a"], Kres="act_a",
                   Qs=s["act_b"], Qres="act_b", nrm=s["nrm0"], nres="nrmX")


def attention_core(k, bi, layer, Ks, Kres, Qs, Qres, nrm, nres):
    p, s, d = k.p, k.s, k.d
    T, NTC, NT128 = k.T, k.NTC, k.NT128
    st = s["stat"]
    for idx, (src, sres) in enumerate(((Qs, Qres), (Ks, Kres))):
        off = 0 if idx == 0 else T
        for tc in range(NTC):
            for dc in range(NDC):
                def sqr(e, src=src, dc=dc, tc=tc):
                    return e.activation(s["sq"][:, 0, 0:512],
                                        src[:, dc, tc * 512:(tc + 1) * 512],
                                        ACTF.Square)
                p.act(sqr, reads=(f"{sres}.{dc}",), writes=("sq0",),
                      tag=f"n{layer}{bi}.s{idx}.{tc}.{dc}")

                def mm(e, dc=dc):
                    return e.matmul(k.psum["pb6"][0:16, :], s["hind"][:, dc, :],
                                    s["sq"][:, 0, 0:512],
                                    start=(dc == 0), stop=(dc == NDC - 1))
                p.pe(mm, reads=("sq0", "hind"), writes=("pt6",),
                     tag=f"n{layer}{bi}.m{idx}.{tc}.{dc}")

            def rt(e, off=off, tc=tc):
                return e.activation(nrm[0:16, off + tc * 512:off + (tc + 1) * 512],
                                    k.psum["pb6"][0:16, :], ACTF.Sqrt)
            p.act(rt, reads=("pt6",), writes=(nres,), tag=f"n{layer}{bi}.r{idx}.{tc}")

    if layer == 0 and bi == 0 and "dbg_hind" in K.DEBUG_OUT:
        p.dma("pl_dma", d["dbg_hind"][:, :, :], s["hind"][:, :, :],
              reads=("hind",), writes=("hind",), tag="dbg.hind")
    if layer == 0 and bi == 0 and "dbg_nrm" in K.DEBUG_OUT:
        p.dma("pl_dma", d["dbg_nrm"][:, 0, 0:T], nrm[0:16, 0:T],
              reads=(nres,), writes=(nres,), tag="dbg.nrmq")
        p.dma("pl_dma", d["dbg_nrm"][:, 1, 0:T], nrm[0:16, T:2 * T],
              reads=(nres,), writes=(nres,), tag="dbg.nrmk")

    def cmax(e):
        e.tensor_reduce(s["small"][0:16, 20:21], nrm[0:16, 0:T], AX.X, ALU.max)
        e.drain()
        e.tensor_reduce(s["small"][0:16, 21:22], nrm[0:16, T:2 * T], AX.X, ALU.max)
        e.drain()
        e.tensor_mul(s["small"][0:16, 22:23], s["small"][0:16, 20:21],
                     s["small"][0:16, 21:22])
        e.drain()
        return e.tensor_scalar(s["small"][0:16, 23:24], s["small"][0:16, 22:23],
                               -0.125, None, ALU.mult)
    p.dve(cmax, reads=(nres,), writes=("small",), tag=f"c{layer}{bi}.mx")

    if layer == 0 and bi == 0 and "dbg_sm" in K.DEBUG_OUT:
        p.dma("sp_dma", d["dbg_sm"][:, :], s["small"][0:16, 20:24],
              reads=("small",), writes=("small",), tag="dbg.sm")

    def ctp(e):
        return e.transpose(k.psum["pb7"][0:1, 0:16], s["small"][0:16, 23:24],
                           s["idn_f"][0:16, 0:16])
    p.pe(ctp, reads=("small", "idn_f"), writes=("pt7",), tag=f"c{layer}{bi}.tp")

    def cev(e):
        return e.tensor_copy(s["small"][0:1, 32:48], k.psum["pb7"][0:1, 0:16])
    p.dve(cev, reads=("pt7",), writes=("small",), tag=f"c{layer}{bi}.ev")

    def cbc(e):
        return e.matmul(k.psum["pb7"][:, 0:16], s["ones_f"][0:1, :],
                        s["small"][0:1, 32:48], start=True, stop=True)
    p.pe(cbc, reads=("small", "ones_f"), writes=("pt7",), tag=f"c{layer}{bi}.bc")

    def cbe(e):
        return e.tensor_copy(s["cbias"][:, :], k.psum["pb7"][:, 0:16])
    p.dve(cbe, reads=("pt7",), writes=("cbias",), tag=f"c{layer}{bi}.eb")
    if layer == 0 and bi == 0 and "dbg_cb" in K.DEBUG_OUT:
        p.dma("sp_dma", d["dbg_cb"][:, :], s["cbias"][:, :],
              reads=("cbias",), writes=("cbias",), tag="dbg.cb")

    for hp in range(8):
        for qc in range(NTC):
            qsl = slice(qc * 512, (qc + 1) * 512)
            NJ = NT128 // 2

            def emit_S(j, sub):
                h = 2 * hp + sub
                po = 64 * sub
                for half in range(2):
                    kc = 2 * j + half

                    def mmS(e, hp=hp, po=po, kc=kc, qsl=qsl,
                            sub=sub, half=half):
                        return e.matmul(
                            k.psum[f"pb{2 + 2 * sub}{3 + 2 * sub}"][
                                :, 512 * half:512 * (half + 1)],
                            Ks[po:po + 64, hp, kc * 128:(kc + 1) * 128],
                            Qs[po:po + 64, hp, qsl],
                            start=True, stop=True)
                    p.pe(mmS, reads=(f"{Kres}.{hp}", f"{Qres}.{hp}"),
                         writes=(f"pt{2 + 2 * sub + half}",),
                         tag=f"A{layer}{bi}.S{hp}.{sub}.{qc}.{kc}")

            def emit_E(j, sub):
                h = 2 * hp + sub

                def expv(e, h=h, sub=sub):
                    return e.activation(s["ptile"][:, sub, :],
                                        k.psum[f"pb{2 + 2 * sub}{3 + 2 * sub}"][:, :],
                                        ACTF.Exp, scale=s["cst"][:, 1:2],
                                        bias=s["cbias"][:, h:h + 1])
                p.act(expv, reads=(f"pt{2 + 2 * sub}", f"pt{3 + 2 * sub}",
                                   "cbias"),
                      writes=(f"ptile{2 * sub}", f"ptile{2 * sub + 1}"),
                      tag=f"A{layer}{bi}.E{hp}.{sub}.{qc}.{j}")

            def emit_PV(j, sub):
                h = 2 * hp + sub
                pacc = k.psum[f"pb{sub}"]
                for half in range(2):
                    kc = 2 * j + half

                    def mmPV(e, h=h, kc=kc, sub=sub, half=half, pacc=pacc):
                        return e.matmul(pacc[0:66, :],
                                        s["vaug"][:, kc, h, 0:66],
                                        s["ptile"][:, sub, 512 * half:512 * (half + 1)],
                                        start=(kc == 0), stop=(kc == NT128 - 1))
                    p.pe(mmPV, reads=(f"ptile{2 * sub + half}", f"vg.{kc % 4}",
                                      "vaug.ones"),
                         writes=(f"pt{sub}",),
                         tag=f"A{layer}{bi}.P{hp}.{sub}.{qc}.{kc}")

            emit_S(0, 0)
            emit_S(0, 1)
            for j in range(NJ):
                for sub in range(2):
                    emit_E(j, sub)
                    if j + 1 < NJ:
                        emit_S(j + 1, sub)
                    emit_PV(j, sub)

            def dcp0(e):
                return e.tensor_copy(s["den"][64:65, :512], k.psum["pb0"][64:65, :])
            p.dve(dcp0, reads=("pt0",), writes=("den",),
                  tag=f"A{layer}{bi}.D{hp}.0.{qc}")

            def dcp1(e):
                return e.tensor_copy(s["rsc"][64:65, 0:512], k.psum["pb1"][64:65, :])
            p.dve(dcp1, reads=("pt1",), writes=("rsc",),
                  tag=f"A{layer}{bi}.D{hp}.1.{qc}")

            def dbc0(e):
                return e.matmul(k.psum["pb67"][0:64, 0:512],
                                s["onesf64"][64:65, 0:64],
                                s["den"][64:65, :512], start=True, stop=True)
            p.pe(dbc0, reads=("den", "onesf64"), writes=("pt6",),
                 tag=f"A{layer}{bi}.B{hp}.0.{qc}")

            def dbc1(e):
                return e.matmul(k.psum["pb67"][0:64, 512:1024],
                                s["onesf64"][64:65, 0:64],
                                s["rsc"][64:65, 0:512], start=True, stop=True)
            p.pe(dbc1, reads=("rsc", "onesf64"), writes=("pt7",),
                 tag=f"A{layer}{bi}.B{hp}.1.{qc}")

            def rcd(e):
                e.activation(s["ev"][0:64, 2:4, :], k.psum["pb67"][0:64, :],
                             ACTF.Ln)
                e.drain()
                e.activation(s["rec"][0:64, 0:512], s["ev"][0:64, 2, :],
                             ACTF.Exp, scale=s["cst"][0:64, 2:3],
                             bias=s["cst"][0:64, 3:4])
                return e.activation(s["rsc"][0:64, 0:512], s["ev"][0:64, 3, :],
                                    ACTF.Exp, scale=s["cst"][0:64, 2:3],
                                    bias=s["cst"][0:64, 3:4])
            p.act(rcd, reads=("pt6", "pt7", "cst"),
                  writes=("rec", "rsc", "ev2", "ev3"),
                  tag=f"A{layer}{bi}.R{hp}.{qc}")

            for sub in range(2):
                h = 2 * hp + sub
                po = 64 * sub
                pacc = k.psum[f"pb{sub}"]
                rtile = "rec" if sub == 0 else "rsc"
                if layer == 0:
                    def div(e, pacc=pacc, sub=sub, rtile=rtile):
                        return e.tensor_mul(s["ev"][0:64, sub, :], pacc[0:64, :],
                                            s[rtile][0:64, 0:512])
                    p.dve(div, reads=(f"pt{sub}", rtile), writes=(f"ev{sub}",),
                          tag=f"A{layer}{bi}.V{hp}.{sub}.{qc}")

                    def omn(e, sub=sub):
                        e.tensor_reduce(s["small"][0:64, 24:25], s["ev"][0:64, sub, :],
                                        AX.X, ALU.max)
                        e.drain()
                        e.tensor_reduce(s["small"][0:64, 25:26], s["ev"][0:64, sub, :],
                                        AX.X, ALU.min)
                        e.drain()
                        e.tensor_tensor(st[0:64, K.S_OMX:K.S_OMX + 1],
                                        st[0:64, K.S_OMX:K.S_OMX + 1],
                                        s["small"][0:64, 24:25], ALU.max)
                        e.drain()
                        return e.tensor_tensor(st[0:64, K.S_OMN:K.S_OMN + 1],
                                               st[0:64, K.S_OMN:K.S_OMN + 1],
                                               s["small"][0:64, 25:26], ALU.min)
                    p.dve(omn, reads=(f"ev{sub}",), writes=("ostat", "small"),
                          tag=f"A{layer}{bi}.O{hp}.{sub}.{qc}")
                    p.dma("sp_dma", d["oraw"][bi, h, qc, :, :], s["ev"][0:64, sub, :],
                          reads=(f"ev{sub}",), writes=(f"oraw.{bi}.{h}.{qc}",),
                          tag=f"A{layer}{bi}.W{hp}.{sub}.{qc}")
                else:
                    def divb(e, pacc=pacc, po=po, hp=hp, qsl=qsl, rtile=rtile):
                        return e.tensor_mul(s["act_a"][po:po + 64, hp, qsl],
                                            pacc[0:64, :], s[rtile][0:64, 0:512])
                    p.dve(divb, reads=(f"pt{sub}", rtile), writes=(f"act_a.{hp}",),
                          tag=f"A{layer}{bi}.V{hp}.{sub}.{qc}")


def emit_ar3(k):
    p, s, d = k.p, k.s, k.d
    st = s["stat"]

    def pack(e):
        e.tensor_copy(s["tmp8"][0:64, 40:41], st[0:64, K.S_OMX:K.S_OMX + 1])
        e.drain()
        return e.tensor_scalar(s["tmp8"][0:64, 41:42], st[0:64, K.S_OMN:K.S_OMN + 1],
                               -1.0, None, ALU.mult)
    p.dve(pack, reads=("ostat",), writes=("tmp8d",), tag="ar3.pk")

    def tp(e):
        return e.transpose(k.psum["pb6"][0:2, 0:64], s["tmp8"][0:64, 40:42],
                           s["idn_f"][0:64, 0:64])
    p.pe(tp, reads=("tmp8d", "idn_f"), writes=("pt6",), tag="ar3.tp")

    def red(e):
        return e.tensor_reduce(s["small"][0:2, 28:29], k.psum["pb6"][0:2, 0:64],
                               AX.X, ALU.max)
    p.dve(red, reads=("pt6",), writes=("small",), tag="ar3.rd")
    p.dma("pl_dma", d["ar3_in"][:, :], s["small"][0:2, 28:29], reads=("small",),
          writes=("ar3_in",), tag="ar3.st")
    p.collective(ALU.max, [list(range(k.n_cores))],
                 d["ar3_in"].ap().opt(), d["ar3_out"].ap().opt(),
                 reads=("ar3_in",), writes=("ar3_out",), tag="ar3.cc")
    p.dma("pl_dma", s["small"][0:2, 29:30], d["ar3_out"][:, :], reads=("ar3_out",),
          writes=("small29",), tag="ar3.ld")

    def tp2(e):
        return e.transpose(k.psum["pb7"][0:1, 0:2], s["small"][0:2, 29:30],
                           s["idn_f"][0:2, 0:2])
    p.pe(tp2, reads=("small29", "idn_f"), writes=("pt7",), tag="ar3.tp2")

    def ev(e):
        return e.tensor_copy(s["small"][0:1, 26:28], k.psum["pb7"][0:1, 0:2])
    p.dve(ev, reads=("pt7",), writes=("small",), tag="ar3.ev")

    def bc(e):
        return e.matmul(k.psum["pb7"][:, 0:2], s["ones_f"][0:1, :],
                        s["small"][0:1, 26:28], start=True, stop=True)
    p.pe(bc, reads=("small", "ones_f"), writes=("pt7",), tag="ar3.bc")

    def par(e):
        qp = s["qp"]
        pbv = k.psum["pb7"]
        sm = s["small"]
        e.tensor_scalar(sm[:, 9:10], pbv[:, 1:2], -1.0, None, ALU.mult)
        e.drain()
        e.tensor_scalar(sm[:, 9:10], sm[:, 9:10], 0.0, None, ALU.min)
        e.drain()
        e.tensor_scalar(sm[:, 10:11], pbv[:, 0:1], 0.0, None, ALU.max)
        e.drain()
        e.tensor_sub(sm[:, 11:12], sm[:, 10:11], sm[:, 9:10])
        e.drain()
        e.tensor_scalar(qp[:, K.QP_OS:K.QP_OS + 1], sm[:, 11:12], 1.0 / 255.0, 1e-8,
                        ALU.mult, ALU.max)
        e.drain()
        recip_acc(e, qp[:, K.QP_OINV:K.QP_OINV + 1],
                                     qp[:, K.QP_OS:K.QP_OS + 1], sm[:, 11:12])
        e.drain()
        e.tensor_mul(sm[:, 11:12], sm[:, 9:10], qp[:, K.QP_OINV:K.QP_OINV + 1])
        e.drain()
        e.tensor_scalar(s["qi"][:, 0, 0:1], sm[:, 11:12], -1.0, None, ALU.mult)
        e.drain()
        e.tensor_copy(qp[:, K.QP_OZP:K.QP_OZP + 1], s["qi"][:, 0, 0:1])
        e.drain()
        e.tensor_mul(sm[:, 11:12], qp[:, K.QP_OZP:K.QP_OZP + 1],
                     qp[:, K.QP_OS:K.QP_OS + 1])
        e.drain()
        return e.tensor_scalar(qp[:, K.QP_ONZS:K.QP_ONZS + 1], sm[:, 11:12],
                               -1.0, None, ALU.mult)
    p.dve(par, reads=("pt7",), writes=("opar", "small", "qi0"), tag="ar3.par")


def emit_l0_proj_update(k, bi):
    p, s, d = k.p, k.s, k.d
    T, NTC = k.T, k.NTC
    qp = s["qp"]
    for hh in range(H):
        dc, po = hh // 2, 64 * (hh % 2)
        for qc in range(NTC):
            sl = (hh * NTC + qc) % 2
            p.dma("sp_dma" if sl == 0 else "pl_dma",
                  s["fx"][0:64, sl, 0:512], d["oraw"][bi, hh, qc, :, :],
                  reads=(f"oraw.{bi}.{hh}.{qc}",), writes=(f"fx{sl}",),
                  tag=f"o{bi}.{hh}.{qc}")
            quant_chain(k, s["fx"][0:64, sl, 0:512],
                        s["act_a"][po:po + 64, dc, qc * 512:(qc + 1) * 512],
                        qp[0:64, K.QP_OINV:K.QP_OINV + 1],
                        qp[0:64, K.QP_OZP:K.QP_OZP + 1],
                        qp[0:64, K.QP_OS:K.QP_OS + 1],
                        qp[0:64, K.QP_ONZS:K.QP_ONZS + 1],
                        reads=(f"fx{sl}", "opar"), writes=(f"act_a.{dc}",),
                        sl=sl, tag=f"o{bi}.{hh}.{qc}", np_=64)
    for fc in range(NDC):
        wsl = fc % 2
        p.dma("sp_dma", s["wtile"][:, wsl, :, :],
              d["pwT"][fc, :, :, :].rearrange("a p j -> p a j"),
              reads=tuple(f"pw.{fc}.{dcc}" for dcc in range(NDC)),
              writes=(f"wtile{wsl}",), tag=f"pj{bi}.w{fc}")
        for dc in range(NDC):
            for tc in range(NTC):
                def mm(e, wsl=wsl, dc=dc, tc=tc):
                    return e.matmul(k.psum[f"pb{tc}"][:, :], s["wtile"][:, wsl, dc, :],
                                    s["act_a"][:, dc, tc * 512:(tc + 1) * 512],
                                    start=(dc == 0), stop=(dc == NDC - 1))
                p.pe(mm, reads=(f"wtile{wsl}", f"act_a.{dc}"), writes=(f"pt{tc}",),
                     tag=f"pj{bi}.m{fc}.{dc}.{tc}")
        for tc in range(NTC):
            def evac(e, fc=fc, tc=tc):
                return e.activation(s["ev"][:, tc % 2, :], k.psum[f"pb{tc}"][:, :],
                                    ACTF.Identity,
                                    bias=s["bias"][:, K.B_L0PROJ + fc:K.B_L0PROJ + fc + 1])
            p.act(evac, reads=(f"pt{tc}", "bias"), writes=(f"ev{tc % 2}",),
                  tag=f"pj{bi}.e{fc}.{tc}")
            sl = tc % 2
            p.dma("sp_dma" if sl == 0 else "pl_dma", s["fx"][:, sl, 0:512],
                  d["x_fm"][bi, fc, :, tc * 512:(tc + 1) * 512],
                  writes=(f"fx{sl}",), tag=f"pj{bi}.x{fc}.{tc}")
            p.dma("pl_dma" if sl == 0 else "sp_dma", s["fy"][:, sl, :],
                  d["g0"][bi, fc, :, tc * 512:(tc + 1) * 512],
                  reads=(f"g0.{bi}.{fc}.{tc}",), writes=(f"fy{sl}",),
                  tag=f"pj{bi}.g{fc}.{tc}")

            def upd(e, sl=sl):
                e.tensor_mul(s["ev"][:, sl, :], s["ev"][:, sl, :], s["fy"][:, sl, :])
                e.drain()
                return e.tensor_add(s["ev"][:, sl, :], s["ev"][:, sl, :],
                                    s["fx"][:, sl, 0:512])
            p.dve(upd, reads=(f"ev{sl}", f"fx{sl}", f"fy{sl}"), writes=(f"ev{sl}",),
                  tag=f"pj{bi}.u{fc}.{tc}")
            p.dma("sp_dma", d["xmid"][bi, fc, :, tc * 512:(tc + 1) * 512],
                  s["ev"][:, sl, :], reads=(f"ev{sl}",),
                  writes=(f"xmid.{bi}.{fc}.{tc}",), tag=f"pj{bi}.s{fc}.{tc}")


def emit_l1_stats(k, bi):
    p, s, d = k.p, k.s, k.d
    T, NTC, LC = k.T, k.NTC, k.LC
    nl = T // LC
    for dc in range(NDC):
        for li in range(nl):
            sl = (dc * nl + li) % 2
            p.dma("sp_dma", s["fx"][:, sl, 0:LC],
                  d["xmid"][bi, dc, :, li * LC:(li + 1) * LC],
                  reads=tuple(f"xmid.{bi}.{dc}.{li * (LC // 512) + j}"
                              for j in range(LC // 512)),
                  writes=(f"fx{sl}",), tag=f"t{bi}.x{dc}.{li}")

            def sqc(e, sl=sl, dc=dc, li=li):
                e.activation(s["sq"][:, sl, :], s["fx"][:, sl, 0:LC], ACTF.Square)
                e.drain()
                return e.activation(s["act_b"][:, dc, li * LC:(li + 1) * LC],
                                    s["fx"][:, sl, 0:LC], ACTF.Identity)
            p.act(sqc, reads=(f"fx{sl}",), writes=(f"sq{sl}", f"act_b.{dc}"),
                  tag=f"t{bi}.q{dc}.{li}")
            for q5 in range(LC // 512):
                tc = li * (LC // 512) + q5

                def mm1(e, dc=dc, li=li, q5=q5, tc=tc):
                    return e.matmul(k.psum[f"pb{tc}"][:, :], s["ones_b"][:, :],
                                    s["act_b"][:, dc, li * LC + q5 * 512:li * LC + (q5 + 1) * 512],
                                    start=(dc == 0), stop=(dc == NDC - 1))
                p.pe(mm1, reads=(f"act_b.{dc}", "ones_b"), writes=(f"pt{tc}",),
                     tag=f"t{bi}.mS{dc}.{tc}")

                def mm2(e, sl=sl, q5=q5, dc=dc, tc=tc):
                    return e.matmul(k.psum[f"pb{NTC + tc}"][:, :], s["ones_b"][:, :],
                                    s["sq"][:, sl, q5 * 512:(q5 + 1) * 512],
                                    start=(dc == 0), stop=(dc == NDC - 1))
                p.pe(mm2, reads=(f"sq{sl}", "ones_b"), writes=(f"pt{NTC + tc}",),
                     tag=f"t{bi}.mQ{dc}.{tc}")
    for tc in range(NTC):
        def stats(e, tc=tc):
            e.tensor_scalar(s["mur"][:, 0, :], k.psum[f"pb{tc}"][:, :],
                            1.0 / 1024.0, None, ALU.mult)
            e.drain()
            e.tensor_scalar(s["rr"][:, 0, :], k.psum[f"pb{NTC + tc}"][:, :],
                            1.0 / 1024.0, None, ALU.mult)
            e.drain()
            e.scalar_tensor_tensor(s["rsc"][:, 0:512], s["mur"][:, 0, :], 1.0,
                                   s["mur"][:, 0, :], ALU.mult, ALU.mult)
            e.drain()
            return e.tensor_sub(s["rsc"][:, 0:512], s["rr"][:, 0, :], s["rsc"][:, 0:512])
        p.dve(stats, reads=(f"pt{tc}", f"pt{NTC + tc}"), writes=("rsc", "mur0", "rr0"),
              tag=f"t{bi}.st{tc}")
        p.act(lambda e: e.activation(s["rsc"][:, 0:512], s["rsc"][:, 0:512], ACTF.Sqrt,
                                     bias=s["cst"][:, 0:1]),
              reads=("rsc", "cst"), writes=("rsc",), tag=f"t{bi}.sd{tc}")

        def rcp(e):
            recip_acc(e, s["rr"][:, 1, :], s["rsc"][:, 0:512], s["rec"][:, 0:512])
            e.drain()
            return e.tensor_mul(s["mur"][:, 1, :], s["mur"][:, 0, :], s["rr"][:, 1, :])
        p.dve(rcp, reads=("rsc", "mur0"), writes=("rr1", "mur1", "rec"),
              tag=f"t{bi}.rc{tc}")
        for dc in range(NDC):
            sl = (tc * NDC + dc) % 2
            p.dma("sp_dma", s["fx"][:, sl, 0:512],
                  d["xmid"][bi, dc, :, tc * 512:(tc + 1) * 512],
                  reads=(f"xmid.{bi}.{dc}.{tc}",), writes=(f"fx{sl}",),
                  tag=f"t{bi}.x2{dc}.{tc}")

            def app(e, sl=sl):
                e.tensor_mul(s["ev"][:, 3, :], s["fx"][:, sl, 0:512], s["rr"][:, 1, :])
                e.drain()
                return e.tensor_sub(s["fy"][:, sl, :], s["ev"][:, 3, :],
                                    s["mur"][:, 1, :])
            p.dve(app, reads=(f"fx{sl}", "rr1", "mur1"), writes=(f"fy{sl}", "ev3"),
                  tag=f"t{bi}.ap{dc}.{tc}")
            p.dma("sp_dma", d["xn132"][bi, dc, :, tc * 512:(tc + 1) * 512],
                  s["fy"][:, sl, :], reads=(f"fy{sl}",),
                  writes=(f"xn132.{bi}.{dc}.{tc}",), tag=f"t{bi}.w{dc}.{tc}")
    for dc in range(NDC):
        for li in range(nl):
            sl = (dc * nl + li) % 2
            p.dma("sp_dma", s["fx"][:, sl, 0:LC],
                  d["xn132"][bi, dc, :, li * LC:(li + 1) * LC],
                  reads=tuple(f"xn132.{bi}.{dc}.{li * (LC // 512) + j}"
                              for j in range(LC // 512)),
                  writes=(f"fx{sl}",), tag=f"u{bi}.x{dc}.{li}")

            def sqc(e, sl=sl, dc=dc, li=li):
                e.activation(s["sq"][:, sl, :], s["fx"][:, sl, 0:LC], ACTF.Square)
                e.drain()
                return e.activation(s["act_b"][:, dc, li * LC:(li + 1) * LC],
                                    s["fx"][:, sl, 0:LC], ACTF.Identity)
            p.act(sqc, reads=(f"fx{sl}",), writes=(f"sq{sl}", f"act_b.{dc}"),
                  tag=f"u{bi}.q{dc}.{li}")
            for q5 in range(LC // 512):
                tc = li * (LC // 512) + q5

                def mm1(e, dc=dc, li=li, q5=q5, tc=tc):
                    return e.matmul(k.psum[f"pb{tc}"][:, :], s["ones_b"][:, :],
                                    s["act_b"][:, dc, li * LC + q5 * 512:li * LC + (q5 + 1) * 512],
                                    start=(dc == 0), stop=(dc == NDC - 1))
                p.pe(mm1, reads=(f"act_b.{dc}", "ones_b"), writes=(f"pt{tc}",),
                     tag=f"u{bi}.mS{dc}.{tc}")

                def mm2(e, sl=sl, q5=q5, dc=dc, tc=tc):
                    return e.matmul(k.psum[f"pb{NTC + tc}"][:, :], s["ones_b"][:, :],
                                    s["sq"][:, sl, q5 * 512:(q5 + 1) * 512],
                                    start=(dc == 0), stop=(dc == NDC - 1))
                p.pe(mm2, reads=(f"sq{sl}", "ones_b"), writes=(f"pt{NTC + tc}",),
                     tag=f"u{bi}.mQ{dc}.{tc}")
    for tc in range(NTC):
        def stats(e, tc=tc):
            e.tensor_scalar(s["mur"][:, 0, :], k.psum[f"pb{tc}"][:, :],
                            1.0 / 1024.0, None, ALU.mult)
            e.drain()
            e.tensor_scalar(s["rr"][:, 0, :], k.psum[f"pb{NTC + tc}"][:, :],
                            1.0 / 1024.0, None, ALU.mult)
            e.drain()
            e.scalar_tensor_tensor(s["rsc"][:, 0:512], s["mur"][:, 0, :], 1.0,
                                   s["mur"][:, 0, :], ALU.mult, ALU.mult)
            e.drain()
            return e.tensor_sub(s["rsc"][:, 0:512], s["rr"][:, 0, :], s["rsc"][:, 0:512])
        p.dve(stats, reads=(f"pt{tc}", f"pt{NTC + tc}"), writes=("rsc", "mur0", "rr0"),
              tag=f"u{bi}.st{tc}")
        p.act(lambda e: e.activation(s["rsc"][:, 0:512], s["rsc"][:, 0:512], ACTF.Sqrt,
                                     bias=s["cst"][:, 0:1]),
              reads=("rsc", "cst"), writes=("rsc",), tag=f"u{bi}.sd{tc}")

        def rcp(e):
            recip_acc(e, s["rr"][:, 1, :], s["rsc"][:, 0:512], s["rec"][:, 0:512])
            e.drain()
            return e.tensor_mul(s["mur"][:, 1, :], s["mur"][:, 0, :], s["rr"][:, 1, :])
        p.dve(rcp, reads=("rsc", "mur0"), writes=("rr1", "mur1", "rec"),
              tag=f"u{bi}.rc{tc}")
        for dc in range(NDC):
            sl = (tc * NDC + dc) % 2
            p.dma("sp_dma", s["fx"][:, sl, 0:512],
                  d["xn132"][bi, dc, :, tc * 512:(tc + 1) * 512],
                  reads=(f"xn132.{bi}.{dc}.{tc}",), writes=(f"fx{sl}",),
                  tag=f"u{bi}.x2{dc}.{tc}")

            def app(e, sl=sl):
                e.tensor_mul(s["ev"][:, 3, :], s["fx"][:, sl, 0:512], s["rr"][:, 1, :])
                e.drain()
                return e.tensor_sub(s["ev"][:, 3, :], s["ev"][:, 3, :],
                                    s["mur"][:, 1, :])
            p.dve(app, reads=(f"fx{sl}", "rr1", "mur1"), writes=("ev3",),
                  tag=f"u{bi}.a2{dc}.{tc}")

            def aff(e, dc=dc, tc=tc):
                return e.activation(s["act_a"][:, dc, tc * 512:(tc + 1) * 512],
                                    s["ev"][:, 3, :], ACTF.Identity,
                                    bias=s["bias"][:, K.B_LNB + dc:K.B_LNB + dc + 1],
                                    scale=s["bias"][:, K.B_LNW + dc:K.B_LNW + dc + 1])
            p.act(aff, reads=("ev3", "bias"), writes=(f"act_a.{dc}",),
                  tag=f"u{bi}.af{dc}.{tc}")


def l1w_view(k, name):
    return k.d[name][:, :].rearrange("(a p) (b j) -> p a b j", p=128, j=128)


def emit_l1_qkv(k, bi):
    p, s, d = k.p, k.s, k.d
    T, NTC, NT128 = k.T, k.NTC, k.NT128
    if bi == 0:
        p.fence(reads=(), writes=("nrmX",) + tuple(f"kst.{j}" for j in range(NDC)),
                tag="f.kst")
    p.fence(reads=(), writes=("lgt", "wrt", "axw", "nrmY", "wvb"), tag=f"f.wvb{bi}")
    for fc in range(16):
        wsl = fc % 2
        p.dma("pl_dma", s["wtile"][:, wsl, :, :], l1w_view(k, "l1_qkv_w")[:, :, fc, :],
              writes=(f"wtile{wsl}",), tag=f"q1{bi}.w{fc}")
        for dc in range(NDC):
            for tc in range(NTC):
                def mm(e, wsl=wsl, dc=dc, tc=tc):
                    return e.matmul(k.psum[f"pb{tc}"][:, :], s["wtile"][:, wsl, dc, :],
                                    s["act_a"][:, dc, tc * 512:(tc + 1) * 512],
                                    start=(dc == 0), stop=(dc == NDC - 1))
                p.pe(mm, reads=(f"wtile{wsl}", f"act_a.{dc}"), writes=(f"pt{tc}",),
                     tag=f"q1{bi}.m{fc}.{dc}.{tc}")
        for tc in range(NTC):
            if fc < 8:
                dst = s["act_b"][:, fc, tc * 512:(tc + 1) * 512]
                res = f"act_b.{fc}"
            else:
                dst = s["kst"][:, fc - 8, tc * 512:(tc + 1) * 512]
                res = f"kst.{fc - 8}"

            def evac(e, fc=fc, tc=tc, dst=dst):
                return e.activation(dst, k.psum[f"pb{tc}"][:, :], ACTF.Identity,
                                    bias=s["bias"][:, K.B_L1QKV + fc:K.B_L1QKV + fc + 1])
            p.act(evac, reads=(f"pt{tc}", "bias"), writes=(res,),
                  tag=f"q1{bi}.e{fc}.{tc}")
    for fv in range(2):
        p.dma("pl_dma", s["wvb"][:, :, :],
              d["l1_qkv_w"][:, 2048 + fv * 512:2048 + (fv + 1) * 512]
              .rearrange("(a p) j -> p a j", p=128),
              writes=("wvb",), tag=f"v1{bi}.w{fv}")
        for t128 in range(NT128):
            pq = 4 + t128 % 2
            pb = k.psum[f"pb{pq}"]
            for dc in range(NDC):
                def mm(e, pb=pb, t128=t128, dc=dc):
                    return e.matmul(pb[:, :],
                                    s["act_a"][:, dc, t128 * 128:(t128 + 1) * 128],
                                    s["wvb"][:, dc, :],
                                    start=(dc == 0), stop=False)
                p.pe(mm, reads=("wvb", f"act_a.{dc}"), writes=(f"pt{pq}",),
                     tag=f"v1{bi}.m{fv}.{t128}.{dc}")

            def mmb(e, pb=pb, fv=fv):
                return e.matmul(pb[:, :], s["ones_b"][0:1, 0:128],
                                s["brow"][0:1, 1, fv * 512:(fv + 1) * 512],
                                start=False, stop=True)
            p.pe(mmb, reads=("brow1", "ones_b"), writes=(f"pt{pq}",),
                 tag=f"v1{bi}.b{fv}.{t128}")

            def evac(e, pb=pb, t128=t128, fv=fv):
                return e.tensor_copy(s["vaug"][:, t128, fv * 8:(fv + 1) * 8, 0:64],
                                     pb[:, :])
            p.dve(evac, reads=(f"pt{pq}", "vaug.ones"), writes=(f"vg.{t128 % 4}",),
                  tag=f"v1{bi}.e{fv}.{t128}")


def emit_l1_ffn(k, bi):
    p, s, d = k.p, k.s, k.d
    T, NTC, NT128 = k.T, k.NTC, k.NT128
    p.fence(reads=(), writes=("wvb", "lgt", "wrt", "axw"), tag=f"f.lgt{bi}")
    for dc in range(NDC):
        p.dma("pl_dma", s["act_a"][:, dc, :], d["xn132"][bi, dc, :, :],
              reads=tuple(f"xn132.{bi}.{dc}.{t}" for t in range(NTC)),
              writes=(f"act_a.{dc}",), tag=f"ff{bi}.xn{dc}")
    for tc in range(NTC):
        for dc in range(NDC):
            def mm(e, dc=dc, tc=tc):
                return e.matmul(k.psum["pb6"][0:8, :], s["lR"][:, dc, :],
                                s["act_a"][:, dc, tc * 512:(tc + 1) * 512],
                                start=(dc == 0), stop=(dc == NDC - 1))
            p.pe(mm, reads=("lR", f"act_a.{dc}"), writes=("pt6",),
                 tag=f"rt{bi}.m{tc}.{dc}")

        def evac(e, tc=tc):
            return e.activation(s["lgt"][0:8, tc * 512:(tc + 1) * 512],
                                k.psum["pb6"][0:8, :], ACTF.Identity,
                                bias=s["small"][0:8, 30:31])
        p.act(evac, reads=("pt6", "rb"), writes=("lgt",), tag=f"rt{bi}.e{tc}")
    if "dbg_lgt" in K.DEBUG_OUT:
        p.dma("sp_dma", d["dbg_lgt"][bi, :, :], s["lgt"][0:8, :],
              reads=("lgt",), writes=("dbgl",), tag=f"dbg.lgt{bi}")
    for t128 in range(NT128):
        def tp(e, t128=t128):
            return e.transpose(k.psum["pb7"][0:128, 0:8],
                               s["lgt"][0:8, t128 * 128:(t128 + 1) * 128],
                               s["idn_f"][0:8, 0:8])
        p.pe(tp, reads=("lgt", "idn_f"), writes=("pt7",), tag=f"rt{bi}.t{t128}")

        def chain(e):
            t8 = s["tmp8"]
            e.tensor_copy(t8[:, 50:58], k.psum["pb7"][0:128, 0:8])
            e.drain()
            e.tensor_reduce(t8[:, 58:59], t8[:, 50:58], AX.X, ALU.max)
            e.drain()
            e.tensor_scalar(t8[:, 59:60], t8[:, 58:59], -1.0, None, ALU.mult)
            e.drain()
            e.tensor_scalar(t8[:, 64:72], t8[:, 50:58], t8[:, 58:59], None,
                            ALU.is_equal)
            e.drain()
            e.scalar_tensor_tensor(t8[:, 64:72], t8[:, 64:72], -1e30, t8[:, 50:58],
                                   ALU.mult, ALU.add)
            e.drain()
            e.tensor_reduce(t8[:, 60:61], t8[:, 64:72], AX.X, ALU.max)
            e.drain()
            return e.tensor_scalar(t8[:, 64:72], t8[:, 50:58], t8[:, 60:61], None,
                                   ALU.is_ge)
        p.dve(chain, reads=("pt7",), writes=("t8r",), tag=f"rt{bi}.c{t128}")

        def ex(e):
            return e.activation(s["tmp8"][:, 72:80], s["tmp8"][:, 50:58], ACTF.Exp,
                                bias=s["tmp8"][:, 59:60])
        p.act(ex, reads=("t8r",), writes=("t8x",), tag=f"rt{bi}.x{t128}")

        def fin(e):
            t8 = s["tmp8"]
            e.tensor_mul(t8[:, 72:80], t8[:, 72:80], t8[:, 64:72])
            e.drain()
            e.tensor_reduce(t8[:, 61:62], t8[:, 72:80], AX.X, ALU.add)
            e.drain()
            recip_acc(e, t8[:, 62:63], t8[:, 61:62], t8[:, 63:64])
            e.drain()
            return e.tensor_scalar(t8[:, 72:80], t8[:, 72:80], t8[:, 62:63], None,
                                   ALU.mult)
        p.dve(fin, reads=("t8x", "t8r"), writes=("t8x",), tag=f"rt{bi}.f{t128}")

        def tpb(e):
            return e.transpose(k.psum["pb7"][0:8, 0:128], s["tmp8"][:, 72:80],
                               s["idn_f"][:, :])
        p.pe(tpb, reads=("t8x", "idn_f"), writes=("pt7",), tag=f"rt{bi}.b{t128}")

        def wev(e, t128=t128):
            return e.tensor_copy(s["wrt"][0:8, t128 * 128:(t128 + 1) * 128],
                                 k.psum["pb7"][0:8, 0:128])
        p.dve(wev, reads=("pt7",), writes=("wrt",), tag=f"rt{bi}.w{t128}")
    if "dbg_wrt" in K.DEBUG_OUT:
        def wrtc(e):
            return e.tensor_copy(s["rr"][:, 0, 0:8].rearrange("p a -> p a") if False else s["mur"][0:8, 0, 0:512], s["wrt"][0:8, 0:512])
        p.dma("pl_dma", d["dbg_wrt"][bi, :, :], s["wrt"][0:8, :],
              reads=("wrt",), writes=("dbgw",), tag=f"dbg.wrt{bi}")
    p.fence(reads=(), writes=("lgt", "axw"), tag=f"f.axw{bi}")
    for tc in range(NTC):
        pb2 = k.psum[f"pb{tc % 2}"]
        for dc in range(NDC):
            def mm(e, pb2=pb2, dc=dc, tc=tc):
                return e.matmul(pb2[:, :], s["lA"][:, dc, :],
                                s["act_a"][:, dc, tc * 512:(tc + 1) * 512],
                                start=(dc == 0), stop=(dc == NDC - 1))
            p.pe(mm, reads=("lA", f"act_a.{dc}"), writes=(f"pt{tc % 2}",),
                 tag=f"ax{bi}.m{tc}.{dc}")
        pb3 = k.psum[f"pb{2 + tc % 2}"]

        def mmw(e, pb3=pb3, tc=tc):
            return e.matmul(pb3[:, :], s["eind"][:, :],
                            s["wrt"][0:8, tc * 512:(tc + 1) * 512],
                            start=True, stop=True)
        p.pe(mmw, reads=("eind", "wrt"), writes=(f"pt{2 + tc % 2}",),
             tag=f"ax{bi}.w{tc}")

        def axw(e, pb2=pb2, pb3=pb3, tc=tc):
            e.tensor_copy(s["rsc"][:, 0:512], pb3[:, :])
            e.drain()
            return e.tensor_mul(s["axw"][:, tc * 512:(tc + 1) * 512],
                                pb2[:, :], s["rsc"][:, 0:512])
        p.dve(axw, reads=(f"pt{tc % 2}", f"pt{2 + tc % 2}"), writes=("axw", "rsc"),
              tag=f"ax{bi}.s{tc}")
    if "dbg_axw" in K.DEBUG_OUT:
        p.dma("pl_dma", d["dbg_axw"][bi, :, :], s["axw"][:, :],
              reads=("axw",), writes=("dbga",), tag=f"dbg.axw{bi}")
    for fc in range(NDC):
        wsl = fc % 2
        p.dma("pl_dma", s["wtile"][:, wsl, :, :], l1w_view(k, "l1_base_wT")[:, :, fc, :],
              writes=(f"wtile{wsl}",), tag=f"bs{bi}.w{fc}")
        if fc == 0 and bi == 0 and "dbg_wt" in K.DEBUG_OUT:
            p.dma("pl_dma", d["dbg_wt"][:, :, :], s["wtile"][:, wsl, :, :],
                  reads=(f"wtile{wsl}",), writes=("dbgwt",), tag="dbg.wt")
        for dc in range(NDC):
            for tc in range(NTC):
                pq = 4 + tc

                def mm(e, wsl=wsl, dc=dc, tc=tc, pq=pq):
                    return e.matmul(k.psum[f"pb{pq}"][:, :], s["wtile"][:, wsl, dc, :],
                                    s["act_a"][:, dc, tc * 512:(tc + 1) * 512],
                                    start=(dc == 0), stop=False)
                p.pe(mm, reads=(f"wtile{wsl}", f"act_a.{dc}"), writes=(f"pt{pq}",),
                     tag=f"bs{bi}.m{fc}.{dc}.{tc}")
        for tc in range(NTC):
            pq = 4 + tc

            def mml(e, fc=fc, tc=tc, pq=pq):
                return e.matmul(k.psum[f"pb{pq}"][:, :],
                                s["lB"][:, fc * 128:(fc + 1) * 128],
                                s["axw"][:, tc * 512:(tc + 1) * 512],
                                start=False, stop=True)
            p.pe(mml, reads=("lB", "axw"), writes=(f"pt{pq}",),
                 tag=f"bs{bi}.l{fc}.{tc}")

            def silu(e, fc=fc, tc=tc, pq=pq):
                return e.activation(s["ev"][:, tc % 2, :], k.psum[f"pb{pq}"][:, :],
                                    ACTF.Silu,
                                    bias=s["bias"][:, K.B_BASE + fc:K.B_BASE + fc + 1])
            p.act(silu, reads=(f"pt{pq}", "bias"), writes=(f"ev{tc % 2}",),
                  tag=f"bs{bi}.s{fc}.{tc}")
            p.dma("sp_dma", d["g1"][bi, fc, :, tc * 512:(tc + 1) * 512],
                  s["ev"][:, tc % 2, :], reads=(f"ev{tc % 2}",),
                  writes=(f"g1.{bi}.{fc}.{tc}",), tag=f"bs{bi}.g{fc}.{tc}")


def emit_l1_attention(k, bi):
    p, s = k.p, k.s
    p.fence(reads=(), writes=("lgt", "wrt", "axw", "wvb", "nrmY"), tag=f"f.n1{bi}")
    attention_core(k, bi, layer=1, Ks=s["kst"], Kres="kst",
                   Qs=s["act_b"], Qres="act_b", nrm=s["nrm1"], nres="nrmY")


def emit_l1_proj_out(k, bi):
    p, s, d = k.p, k.s, k.d
    T, NTC = k.T, k.NTC
    for fc in range(NDC):
        wsl = fc % 2
        p.dma("pl_dma", s["wtile"][:, wsl, :, :], l1w_view(k, "l1_proj_w")[:, :, fc, :],
              writes=(f"wtile{wsl}",), tag=f"p1{bi}.w{fc}")
        for dc in range(NDC):
            for tc in range(NTC):
                def mm(e, wsl=wsl, dc=dc, tc=tc):
                    return e.matmul(k.psum[f"pb{tc}"][:, :], s["wtile"][:, wsl, dc, :],
                                    s["act_a"][:, dc, tc * 512:(tc + 1) * 512],
                                    start=(dc == 0), stop=(dc == NDC - 1))
                p.pe(mm, reads=(f"wtile{wsl}", f"act_a.{dc}"), writes=(f"pt{tc}",),
                     tag=f"p1{bi}.m{fc}.{dc}.{tc}")
        for tc in range(NTC):
            def evac(e, fc=fc, tc=tc):
                return e.activation(s["ev"][:, 2 + tc % 2, :], k.psum[f"pb{tc}"][:, :],
                                    ACTF.Identity,
                                    bias=s["bias"][:, K.B_L1PROJ + fc:K.B_L1PROJ + fc + 1])
            p.act(evac, reads=(f"pt{tc}", "bias"), writes=(f"ev{2 + tc % 2}",),
                  tag=f"p1{bi}.e{fc}.{tc}")
            sl = tc % 2
            p.dma("sp_dma", s["fx"][:, sl, 0:512],
                  d["xn132"][bi, fc, :, tc * 512:(tc + 1) * 512],
                  reads=(f"xn132.{bi}.{fc}.{tc}",), writes=(f"fx{sl}",),
                  tag=f"p1{bi}.n{fc}.{tc}")
            p.dma("sp_dma", s["fy"][:, sl, :],
                  d["g1"][bi, fc, :, tc * 512:(tc + 1) * 512],
                  reads=(f"g1.{bi}.{fc}.{tc}",), writes=(f"fy{sl}",),
                  tag=f"p1{bi}.g{fc}.{tc}")
            p.dma("sp_dma", s["mur"][:, sl, :],
                  d["xmid"][bi, fc, :, tc * 512:(tc + 1) * 512],
                  reads=(f"xmid.{bi}.{fc}.{tc}",), writes=(f"mur{sl}",),
                  tag=f"p1{bi}.xm{fc}.{tc}")

            def upd(e, sl=sl, tc=tc):
                e.tensor_add(s["ev"][:, 2 + sl, :], s["ev"][:, 2 + sl, :],
                             s["fx"][:, sl, 0:512])
                e.drain()
                e.tensor_mul(s["ev"][:, 2 + sl, :], s["ev"][:, 2 + sl, :],
                             s["fy"][:, sl, :])
                e.drain()
                return e.tensor_add(s["ev"][:, 2 + sl, :], s["ev"][:, 2 + sl, :],
                                    s["mur"][:, sl, :])
            p.dve(upd, reads=(f"ev{2 + sl}", f"fx{sl}", f"fy{sl}", f"mur{sl}"),
                  writes=(f"ev{2 + sl}",), tag=f"p1{bi}.u{fc}.{tc}")
            p.dma("sp_dma", d["out_fm"][bi, fc, :, tc * 512:(tc + 1) * 512],
                  s["ev"][:, 2 + sl, :], reads=(f"ev{2 + sl}",),
                  writes=(f"OUT.{bi}.{fc}.{tc}",), tag=f"p1{bi}.o{fc}.{tc}")


def emit_final(k):
    outs = tuple(f"OUT.{bi}.{fc}.{tc}" for bi in range(NB) for fc in range(NDC)
                 for tc in range(k.NTC))
    k.p.fence(reads=outs, writes=(), tag="final")


def build(T=2048, n_cores=8, sim_strict=False, debug_out=()):
    K.DEBUG_OUT = tuple(debug_out)
    k = K(T=T, n_cores=n_cores, sim_strict=sim_strict)
    emit_setup(k)
    emit_wq_preproc(k, "l0_qkv_w", 24)
    emit_wq_preproc(k, "l0_proj_w", 8)
    emit_wcast(k)
    for bi in range(NB):
        emit_l0_stats(k, bi)
    emit_ar1(k)
    for bi in range(NB):
        emit_ffn(k, bi)
        emit_xq_qkv(k, bi)
    emit_ar2(k)
    for bi in range(NB):
        emit_l0_attention(k, bi)
    emit_ar3(k)
    for bi in range(NB):
        emit_l0_proj_update(k, bi)
    for bi in range(NB):
        emit_l1_stats(k, bi)
        emit_l1_qkv(k, bi)
        emit_l1_ffn(k, bi)
        emit_l1_attention(k, bi)
        emit_l1_proj_out(k, bi)
    emit_final(k)
    k.p.emit()
    return k




def make_inmaps(inputs, n_cores=8, T=2048):
    x = np.asarray(inputs["L0_attn_x"], np.float32)
    Bt, N, C = x.shape
    f32 = lambda a: np.ascontiguousarray(np.asarray(a, np.float32))

    def col(a, w):
        return np.ascontiguousarray(np.asarray(a, np.float32).reshape(w, 128).T)

    qkvb = np.asarray(inputs["L0_attn_qkv_b"], np.float32)
    l1qkvb = np.asarray(inputs["L1_attn_b_qkv"], np.float32)
    hi = np.zeros((128, NDC, 16), np.float32)
    for dc in range(NDC):
        for pp in range(128):
            hi[pp, dc, (dc * 128 + pp) // 64] = 1.0
    ei = np.zeros((8, 128), np.float32)
    for e in range(8):
        ei[e, e * 16:(e + 1) * 16] = 2.0

    shared = {
        "l0_qkv_w": f32(inputs["L0_attn_qkv_w"]),
        "l0_qkv_b": col(qkvb, 24),
        "l0_qkv_bv_row": f32(qkvb[2048:3072]).reshape(1, D),
        "l0_proj_w": f32(inputs["L0_attn_proj_w"]),
        "l0_proj_b": col(inputs["L0_attn_proj_b"], 8),
        "l1_ln_w": col(inputs["L1_attn_ln_w"], 8),
        "l1_ln_b": col(inputs["L1_attn_ln_b"], 8),
        "l1_qkv_w": f32(inputs["L1_attn_W_qkv"]),
        "l1_qkv_b": col(l1qkvb, 24),
        "l1_qkv_bv_row": f32(l1qkvb[2048:3072]).reshape(1, D),
        "l1_proj_w": f32(inputs["L1_attn_W_proj"]),
        "l1_proj_b": col(inputs["L1_attn_b_proj"], 8),
        "l1_base_wT": f32(np.asarray(inputs["L1_ffn_base_w"], np.float32).T),
        "l1_base_b": col(inputs["L1_ffn_base_b"], 8),
        "l1_lora_A": f32(np.concatenate(
            [np.asarray(inputs["L1_ffn_lora_A"], np.float32)[e] for e in range(8)],
            axis=1)),
        "l1_lora_B": f32(np.asarray(inputs["L1_ffn_lora_B"], np.float32)
                         .reshape(128, D)),
        "l1_router_wT": f32(np.asarray(inputs["L1_ffn_router_w"], np.float32).T),
        "l1_router_b": f32(np.asarray(inputs["L1_ffn_router_b"], np.float32)
                           .reshape(8, 1)),
        "c_idn": np.eye(128, dtype=np.float32),
        "c_headind": hi,
        "c_expind2": ei,
    }
    w1 = np.asarray(inputs["L0_ffn_w1"], np.float32)
    b1 = np.asarray(inputs["L0_ffn_b1"], np.float32)
    w2 = np.asarray(inputs["L0_ffn_w2"], np.float32)
    b2 = np.asarray(inputs["L0_ffn_b2"], np.float32)
    in_maps = []
    for c in range(n_cores):
        m = dict(shared)
        xs = x[2 * c:2 * c + 2]
        m["x_fm"] = np.ascontiguousarray(
            xs.reshape(2, N, NDC, 128).transpose(0, 2, 3, 1))
        m["l0_w1"] = f32(w1[c])
        m["l0_b1"] = col(b1[c], NFC)
        m["l0_w2"] = f32(w2[c])
        m["l0_b2"] = col(b2[c], NDC)
        in_maps.append(m)
    return in_maps


def kernel(**inputs):
    n_cores, T = 8, 2048
    k = build(T=T, n_cores=n_cores, sim_strict=True)
    in_maps = make_inmaps(inputs, n_cores=n_cores, T=T)
    res = run_bass_kernel_spmd(k.nc, in_maps, core_ids=list(range(n_cores)))
    outs = []
    for c in range(n_cores):
        o = np.asarray(res.results[c]["out_fm"], np.float32)
        outs.append(o.transpose(0, 3, 1, 2).reshape(2, T, D))
    return np.concatenate(outs, axis=0).astype(np.float32)


# version 11
# speedup vs baseline: 1.0318x; 1.0209x over previous
import numpy as np
import ml_dtypes
import concourse.bass as bass
import concourse.mybir as mybir
import concourse.dve_ops as dve_ops
from concourse.bass_utils import run_bass_kernel_spmd


def recip_acc(e, out, in_, scratch):
    return e.reciprocal(out, in_)

F32 = mybir.dt.float32
BF16 = mybir.dt.bfloat16
I32 = mybir.dt.int32
AX = mybir.AxisListType
ALU = mybir.AluOpType
ACTF = mybir.ActivationFunctionType

D = 1024
NDC = 8
H = 16
DH = 64
FF = 4096
NFC = 32
NB = 2

PHYS = {"pe": "pe", "act": "act", "dve": "dve", "sp": "sp", "pl": "pl",
        "sp_dma": "sp", "pl_dma": "pl", "pl_cc": "pl"}
N_DMA_SLOTS = 6


class Op:
    __slots__ = ("i", "eng", "fn", "reads", "writes", "is_dma", "deps",
                 "sem_key", "sem_val", "marked", "vc", "tag")

    def __init__(self, i, eng, fn, reads, writes, is_dma, tag):
        self.i, self.eng, self.fn = i, eng, fn
        self.reads, self.writes, self.is_dma = tuple(reads), tuple(writes), is_dma
        self.deps = set()
        self.sem_key = None
        self.sem_val = None
        self.marked = False
        self.vc = None
        self.tag = tag


class Prog:
    NO_COALESCE = False

    def __init__(self, nc, sim_strict=False):
        self.nc = nc
        self.ops = []
        self.res_w = {}
        self.res_r = {}
        self.sim_strict = sim_strict

    def op(self, eng, fn, reads=(), writes=(), tag=""):
        o = Op(len(self.ops), eng, fn, reads, writes,
               eng in ("sp_dma", "pl_dma"), tag)
        for r in o.reads:
            if r in self.res_w:
                o.deps.add(self.res_w[r])
        for w in o.writes:
            if w in self.res_w:
                o.deps.add(self.res_w[w])
            for rd in self.res_r.get(w, ()):
                o.deps.add(rd)
        for r in o.reads:
            self.res_r.setdefault(r, []).append(o.i)
        for w in o.writes:
            self.res_w[w] = o.i
            self.res_r[w] = []
        o.deps.discard(o.i)
        if len(o.deps) > 1:
            keep = {}
            extra = []
            for di in o.deps:
                pp = self.ops[di]
                if pp.is_dma or pp.eng == "pl_cc":
                    extra.append(di)
                else:
                    e = PHYS[pp.eng]
                    if keep.get(e, -1) < di:
                        keep[e] = di
            o.deps = set(extra) | set(keep.values())
        self.ops.append(o)
        return o

    def pe(self, fn, reads=(), writes=(), tag=""):
        return self.op("pe", fn, reads, writes, tag)

    def act(self, fn, reads=(), writes=(), tag=""):
        return self.op("act", fn, reads, writes, tag)

    def dve(self, fn, reads=(), writes=(), tag=""):
        return self.op("dve", fn, reads, writes, tag)

    def sp(self, fn, reads=(), writes=(), tag=""):
        return self.op("sp", fn, reads, writes, tag)

    def pl(self, fn, reads=(), writes=(), tag=""):
        return self.op("pl", fn, reads, writes, tag)

    def dma(self, queue, out_ap, in_ap, reads=(), writes=(), tag=""):
        return self.op(queue, lambda e, o=out_ap, i=in_ap: e.dma_start(o, i),
                       reads, writes, tag)

    def collective(self, alu, rg, in_ap, out_ap, reads=(), writes=(), tag=""):
        def fn(e):
            return e.collective_compute("AllReduce", alu, replica_groups=rg,
                                        ins=[in_ap], outs=[out_ap])
        return self.op("pl_cc", fn, reads, writes, tag)

    def fence(self, reads, writes, tag="fence"):
        return self.op("sp", lambda e: None, reads, writes, tag)

    def barrier(self, tag="barrier"):
        for eng in ("pe", "act", "dve", "sp", "pl"):
            self.op(eng, lambda e: None, ("BARRIER",), ("BARRIER",), tag + "." + eng)

    def emit(self):
        nc = self.nc

        def relaxed(pp, o):
            if pp.is_dma or pp.eng == "pl_cc" or PHYS[pp.eng] != PHYS[o.eng]:
                return False
            return (not self.sim_strict) or PHYS[pp.eng] == "pe"

        for o in self.ops:
            for di in o.deps:
                pp = self.ops[di]
                if not relaxed(pp, o):
                    pp.marked = True
        sems = {e: nc.alloc_semaphore(name=f"sem_{e}")
                for e in ("pe", "act", "dve", "sp", "pl")}
        dma_sems = {q: [nc.alloc_semaphore(name=f"{q}{i}")
                        for i in range(N_DMA_SLOTS)]
                    for q in ("sp_dma", "pl_dma")}
        cc_sem = nc.alloc_semaphore(name="sem_cc")

        counts = {e: 0 for e in ("pe", "act", "dve", "sp", "pl")}
        dma_counts = {q: [0] * N_DMA_SLOTS for q in ("sp_dma", "pl_dma")}
        dma_rr = {q: 0 for q in ("sp_dma", "pl_dma")}
        ccn = 0
        for o in self.ops:
            if o.is_dma:
                sl = dma_rr[o.eng] % N_DMA_SLOTS
                dma_rr[o.eng] += 1
                dma_counts[o.eng][sl] += 1
                o.sem_key = (o.eng, sl)
                o.sem_val = 16 * dma_counts[o.eng][sl]
                o.marked = True
            elif o.eng == "pl_cc":
                ccn += 1
                o.sem_key = ("cc",)
                o.sem_val = ccn
                o.marked = True
            elif o.marked:
                counts[o.eng] += 1
                o.sem_key = (o.eng,)
                o.sem_val = counts[o.eng]
        for e, n in counts.items():
            assert n < 60000, (e, n)
        for q in dma_counts:
            for n in dma_counts[q]:
                assert 16 * n < 60000, (q, n)

        def sem_of(key):
            if key[0] in ("sp_dma", "pl_dma"):
                return dma_sems[key[0]][key[1]]
            if key[0] == "cc":
                return cc_sem
            return sems[key[0]]

        per_eng = {e: [] for e in ("pe", "act", "dve", "sp", "pl")}
        for o in self.ops:
            per_eng[PHYS[o.eng]].append(o)
        vclocks = {e: {} for e in per_eng}

        def body(eng, name):
            vc = vclocks[name]
            for o in per_eng[name]:
                for di in sorted(o.deps):
                    pp = self.ops[di]
                    if relaxed(pp, o):
                        if pp.vc:
                            for kk, vv in pp.vc.items():
                                if vc.get(kk, 0) < vv:
                                    vc[kk] = vv
                        continue
                    if pp.sem_key is None:
                        continue
                    if vc.get(pp.sem_key, 0) < pp.sem_val or Prog.NO_COALESCE:
                        eng.wait_ge(sem_of(pp.sem_key), pp.sem_val)
                        vc[pp.sem_key] = max(vc.get(pp.sem_key, 0), pp.sem_val)
                    if (not Prog.NO_COALESCE) and pp.vc:
                        for kk, vv in pp.vc.items():
                            if vc.get(kk, 0) < vv:
                                vc[kk] = vv
                if o.is_dma:
                    prev = o.sem_val - 16
                    if prev > 0 and vc.get(o.sem_key, 0) < prev:
                        eng.wait_ge(sem_of(o.sem_key), prev)
                        vc[o.sem_key] = prev
                ins = o.fn(eng)
                if o.sem_key is not None:
                    if ins is None:
                        ins = eng.nop()
                    ins.then_inc(sem_of(o.sem_key), 16 if o.is_dma else 1)
                snap = dict(vc)
                if o.sem_key is not None:
                    snap[o.sem_key] = o.sem_val
                o.vc = snap

        with nc.Block("main") as blk:
            @blk.tensor
            def _(t):
                body(t, "pe")

            @blk.scalar
            def _(sc):
                body(sc, "act")

            @blk.vector
            def _(v):
                body(v, "dve")

            @blk.sync
            def _(sy):
                body(sy, "sp")

            @blk.gpsimd
            def _(g):
                body(g, "pl")




class K:
    B_L0QKV, B_L0PROJ, B_B1, B_B2 = 0, 24, 32, 64
    B_LNW, B_LNB, B_L1QKV, B_L1PROJ, B_BASE = 72, 80, 88, 112, 120
    S_MXC, S_MNC = 0, 8
    S_CINV, S_CZP, S_CS, S_CNZS = 16, 24, 32, 40
    S_WMN, S_WMX, S_WINV, S_WZP, S_WS, S_WNZS = 48, 49, 50, 51, 52, 53
    S_RMX, S_RMN = 54, 57
    S_OMX, S_OMN = 60, 61
    S_T0 = 62
    QP_INV, QP_ZP, QP_S, QP_NZS = 0, 3, 6, 9
    QP_OINV, QP_OZP, QP_OS, QP_ONZS = 12, 13, 14, 15

    DEBUG_OUT = ()

    def __init__(self, T=2048, n_cores=8, sim_strict=False):
        assert T % 512 == 0 and (2 * (T // 512)) <= 8
        self.T = T
        self.NTC = T // 512
        self.NT128 = T // 128
        self.LC = min(1024, T)
        self.n_cores = n_cores
        self.nc = bass.Bass("TRN2", target_bir_lowering=False,
                            detect_race_conditions=sim_strict)
        self.p = Prog(self.nc, sim_strict=sim_strict)
        self.d = {}
        self.s = {}
        self.psum = {}
        self._declare()

    def dram(self, name, shape, dt, kind="Internal"):
        if kind == "Internal" and name in K.DEBUG_OUT:
            kind = "ExternalOutput"
        self.d[name] = self.nc.dram_tensor(name, list(shape), dt, kind=kind)
        return self.d[name]

    def sbuf(self, name, shape, dt):
        self.s[name] = self.nc.alloc_sbuf_tensor(name, list(shape), dt)
        return self.s[name]

    def sbuf_at(self, name, shape, dt, off):
        self.s[name] = self.nc.alloc_sbuf_tensor_at(name, list(shape), dt,
                                                    offset=off)
        return self.s[name]

    def _declare(self):
        T, NTC, NT128 = self.T, self.NTC, self.NT128
        dr = self.dram
        dr("x_fm", [NB, NDC, 128, T], F32, "ExternalInput")
        dr("l0_qkv_w", [3 * D, D], F32, "ExternalInput")
        dr("l0_qkv_b", [128, 24], F32, "ExternalInput")
        dr("l0_qkv_bv_row", [1, D], F32, "ExternalInput")
        dr("l0_proj_w", [D, D], F32, "ExternalInput")
        dr("l0_proj_b", [128, 8], F32, "ExternalInput")
        dr("l0_w1", [D, FF], F32, "ExternalInput")
        dr("l0_b1", [128, NFC], F32, "ExternalInput")
        dr("l0_w2", [FF, D], F32, "ExternalInput")
        dr("l0_b2", [128, NDC], F32, "ExternalInput")
        dr("l1_ln_w", [128, NDC], F32, "ExternalInput")
        dr("l1_ln_b", [128, NDC], F32, "ExternalInput")
        dr("l1_qkv_w", [D, 3 * D], F32, "ExternalInput")
        dr("l1_qkv_b", [128, 24], F32, "ExternalInput")
        dr("l1_qkv_bv_row", [1, D], F32, "ExternalInput")
        dr("l1_proj_w", [D, D], F32, "ExternalInput")
        dr("l1_proj_b", [128, 8], F32, "ExternalInput")
        dr("l1_base_wT", [D, D], F32, "ExternalInput")
        dr("l1_base_b", [128, 8], F32, "ExternalInput")
        dr("l1_lora_A", [D, 128], F32, "ExternalInput")
        dr("l1_lora_B", [128, D], F32, "ExternalInput")
        dr("l1_router_wT", [D, 8], F32, "ExternalInput")
        dr("l1_router_b", [8, 1], F32, "ExternalInput")
        dr("c_idn", [128, 128], F32, "ExternalInput")
        dr("c_headind", [128, NDC, 16], F32, "ExternalInput")
        dr("c_expind2", [8, 128], F32, "ExternalInput")
        dr("out_fm", [NB, NDC, 128, T], F32, "ExternalOutput")
        dr("xn32", [NB, NDC, 128, T], F32)
        dr("g0", [NB, NDC, 128, T], F32)
        dr("xmid", [NB, NDC, 128, T], F32)
        dr("xn132", [NB, NDC, 128, T], F32)
        dr("g1", [NB, NDC, 128, T], F32)
        dr("qkraw", [NB, 16, NTC, 128, 512], F32)
        dr("vraw", [NB, NT128, 2, 128, 512], F32)
        dr("oraw", [NB, H, NTC, 64, 512], F32)
        dr("wqT_qk", [16, NDC, 128, 128], BF16)
        dr("w1b", [128, NDC, NFC * 128], BF16)
        dr("w2b", [128, NFC, NDC * 128], BF16)
        dr("wqT_v", [NDC, 128, D], BF16)
        dr("pwT", [NDC, NDC, 128, 128], BF16)
        dr("ar1_in", [128, 16], F32)
        dr("ar1_out", [128, 16], F32)
        dr("ar2_in", [8, 1], F32)
        dr("ar2_out", [8, 1], F32)
        dr("ar3_in", [2, 1], F32)
        dr("dbg_lgt", [NB, 8, T], F32)
        dr("dbg_wrt", [NB, 8, T], F32)
        dr("dbg_axw", [NB, 128, T], F32)
        dr("dbg_wt", [128, NDC, 128], F32)
        dr("dbg_den", [8, 4, 2, 512], F32)
        dr("dbg_cb", [128, 16], F32)
        dr("dbg_nrm", [16, 2, 2048], F32)
        dr("dbg_sm", [16, 4], F32)
        dr("dbg_hind", [128, NDC, 16], F32)
        dr("dbg_s", [2, 128, 512], F32)
        dr("dbg_ka", [128, NDC, T], F32)
        dr("dbg_qa", [128, NDC, T], F32)
        dr("ar3_out", [2, 1], F32)

        sb = self.sbuf
        sb("act_a", [128, NDC, T], BF16)
        sb("act_b", [128, NDC, T], BF16)
        sb("fx", [128, 2, max(self.LC, 1024)], F32)
        sb("fy", [128, 2, 512], F32)
        sb("sq", [128, 2, self.LC], BF16)
        sb("ev", [128, 4, 512], F32)
        sb("qi", [128, 2, 512], I32)
        sb("ptile", [128, 2, 1024], BF16)
        sb("wtile", [128, 2, NDC, 128], BF16)
        sb("wst", [128, 8, 128], BF16)
        sb("mur", [128, 2, 512], F32)
        sb("rr", [128, 2, 512], F32)
        sb("den", [128, 512], F32)
        sb("rec", [128, 512], F32)
        sb("rsc", [128, 512], F32)
        sb("lA", [128, NDC, 128], BF16)
        sb("lB", [128, D], BF16)
        sb("lR", [128, NDC, 8], BF16)
        sb("bias", [128, 128], F32)
        sb("brow", [1, 2, D], BF16)
        sb("ones_b", [128, 128], BF16)
        sb("ones_f", [1, 128], F32)
        sb("onesf64", [66, 64], F32)
        sb("idn_f", [128, 128], F32)
        sb("idn_bt", [128, 128], BF16)
        sb("hind", [128, NDC, 16], BF16)
        sb("eind", [8, 128], BF16)
        sb("tmp8", [128, 96], F32)
        sb("small", [128, 48], F32)
        sb("qp", [128, 16], F32)
        sb("cbias", [128, 16], F32)
        sb("stat", [128, 64], F32)
        sb("cst", [128, 4], F32)

        base = (self.nc.sbuf_base + 31) & ~31
        top = self.nc.sbuf_top
        avail = top - base
        need = max(NFC * self.LC * 2 + 2 * NDC * 128 * 2 + NFC * 128 * 2,
                   NT128 * H * 66 * 2 + NDC * T * 2
                   + max(T * 4, NDC * 512 * 2, 2 * T * 2) + T * 2)
        assert need <= avail, (need, avail, base, top)
        A = base
        self.sbuf_at("hbuf", [128, NFC, self.LC], BF16, A)
        o1 = A + NFC * self.LC * 2
        self.sbuf_at("w1t", [128, 2, NDC, 128], BF16, o1)
        o2 = o1 + 2 * NDC * 128 * 2
        self.sbuf_at("w2t", [128, NFC, 128], BF16, o2)
        self.sbuf_at("wvb0", [128, NDC, D], BF16, o1)
        self.sbuf_at("vaug", [128, NT128, H, 66], BF16, A)
        ov = A + NT128 * H * 66 * 2
        self.sbuf_at("nrm0", [16, 2 * T], BF16, ov)
        self.sbuf_at("kst", [128, NDC, T], BF16, ov)
        oc = ov + NDC * T * 2
        self.sbuf_at("lgt", [8, T], F32, oc)
        self.sbuf_at("wvb", [128, NDC, 512], BF16, oc)
        self.sbuf_at("nrm1", [16, 2 * T], BF16, oc)
        self.sbuf_at("axw", [128, T], BF16, oc)
        oc2 = oc + max(T * 4, NDC * 512 * 2, 2 * T * 2)
        self.sbuf_at("wrt", [8, T], BF16, oc2)
        assert oc2 + T * 2 <= top, (oc2 + T * 2, top)

        for i in range(8):
            self.psum[f"pb{i}"] = self.nc.alloc_psum_tensor(f"pb{i}", [128, 512], F32)
        pbase = self.nc.psum_base
        self.nc.psum_base = 0
        for i in range(2):
            self.psum[f"pbt{i}"] = self.nc.alloc_psum_tensor(f"pbt{i}", [128, 1024], BF16)
        for b in (2, 4, 6):
            self.nc.psum_base = b
            self.psum[f"pb{b}{b + 1}"] = self.nc.alloc_psum_tensor(
                f"pb{b}{b + 1}", [128, 1024], F32)
        self.nc.psum_base = pbase




def quant_chain(k, src_ap, dst_ap, inv_ap, zp_ap, s_ap, nzs_ap,
                reads, writes, sl, tag="", np_=128):
    p, s = k.p, k.s
    n = src_ap.shape[-1]
    assert n <= 512

    def f1(e):
        return e.activation(s["ev"][0:np_, 2 + sl, 0:n], src_ap, ACTF.Relu,
                            bias=zp_ap, scale=inv_ap)
    p.act(f1, reads=reads, writes=(f"ev{2 + sl}",), tag=tag + ".r")

    def f2(e):
        return e.tensor_scalar(s["qi"][0:np_, sl, 0:n], s["ev"][0:np_, 2 + sl, 0:n],
                               255.0, None, ALU.min)
    p.dve(f2, reads=(f"ev{2 + sl}",), writes=(f"qi{sl}",), tag=tag + ".q")

    def f3(e):
        return e.tensor_scalar(dst_ap, s["qi"][0:np_, sl, 0:n],
                               s_ap, nzs_ap, ALU.mult, ALU.add)
    p.dve(f3, reads=(f"qi{sl}", reads[-1] if reads else "qp"), writes=writes,
          tag=tag + ".d")


def qparams_perpart(k, mn_ap, mx_ap, inv_ap, zp_ap, s_ap, nzs_ap,
                    res, reads, t0_ap, tag=""):
    p, s = k.p, k.s
    w = mn_ap.ap[-1][1]
    qi = s["qi"]

    def f1(e):
        e.tensor_scalar(mn_ap, mn_ap, 0.0, None, ALU.min)
        e.drain()
        e.tensor_scalar(mx_ap, mx_ap, 0.0, None, ALU.max)
        e.drain()
        e.tensor_sub(t0_ap, mx_ap, mn_ap)
        e.drain()
        e.tensor_scalar(s_ap, t0_ap, 1.0 / 255.0, 1e-8, ALU.mult, ALU.max)
        e.drain()
        recip_acc(e, inv_ap, s_ap, t0_ap)
        e.drain()
        e.tensor_mul(t0_ap, mn_ap, inv_ap)
        e.drain()
        e.tensor_scalar(qi[:, 0, 0:w], t0_ap, -1.0, None, ALU.mult)
        e.drain()
        e.tensor_copy(zp_ap, qi[:, 0, 0:w])
        e.drain()
        e.tensor_mul(t0_ap, zp_ap, s_ap)
        e.drain()
        return e.tensor_scalar(nzs_ap, t0_ap, -1.0, None, ALU.mult)
    p.dve(f1, reads=reads, writes=(res, "qi0"), tag=tag)


def w1_view(k):
    return k.d["l0_w1"][:, :].rearrange("(a p) (b j) -> p a b j", p=128, j=128)


def w2_view(k):
    return k.d["l0_w2"][:, :].rearrange("(b p) (a j) -> p b a j", p=128, j=128)


def qkt_view(k, name):
    return k.d[name][:, :, :, :]




def emit_setup(k):
    p, s, d = k.p, k.s, k.d
    for dst, src, res in [
        (s["bias"][:, K.B_L0QKV:K.B_L0QKV + 24], d["l0_qkv_b"][:, :], "bias"),
        (s["bias"][:, K.B_L0PROJ:K.B_L0PROJ + 8], d["l0_proj_b"][:, :], "bias"),
        (s["bias"][:, K.B_B1:K.B_B1 + NFC], d["l0_b1"][:, :], "bias"),
        (s["bias"][:, K.B_B2:K.B_B2 + NDC], d["l0_b2"][:, :], "bias"),
        (s["bias"][:, K.B_LNW:K.B_LNW + 8], d["l1_ln_w"][:, :], "bias"),
        (s["bias"][:, K.B_LNB:K.B_LNB + 8], d["l1_ln_b"][:, :], "bias"),
        (s["bias"][:, K.B_L1QKV:K.B_L1QKV + 24], d["l1_qkv_b"][:, :], "bias"),
        (s["bias"][:, K.B_L1PROJ:K.B_L1PROJ + 8], d["l1_proj_b"][:, :], "bias"),
        (s["bias"][:, K.B_BASE:K.B_BASE + 8], d["l1_base_b"][:, :], "bias"),
        (s["idn_f"][:, :], d["c_idn"][:, :], "idn_f"),
        (s["small"][0:8, 30:31], d["l1_router_b"][:, :], "rb"),
    ]:
        p.dma("sp_dma", dst, src, writes=(res,), tag="setup")
    p.dma("pl_dma", s["hind"][:, :, :], d["c_headind"][:, :, :], writes=("hind",))
    p.dma("pl_dma", s["eind"][:, :], d["c_expind2"][:, :], writes=("eind",))
    p.dma("pl_dma", s["brow"][0:1, 0, :], d["l0_qkv_bv_row"][:, :], writes=("brow0",))
    p.dma("pl_dma", s["brow"][0:1, 1, :], d["l1_qkv_bv_row"][:, :], writes=("brow1",))
    p.dma("pl_dma", s["lA"][:, :, :],
          d["l1_lora_A"][:, :].rearrange("(a p) j -> p a j", p=128), writes=("lA",))
    p.dma("pl_dma", s["lB"][:, :], d["l1_lora_B"][:, :], writes=("lB",))
    p.dma("pl_dma", s["lR"][:, :, :],
          d["l1_router_wT"][:, :].rearrange("(a p) j -> p a j", p=128), writes=("lR",))

    st = s["stat"]

    def consts(e):
        e.memset(s["ones_b"][:, :], 1.0)
        e.drain()
        e.memset(s["ones_f"][:, :], 1.0)
        e.drain()
        e.memset(s["onesf64"][:, :], 4.503599627370496e15)
        e.drain()
        e.memset(st[:, :], -1e30)
        e.drain()
        e.memset(st[:, K.S_MNC:K.S_MNC + 8], 1e30)
        e.drain()
        e.memset(st[:, K.S_RMN:K.S_RMN + 3], 1e30)
        e.drain()
        e.memset(st[:, K.S_OMN:K.S_OMN + 1], 1e30)
        e.drain()
        e.memset(s["cst"][:, 0:1], 1e-5)
        e.drain()
        e.memset(s["cst"][:, 1:2], 0.125)
        e.drain()
        e.memset(s["cst"][:, 2:3], -1.0)
        e.drain()
        return e.memset(s["cst"][:, 3:4], 36.04365338911715)
    p.dve(consts, writes=("ones_b", "ones_f", "onesf64", "chanstat",
                          "qkvstat", "ostat", "wstat", "cst"),
          tag="setup.c")

    def idnb(e):
        return e.tensor_copy(s["idn_bt"][:, :], s["idn_f"][:, :])
    p.dve(idnb, reads=("idn_f",), writes=("idn_bt",), tag="setup.idnb")


def emit_wq_preproc(k, wname, nfc):
    p, s, d = k.p, k.s, k.d
    st = s["stat"]
    for fc in range(nfc):
        sl = fc % 2
        p.dma("sp_dma", s["fx"][:, sl, 0:1024], d[wname][fc * 128:(fc + 1) * 128, :],
              writes=(f"fx{sl}",), tag=f"wq.{wname}.{fc}")

        def mnmx(e, sl=sl):
            e.tensor_reduce(st[:, K.S_WMN:K.S_WMN + 1], s["fx"][:, sl, 0:1024],
                            AX.X, ALU.min)
            e.drain()
            return e.tensor_reduce(st[:, K.S_WMX:K.S_WMX + 1], s["fx"][:, sl, 0:1024],
                                   AX.X, ALU.max)
        p.dve(mnmx, reads=(f"fx{sl}",), writes=("wstat",), tag=f"wq.mm{fc}")
        qparams_perpart(k, st[:, K.S_WMN:K.S_WMN + 1], st[:, K.S_WMX:K.S_WMX + 1],
                        st[:, K.S_WINV:K.S_WINV + 1], st[:, K.S_WZP:K.S_WZP + 1],
                        st[:, K.S_WS:K.S_WS + 1], st[:, K.S_WNZS:K.S_WNZS + 1],
                        "wpar", ("wstat",), st[:, K.S_T0:K.S_T0 + 1], tag=f"wq.p{fc}")
        for h2 in range(2):
            quant_chain(k, s["fx"][:, sl, h2 * 512:(h2 + 1) * 512],
                        s["ptile"][:, 0, h2 * 512:(h2 + 1) * 512],
                        st[:, K.S_WINV:K.S_WINV + 1], st[:, K.S_WZP:K.S_WZP + 1],
                        st[:, K.S_WS:K.S_WS + 1], st[:, K.S_WNZS:K.S_WNZS + 1],
                        reads=(f"fx{sl}", "wpar"), writes=(f"ptile{h2}",),
                        sl=h2, tag=f"wq.q{fc}.{h2}")
        for dc in range(NDC):
            pb = k.psum[f"pbt{dc % 2}"]

            def tp(e, dc=dc, pb=pb):
                return e.transpose(pb[:, 0:128],
                                   s["ptile"][:, 0, dc * 128:(dc + 1) * 128],
                                   s["idn_bt"][:, :])
            p.pe(tp, reads=(f"ptile{dc // 4}", "idn_bt"), writes=(f"pt{dc % 2}",),
                 tag=f"wq.t{fc}.{dc}")

            def ev(e, dc=dc, pb=pb):
                return e.tensor_copy(s["wst"][:, dc, :], pb[:, 0:128])
            p.dve(ev, reads=(f"pt{dc % 2}",), writes=(f"wst{dc}",),
                  tag=f"wq.e{fc}.{dc}")
            if wname == "l0_proj_w":
                dst = d["pwT"][fc, dc, :, :]
                res = f"pw.{fc}.{dc}"
            elif fc >= 16:
                dst = d["wqT_v"][dc, :, (fc - 16) * 128:(fc - 15) * 128]
                res = f"wv.{dc}.{fc - 16}"
            else:
                dst = d["wqT_qk"][fc, dc, :, :]
                res = f"wqk.{fc}.{dc}"
            p.dma("sp_dma", dst, s["wst"][:, dc, :], reads=(f"wst{dc}",),
                  writes=(res,), tag=f"wq.s{fc}.{dc}")


def emit_wcast(k):
    p, s, d = k.p, k.s, k.d
    CW = k.T
    w1f = d["l0_w1"][:, :].rearrange("(a p) f -> p a f", p=128)
    w2f = d["l0_w2"][:, :].rearrange("(b p) f -> p b f", p=128)
    for dc in range(NDC):
        for r in range(FF // CW):
            p.dma("pl_dma", s["act_b"][:, dc, :],
                  w1f[:, dc, r * CW:(r + 1) * CW],
                  writes=(f"act_b.{dc}",), tag=f"wc1i.{r}.{dc}")
            p.dma("pl_dma", d["w1b"][:, dc, r * CW:(r + 1) * CW],
                  s["act_b"][:, dc, :], reads=(f"act_b.{dc}",),
                  writes=(f"w1b.{r}.{dc}",), tag=f"wc1o.{r}.{dc}")
    cc = min(CW, D)
    g = 0
    for b in range(NFC):
        for r in range(D // cc):
            dc = g % NDC
            g += 1
            p.dma("pl_dma", s["act_b"][:, dc, 0:cc],
                  w2f[:, b, r * cc:(r + 1) * cc],
                  writes=(f"act_b.{dc}",), tag=f"wc2i.{b}.{r}")
            p.dma("pl_dma", d["w2b"][:, b, r * cc:(r + 1) * cc],
                  s["act_b"][:, dc, 0:cc], reads=(f"act_b.{dc}",),
                  writes=(f"w2b.{b}.{r}",), tag=f"wc2o.{b}.{r}")


def emit_l0_stats(k, bi):
    p, s, d = k.p, k.s, k.d
    T, NTC, LC = k.T, k.NTC, k.LC
    st = s["stat"]
    nl = T // LC
    for dc in range(NDC):
        for li in range(nl):
            sl = (dc * nl + li) % 2
            p.dma("sp_dma", s["fx"][:, sl, 0:LC], d["x_fm"][bi, dc, :, li * LC:(li + 1) * LC],
                  writes=(f"fx{sl}",), tag=f"s{bi}.x{dc}.{li}")

            def sqc(e, sl=sl, dc=dc, li=li):
                e.activation(s["sq"][:, sl, :], s["fx"][:, sl, 0:LC], ACTF.Square)
                e.drain()
                return e.activation(s["act_a"][:, dc, li * LC:(li + 1) * LC],
                                    s["fx"][:, sl, 0:LC], ACTF.Identity)
            p.act(sqc, reads=(f"fx{sl}",), writes=(f"sq{sl}", f"act_a.{dc}"),
                  tag=f"s{bi}.sq{dc}.{li}")
            for q5 in range(LC // 512):
                tc = li * (LC // 512) + q5

                def mm1(e, dc=dc, li=li, q5=q5, tc=tc):
                    return e.matmul(k.psum[f"pb{tc}"][:, :], s["ones_b"][:, :],
                                    s["act_a"][:, dc, (li * LC + q5 * 512):(li * LC + (q5 + 1) * 512)],
                                    start=(dc == 0), stop=(dc == NDC - 1))
                p.pe(mm1, reads=(f"act_a.{dc}", "ones_b"), writes=(f"pt{tc}",),
                     tag=f"s{bi}.mS{dc}.{tc}")

                def mm2(e, sl=sl, q5=q5, dc=dc, tc=tc):
                    return e.matmul(k.psum[f"pb{NTC + tc}"][:, :], s["ones_b"][:, :],
                                    s["sq"][:, sl, q5 * 512:(q5 + 1) * 512],
                                    start=(dc == 0), stop=(dc == NDC - 1))
                p.pe(mm2, reads=(f"sq{sl}", "ones_b"), writes=(f"pt{NTC + tc}",),
                     tag=f"s{bi}.mQ{dc}.{tc}")
    for tc in range(NTC):
        def stats(e, tc=tc):
            e.tensor_scalar(s["mur"][:, 0, :], k.psum[f"pb{tc}"][:, :],
                            1.0 / 1024.0, None, ALU.mult)
            e.drain()
            e.tensor_scalar(s["rr"][:, 0, :], k.psum[f"pb{NTC + tc}"][:, :],
                            1.0 / 1024.0, None, ALU.mult)
            e.drain()
            e.scalar_tensor_tensor(s["rsc"][:, 0:512], s["mur"][:, 0, :], 1.0,
                                   s["mur"][:, 0, :], ALU.mult, ALU.mult)
            e.drain()
            return e.tensor_sub(s["rsc"][:, 0:512], s["rr"][:, 0, :], s["rsc"][:, 0:512])
        p.dve(stats, reads=(f"pt{tc}", f"pt{NTC + tc}"), writes=("rsc", "mur0", "rr0"),
              tag=f"s{bi}.st{tc}")

        def sqr(e):
            return e.activation(s["rsc"][:, 0:512], s["rsc"][:, 0:512], ACTF.Sqrt, bias=s["cst"][:, 0:1])
        p.act(sqr, reads=("rsc",), writes=("rsc",), tag=f"s{bi}.sd{tc}")

        def rcp(e):
            recip_acc(e, s["rr"][:, 1, :], s["rsc"][:, 0:512], s["rec"][:, 0:512])
            e.drain()
            return e.tensor_mul(s["mur"][:, 1, :], s["mur"][:, 0, :], s["rr"][:, 1, :])
        p.dve(rcp, reads=("rsc", "mur0"), writes=("rr1", "mur1", "rec"),
              tag=f"s{bi}.rc{tc}")
        for dc in range(NDC):
            sl = (tc * NDC + dc) % 2
            p.dma("sp_dma", s["fx"][:, sl, 0:512],
                  d["x_fm"][bi, dc, :, tc * 512:(tc + 1) * 512],
                  writes=(f"fx{sl}",), tag=f"s{bi}.x2{dc}.{tc}")

            def app(e, sl=sl, dc=dc):
                e.tensor_mul(s["ev"][:, 3, :], s["fx"][:, sl, 0:512], s["rr"][:, 1, :])
                e.drain()
                e.tensor_sub(s["fy"][:, sl, :], s["ev"][:, 3, :], s["mur"][:, 1, :])
                e.drain()
                e.tensor_reduce(s["small"][:, 0:1], s["fy"][:, sl, :], AX.X, ALU.max)
                e.drain()
                e.tensor_reduce(s["small"][:, 1:2], s["fy"][:, sl, :], AX.X, ALU.min)
                e.drain()
                e.tensor_tensor(st[:, K.S_MXC + dc:K.S_MXC + dc + 1],
                                st[:, K.S_MXC + dc:K.S_MXC + dc + 1],
                                s["small"][:, 0:1], ALU.max)
                e.drain()
                return e.tensor_tensor(st[:, K.S_MNC + dc:K.S_MNC + dc + 1],
                                       st[:, K.S_MNC + dc:K.S_MNC + dc + 1],
                                       s["small"][:, 1:2], ALU.min)
            p.dve(app, reads=(f"fx{sl}", "rr1", "mur1"),
                  writes=(f"fy{sl}", "chanstat", "ev3", "small"),
                  tag=f"s{bi}.ap{dc}.{tc}")
            p.dma("sp_dma", d["xn32"][bi, dc, :, tc * 512:(tc + 1) * 512],
                  s["fy"][:, sl, :], reads=(f"fy{sl}",),
                  writes=(f"xn32.{bi}.{dc}.{tc}",), tag=f"s{bi}.w{dc}.{tc}")


def emit_ar1(k):
    p, s, d = k.p, k.s, k.d
    st = s["stat"]

    def pack(e):
        e.tensor_copy(s["tmp8"][:, 0:8], st[:, K.S_MXC:K.S_MXC + 8])
        e.drain()
        return e.tensor_scalar(s["tmp8"][:, 8:16], st[:, K.S_MNC:K.S_MNC + 8],
                               -1.0, None, ALU.mult)
    p.dve(pack, reads=("chanstat",), writes=("tmp8a",), tag="ar1.pk")
    p.dma("pl_dma", d["ar1_in"][:, :], s["tmp8"][:, 0:16], reads=("tmp8a",),
          writes=("ar1_in",), tag="ar1.st")
    p.collective(ALU.max, [list(range(k.n_cores))],
                 d["ar1_in"].ap().opt(), d["ar1_out"].ap().opt(),
                 reads=("ar1_in",), writes=("ar1_out",), tag="ar1.cc")
    p.dma("pl_dma", s["tmp8"][:, 16:32], d["ar1_out"][:, :], reads=("ar1_out",),
          writes=("tmp8b",), tag="ar1.ld")

    def unpack(e):
        e.tensor_copy(st[:, K.S_MXC:K.S_MXC + 8], s["tmp8"][:, 16:24])
        e.drain()
        return e.tensor_scalar(st[:, K.S_MNC:K.S_MNC + 8], s["tmp8"][:, 24:32],
                               -1.0, None, ALU.mult)
    p.dve(unpack, reads=("tmp8b",), writes=("chanstat",), tag="ar1.un")
    qparams_perpart(k, st[:, K.S_MNC:K.S_MNC + 8], st[:, K.S_MXC:K.S_MXC + 8],
                    st[:, K.S_CINV:K.S_CINV + 8], st[:, K.S_CZP:K.S_CZP + 8],
                    st[:, K.S_CS:K.S_CS + 8], st[:, K.S_CNZS:K.S_CNZS + 8],
                    "chanpar", ("chanstat",), s["tmp8"][:, 88:96], tag="ar1.par")


def emit_ffn(k, bi):
    p, s, d = k.p, k.s, k.d
    T, NTC, LC = k.T, k.NTC, k.LC
    if bi == 1:
        p.fence(reads=(), writes=("wvb0", "w1t0", "w1t1", "w2t0", "w2t1"), tag="f.ffn1")
    for dc in range(NDC):
        p.dma("pl_dma", s["act_a"][:, dc, :], d["xn32"][bi, dc, :, :],
              reads=tuple(f"xn32.{bi}.{dc}.{t}" for t in range(NTC)),
              writes=(f"act_a.{dc}",), tag=f"fn{bi}.xn{dc}")
    for tcb in range(T // LC):
        for fc in range(NFC):
            wsl = fc % 2
            p.dma("pl_dma", s["w1t"][:, wsl, :, :],
                  d["w1b"][:, :, fc * 128:(fc + 1) * 128],
                  reads=tuple(f"w1b.{fc * 128 // k.T}.{dcc}"
                              for dcc in range(NDC)),
                  writes=(f"w1t{wsl}",), tag=f"fn{bi}.w1.{tcb}.{fc}")
            for h2 in range(LC // 512):
                pb = k.psum[f"pb{h2}"]
                tsl = slice(tcb * LC + h2 * 512, tcb * LC + (h2 + 1) * 512)
                for dc in range(NDC):
                    def mm(e, pb=pb, wsl=wsl, dc=dc, tsl=tsl):
                        return e.matmul(pb[:, :], s["w1t"][:, wsl, dc, :],
                                        s["act_a"][:, dc, tsl],
                                        start=(dc == 0), stop=(dc == NDC - 1))
                    p.pe(mm, reads=(f"w1t{wsl}", f"act_a.{dc}"), writes=(f"pt{h2}",),
                         tag=f"fn{bi}.h{tcb}.{fc}.{h2}.{dc}")

                def gelu(e, pb=pb, fc=fc, h2=h2):
                    return e.activation(s["hbuf"][:, fc, h2 * 512:(h2 + 1) * 512],
                                        pb[:, :], ACTF.Gelu_apprx_tanh,
                                        bias=s["bias"][:, K.B_B1 + fc:K.B_B1 + fc + 1])
                p.act(gelu, reads=(f"pt{h2}", "bias"), writes=(f"hbuf.{fc}",),
                      tag=f"fn{bi}.g{tcb}.{fc}.{h2}")
        for dc in range(NDC):
            for hf in range(2):
                p.dma("pl_dma" if hf else "sp_dma",
                      s["w2t"][:, hf * (NFC // 2):(hf + 1) * (NFC // 2), :],
                      d["w2b"][:, hf * (NFC // 2):(hf + 1) * (NFC // 2),
                               dc * 128:(dc + 1) * 128],
                      reads=tuple(f"w2b.{b}.{dc * 128 // min(k.T, D)}"
                                  for b in range(hf * (NFC // 2),
                                                 (hf + 1) * (NFC // 2))),
                      writes=(f"w2t{hf}",), tag=f"fn{bi}.w2.{tcb}.{dc}.{hf}")
            for h2 in range(LC // 512):
                pb = k.psum[f"pb{2 + h2}"]
                for fc in range(NFC):
                    def mm(e, pb=pb, fc=fc, h2=h2):
                        return e.matmul(pb[:, :], s["w2t"][:, fc, :],
                                        s["hbuf"][:, fc, h2 * 512:(h2 + 1) * 512],
                                        start=(fc == 0), stop=(fc == NFC - 1))
                    p.pe(mm, reads=(f"w2t{fc // (NFC // 2)}", f"hbuf.{fc}"),
                         writes=(f"pt{2 + h2}",),
                         tag=f"fn{bi}.y{tcb}.{dc}.{h2}.{fc}")

                def silu(e, pb=pb, dc=dc, h2=h2):
                    return e.activation(s["ev"][:, h2, :], pb[:, :], ACTF.Silu,
                                        bias=s["bias"][:, K.B_B2 + dc:K.B_B2 + dc + 1])
                p.act(silu, reads=(f"pt{2 + h2}", "bias"), writes=(f"ev{h2}",),
                      tag=f"fn{bi}.s{tcb}.{dc}.{h2}")
                tc = tcb * (LC // 512) + h2
                p.dma("sp_dma", d["g0"][bi, dc, :, tc * 512:(tc + 1) * 512],
                      s["ev"][:, h2, :], reads=(f"ev{h2}",),
                      writes=(f"g0.{bi}.{dc}.{tc}",), tag=f"fn{bi}.gs{tcb}.{dc}.{h2}")


def emit_xq_qkv(k, bi):
    p, s, d = k.p, k.s, k.d
    T, NTC = k.T, k.NTC
    st = s["stat"]
    for dc in range(NDC):
        for tc in range(NTC):
            sl = (dc * NTC + tc) % 2
            p.dma("sp_dma", s["fx"][:, sl, 0:512],
                  d["xn32"][bi, dc, :, tc * 512:(tc + 1) * 512],
                  reads=(f"xn32.{bi}.{dc}.{tc}",), writes=(f"fx{sl}",),
                  tag=f"xq{bi}.{dc}.{tc}")
            quant_chain(k, s["fx"][:, sl, 0:512],
                        s["act_a"][:, dc, tc * 512:(tc + 1) * 512],
                        st[:, K.S_CINV + dc:K.S_CINV + dc + 1],
                        st[:, K.S_CZP + dc:K.S_CZP + dc + 1],
                        st[:, K.S_CS + dc:K.S_CS + dc + 1],
                        st[:, K.S_CNZS + dc:K.S_CNZS + dc + 1],
                        reads=(f"fx{sl}", "chanpar"), writes=(f"act_a.{dc}",),
                        sl=sl, tag=f"xq{bi}.{dc}.{tc}")
    p.fence(reads=(), writes=("w1t0", "w1t1", "w2t0", "w2t1", "wvb0"),
            tag=f"f.v{bi}")
    for dc in range(NDC):
        p.dma("pl_dma", s["wvb0"][:, dc, :], d["wqT_v"][dc, :, :],
              reads=tuple(f"wv.{dc}.{j}" for j in range(8)),
              writes=("wvb0",), tag=f"qv{bi}.w{dc}")
    for fc in range(16):
        wsl = fc % 2
        p.dma("sp_dma", s["wtile"][:, wsl, :, :],
              d["wqT_qk"][fc, :, :, :].rearrange("a p j -> p a j"),
              reads=tuple(f"wqk.{fc}.{dcc}" for dcc in range(NDC)),
              writes=(f"wtile{wsl}",), tag=f"qk{bi}.w{fc}")
        for dc in range(NDC):
            for tc in range(NTC):
                def mm(e, wsl=wsl, dc=dc, tc=tc):
                    return e.matmul(k.psum[f"pb{tc}"][:, :], s["wtile"][:, wsl, dc, :],
                                    s["act_a"][:, dc, tc * 512:(tc + 1) * 512],
                                    start=(dc == 0), stop=(dc == NDC - 1))
                p.pe(mm, reads=(f"wtile{wsl}", f"act_a.{dc}"), writes=(f"pt{tc}",),
                     tag=f"qk{bi}.m{fc}.{dc}.{tc}")
        for tc in range(NTC):
            def evac(e, fc=fc, tc=tc):
                return e.activation(s["ev"][:, tc % 2, :], k.psum[f"pb{tc}"][:, :],
                                    ACTF.Identity,
                                    bias=s["bias"][:, K.B_L0QKV + fc:K.B_L0QKV + fc + 1])
            p.act(evac, reads=(f"pt{tc}", "bias"), writes=(f"ev{tc % 2}",),
                  tag=f"qk{bi}.e{fc}.{tc}")

            def mnmx(e, fc=fc, tc=tc):
                cx = K.S_RMX + (0 if fc < 8 else 1)
                cn = K.S_RMN + (0 if fc < 8 else 1)
                e.tensor_reduce(s["small"][:, 2:3], s["ev"][:, tc % 2, :], AX.X, ALU.max)
                e.drain()
                e.tensor_reduce(s["small"][:, 3:4], s["ev"][:, tc % 2, :], AX.X, ALU.min)
                e.drain()
                e.tensor_tensor(st[:, cx:cx + 1], st[:, cx:cx + 1],
                                s["small"][:, 2:3], ALU.max)
                e.drain()
                return e.tensor_tensor(st[:, cn:cn + 1], st[:, cn:cn + 1],
                                       s["small"][:, 3:4], ALU.min)
            p.dve(mnmx, reads=(f"ev{tc % 2}",), writes=("qkvstat", "small"),
                  tag=f"qk{bi}.n{fc}.{tc}")
            p.dma("sp_dma" if (fc + tc) % 2 == 0 else "pl_dma",
                  d["qkraw"][bi, fc, tc, :, :], s["ev"][:, tc % 2, :],
                  reads=(f"ev{tc % 2}",), writes=(f"qkraw.{bi}.{fc}.{tc}",),
                  tag=f"qk{bi}.s{fc}.{tc}")
    for t128 in range(k.NT128):
        for fv in range(2):
            pq = 4 + (2 * t128 + fv) % 2
            pb = k.psum[f"pb{pq}"]
            for dc in range(NDC):
                def mm(e, pb=pb, t128=t128, fv=fv, dc=dc):
                    return e.matmul(pb[:, :],
                                    s["act_a"][:, dc, t128 * 128:(t128 + 1) * 128],
                                    s["wvb0"][:, dc, fv * 512:(fv + 1) * 512],
                                    start=(dc == 0), stop=False)
                p.pe(mm, reads=("wvb0", f"act_a.{dc}"), writes=(f"pt{pq}",),
                     tag=f"qv{bi}.m{t128}.{fv}.{dc}")

            def mmb(e, pb=pb, fv=fv):
                return e.matmul(pb[:, :], s["ones_b"][0:1, 0:128],
                                s["brow"][0:1, 0, fv * 512:(fv + 1) * 512],
                                start=False, stop=True)
            p.pe(mmb, reads=("brow0", "ones_b"), writes=(f"pt{pq}",),
                 tag=f"qv{bi}.b{t128}.{fv}")

            def evac(e, pb=pb, fv=fv):
                return e.tensor_copy(s["ev"][:, fv, :], pb[:, :])
            p.dve(evac, reads=(f"pt{pq}",), writes=(f"ev{fv}",),
                  tag=f"qv{bi}.e{t128}.{fv}")

            def mnmx(e, fv=fv):
                e.tensor_reduce(s["small"][:, 4:5], s["ev"][:, fv, :], AX.X, ALU.max)
                e.drain()
                e.tensor_reduce(s["small"][:, 5:6], s["ev"][:, fv, :], AX.X, ALU.min)
                e.drain()
                cx, cn = K.S_RMX + 2, K.S_RMN + 2
                e.tensor_tensor(st[:, cx:cx + 1], st[:, cx:cx + 1],
                                s["small"][:, 4:5], ALU.max)
                e.drain()
                return e.tensor_tensor(st[:, cn:cn + 1], st[:, cn:cn + 1],
                                       s["small"][:, 5:6], ALU.min)
            p.dve(mnmx, reads=(f"ev{fv}",), writes=("qkvstat", "small"),
                  tag=f"qv{bi}.n{t128}.{fv}")
            p.dma("sp_dma" if (t128 + fv) % 2 == 0 else "pl_dma",
                  d["vraw"][bi, t128, fv, :, :], s["ev"][:, fv, :],
                  reads=(f"ev{fv}",), writes=(f"vraw.{bi}.{t128}.{fv}",),
                  tag=f"qv{bi}.s{t128}.{fv}")


def bcast_params(k, pbname, src_res, dst_fn, tag):
    pass


def emit_ar2(k):
    p, s, d = k.p, k.s, k.d
    st = s["stat"]

    def pack(e):
        e.memset(s["tmp8"][:, 38:40], 0.0)
        e.drain()
        e.tensor_copy(s["tmp8"][:, 32:35], st[:, K.S_RMX:K.S_RMX + 3])
        e.drain()
        return e.tensor_scalar(s["tmp8"][:, 35:38], st[:, K.S_RMN:K.S_RMN + 3],
                               -1.0, None, ALU.mult)
    p.dve(pack, reads=("qkvstat",), writes=("tmp8c",), tag="ar2.pk")

    def tp(e):
        return e.transpose(k.psum["pb6"][0:8, 0:128], s["tmp8"][:, 32:40],
                           s["idn_f"][:, :])
    p.pe(tp, reads=("tmp8c", "idn_f"), writes=("pt6",), tag="ar2.tp")

    def red(e):
        return e.tensor_reduce(s["small"][0:8, 6:7], k.psum["pb6"][0:8, 0:128],
                               AX.X, ALU.max)
    p.dve(red, reads=("pt6",), writes=("small",), tag="ar2.rd")
    p.dma("pl_dma", d["ar2_in"][:, :], s["small"][0:8, 6:7], reads=("small",),
          writes=("ar2_in",), tag="ar2.st")
    p.collective(ALU.max, [list(range(k.n_cores))],
                 d["ar2_in"].ap().opt(), d["ar2_out"].ap().opt(),
                 reads=("ar2_in",), writes=("ar2_out",), tag="ar2.cc")
    p.dma("pl_dma", s["small"][0:8, 7:8], d["ar2_out"][:, :], reads=("ar2_out",),
          writes=("small78",), tag="ar2.ld")

    def tp2(e):
        return e.transpose(k.psum["pb7"][0:1, 0:8], s["small"][0:8, 7:8],
                           s["idn_f"][0:8, 0:8])
    p.pe(tp2, reads=("small78", "idn_f"), writes=("pt7",), tag="ar2.tp2")

    def ev7(e):
        return e.tensor_copy(s["small"][0:1, 8:16], k.psum["pb7"][0:1, 0:8])
    p.dve(ev7, reads=("pt7",), writes=("smallrow",), tag="ar2.e7")

    def bc(e):
        return e.matmul(k.psum["pb7"][:, 0:8], s["ones_f"][0:1, :],
                        s["small"][0:1, 8:16], start=True, stop=True)
    p.pe(bc, reads=("smallrow", "ones_f"), writes=("pt7",), tag="ar2.bc")

    def par(e):
        qp = s["qp"]
        pbv = k.psum["pb7"]
        sm = s["small"]
        e.tensor_scalar(sm[:, 9:12], pbv[:, 3:6], -1.0, None, ALU.mult)
        e.drain()
        e.tensor_scalar(sm[:, 9:12], sm[:, 9:12], 0.0, None, ALU.min)
        e.drain()
        e.tensor_scalar(sm[:, 12:15], pbv[:, 0:3], 0.0, None, ALU.max)
        e.drain()
        e.tensor_sub(sm[:, 15:18], sm[:, 12:15], sm[:, 9:12])
        e.drain()
        e.tensor_scalar(qp[:, K.QP_S:K.QP_S + 3], sm[:, 15:18], 1.0 / 255.0, 1e-8,
                        ALU.mult, ALU.max)
        e.drain()
        recip_acc(e, qp[:, K.QP_INV:K.QP_INV + 3],
                                     qp[:, K.QP_S:K.QP_S + 3], sm[:, 15:18])
        e.drain()
        e.tensor_mul(sm[:, 15:18], sm[:, 9:12], qp[:, K.QP_INV:K.QP_INV + 3])
        e.drain()
        e.tensor_scalar(s["qi"][:, 0, 0:3], sm[:, 15:18], -1.0, None, ALU.mult)
        e.drain()
        e.tensor_copy(qp[:, K.QP_ZP:K.QP_ZP + 3], s["qi"][:, 0, 0:3])
        e.drain()
        e.tensor_mul(sm[:, 15:18], qp[:, K.QP_ZP:K.QP_ZP + 3], qp[:, K.QP_S:K.QP_S + 3])
        e.drain()
        return e.tensor_scalar(qp[:, K.QP_NZS:K.QP_NZS + 3], sm[:, 15:18],
                               -1.0, None, ALU.mult)
    p.dve(par, reads=("pt7",), writes=("qpar", "small", "qi0"), tag="ar2.par")


def emit_l0_attention(k, bi):
    p, s, d = k.p, k.s, k.d
    T, NTC, NT128 = k.T, k.NTC, k.NT128
    qp = s["qp"]
    if bi == 0:
        p.fence(reads=(), writes=("wvb0", "w1t0", "w1t1", "w2t",
                       *(f"hbuf.{fc}" for fc in range(NFC)),
                       "vaug.ones", "nrmX"), tag="f.attn0")

        def vm(e):
            return e.memset(s["vaug"][:, :, :, :], 1.0)
        p.dve(vm, reads=("vaug.ones",), writes=("vaug.ones",), tag="vmemset")
    for sec, dst, base, ti in (("k", "act_a", 8, 1), ("q", "act_b", 0, 0)):
        for fcl in range(8):
            fc = base + fcl
            for tc in range(NTC):
                sl = (fcl * NTC + tc) % 2
                p.dma("sp_dma", s["fx"][:, sl, 0:512], d["qkraw"][bi, fc, tc, :, :],
                      reads=(f"qkraw.{bi}.{fc}.{tc}",), writes=(f"fx{sl}",),
                      tag=f"a{bi}.l{sec}{fcl}.{tc}")
                quant_chain(k, s["fx"][:, sl, 0:512],
                            s[dst][:, fcl, tc * 512:(tc + 1) * 512],
                            qp[:, K.QP_INV + ti:K.QP_INV + ti + 1],
                            qp[:, K.QP_ZP + ti:K.QP_ZP + ti + 1],
                            qp[:, K.QP_S + ti:K.QP_S + ti + 1],
                            qp[:, K.QP_NZS + ti:K.QP_NZS + ti + 1],
                            reads=(f"fx{sl}", "qpar"), writes=(f"{dst}.{fcl}",),
                            sl=sl, tag=f"a{bi}.q{sec}{fcl}.{tc}")
    for t128 in range(NT128):
        for fv in range(2):
            sl = (t128 * 2 + fv) % 2
            p.dma("sp_dma", s["fx"][:, sl, 0:512], d["vraw"][bi, t128, fv, :, :],
                  reads=(f"vraw.{bi}.{t128}.{fv}",), writes=(f"fx{sl}",),
                  tag=f"a{bi}.lv{t128}.{fv}")
            quant_chain(k, s["fx"][:, sl, 0:512],
                        s["vaug"][:, t128, fv * 8:(fv + 1) * 8, 0:64],
                        qp[:, K.QP_INV + 2:K.QP_INV + 3],
                        qp[:, K.QP_ZP + 2:K.QP_ZP + 3],
                        qp[:, K.QP_S + 2:K.QP_S + 3],
                        qp[:, K.QP_NZS + 2:K.QP_NZS + 3],
                        reads=(f"fx{sl}", "qpar"), writes=(f"vg.{t128 % 4}",),
                        sl=sl, tag=f"a{bi}.qv{t128}.{fv}")
    if bi == 0 and "dbg_ka" in K.DEBUG_OUT:
        for dc in range(NDC):
            p.dma("pl_dma", d["dbg_ka"][:, dc, :], s["act_a"][:, dc, :],
                  reads=(f"act_a.{dc}",), writes=("dbgka",), tag=f"dbg.ka{dc}")
            p.dma("pl_dma", d["dbg_qa"][:, dc, :], s["act_b"][:, dc, :],
                  reads=(f"act_b.{dc}",), writes=("dbgqa",), tag=f"dbg.qa{dc}")
    attention_core(k, bi, layer=0, Ks=s["act_a"], Kres="act_a",
                   Qs=s["act_b"], Qres="act_b", nrm=s["nrm0"], nres="nrmX")


def attention_core(k, bi, layer, Ks, Kres, Qs, Qres, nrm, nres):
    p, s, d = k.p, k.s, k.d
    T, NTC, NT128 = k.T, k.NTC, k.NT128
    st = s["stat"]
    for idx, (src, sres) in enumerate(((Qs, Qres), (Ks, Kres))):
        off = 0 if idx == 0 else T
        for tc in range(NTC):
            for dc in range(NDC):
                def sqr(e, src=src, dc=dc, tc=tc):
                    return e.activation(s["sq"][:, 0, 0:512],
                                        src[:, dc, tc * 512:(tc + 1) * 512],
                                        ACTF.Square)
                p.act(sqr, reads=(f"{sres}.{dc}",), writes=("sq0",),
                      tag=f"n{layer}{bi}.s{idx}.{tc}.{dc}")

                def mm(e, dc=dc):
                    return e.matmul(k.psum["pb6"][0:16, :], s["hind"][:, dc, :],
                                    s["sq"][:, 0, 0:512],
                                    start=(dc == 0), stop=(dc == NDC - 1))
                p.pe(mm, reads=("sq0", "hind"), writes=("pt6",),
                     tag=f"n{layer}{bi}.m{idx}.{tc}.{dc}")

            def rt(e, off=off, tc=tc):
                return e.activation(nrm[0:16, off + tc * 512:off + (tc + 1) * 512],
                                    k.psum["pb6"][0:16, :], ACTF.Sqrt)
            p.act(rt, reads=("pt6",), writes=(nres,), tag=f"n{layer}{bi}.r{idx}.{tc}")

    if layer == 0 and bi == 0 and "dbg_hind" in K.DEBUG_OUT:
        p.dma("pl_dma", d["dbg_hind"][:, :, :], s["hind"][:, :, :],
              reads=("hind",), writes=("hind",), tag="dbg.hind")
    if layer == 0 and bi == 0 and "dbg_nrm" in K.DEBUG_OUT:
        p.dma("pl_dma", d["dbg_nrm"][:, 0, 0:T], nrm[0:16, 0:T],
              reads=(nres,), writes=(nres,), tag="dbg.nrmq")
        p.dma("pl_dma", d["dbg_nrm"][:, 1, 0:T], nrm[0:16, T:2 * T],
              reads=(nres,), writes=(nres,), tag="dbg.nrmk")

    def cmax(e):
        e.tensor_reduce(s["small"][0:16, 20:21], nrm[0:16, 0:T], AX.X, ALU.max)
        e.drain()
        e.tensor_reduce(s["small"][0:16, 21:22], nrm[0:16, T:2 * T], AX.X, ALU.max)
        e.drain()
        e.tensor_mul(s["small"][0:16, 22:23], s["small"][0:16, 20:21],
                     s["small"][0:16, 21:22])
        e.drain()
        return e.tensor_scalar(s["small"][0:16, 23:24], s["small"][0:16, 22:23],
                               -0.125, None, ALU.mult)
    p.dve(cmax, reads=(nres,), writes=("small",), tag=f"c{layer}{bi}.mx")

    if layer == 0 and bi == 0 and "dbg_sm" in K.DEBUG_OUT:
        p.dma("sp_dma", d["dbg_sm"][:, :], s["small"][0:16, 20:24],
              reads=("small",), writes=("small",), tag="dbg.sm")

    def ctp(e):
        return e.transpose(k.psum["pb7"][0:1, 0:16], s["small"][0:16, 23:24],
                           s["idn_f"][0:16, 0:16])
    p.pe(ctp, reads=("small", "idn_f"), writes=("pt7",), tag=f"c{layer}{bi}.tp")

    def cev(e):
        return e.tensor_copy(s["small"][0:1, 32:48], k.psum["pb7"][0:1, 0:16])
    p.dve(cev, reads=("pt7",), writes=("small",), tag=f"c{layer}{bi}.ev")

    def cbc(e):
        return e.matmul(k.psum["pb7"][:, 0:16], s["ones_f"][0:1, :],
                        s["small"][0:1, 32:48], start=True, stop=True)
    p.pe(cbc, reads=("small", "ones_f"), writes=("pt7",), tag=f"c{layer}{bi}.bc")

    def cbe(e):
        return e.tensor_copy(s["cbias"][:, :], k.psum["pb7"][:, 0:16])
    p.dve(cbe, reads=("pt7",), writes=("cbias",), tag=f"c{layer}{bi}.eb")
    if layer == 0 and bi == 0 and "dbg_cb" in K.DEBUG_OUT:
        p.dma("sp_dma", d["dbg_cb"][:, :], s["cbias"][:, :],
              reads=("cbias",), writes=("cbias",), tag="dbg.cb")

    for hp in range(8):
        for qc in range(NTC):
            qsl = slice(qc * 512, (qc + 1) * 512)
            NJ = NT128 // 2

            def emit_S(j, sub):
                h = 2 * hp + sub
                po = 64 * sub
                for half in range(2):
                    kc = 2 * j + half

                    def mmS(e, hp=hp, po=po, kc=kc, qsl=qsl,
                            sub=sub, half=half):
                        return e.matmul(
                            k.psum[f"pb{2 + 2 * sub}{3 + 2 * sub}"][
                                :, 512 * half:512 * (half + 1)],
                            Ks[po:po + 64, hp, kc * 128:(kc + 1) * 128],
                            Qs[po:po + 64, hp, qsl],
                            start=True, stop=True)
                    p.pe(mmS, reads=(f"{Kres}.{hp}", f"{Qres}.{hp}"),
                         writes=(f"pt{2 + 2 * sub + half}",),
                         tag=f"A{layer}{bi}.S{hp}.{sub}.{qc}.{kc}")

            def emit_E(j, sub):
                h = 2 * hp + sub

                def expv(e, h=h, sub=sub):
                    return e.activation(s["ptile"][:, sub, :],
                                        k.psum[f"pb{2 + 2 * sub}{3 + 2 * sub}"][:, :],
                                        ACTF.Exp, scale=s["cst"][:, 1:2],
                                        bias=s["cbias"][:, h:h + 1])
                p.act(expv, reads=(f"pt{2 + 2 * sub}", f"pt{3 + 2 * sub}",
                                   "cbias"),
                      writes=(f"ptile{2 * sub}", f"ptile{2 * sub + 1}"),
                      tag=f"A{layer}{bi}.E{hp}.{sub}.{qc}.{j}")

            def emit_PV(j, sub):
                h = 2 * hp + sub
                pacc = k.psum[f"pb{sub}"]
                for half in range(2):
                    kc = 2 * j + half

                    def mmPV(e, h=h, kc=kc, sub=sub, half=half, pacc=pacc):
                        return e.matmul(pacc[0:66, :],
                                        s["vaug"][:, kc, h, 0:66],
                                        s["ptile"][:, sub, 512 * half:512 * (half + 1)],
                                        start=(kc == 0), stop=(kc == NT128 - 1))
                    p.pe(mmPV, reads=(f"ptile{2 * sub + half}", f"vg.{kc % 4}",
                                      "vaug.ones"),
                         writes=(f"pt{sub}",),
                         tag=f"A{layer}{bi}.P{hp}.{sub}.{qc}.{kc}")

            emit_S(0, 0)
            emit_S(0, 1)
            for j in range(NJ):
                for sub in range(2):
                    emit_E(j, sub)
                    if j + 1 < NJ:
                        emit_S(j + 1, sub)
                    emit_PV(j, sub)

            def dcp0(e):
                return e.tensor_copy(s["den"][64:65, :512], k.psum["pb0"][64:65, :])
            p.dve(dcp0, reads=("pt0",), writes=("den",),
                  tag=f"A{layer}{bi}.D{hp}.0.{qc}")

            def dcp1(e):
                return e.tensor_copy(s["rsc"][64:65, 0:512], k.psum["pb1"][64:65, :])
            p.dve(dcp1, reads=("pt1",), writes=("rsc",),
                  tag=f"A{layer}{bi}.D{hp}.1.{qc}")

            def dbc0(e):
                return e.matmul(k.psum["pb67"][0:64, 0:512],
                                s["onesf64"][64:65, 0:64],
                                s["den"][64:65, :512], start=True, stop=True)
            p.pe(dbc0, reads=("den", "onesf64"), writes=("pt6",),
                 tag=f"A{layer}{bi}.B{hp}.0.{qc}")

            def dbc1(e):
                return e.matmul(k.psum["pb67"][0:64, 512:1024],
                                s["onesf64"][64:65, 0:64],
                                s["rsc"][64:65, 0:512], start=True, stop=True)
            p.pe(dbc1, reads=("rsc", "onesf64"), writes=("pt7",),
                 tag=f"A{layer}{bi}.B{hp}.1.{qc}")

            def rcd(e):
                e.activation(s["ev"][0:64, 2:4, :], k.psum["pb67"][0:64, :],
                             ACTF.Ln)
                e.drain()
                e.activation(s["rec"][0:64, 0:512], s["ev"][0:64, 2, :],
                             ACTF.Exp, scale=s["cst"][0:64, 2:3],
                             bias=s["cst"][0:64, 3:4])
                return e.activation(s["rsc"][0:64, 0:512], s["ev"][0:64, 3, :],
                                    ACTF.Exp, scale=s["cst"][0:64, 2:3],
                                    bias=s["cst"][0:64, 3:4])
            p.act(rcd, reads=("pt6", "pt7", "cst"),
                  writes=("rec", "rsc", "ev2", "ev3"),
                  tag=f"A{layer}{bi}.R{hp}.{qc}")

            for sub in range(2):
                h = 2 * hp + sub
                po = 64 * sub
                pacc = k.psum[f"pb{sub}"]
                rtile = "rec" if sub == 0 else "rsc"
                if layer == 0:
                    def div(e, pacc=pacc, sub=sub, rtile=rtile):
                        return e.tensor_mul(s["ev"][0:64, sub, :], pacc[0:64, :],
                                            s[rtile][0:64, 0:512])
                    p.dve(div, reads=(f"pt{sub}", rtile), writes=(f"ev{sub}",),
                          tag=f"A{layer}{bi}.V{hp}.{sub}.{qc}")

                    def omn(e, sub=sub):
                        e.tensor_reduce(s["small"][0:64, 24:25], s["ev"][0:64, sub, :],
                                        AX.X, ALU.max)
                        e.drain()
                        e.tensor_reduce(s["small"][0:64, 25:26], s["ev"][0:64, sub, :],
                                        AX.X, ALU.min)
                        e.drain()
                        e.tensor_tensor(st[0:64, K.S_OMX:K.S_OMX + 1],
                                        st[0:64, K.S_OMX:K.S_OMX + 1],
                                        s["small"][0:64, 24:25], ALU.max)
                        e.drain()
                        return e.tensor_tensor(st[0:64, K.S_OMN:K.S_OMN + 1],
                                               st[0:64, K.S_OMN:K.S_OMN + 1],
                                               s["small"][0:64, 25:26], ALU.min)
                    p.dve(omn, reads=(f"ev{sub}",), writes=("ostat", "small"),
                          tag=f"A{layer}{bi}.O{hp}.{sub}.{qc}")
                    p.dma("sp_dma", d["oraw"][bi, h, qc, :, :], s["ev"][0:64, sub, :],
                          reads=(f"ev{sub}",), writes=(f"oraw.{bi}.{h}.{qc}",),
                          tag=f"A{layer}{bi}.W{hp}.{sub}.{qc}")
                else:
                    def divb(e, pacc=pacc, po=po, hp=hp, qsl=qsl, rtile=rtile):
                        return e.tensor_mul(s["act_a"][po:po + 64, hp, qsl],
                                            pacc[0:64, :], s[rtile][0:64, 0:512])
                    p.dve(divb, reads=(f"pt{sub}", rtile), writes=(f"act_a.{hp}",),
                          tag=f"A{layer}{bi}.V{hp}.{sub}.{qc}")


def emit_ar3(k):
    p, s, d = k.p, k.s, k.d
    st = s["stat"]

    def pack(e):
        e.tensor_copy(s["tmp8"][0:64, 40:41], st[0:64, K.S_OMX:K.S_OMX + 1])
        e.drain()
        return e.tensor_scalar(s["tmp8"][0:64, 41:42], st[0:64, K.S_OMN:K.S_OMN + 1],
                               -1.0, None, ALU.mult)
    p.dve(pack, reads=("ostat",), writes=("tmp8d",), tag="ar3.pk")

    def tp(e):
        return e.transpose(k.psum["pb6"][0:2, 0:64], s["tmp8"][0:64, 40:42],
                           s["idn_f"][0:64, 0:64])
    p.pe(tp, reads=("tmp8d", "idn_f"), writes=("pt6",), tag="ar3.tp")

    def red(e):
        return e.tensor_reduce(s["small"][0:2, 28:29], k.psum["pb6"][0:2, 0:64],
                               AX.X, ALU.max)
    p.dve(red, reads=("pt6",), writes=("small",), tag="ar3.rd")
    p.dma("pl_dma", d["ar3_in"][:, :], s["small"][0:2, 28:29], reads=("small",),
          writes=("ar3_in",), tag="ar3.st")
    p.collective(ALU.max, [list(range(k.n_cores))],
                 d["ar3_in"].ap().opt(), d["ar3_out"].ap().opt(),
                 reads=("ar3_in",), writes=("ar3_out",), tag="ar3.cc")
    p.dma("pl_dma", s["small"][0:2, 29:30], d["ar3_out"][:, :], reads=("ar3_out",),
          writes=("small29",), tag="ar3.ld")

    def tp2(e):
        return e.transpose(k.psum["pb7"][0:1, 0:2], s["small"][0:2, 29:30],
                           s["idn_f"][0:2, 0:2])
    p.pe(tp2, reads=("small29", "idn_f"), writes=("pt7",), tag="ar3.tp2")

    def ev(e):
        return e.tensor_copy(s["small"][0:1, 26:28], k.psum["pb7"][0:1, 0:2])
    p.dve(ev, reads=("pt7",), writes=("small",), tag="ar3.ev")

    def bc(e):
        return e.matmul(k.psum["pb7"][:, 0:2], s["ones_f"][0:1, :],
                        s["small"][0:1, 26:28], start=True, stop=True)
    p.pe(bc, reads=("small", "ones_f"), writes=("pt7",), tag="ar3.bc")

    def par(e):
        qp = s["qp"]
        pbv = k.psum["pb7"]
        sm = s["small"]
        e.tensor_scalar(sm[:, 9:10], pbv[:, 1:2], -1.0, None, ALU.mult)
        e.drain()
        e.tensor_scalar(sm[:, 9:10], sm[:, 9:10], 0.0, None, ALU.min)
        e.drain()
        e.tensor_scalar(sm[:, 10:11], pbv[:, 0:1], 0.0, None, ALU.max)
        e.drain()
        e.tensor_sub(sm[:, 11:12], sm[:, 10:11], sm[:, 9:10])
        e.drain()
        e.tensor_scalar(qp[:, K.QP_OS:K.QP_OS + 1], sm[:, 11:12], 1.0 / 255.0, 1e-8,
                        ALU.mult, ALU.max)
        e.drain()
        recip_acc(e, qp[:, K.QP_OINV:K.QP_OINV + 1],
                                     qp[:, K.QP_OS:K.QP_OS + 1], sm[:, 11:12])
        e.drain()
        e.tensor_mul(sm[:, 11:12], sm[:, 9:10], qp[:, K.QP_OINV:K.QP_OINV + 1])
        e.drain()
        e.tensor_scalar(s["qi"][:, 0, 0:1], sm[:, 11:12], -1.0, None, ALU.mult)
        e.drain()
        e.tensor_copy(qp[:, K.QP_OZP:K.QP_OZP + 1], s["qi"][:, 0, 0:1])
        e.drain()
        e.tensor_mul(sm[:, 11:12], qp[:, K.QP_OZP:K.QP_OZP + 1],
                     qp[:, K.QP_OS:K.QP_OS + 1])
        e.drain()
        return e.tensor_scalar(qp[:, K.QP_ONZS:K.QP_ONZS + 1], sm[:, 11:12],
                               -1.0, None, ALU.mult)
    p.dve(par, reads=("pt7",), writes=("opar", "small", "qi0"), tag="ar3.par")


def emit_l0_proj_update(k, bi):
    p, s, d = k.p, k.s, k.d
    T, NTC = k.T, k.NTC
    qp = s["qp"]
    ostg = [(s["fx"][0:64, 0, 0:512], "fx0"), (s["fx"][0:64, 1, 0:512], "fx1"),
            (s["den"][0:64, 0:512], "den"), (s["rec"][0:64, 0:512], "rec")]
    for hh in range(H):
        dc, po = hh // 2, 64 * (hh % 2)
        for qc in range(NTC):
            r4 = (hh * NTC + qc) % 4
            sl = r4 % 2
            sap, skey = ostg[r4]
            p.dma("sp_dma" if r4 % 2 == 0 else "pl_dma",
                  sap, d["oraw"][bi, hh, qc, :, :],
                  reads=(f"oraw.{bi}.{hh}.{qc}",), writes=(skey,),
                  tag=f"o{bi}.{hh}.{qc}")
            quant_chain(k, sap,
                        s["act_a"][po:po + 64, dc, qc * 512:(qc + 1) * 512],
                        qp[0:64, K.QP_OINV:K.QP_OINV + 1],
                        qp[0:64, K.QP_OZP:K.QP_OZP + 1],
                        qp[0:64, K.QP_OS:K.QP_OS + 1],
                        qp[0:64, K.QP_ONZS:K.QP_ONZS + 1],
                        reads=(skey, "opar"), writes=(f"act_a.{dc}",),
                        sl=sl, tag=f"o{bi}.{hh}.{qc}", np_=64)
    for fc in range(NDC):
        wsl = fc % 2
        p.dma("sp_dma", s["wtile"][:, wsl, :, :],
              d["pwT"][fc, :, :, :].rearrange("a p j -> p a j"),
              reads=tuple(f"pw.{fc}.{dcc}" for dcc in range(NDC)),
              writes=(f"wtile{wsl}",), tag=f"pj{bi}.w{fc}")
        for dc in range(NDC):
            for tc in range(NTC):
                def mm(e, wsl=wsl, dc=dc, tc=tc):
                    return e.matmul(k.psum[f"pb{tc}"][:, :], s["wtile"][:, wsl, dc, :],
                                    s["act_a"][:, dc, tc * 512:(tc + 1) * 512],
                                    start=(dc == 0), stop=(dc == NDC - 1))
                p.pe(mm, reads=(f"wtile{wsl}", f"act_a.{dc}"), writes=(f"pt{tc}",),
                     tag=f"pj{bi}.m{fc}.{dc}.{tc}")
        xstg = [(s["fx"][:, 0, 0:512], "fx0"), (s["fx"][:, 1, 0:512], "fx1"),
                (s["den"][:, 0:512], "den"), (s["rec"][:, 0:512], "rec")]
        gstg = [(s["fy"][:, 0, :], "fy0"), (s["fy"][:, 1, :], "fy1"),
                (s["rsc"][:, 0:512], "rsc")]
        for tc in range(NTC):
            def evac(e, fc=fc, tc=tc):
                return e.activation(s["ev"][:, tc % 2, :], k.psum[f"pb{tc}"][:, :],
                                    ACTF.Identity,
                                    bias=s["bias"][:, K.B_L0PROJ + fc:K.B_L0PROJ + fc + 1])
            p.act(evac, reads=(f"pt{tc}", "bias"), writes=(f"ev{tc % 2}",),
                  tag=f"pj{bi}.e{fc}.{tc}")
            sl = tc % 2
            r4 = (fc * NTC + tc) % 4
            r3 = (fc * NTC + tc) % 3
            xap, xkey = xstg[r4]
            gap_, gkey = gstg[r3]
            p.dma("sp_dma" if r4 % 2 == 0 else "pl_dma", xap,
                  d["x_fm"][bi, fc, :, tc * 512:(tc + 1) * 512],
                  writes=(xkey,), tag=f"pj{bi}.x{fc}.{tc}")
            p.dma("pl_dma" if r4 % 2 == 0 else "sp_dma", gap_,
                  d["g0"][bi, fc, :, tc * 512:(tc + 1) * 512],
                  reads=(f"g0.{bi}.{fc}.{tc}",), writes=(gkey,),
                  tag=f"pj{bi}.g{fc}.{tc}")

            def upd(e, sl=sl, xap=xap, gap_=gap_):
                e.tensor_mul(s["ev"][:, sl, :], s["ev"][:, sl, :], gap_)
                e.drain()
                return e.tensor_add(s["ev"][:, sl, :], s["ev"][:, sl, :], xap)
            p.dve(upd, reads=(f"ev{sl}", xkey, gkey), writes=(f"ev{sl}",),
                  tag=f"pj{bi}.u{fc}.{tc}")
            p.dma("sp_dma" if tc % 2 == 0 else "pl_dma",
                  d["xmid"][bi, fc, :, tc * 512:(tc + 1) * 512],
                  s["ev"][:, sl, :], reads=(f"ev{sl}",),
                  writes=(f"xmid.{bi}.{fc}.{tc}",), tag=f"pj{bi}.s{fc}.{tc}")


def emit_l1_stats(k, bi):
    p, s, d = k.p, k.s, k.d
    T, NTC, LC = k.T, k.NTC, k.LC
    nl = T // LC
    for dc in range(NDC):
        for li in range(nl):
            sl = (dc * nl + li) % 2
            p.dma("sp_dma", s["fx"][:, sl, 0:LC],
                  d["xmid"][bi, dc, :, li * LC:(li + 1) * LC],
                  reads=tuple(f"xmid.{bi}.{dc}.{li * (LC // 512) + j}"
                              for j in range(LC // 512)),
                  writes=(f"fx{sl}",), tag=f"t{bi}.x{dc}.{li}")

            def sqc(e, sl=sl, dc=dc, li=li):
                e.activation(s["sq"][:, sl, :], s["fx"][:, sl, 0:LC], ACTF.Square)
                e.drain()
                return e.activation(s["act_b"][:, dc, li * LC:(li + 1) * LC],
                                    s["fx"][:, sl, 0:LC], ACTF.Identity)
            p.act(sqc, reads=(f"fx{sl}",), writes=(f"sq{sl}", f"act_b.{dc}"),
                  tag=f"t{bi}.q{dc}.{li}")
            for q5 in range(LC // 512):
                tc = li * (LC // 512) + q5

                def mm1(e, dc=dc, li=li, q5=q5, tc=tc):
                    return e.matmul(k.psum[f"pb{tc}"][:, :], s["ones_b"][:, :],
                                    s["act_b"][:, dc, li * LC + q5 * 512:li * LC + (q5 + 1) * 512],
                                    start=(dc == 0), stop=(dc == NDC - 1))
                p.pe(mm1, reads=(f"act_b.{dc}", "ones_b"), writes=(f"pt{tc}",),
                     tag=f"t{bi}.mS{dc}.{tc}")

                def mm2(e, sl=sl, q5=q5, dc=dc, tc=tc):
                    return e.matmul(k.psum[f"pb{NTC + tc}"][:, :], s["ones_b"][:, :],
                                    s["sq"][:, sl, q5 * 512:(q5 + 1) * 512],
                                    start=(dc == 0), stop=(dc == NDC - 1))
                p.pe(mm2, reads=(f"sq{sl}", "ones_b"), writes=(f"pt{NTC + tc}",),
                     tag=f"t{bi}.mQ{dc}.{tc}")
    for tc in range(NTC):
        def stats(e, tc=tc):
            e.tensor_scalar(s["mur"][:, 0, :], k.psum[f"pb{tc}"][:, :],
                            1.0 / 1024.0, None, ALU.mult)
            e.drain()
            e.tensor_scalar(s["rr"][:, 0, :], k.psum[f"pb{NTC + tc}"][:, :],
                            1.0 / 1024.0, None, ALU.mult)
            e.drain()
            e.scalar_tensor_tensor(s["rsc"][:, 0:512], s["mur"][:, 0, :], 1.0,
                                   s["mur"][:, 0, :], ALU.mult, ALU.mult)
            e.drain()
            return e.tensor_sub(s["rsc"][:, 0:512], s["rr"][:, 0, :], s["rsc"][:, 0:512])
        p.dve(stats, reads=(f"pt{tc}", f"pt{NTC + tc}"), writes=("rsc", "mur0", "rr0"),
              tag=f"t{bi}.st{tc}")
        p.act(lambda e: e.activation(s["rsc"][:, 0:512], s["rsc"][:, 0:512], ACTF.Sqrt,
                                     bias=s["cst"][:, 0:1]),
              reads=("rsc", "cst"), writes=("rsc",), tag=f"t{bi}.sd{tc}")

        def rcp(e):
            recip_acc(e, s["rr"][:, 1, :], s["rsc"][:, 0:512], s["rec"][:, 0:512])
            e.drain()
            return e.tensor_mul(s["mur"][:, 1, :], s["mur"][:, 0, :], s["rr"][:, 1, :])
        p.dve(rcp, reads=("rsc", "mur0"), writes=("rr1", "mur1", "rec"),
              tag=f"t{bi}.rc{tc}")
        for dc in range(NDC):
            sl = (tc * NDC + dc) % 2
            p.dma("sp_dma", s["fx"][:, sl, 0:512],
                  d["xmid"][bi, dc, :, tc * 512:(tc + 1) * 512],
                  reads=(f"xmid.{bi}.{dc}.{tc}",), writes=(f"fx{sl}",),
                  tag=f"t{bi}.x2{dc}.{tc}")

            def app(e, sl=sl):
                e.tensor_mul(s["ev"][:, 3, :], s["fx"][:, sl, 0:512], s["rr"][:, 1, :])
                e.drain()
                return e.tensor_sub(s["fy"][:, sl, :], s["ev"][:, 3, :],
                                    s["mur"][:, 1, :])
            p.dve(app, reads=(f"fx{sl}", "rr1", "mur1"), writes=(f"fy{sl}", "ev3"),
                  tag=f"t{bi}.ap{dc}.{tc}")
            p.dma("sp_dma", d["xn132"][bi, dc, :, tc * 512:(tc + 1) * 512],
                  s["fy"][:, sl, :], reads=(f"fy{sl}",),
                  writes=(f"xn132.{bi}.{dc}.{tc}",), tag=f"t{bi}.w{dc}.{tc}")
    for dc in range(NDC):
        for li in range(nl):
            sl = (dc * nl + li) % 2
            p.dma("sp_dma", s["fx"][:, sl, 0:LC],
                  d["xn132"][bi, dc, :, li * LC:(li + 1) * LC],
                  reads=tuple(f"xn132.{bi}.{dc}.{li * (LC // 512) + j}"
                              for j in range(LC // 512)),
                  writes=(f"fx{sl}",), tag=f"u{bi}.x{dc}.{li}")

            def sqc(e, sl=sl, dc=dc, li=li):
                e.activation(s["sq"][:, sl, :], s["fx"][:, sl, 0:LC], ACTF.Square)
                e.drain()
                return e.activation(s["act_b"][:, dc, li * LC:(li + 1) * LC],
                                    s["fx"][:, sl, 0:LC], ACTF.Identity)
            p.act(sqc, reads=(f"fx{sl}",), writes=(f"sq{sl}", f"act_b.{dc}"),
                  tag=f"u{bi}.q{dc}.{li}")
            for q5 in range(LC // 512):
                tc = li * (LC // 512) + q5

                def mm1(e, dc=dc, li=li, q5=q5, tc=tc):
                    return e.matmul(k.psum[f"pb{tc}"][:, :], s["ones_b"][:, :],
                                    s["act_b"][:, dc, li * LC + q5 * 512:li * LC + (q5 + 1) * 512],
                                    start=(dc == 0), stop=(dc == NDC - 1))
                p.pe(mm1, reads=(f"act_b.{dc}", "ones_b"), writes=(f"pt{tc}",),
                     tag=f"u{bi}.mS{dc}.{tc}")

                def mm2(e, sl=sl, q5=q5, dc=dc, tc=tc):
                    return e.matmul(k.psum[f"pb{NTC + tc}"][:, :], s["ones_b"][:, :],
                                    s["sq"][:, sl, q5 * 512:(q5 + 1) * 512],
                                    start=(dc == 0), stop=(dc == NDC - 1))
                p.pe(mm2, reads=(f"sq{sl}", "ones_b"), writes=(f"pt{NTC + tc}",),
                     tag=f"u{bi}.mQ{dc}.{tc}")
    for tc in range(NTC):
        def stats(e, tc=tc):
            e.tensor_scalar(s["mur"][:, 0, :], k.psum[f"pb{tc}"][:, :],
                            1.0 / 1024.0, None, ALU.mult)
            e.drain()
            e.tensor_scalar(s["rr"][:, 0, :], k.psum[f"pb{NTC + tc}"][:, :],
                            1.0 / 1024.0, None, ALU.mult)
            e.drain()
            e.scalar_tensor_tensor(s["rsc"][:, 0:512], s["mur"][:, 0, :], 1.0,
                                   s["mur"][:, 0, :], ALU.mult, ALU.mult)
            e.drain()
            return e.tensor_sub(s["rsc"][:, 0:512], s["rr"][:, 0, :], s["rsc"][:, 0:512])
        p.dve(stats, reads=(f"pt{tc}", f"pt{NTC + tc}"), writes=("rsc", "mur0", "rr0"),
              tag=f"u{bi}.st{tc}")
        p.act(lambda e: e.activation(s["rsc"][:, 0:512], s["rsc"][:, 0:512], ACTF.Sqrt,
                                     bias=s["cst"][:, 0:1]),
              reads=("rsc", "cst"), writes=("rsc",), tag=f"u{bi}.sd{tc}")

        def rcp(e):
            recip_acc(e, s["rr"][:, 1, :], s["rsc"][:, 0:512], s["rec"][:, 0:512])
            e.drain()
            return e.tensor_mul(s["mur"][:, 1, :], s["mur"][:, 0, :], s["rr"][:, 1, :])
        p.dve(rcp, reads=("rsc", "mur0"), writes=("rr1", "mur1", "rec"),
              tag=f"u{bi}.rc{tc}")
        for dc in range(NDC):
            sl = (tc * NDC + dc) % 2
            p.dma("sp_dma", s["fx"][:, sl, 0:512],
                  d["xn132"][bi, dc, :, tc * 512:(tc + 1) * 512],
                  reads=(f"xn132.{bi}.{dc}.{tc}",), writes=(f"fx{sl}",),
                  tag=f"u{bi}.x2{dc}.{tc}")

            def app(e, sl=sl):
                e.tensor_mul(s["ev"][:, 3, :], s["fx"][:, sl, 0:512], s["rr"][:, 1, :])
                e.drain()
                return e.tensor_sub(s["ev"][:, 3, :], s["ev"][:, 3, :],
                                    s["mur"][:, 1, :])
            p.dve(app, reads=(f"fx{sl}", "rr1", "mur1"), writes=("ev3",),
                  tag=f"u{bi}.a2{dc}.{tc}")

            def aff(e, dc=dc, tc=tc):
                return e.activation(s["act_a"][:, dc, tc * 512:(tc + 1) * 512],
                                    s["ev"][:, 3, :], ACTF.Identity,
                                    bias=s["bias"][:, K.B_LNB + dc:K.B_LNB + dc + 1],
                                    scale=s["bias"][:, K.B_LNW + dc:K.B_LNW + dc + 1])
            p.act(aff, reads=("ev3", "bias"), writes=(f"act_a.{dc}",),
                  tag=f"u{bi}.af{dc}.{tc}")


def l1w_view(k, name):
    return k.d[name][:, :].rearrange("(a p) (b j) -> p a b j", p=128, j=128)


def emit_l1_qkv(k, bi):
    p, s, d = k.p, k.s, k.d
    T, NTC, NT128 = k.T, k.NTC, k.NT128
    if bi == 0:
        p.fence(reads=(), writes=("nrmX",) + tuple(f"kst.{j}" for j in range(NDC)),
                tag="f.kst")
    p.fence(reads=(), writes=("lgt", "wrt", "axw", "nrmY", "wvb"), tag=f"f.wvb{bi}")
    for fc in range(16):
        wsl = fc % 2
        p.dma("pl_dma", s["wtile"][:, wsl, :, :], l1w_view(k, "l1_qkv_w")[:, :, fc, :],
              writes=(f"wtile{wsl}",), tag=f"q1{bi}.w{fc}")
        for dc in range(NDC):
            for tc in range(NTC):
                def mm(e, wsl=wsl, dc=dc, tc=tc):
                    return e.matmul(k.psum[f"pb{tc}"][:, :], s["wtile"][:, wsl, dc, :],
                                    s["act_a"][:, dc, tc * 512:(tc + 1) * 512],
                                    start=(dc == 0), stop=(dc == NDC - 1))
                p.pe(mm, reads=(f"wtile{wsl}", f"act_a.{dc}"), writes=(f"pt{tc}",),
                     tag=f"q1{bi}.m{fc}.{dc}.{tc}")
        for tc in range(NTC):
            if fc < 8:
                dst = s["act_b"][:, fc, tc * 512:(tc + 1) * 512]
                res = f"act_b.{fc}"
            else:
                dst = s["kst"][:, fc - 8, tc * 512:(tc + 1) * 512]
                res = f"kst.{fc - 8}"

            def evac(e, fc=fc, tc=tc, dst=dst):
                return e.activation(dst, k.psum[f"pb{tc}"][:, :], ACTF.Identity,
                                    bias=s["bias"][:, K.B_L1QKV + fc:K.B_L1QKV + fc + 1])
            p.act(evac, reads=(f"pt{tc}", "bias"), writes=(res,),
                  tag=f"q1{bi}.e{fc}.{tc}")
    for fv in range(2):
        p.dma("pl_dma", s["wvb"][:, :, :],
              d["l1_qkv_w"][:, 2048 + fv * 512:2048 + (fv + 1) * 512]
              .rearrange("(a p) j -> p a j", p=128),
              writes=("wvb",), tag=f"v1{bi}.w{fv}")
        for t128 in range(NT128):
            pq = 4 + t128 % 2
            pb = k.psum[f"pb{pq}"]
            for dc in range(NDC):
                def mm(e, pb=pb, t128=t128, dc=dc):
                    return e.matmul(pb[:, :],
                                    s["act_a"][:, dc, t128 * 128:(t128 + 1) * 128],
                                    s["wvb"][:, dc, :],
                                    start=(dc == 0), stop=False)
                p.pe(mm, reads=("wvb", f"act_a.{dc}"), writes=(f"pt{pq}",),
                     tag=f"v1{bi}.m{fv}.{t128}.{dc}")

            def mmb(e, pb=pb, fv=fv):
                return e.matmul(pb[:, :], s["ones_b"][0:1, 0:128],
                                s["brow"][0:1, 1, fv * 512:(fv + 1) * 512],
                                start=False, stop=True)
            p.pe(mmb, reads=("brow1", "ones_b"), writes=(f"pt{pq}",),
                 tag=f"v1{bi}.b{fv}.{t128}")

            def evac(e, pb=pb, t128=t128, fv=fv):
                return e.tensor_copy(s["vaug"][:, t128, fv * 8:(fv + 1) * 8, 0:64],
                                     pb[:, :])
            p.dve(evac, reads=(f"pt{pq}", "vaug.ones"), writes=(f"vg.{t128 % 4}",),
                  tag=f"v1{bi}.e{fv}.{t128}")


def emit_l1_ffn(k, bi):
    p, s, d = k.p, k.s, k.d
    T, NTC, NT128 = k.T, k.NTC, k.NT128
    p.fence(reads=(), writes=("wvb", "lgt", "wrt", "axw"), tag=f"f.lgt{bi}")
    for dc in range(NDC):
        p.dma("pl_dma", s["act_a"][:, dc, :], d["xn132"][bi, dc, :, :],
              reads=tuple(f"xn132.{bi}.{dc}.{t}" for t in range(NTC)),
              writes=(f"act_a.{dc}",), tag=f"ff{bi}.xn{dc}")
    for tc in range(NTC):
        for dc in range(NDC):
            def mm(e, dc=dc, tc=tc):
                return e.matmul(k.psum["pb6"][0:8, :], s["lR"][:, dc, :],
                                s["act_a"][:, dc, tc * 512:(tc + 1) * 512],
                                start=(dc == 0), stop=(dc == NDC - 1))
            p.pe(mm, reads=("lR", f"act_a.{dc}"), writes=("pt6",),
                 tag=f"rt{bi}.m{tc}.{dc}")

        def evac(e, tc=tc):
            return e.activation(s["lgt"][0:8, tc * 512:(tc + 1) * 512],
                                k.psum["pb6"][0:8, :], ACTF.Identity,
                                bias=s["small"][0:8, 30:31])
        p.act(evac, reads=("pt6", "rb"), writes=("lgt",), tag=f"rt{bi}.e{tc}")
    if "dbg_lgt" in K.DEBUG_OUT:
        p.dma("sp_dma", d["dbg_lgt"][bi, :, :], s["lgt"][0:8, :],
              reads=("lgt",), writes=("dbgl",), tag=f"dbg.lgt{bi}")
    for t128 in range(NT128):
        def tp(e, t128=t128):
            return e.transpose(k.psum["pb7"][0:128, 0:8],
                               s["lgt"][0:8, t128 * 128:(t128 + 1) * 128],
                               s["idn_f"][0:8, 0:8])
        p.pe(tp, reads=("lgt", "idn_f"), writes=("pt7",), tag=f"rt{bi}.t{t128}")

        def chain(e):
            t8 = s["tmp8"]
            e.tensor_copy(t8[:, 50:58], k.psum["pb7"][0:128, 0:8])
            e.drain()
            e.tensor_reduce(t8[:, 58:59], t8[:, 50:58], AX.X, ALU.max)
            e.drain()
            e.tensor_scalar(t8[:, 59:60], t8[:, 58:59], -1.0, None, ALU.mult)
            e.drain()
            e.tensor_scalar(t8[:, 64:72], t8[:, 50:58], t8[:, 58:59], None,
                            ALU.is_equal)
            e.drain()
            e.scalar_tensor_tensor(t8[:, 64:72], t8[:, 64:72], -1e30, t8[:, 50:58],
                                   ALU.mult, ALU.add)
            e.drain()
            e.tensor_reduce(t8[:, 60:61], t8[:, 64:72], AX.X, ALU.max)
            e.drain()
            return e.tensor_scalar(t8[:, 64:72], t8[:, 50:58], t8[:, 60:61], None,
                                   ALU.is_ge)
        p.dve(chain, reads=("pt7",), writes=("t8r",), tag=f"rt{bi}.c{t128}")

        def ex(e):
            return e.activation(s["tmp8"][:, 72:80], s["tmp8"][:, 50:58], ACTF.Exp,
                                bias=s["tmp8"][:, 59:60])
        p.act(ex, reads=("t8r",), writes=("t8x",), tag=f"rt{bi}.x{t128}")

        def fin(e):
            t8 = s["tmp8"]
            e.tensor_mul(t8[:, 72:80], t8[:, 72:80], t8[:, 64:72])
            e.drain()
            e.tensor_reduce(t8[:, 61:62], t8[:, 72:80], AX.X, ALU.add)
            e.drain()
            recip_acc(e, t8[:, 62:63], t8[:, 61:62], t8[:, 63:64])
            e.drain()
            return e.tensor_scalar(t8[:, 72:80], t8[:, 72:80], t8[:, 62:63], None,
                                   ALU.mult)
        p.dve(fin, reads=("t8x", "t8r"), writes=("t8x",), tag=f"rt{bi}.f{t128}")

        def tpb(e):
            return e.transpose(k.psum["pb7"][0:8, 0:128], s["tmp8"][:, 72:80],
                               s["idn_f"][:, :])
        p.pe(tpb, reads=("t8x", "idn_f"), writes=("pt7",), tag=f"rt{bi}.b{t128}")

        def wev(e, t128=t128):
            return e.tensor_copy(s["wrt"][0:8, t128 * 128:(t128 + 1) * 128],
                                 k.psum["pb7"][0:8, 0:128])
        p.dve(wev, reads=("pt7",), writes=("wrt",), tag=f"rt{bi}.w{t128}")
    if "dbg_wrt" in K.DEBUG_OUT:
        def wrtc(e):
            return e.tensor_copy(s["rr"][:, 0, 0:8].rearrange("p a -> p a") if False else s["mur"][0:8, 0, 0:512], s["wrt"][0:8, 0:512])
        p.dma("pl_dma", d["dbg_wrt"][bi, :, :], s["wrt"][0:8, :],
              reads=("wrt",), writes=("dbgw",), tag=f"dbg.wrt{bi}")
    p.fence(reads=(), writes=("lgt", "axw"), tag=f"f.axw{bi}")
    for tc in range(NTC):
        pb2 = k.psum[f"pb{tc % 2}"]
        for dc in range(NDC):
            def mm(e, pb2=pb2, dc=dc, tc=tc):
                return e.matmul(pb2[:, :], s["lA"][:, dc, :],
                                s["act_a"][:, dc, tc * 512:(tc + 1) * 512],
                                start=(dc == 0), stop=(dc == NDC - 1))
            p.pe(mm, reads=("lA", f"act_a.{dc}"), writes=(f"pt{tc % 2}",),
                 tag=f"ax{bi}.m{tc}.{dc}")
        pb3 = k.psum[f"pb{2 + tc % 2}"]

        def mmw(e, pb3=pb3, tc=tc):
            return e.matmul(pb3[:, :], s["eind"][:, :],
                            s["wrt"][0:8, tc * 512:(tc + 1) * 512],
                            start=True, stop=True)
        p.pe(mmw, reads=("eind", "wrt"), writes=(f"pt{2 + tc % 2}",),
             tag=f"ax{bi}.w{tc}")

        def axw(e, pb2=pb2, pb3=pb3, tc=tc):
            e.tensor_copy(s["rsc"][:, 0:512], pb3[:, :])
            e.drain()
            return e.tensor_mul(s["axw"][:, tc * 512:(tc + 1) * 512],
                                pb2[:, :], s["rsc"][:, 0:512])
        p.dve(axw, reads=(f"pt{tc % 2}", f"pt{2 + tc % 2}"), writes=("axw", "rsc"),
              tag=f"ax{bi}.s{tc}")
    if "dbg_axw" in K.DEBUG_OUT:
        p.dma("pl_dma", d["dbg_axw"][bi, :, :], s["axw"][:, :],
              reads=("axw",), writes=("dbga",), tag=f"dbg.axw{bi}")
    for fc in range(NDC):
        wsl = fc % 2
        p.dma("pl_dma", s["wtile"][:, wsl, :, :], l1w_view(k, "l1_base_wT")[:, :, fc, :],
              writes=(f"wtile{wsl}",), tag=f"bs{bi}.w{fc}")
        if fc == 0 and bi == 0 and "dbg_wt" in K.DEBUG_OUT:
            p.dma("pl_dma", d["dbg_wt"][:, :, :], s["wtile"][:, wsl, :, :],
                  reads=(f"wtile{wsl}",), writes=("dbgwt",), tag="dbg.wt")
        for dc in range(NDC):
            for tc in range(NTC):
                pq = 4 + tc

                def mm(e, wsl=wsl, dc=dc, tc=tc, pq=pq):
                    return e.matmul(k.psum[f"pb{pq}"][:, :], s["wtile"][:, wsl, dc, :],
                                    s["act_a"][:, dc, tc * 512:(tc + 1) * 512],
                                    start=(dc == 0), stop=False)
                p.pe(mm, reads=(f"wtile{wsl}", f"act_a.{dc}"), writes=(f"pt{pq}",),
                     tag=f"bs{bi}.m{fc}.{dc}.{tc}")
        for tc in range(NTC):
            pq = 4 + tc

            def mml(e, fc=fc, tc=tc, pq=pq):
                return e.matmul(k.psum[f"pb{pq}"][:, :],
                                s["lB"][:, fc * 128:(fc + 1) * 128],
                                s["axw"][:, tc * 512:(tc + 1) * 512],
                                start=False, stop=True)
            p.pe(mml, reads=("lB", "axw"), writes=(f"pt{pq}",),
                 tag=f"bs{bi}.l{fc}.{tc}")

            def silu(e, fc=fc, tc=tc, pq=pq):
                return e.activation(s["ev"][:, tc % 2, :], k.psum[f"pb{pq}"][:, :],
                                    ACTF.Silu,
                                    bias=s["bias"][:, K.B_BASE + fc:K.B_BASE + fc + 1])
            p.act(silu, reads=(f"pt{pq}", "bias"), writes=(f"ev{tc % 2}",),
                  tag=f"bs{bi}.s{fc}.{tc}")
            p.dma("sp_dma", d["g1"][bi, fc, :, tc * 512:(tc + 1) * 512],
                  s["ev"][:, tc % 2, :], reads=(f"ev{tc % 2}",),
                  writes=(f"g1.{bi}.{fc}.{tc}",), tag=f"bs{bi}.g{fc}.{tc}")


def emit_l1_attention(k, bi):
    p, s = k.p, k.s
    p.fence(reads=(), writes=("lgt", "wrt", "axw", "wvb", "nrmY"), tag=f"f.n1{bi}")
    attention_core(k, bi, layer=1, Ks=s["kst"], Kres="kst",
                   Qs=s["act_b"], Qres="act_b", nrm=s["nrm1"], nres="nrmY")


def emit_l1_proj_out(k, bi):
    p, s, d = k.p, k.s, k.d
    T, NTC = k.T, k.NTC
    for fc in range(NDC):
        wsl = fc % 2
        p.dma("pl_dma", s["wtile"][:, wsl, :, :], l1w_view(k, "l1_proj_w")[:, :, fc, :],
              writes=(f"wtile{wsl}",), tag=f"p1{bi}.w{fc}")
        for dc in range(NDC):
            for tc in range(NTC):
                def mm(e, wsl=wsl, dc=dc, tc=tc):
                    return e.matmul(k.psum[f"pb{tc}"][:, :], s["wtile"][:, wsl, dc, :],
                                    s["act_a"][:, dc, tc * 512:(tc + 1) * 512],
                                    start=(dc == 0), stop=(dc == NDC - 1))
                p.pe(mm, reads=(f"wtile{wsl}", f"act_a.{dc}"), writes=(f"pt{tc}",),
                     tag=f"p1{bi}.m{fc}.{dc}.{tc}")
        for tc in range(NTC):
            def evac(e, fc=fc, tc=tc):
                return e.activation(s["ev"][:, 2 + tc % 2, :], k.psum[f"pb{tc}"][:, :],
                                    ACTF.Identity,
                                    bias=s["bias"][:, K.B_L1PROJ + fc:K.B_L1PROJ + fc + 1])
            p.act(evac, reads=(f"pt{tc}", "bias"), writes=(f"ev{2 + tc % 2}",),
                  tag=f"p1{bi}.e{fc}.{tc}")
            sl = tc % 2
            p.dma("sp_dma", s["fx"][:, sl, 0:512],
                  d["xn132"][bi, fc, :, tc * 512:(tc + 1) * 512],
                  reads=(f"xn132.{bi}.{fc}.{tc}",), writes=(f"fx{sl}",),
                  tag=f"p1{bi}.n{fc}.{tc}")
            p.dma("sp_dma", s["fy"][:, sl, :],
                  d["g1"][bi, fc, :, tc * 512:(tc + 1) * 512],
                  reads=(f"g1.{bi}.{fc}.{tc}",), writes=(f"fy{sl}",),
                  tag=f"p1{bi}.g{fc}.{tc}")
            p.dma("sp_dma", s["mur"][:, sl, :],
                  d["xmid"][bi, fc, :, tc * 512:(tc + 1) * 512],
                  reads=(f"xmid.{bi}.{fc}.{tc}",), writes=(f"mur{sl}",),
                  tag=f"p1{bi}.xm{fc}.{tc}")

            def upd(e, sl=sl, tc=tc):
                e.tensor_add(s["ev"][:, 2 + sl, :], s["ev"][:, 2 + sl, :],
                             s["fx"][:, sl, 0:512])
                e.drain()
                e.tensor_mul(s["ev"][:, 2 + sl, :], s["ev"][:, 2 + sl, :],
                             s["fy"][:, sl, :])
                e.drain()
                return e.tensor_add(s["ev"][:, 2 + sl, :], s["ev"][:, 2 + sl, :],
                                    s["mur"][:, sl, :])
            p.dve(upd, reads=(f"ev{2 + sl}", f"fx{sl}", f"fy{sl}", f"mur{sl}"),
                  writes=(f"ev{2 + sl}",), tag=f"p1{bi}.u{fc}.{tc}")
            p.dma("sp_dma", d["out_fm"][bi, fc, :, tc * 512:(tc + 1) * 512],
                  s["ev"][:, 2 + sl, :], reads=(f"ev{2 + sl}",),
                  writes=(f"OUT.{bi}.{fc}.{tc}",), tag=f"p1{bi}.o{fc}.{tc}")


def emit_final(k):
    outs = tuple(f"OUT.{bi}.{fc}.{tc}" for bi in range(NB) for fc in range(NDC)
                 for tc in range(k.NTC))
    k.p.fence(reads=outs, writes=(), tag="final")


def build(T=2048, n_cores=8, sim_strict=False, debug_out=()):
    K.DEBUG_OUT = tuple(debug_out)
    k = K(T=T, n_cores=n_cores, sim_strict=sim_strict)
    emit_setup(k)
    emit_wq_preproc(k, "l0_qkv_w", 24)
    emit_wq_preproc(k, "l0_proj_w", 8)
    emit_wcast(k)
    for bi in range(NB):
        emit_l0_stats(k, bi)
    emit_ar1(k)
    for bi in range(NB):
        emit_ffn(k, bi)
        emit_xq_qkv(k, bi)
    emit_ar2(k)
    for bi in range(NB):
        emit_l0_attention(k, bi)
    emit_ar3(k)
    for bi in range(NB):
        emit_l0_proj_update(k, bi)
    for bi in range(NB):
        emit_l1_stats(k, bi)
        emit_l1_qkv(k, bi)
        emit_l1_ffn(k, bi)
        emit_l1_attention(k, bi)
        emit_l1_proj_out(k, bi)
    emit_final(k)
    k.p.emit()
    return k




def make_inmaps(inputs, n_cores=8, T=2048):
    x = np.asarray(inputs["L0_attn_x"], np.float32)
    Bt, N, C = x.shape
    f32 = lambda a: np.ascontiguousarray(np.asarray(a, np.float32))

    def col(a, w):
        return np.ascontiguousarray(np.asarray(a, np.float32).reshape(w, 128).T)

    qkvb = np.asarray(inputs["L0_attn_qkv_b"], np.float32)
    l1qkvb = np.asarray(inputs["L1_attn_b_qkv"], np.float32)
    hi = np.zeros((128, NDC, 16), np.float32)
    for dc in range(NDC):
        for pp in range(128):
            hi[pp, dc, (dc * 128 + pp) // 64] = 1.0
    ei = np.zeros((8, 128), np.float32)
    for e in range(8):
        ei[e, e * 16:(e + 1) * 16] = 2.0

    shared = {
        "l0_qkv_w": f32(inputs["L0_attn_qkv_w"]),
        "l0_qkv_b": col(qkvb, 24),
        "l0_qkv_bv_row": f32(qkvb[2048:3072]).reshape(1, D),
        "l0_proj_w": f32(inputs["L0_attn_proj_w"]),
        "l0_proj_b": col(inputs["L0_attn_proj_b"], 8),
        "l1_ln_w": col(inputs["L1_attn_ln_w"], 8),
        "l1_ln_b": col(inputs["L1_attn_ln_b"], 8),
        "l1_qkv_w": f32(inputs["L1_attn_W_qkv"]),
        "l1_qkv_b": col(l1qkvb, 24),
        "l1_qkv_bv_row": f32(l1qkvb[2048:3072]).reshape(1, D),
        "l1_proj_w": f32(inputs["L1_attn_W_proj"]),
        "l1_proj_b": col(inputs["L1_attn_b_proj"], 8),
        "l1_base_wT": f32(np.asarray(inputs["L1_ffn_base_w"], np.float32).T),
        "l1_base_b": col(inputs["L1_ffn_base_b"], 8),
        "l1_lora_A": f32(np.concatenate(
            [np.asarray(inputs["L1_ffn_lora_A"], np.float32)[e] for e in range(8)],
            axis=1)),
        "l1_lora_B": f32(np.asarray(inputs["L1_ffn_lora_B"], np.float32)
                         .reshape(128, D)),
        "l1_router_wT": f32(np.asarray(inputs["L1_ffn_router_w"], np.float32).T),
        "l1_router_b": f32(np.asarray(inputs["L1_ffn_router_b"], np.float32)
                           .reshape(8, 1)),
        "c_idn": np.eye(128, dtype=np.float32),
        "c_headind": hi,
        "c_expind2": ei,
    }
    w1 = np.asarray(inputs["L0_ffn_w1"], np.float32)
    b1 = np.asarray(inputs["L0_ffn_b1"], np.float32)
    w2 = np.asarray(inputs["L0_ffn_w2"], np.float32)
    b2 = np.asarray(inputs["L0_ffn_b2"], np.float32)
    in_maps = []
    for c in range(n_cores):
        m = dict(shared)
        xs = x[2 * c:2 * c + 2]
        m["x_fm"] = np.ascontiguousarray(
            xs.reshape(2, N, NDC, 128).transpose(0, 2, 3, 1))
        m["l0_w1"] = f32(w1[c])
        m["l0_b1"] = col(b1[c], NFC)
        m["l0_w2"] = f32(w2[c])
        m["l0_b2"] = col(b2[c], NDC)
        in_maps.append(m)
    return in_maps


def kernel(**inputs):
    n_cores, T = 8, 2048
    k = build(T=T, n_cores=n_cores, sim_strict=True)
    in_maps = make_inmaps(inputs, n_cores=n_cores, T=T)
    res = run_bass_kernel_spmd(k.nc, in_maps, core_ids=list(range(n_cores)))
    outs = []
    for c in range(n_cores):
        o = np.asarray(res.results[c]["out_fm"], np.float32)
        outs.append(o.transpose(0, 3, 1, 2).reshape(2, T, D))
    return np.concatenate(outs, axis=0).astype(np.float32)


# version 12
# speedup vs baseline: 1.0331x; 1.0012x over previous
import numpy as np
import ml_dtypes
import concourse.bass as bass
import concourse.mybir as mybir
import concourse.dve_ops as dve_ops
from concourse.bass_utils import run_bass_kernel_spmd


def recip_acc(e, out, in_, scratch):
    return e.reciprocal(out, in_)

F32 = mybir.dt.float32
BF16 = mybir.dt.bfloat16
I32 = mybir.dt.int32
AX = mybir.AxisListType
ALU = mybir.AluOpType
ACTF = mybir.ActivationFunctionType

D = 1024
NDC = 8
H = 16
DH = 64
FF = 4096
NFC = 32
NB = 2

PHYS = {"pe": "pe", "act": "act", "dve": "dve", "sp": "sp", "pl": "pl",
        "sp_dma": "sp", "pl_dma": "pl", "pl_cc": "pl"}
N_DMA_SLOTS = 6


class Op:
    __slots__ = ("i", "eng", "fn", "reads", "writes", "is_dma", "deps",
                 "sem_key", "sem_val", "marked", "vc", "tag")

    def __init__(self, i, eng, fn, reads, writes, is_dma, tag):
        self.i, self.eng, self.fn = i, eng, fn
        self.reads, self.writes, self.is_dma = tuple(reads), tuple(writes), is_dma
        self.deps = set()
        self.sem_key = None
        self.sem_val = None
        self.marked = False
        self.vc = None
        self.tag = tag


class Prog:
    NO_COALESCE = False

    def __init__(self, nc, sim_strict=False):
        self.nc = nc
        self.ops = []
        self.res_w = {}
        self.res_r = {}
        self.sim_strict = sim_strict

    def op(self, eng, fn, reads=(), writes=(), tag=""):
        o = Op(len(self.ops), eng, fn, reads, writes,
               eng in ("sp_dma", "pl_dma"), tag)
        for r in o.reads:
            if r in self.res_w:
                o.deps.add(self.res_w[r])
        for w in o.writes:
            if w in self.res_w:
                o.deps.add(self.res_w[w])
            for rd in self.res_r.get(w, ()):
                o.deps.add(rd)
        for r in o.reads:
            self.res_r.setdefault(r, []).append(o.i)
        for w in o.writes:
            self.res_w[w] = o.i
            self.res_r[w] = []
        o.deps.discard(o.i)
        if len(o.deps) > 1:
            keep = {}
            extra = []
            for di in o.deps:
                pp = self.ops[di]
                if pp.is_dma or pp.eng == "pl_cc":
                    extra.append(di)
                else:
                    e = PHYS[pp.eng]
                    if keep.get(e, -1) < di:
                        keep[e] = di
            o.deps = set(extra) | set(keep.values())
        self.ops.append(o)
        return o

    def pe(self, fn, reads=(), writes=(), tag=""):
        return self.op("pe", fn, reads, writes, tag)

    def act(self, fn, reads=(), writes=(), tag=""):
        return self.op("act", fn, reads, writes, tag)

    def dve(self, fn, reads=(), writes=(), tag=""):
        return self.op("dve", fn, reads, writes, tag)

    def sp(self, fn, reads=(), writes=(), tag=""):
        return self.op("sp", fn, reads, writes, tag)

    def pl(self, fn, reads=(), writes=(), tag=""):
        return self.op("pl", fn, reads, writes, tag)

    def dma(self, queue, out_ap, in_ap, reads=(), writes=(), tag=""):
        return self.op(queue, lambda e, o=out_ap, i=in_ap: e.dma_start(o, i),
                       reads, writes, tag)

    def collective(self, alu, rg, in_ap, out_ap, reads=(), writes=(), tag=""):
        def fn(e):
            return e.collective_compute("AllReduce", alu, replica_groups=rg,
                                        ins=[in_ap], outs=[out_ap])
        return self.op("pl_cc", fn, reads, writes, tag)

    def fence(self, reads, writes, tag="fence"):
        return self.op("sp", lambda e: None, reads, writes, tag)

    def barrier(self, tag="barrier"):
        for eng in ("pe", "act", "dve", "sp", "pl"):
            self.op(eng, lambda e: None, ("BARRIER",), ("BARRIER",), tag + "." + eng)

    def emit(self):
        nc = self.nc

        def relaxed(pp, o):
            if pp.is_dma or pp.eng == "pl_cc" or PHYS[pp.eng] != PHYS[o.eng]:
                return False
            return (not self.sim_strict) or PHYS[pp.eng] == "pe"

        for o in self.ops:
            for di in o.deps:
                pp = self.ops[di]
                if not relaxed(pp, o):
                    pp.marked = True
        sems = {e: nc.alloc_semaphore(name=f"sem_{e}")
                for e in ("pe", "act", "dve", "sp", "pl")}
        dma_sems = {q: [nc.alloc_semaphore(name=f"{q}{i}")
                        for i in range(N_DMA_SLOTS)]
                    for q in ("sp_dma", "pl_dma")}
        cc_sem = nc.alloc_semaphore(name="sem_cc")

        counts = {e: 0 for e in ("pe", "act", "dve", "sp", "pl")}
        dma_counts = {q: [0] * N_DMA_SLOTS for q in ("sp_dma", "pl_dma")}
        dma_rr = {q: 0 for q in ("sp_dma", "pl_dma")}
        ccn = 0
        for o in self.ops:
            if o.is_dma:
                sl = dma_rr[o.eng] % N_DMA_SLOTS
                dma_rr[o.eng] += 1
                dma_counts[o.eng][sl] += 1
                o.sem_key = (o.eng, sl)
                o.sem_val = 16 * dma_counts[o.eng][sl]
                o.marked = True
            elif o.eng == "pl_cc":
                ccn += 1
                o.sem_key = ("cc",)
                o.sem_val = ccn
                o.marked = True
            elif o.marked:
                counts[o.eng] += 1
                o.sem_key = (o.eng,)
                o.sem_val = counts[o.eng]
        for e, n in counts.items():
            assert n < 60000, (e, n)
        for q in dma_counts:
            for n in dma_counts[q]:
                assert 16 * n < 60000, (q, n)

        def sem_of(key):
            if key[0] in ("sp_dma", "pl_dma"):
                return dma_sems[key[0]][key[1]]
            if key[0] == "cc":
                return cc_sem
            return sems[key[0]]

        per_eng = {e: [] for e in ("pe", "act", "dve", "sp", "pl")}
        for o in self.ops:
            per_eng[PHYS[o.eng]].append(o)
        vclocks = {e: {} for e in per_eng}

        def body(eng, name):
            vc = vclocks[name]
            for o in per_eng[name]:
                for di in sorted(o.deps):
                    pp = self.ops[di]
                    if relaxed(pp, o):
                        if pp.vc:
                            for kk, vv in pp.vc.items():
                                if vc.get(kk, 0) < vv:
                                    vc[kk] = vv
                        continue
                    if pp.sem_key is None:
                        continue
                    if vc.get(pp.sem_key, 0) < pp.sem_val or Prog.NO_COALESCE:
                        eng.wait_ge(sem_of(pp.sem_key), pp.sem_val)
                        vc[pp.sem_key] = max(vc.get(pp.sem_key, 0), pp.sem_val)
                    if (not Prog.NO_COALESCE) and pp.vc:
                        for kk, vv in pp.vc.items():
                            if vc.get(kk, 0) < vv:
                                vc[kk] = vv
                if o.is_dma:
                    prev = o.sem_val - 16
                    if prev > 0 and vc.get(o.sem_key, 0) < prev:
                        eng.wait_ge(sem_of(o.sem_key), prev)
                        vc[o.sem_key] = prev
                ins = o.fn(eng)
                if o.sem_key is not None:
                    if ins is None:
                        ins = eng.nop()
                    ins.then_inc(sem_of(o.sem_key), 16 if o.is_dma else 1)
                snap = dict(vc)
                if o.sem_key is not None:
                    snap[o.sem_key] = o.sem_val
                o.vc = snap

        with nc.Block("main") as blk:
            @blk.tensor
            def _(t):
                body(t, "pe")

            @blk.scalar
            def _(sc):
                body(sc, "act")

            @blk.vector
            def _(v):
                body(v, "dve")

            @blk.sync
            def _(sy):
                body(sy, "sp")

            @blk.gpsimd
            def _(g):
                body(g, "pl")




class K:
    B_L0QKV, B_L0PROJ, B_B1, B_B2 = 0, 24, 32, 64
    B_LNW, B_LNB, B_L1QKV, B_L1PROJ, B_BASE = 72, 80, 88, 112, 120
    S_MXC, S_MNC = 0, 8
    S_CINV, S_CZP, S_CS, S_CNZS = 16, 24, 32, 40
    S_WMN, S_WMX, S_WINV, S_WZP, S_WS, S_WNZS = 48, 49, 50, 51, 52, 53
    S_RMX, S_RMN = 54, 57
    S_OMX, S_OMN = 60, 61
    S_T0 = 62
    QP_INV, QP_ZP, QP_S, QP_NZS = 0, 3, 6, 9
    QP_OINV, QP_OZP, QP_OS, QP_ONZS = 12, 13, 14, 15

    DEBUG_OUT = ()

    def __init__(self, T=2048, n_cores=8, sim_strict=False):
        assert T % 512 == 0 and (2 * (T // 512)) <= 8
        self.T = T
        self.NTC = T // 512
        self.NT128 = T // 128
        self.LC = min(1024, T)
        self.n_cores = n_cores
        self.nc = bass.Bass("TRN2", target_bir_lowering=False,
                            detect_race_conditions=sim_strict)
        self.p = Prog(self.nc, sim_strict=sim_strict)
        self.d = {}
        self.s = {}
        self.psum = {}
        self._declare()

    def dram(self, name, shape, dt, kind="Internal"):
        if kind == "Internal" and name in K.DEBUG_OUT:
            kind = "ExternalOutput"
        self.d[name] = self.nc.dram_tensor(name, list(shape), dt, kind=kind)
        return self.d[name]

    def sbuf(self, name, shape, dt):
        self.s[name] = self.nc.alloc_sbuf_tensor(name, list(shape), dt)
        return self.s[name]

    def sbuf_at(self, name, shape, dt, off):
        self.s[name] = self.nc.alloc_sbuf_tensor_at(name, list(shape), dt,
                                                    offset=off)
        return self.s[name]

    def _declare(self):
        T, NTC, NT128 = self.T, self.NTC, self.NT128
        dr = self.dram
        dr("x_fm", [NB, NDC, 128, T], F32, "ExternalInput")
        dr("l0_qkv_w", [3 * D, D], F32, "ExternalInput")
        dr("l0_qkv_b", [128, 24], F32, "ExternalInput")
        dr("l0_qkv_bv_row", [1, D], F32, "ExternalInput")
        dr("l0_proj_w", [D, D], F32, "ExternalInput")
        dr("l0_proj_b", [128, 8], F32, "ExternalInput")
        dr("l0_w1", [D, FF], F32, "ExternalInput")
        dr("l0_b1", [128, NFC], F32, "ExternalInput")
        dr("l0_w2", [FF, D], F32, "ExternalInput")
        dr("l0_b2", [128, NDC], F32, "ExternalInput")
        dr("l1_ln_w", [128, NDC], F32, "ExternalInput")
        dr("l1_ln_b", [128, NDC], F32, "ExternalInput")
        dr("l1_qkv_w", [D, 3 * D], F32, "ExternalInput")
        dr("l1_qkv_b", [128, 24], F32, "ExternalInput")
        dr("l1_qkv_bv_row", [1, D], F32, "ExternalInput")
        dr("l1_proj_w", [D, D], F32, "ExternalInput")
        dr("l1_proj_b", [128, 8], F32, "ExternalInput")
        dr("l1_base_wT", [D, D], F32, "ExternalInput")
        dr("l1_base_b", [128, 8], F32, "ExternalInput")
        dr("l1_lora_A", [D, 128], F32, "ExternalInput")
        dr("l1_lora_B", [128, D], F32, "ExternalInput")
        dr("l1_router_wT", [D, 8], F32, "ExternalInput")
        dr("l1_router_b", [8, 1], F32, "ExternalInput")
        dr("c_idn", [128, 128], F32, "ExternalInput")
        dr("c_headind", [128, NDC, 16], F32, "ExternalInput")
        dr("c_expind2", [8, 128], F32, "ExternalInput")
        dr("out_fm", [NB, NDC, 128, T], F32, "ExternalOutput")
        dr("xn32", [NB, NDC, 128, T], F32)
        dr("g0", [NB, NDC, 128, T], F32)
        dr("xmid", [NB, NDC, 128, T], F32)
        dr("xn132", [NB, NDC, 128, T], F32)
        dr("g1", [NB, NDC, 128, T], F32)
        dr("qkraw", [NB, 16, NTC, 128, 512], F32)
        dr("vraw", [NB, NT128, 2, 128, 512], F32)
        dr("oraw", [NB, H, NTC, 64, 512], F32)
        dr("wqT_qk", [16, NDC, 128, 128], BF16)
        dr("w1b", [128, NDC, NFC * 128], BF16)
        dr("w2b", [128, NFC, NDC * 128], BF16)
        dr("wqT_v", [NDC, 128, D], BF16)
        dr("pwT", [NDC, NDC, 128, 128], BF16)
        dr("ar1_in", [128, 16], F32)
        dr("ar1_out", [128, 16], F32)
        dr("ar2_in", [8, 1], F32)
        dr("ar2_out", [8, 1], F32)
        dr("ar3_in", [2, 1], F32)
        dr("dbg_lgt", [NB, 8, T], F32)
        dr("dbg_wrt", [NB, 8, T], F32)
        dr("dbg_axw", [NB, 128, T], F32)
        dr("dbg_wt", [128, NDC, 128], F32)
        dr("dbg_den", [8, 4, 2, 512], F32)
        dr("dbg_cb", [128, 16], F32)
        dr("dbg_nrm", [16, 2, 2048], F32)
        dr("dbg_sm", [16, 4], F32)
        dr("dbg_hind", [128, NDC, 16], F32)
        dr("dbg_s", [2, 128, 512], F32)
        dr("dbg_ka", [128, NDC, T], F32)
        dr("dbg_qa", [128, NDC, T], F32)
        dr("ar3_out", [2, 1], F32)

        sb = self.sbuf
        sb("act_a", [128, NDC, T], BF16)
        sb("act_b", [128, NDC, T], BF16)
        sb("fx", [128, 2, max(self.LC, 1024)], F32)
        sb("fy", [128, 2, 512], F32)
        sb("sq", [128, 2, self.LC], BF16)
        sb("ev", [128, 4, 512], F32)
        sb("qi", [128, 2, 512], I32)
        sb("ptile", [128, 2, 1024], BF16)
        sb("wtile", [128, 2, NDC, 128], BF16)
        sb("wst", [128, 8, 128], BF16)
        sb("mur", [128, 2, 512], F32)
        sb("rr", [128, 2, 512], F32)
        sb("den", [128, 512], F32)
        sb("rec", [128, 512], F32)
        sb("rsc", [128, 512], F32)
        sb("lA", [128, NDC, 128], BF16)
        sb("lB", [128, D], BF16)
        sb("lR", [128, NDC, 8], BF16)
        sb("bias", [128, 128], F32)
        sb("brow", [1, 2, D], BF16)
        sb("ones_b", [128, 128], BF16)
        sb("ones_f", [1, 128], F32)
        sb("onesf64", [66, 64], F32)
        sb("idn_f", [128, 128], F32)
        sb("idn_bt", [128, 128], BF16)
        sb("hind", [128, NDC, 16], BF16)
        sb("eind", [8, 128], BF16)
        sb("tmp8", [128, 96], F32)
        sb("small", [128, 48], F32)
        sb("qp", [128, 16], F32)
        sb("cbias", [128, 16], F32)
        sb("stat", [128, 64], F32)
        sb("cst", [128, 4], F32)

        base = (self.nc.sbuf_base + 31) & ~31
        top = self.nc.sbuf_top
        avail = top - base
        need = max(NFC * self.LC * 2 + 2 * NDC * 128 * 2 + NFC * 128 * 2,
                   NT128 * H * 66 * 2 + NDC * T * 2
                   + max(T * 4, NDC * 512 * 2, 2 * T * 2) + T * 2)
        assert need <= avail, (need, avail, base, top)
        A = base
        self.sbuf_at("hbuf", [128, NFC, self.LC], BF16, A)
        o1 = A + NFC * self.LC * 2
        self.sbuf_at("w1t", [128, 2, NDC, 128], BF16, o1)
        o2 = o1 + 2 * NDC * 128 * 2
        self.sbuf_at("w2t", [128, NFC, 128], BF16, o2)
        self.sbuf_at("wvb0", [128, NDC, D], BF16, o1)
        self.sbuf_at("vaug", [128, NT128, H, 66], BF16, A)
        ov = A + NT128 * H * 66 * 2
        self.sbuf_at("nrm0", [16, 2 * T], BF16, ov)
        self.sbuf_at("kst", [128, NDC, T], BF16, ov)
        oc = ov + NDC * T * 2
        self.sbuf_at("lgt", [8, T], F32, oc)
        self.sbuf_at("wvb", [128, NDC, 512], BF16, oc)
        self.sbuf_at("nrm1", [16, 2 * T], BF16, oc)
        self.sbuf_at("axw", [128, T], BF16, oc)
        oc2 = oc + max(T * 4, NDC * 512 * 2, 2 * T * 2)
        self.sbuf_at("wrt", [8, T], BF16, oc2)
        assert oc2 + T * 2 <= top, (oc2 + T * 2, top)

        for i in range(8):
            self.psum[f"pb{i}"] = self.nc.alloc_psum_tensor(f"pb{i}", [128, 512], F32)
        pbase = self.nc.psum_base
        self.nc.psum_base = 0
        for i in range(2):
            self.psum[f"pbt{i}"] = self.nc.alloc_psum_tensor(f"pbt{i}", [128, 1024], BF16)
        for b in (2, 4, 6):
            self.nc.psum_base = b
            self.psum[f"pb{b}{b + 1}"] = self.nc.alloc_psum_tensor(
                f"pb{b}{b + 1}", [128, 1024], F32)
        self.nc.psum_base = pbase




def quant_chain(k, src_ap, dst_ap, inv_ap, zp_ap, s_ap, nzs_ap,
                reads, writes, sl, tag="", np_=128):
    p, s = k.p, k.s
    n = src_ap.shape[-1]
    assert n <= 512

    def f1(e):
        return e.activation(s["ev"][0:np_, 2 + sl, 0:n], src_ap, ACTF.Relu,
                            bias=zp_ap, scale=inv_ap)
    p.act(f1, reads=reads, writes=(f"ev{2 + sl}",), tag=tag + ".r")

    def f2(e):
        return e.tensor_scalar(s["qi"][0:np_, sl, 0:n], s["ev"][0:np_, 2 + sl, 0:n],
                               255.0, None, ALU.min)
    p.dve(f2, reads=(f"ev{2 + sl}",), writes=(f"qi{sl}",), tag=tag + ".q")

    def f3(e):
        return e.tensor_scalar(dst_ap, s["qi"][0:np_, sl, 0:n],
                               s_ap, nzs_ap, ALU.mult, ALU.add)
    p.dve(f3, reads=(f"qi{sl}", reads[-1] if reads else "qp"), writes=writes,
          tag=tag + ".d")


def qparams_perpart(k, mn_ap, mx_ap, inv_ap, zp_ap, s_ap, nzs_ap,
                    res, reads, t0_ap, tag=""):
    p, s = k.p, k.s
    w = mn_ap.ap[-1][1]
    qi = s["qi"]

    def f1(e):
        e.tensor_scalar(mn_ap, mn_ap, 0.0, None, ALU.min)
        e.drain()
        e.tensor_scalar(mx_ap, mx_ap, 0.0, None, ALU.max)
        e.drain()
        e.tensor_sub(t0_ap, mx_ap, mn_ap)
        e.drain()
        e.tensor_scalar(s_ap, t0_ap, 1.0 / 255.0, 1e-8, ALU.mult, ALU.max)
        e.drain()
        recip_acc(e, inv_ap, s_ap, t0_ap)
        e.drain()
        e.tensor_mul(t0_ap, mn_ap, inv_ap)
        e.drain()
        e.tensor_scalar(qi[:, 0, 0:w], t0_ap, -1.0, None, ALU.mult)
        e.drain()
        e.tensor_copy(zp_ap, qi[:, 0, 0:w])
        e.drain()
        e.tensor_mul(t0_ap, zp_ap, s_ap)
        e.drain()
        return e.tensor_scalar(nzs_ap, t0_ap, -1.0, None, ALU.mult)
    p.dve(f1, reads=reads, writes=(res, "qi0"), tag=tag)


def w1_view(k):
    return k.d["l0_w1"][:, :].rearrange("(a p) (b j) -> p a b j", p=128, j=128)


def w2_view(k):
    return k.d["l0_w2"][:, :].rearrange("(b p) (a j) -> p b a j", p=128, j=128)


def qkt_view(k, name):
    return k.d[name][:, :, :, :]




def emit_setup(k):
    p, s, d = k.p, k.s, k.d
    for dst, src, res in [
        (s["bias"][:, K.B_L0QKV:K.B_L0QKV + 24], d["l0_qkv_b"][:, :], "bias"),
        (s["bias"][:, K.B_L0PROJ:K.B_L0PROJ + 8], d["l0_proj_b"][:, :], "bias"),
        (s["bias"][:, K.B_B1:K.B_B1 + NFC], d["l0_b1"][:, :], "bias"),
        (s["bias"][:, K.B_B2:K.B_B2 + NDC], d["l0_b2"][:, :], "bias"),
        (s["bias"][:, K.B_LNW:K.B_LNW + 8], d["l1_ln_w"][:, :], "bias"),
        (s["bias"][:, K.B_LNB:K.B_LNB + 8], d["l1_ln_b"][:, :], "bias"),
        (s["bias"][:, K.B_L1QKV:K.B_L1QKV + 24], d["l1_qkv_b"][:, :], "bias"),
        (s["bias"][:, K.B_L1PROJ:K.B_L1PROJ + 8], d["l1_proj_b"][:, :], "bias"),
        (s["bias"][:, K.B_BASE:K.B_BASE + 8], d["l1_base_b"][:, :], "bias"),
        (s["idn_f"][:, :], d["c_idn"][:, :], "idn_f"),
        (s["small"][0:8, 30:31], d["l1_router_b"][:, :], "rb"),
    ]:
        p.dma("sp_dma", dst, src, writes=(res,), tag="setup")
    p.dma("pl_dma", s["hind"][:, :, :], d["c_headind"][:, :, :], writes=("hind",))
    p.dma("pl_dma", s["eind"][:, :], d["c_expind2"][:, :], writes=("eind",))
    p.dma("pl_dma", s["brow"][0:1, 0, :], d["l0_qkv_bv_row"][:, :], writes=("brow0",))
    p.dma("pl_dma", s["brow"][0:1, 1, :], d["l1_qkv_bv_row"][:, :], writes=("brow1",))
    p.dma("pl_dma", s["lA"][:, :, :],
          d["l1_lora_A"][:, :].rearrange("(a p) j -> p a j", p=128), writes=("lA",))
    p.dma("pl_dma", s["lB"][:, :], d["l1_lora_B"][:, :], writes=("lB",))
    p.dma("pl_dma", s["lR"][:, :, :],
          d["l1_router_wT"][:, :].rearrange("(a p) j -> p a j", p=128), writes=("lR",))

    st = s["stat"]

    def consts(e):
        e.memset(s["ones_b"][:, :], 1.0)
        e.drain()
        e.memset(s["ones_f"][:, :], 1.0)
        e.drain()
        e.memset(s["onesf64"][:, :], 4.503599627370496e15)
        e.drain()
        e.memset(st[:, :], -1e30)
        e.drain()
        e.memset(st[:, K.S_MNC:K.S_MNC + 8], 1e30)
        e.drain()
        e.memset(st[:, K.S_RMN:K.S_RMN + 3], 1e30)
        e.drain()
        e.memset(st[:, K.S_OMN:K.S_OMN + 1], 1e30)
        e.drain()
        e.memset(s["cst"][:, 0:1], 1e-5)
        e.drain()
        e.memset(s["cst"][:, 1:2], 0.125)
        e.drain()
        e.memset(s["cst"][:, 2:3], -1.0)
        e.drain()
        return e.memset(s["cst"][:, 3:4], 36.04365338911715)
    p.dve(consts, writes=("ones_b", "ones_f", "onesf64", "chanstat",
                          "qkvstat", "ostat", "wstat", "cst"),
          tag="setup.c")

    def idnb(e):
        return e.tensor_copy(s["idn_bt"][:, :], s["idn_f"][:, :])
    p.dve(idnb, reads=("idn_f",), writes=("idn_bt",), tag="setup.idnb")


def emit_wq_preproc(k, wname, nfc):
    p, s, d = k.p, k.s, k.d
    st = s["stat"]
    for fc in range(nfc):
        sl = fc % 2
        p.dma("sp_dma", s["fx"][:, sl, 0:1024], d[wname][fc * 128:(fc + 1) * 128, :],
              writes=(f"fx{sl}",), tag=f"wq.{wname}.{fc}")

        def mnmx(e, sl=sl):
            e.tensor_reduce(st[:, K.S_WMN:K.S_WMN + 1], s["fx"][:, sl, 0:1024],
                            AX.X, ALU.min)
            e.drain()
            return e.tensor_reduce(st[:, K.S_WMX:K.S_WMX + 1], s["fx"][:, sl, 0:1024],
                                   AX.X, ALU.max)
        p.dve(mnmx, reads=(f"fx{sl}",), writes=("wstat",), tag=f"wq.mm{fc}")
        qparams_perpart(k, st[:, K.S_WMN:K.S_WMN + 1], st[:, K.S_WMX:K.S_WMX + 1],
                        st[:, K.S_WINV:K.S_WINV + 1], st[:, K.S_WZP:K.S_WZP + 1],
                        st[:, K.S_WS:K.S_WS + 1], st[:, K.S_WNZS:K.S_WNZS + 1],
                        "wpar", ("wstat",), st[:, K.S_T0:K.S_T0 + 1], tag=f"wq.p{fc}")
        for h2 in range(2):
            quant_chain(k, s["fx"][:, sl, h2 * 512:(h2 + 1) * 512],
                        s["ptile"][:, 0, h2 * 512:(h2 + 1) * 512],
                        st[:, K.S_WINV:K.S_WINV + 1], st[:, K.S_WZP:K.S_WZP + 1],
                        st[:, K.S_WS:K.S_WS + 1], st[:, K.S_WNZS:K.S_WNZS + 1],
                        reads=(f"fx{sl}", "wpar"), writes=(f"ptile{h2}",),
                        sl=h2, tag=f"wq.q{fc}.{h2}")
        for dc in range(NDC):
            pb = k.psum[f"pbt{dc % 2}"]

            def tp(e, dc=dc, pb=pb):
                return e.transpose(pb[:, 0:128],
                                   s["ptile"][:, 0, dc * 128:(dc + 1) * 128],
                                   s["idn_bt"][:, :])
            p.pe(tp, reads=(f"ptile{dc // 4}", "idn_bt"), writes=(f"pt{dc % 2}",),
                 tag=f"wq.t{fc}.{dc}")

            def ev(e, dc=dc, pb=pb):
                return e.tensor_copy(s["wst"][:, dc, :], pb[:, 0:128])
            p.dve(ev, reads=(f"pt{dc % 2}",), writes=(f"wst{dc}",),
                  tag=f"wq.e{fc}.{dc}")
            if wname == "l0_proj_w":
                dst = d["pwT"][fc, dc, :, :]
                res = f"pw.{fc}.{dc}"
            elif fc >= 16:
                dst = d["wqT_v"][dc, :, (fc - 16) * 128:(fc - 15) * 128]
                res = f"wv.{dc}.{fc - 16}"
            else:
                dst = d["wqT_qk"][fc, dc, :, :]
                res = f"wqk.{fc}.{dc}"
            p.dma("sp_dma", dst, s["wst"][:, dc, :], reads=(f"wst{dc}",),
                  writes=(res,), tag=f"wq.s{fc}.{dc}")


def emit_wcast(k):
    p, s, d = k.p, k.s, k.d
    CW = k.T
    w1f = d["l0_w1"][:, :].rearrange("(a p) f -> p a f", p=128)
    w2f = d["l0_w2"][:, :].rearrange("(b p) f -> p b f", p=128)
    for dc in range(NDC):
        for r in range(FF // CW):
            p.dma("pl_dma", s["act_b"][:, dc, :],
                  w1f[:, dc, r * CW:(r + 1) * CW],
                  writes=(f"act_b.{dc}",), tag=f"wc1i.{r}.{dc}")
            p.dma("pl_dma", d["w1b"][:, dc, r * CW:(r + 1) * CW],
                  s["act_b"][:, dc, :], reads=(f"act_b.{dc}",),
                  writes=(f"w1b.{r}.{dc}",), tag=f"wc1o.{r}.{dc}")
    cc = min(CW, D)
    g = 0
    for b in range(NFC):
        for r in range(D // cc):
            dc = g % NDC
            g += 1
            p.dma("pl_dma", s["act_b"][:, dc, 0:cc],
                  w2f[:, b, r * cc:(r + 1) * cc],
                  writes=(f"act_b.{dc}",), tag=f"wc2i.{b}.{r}")
            p.dma("pl_dma", d["w2b"][:, b, r * cc:(r + 1) * cc],
                  s["act_b"][:, dc, 0:cc], reads=(f"act_b.{dc}",),
                  writes=(f"w2b.{b}.{r}",), tag=f"wc2o.{b}.{r}")


def emit_l0_stats(k, bi):
    p, s, d = k.p, k.s, k.d
    T, NTC, LC = k.T, k.NTC, k.LC
    st = s["stat"]
    nl = T // LC
    for dc in range(NDC):
        for li in range(nl):
            sl = (dc * nl + li) % 2
            p.dma("sp_dma", s["fx"][:, sl, 0:LC], d["x_fm"][bi, dc, :, li * LC:(li + 1) * LC],
                  writes=(f"fx{sl}",), tag=f"s{bi}.x{dc}.{li}")

            def sqc(e, sl=sl, dc=dc, li=li):
                e.activation(s["sq"][:, sl, :], s["fx"][:, sl, 0:LC], ACTF.Square)
                e.drain()
                return e.activation(s["act_a"][:, dc, li * LC:(li + 1) * LC],
                                    s["fx"][:, sl, 0:LC], ACTF.Identity)
            p.act(sqc, reads=(f"fx{sl}",), writes=(f"sq{sl}", f"act_a.{dc}"),
                  tag=f"s{bi}.sq{dc}.{li}")
            for q5 in range(LC // 512):
                tc = li * (LC // 512) + q5

                def mm1(e, dc=dc, li=li, q5=q5, tc=tc):
                    return e.matmul(k.psum[f"pb{tc}"][:, :], s["ones_b"][:, :],
                                    s["act_a"][:, dc, (li * LC + q5 * 512):(li * LC + (q5 + 1) * 512)],
                                    start=(dc == 0), stop=(dc == NDC - 1))
                p.pe(mm1, reads=(f"act_a.{dc}", "ones_b"), writes=(f"pt{tc}",),
                     tag=f"s{bi}.mS{dc}.{tc}")

                def mm2(e, sl=sl, q5=q5, dc=dc, tc=tc):
                    return e.matmul(k.psum[f"pb{NTC + tc}"][:, :], s["ones_b"][:, :],
                                    s["sq"][:, sl, q5 * 512:(q5 + 1) * 512],
                                    start=(dc == 0), stop=(dc == NDC - 1))
                p.pe(mm2, reads=(f"sq{sl}", "ones_b"), writes=(f"pt{NTC + tc}",),
                     tag=f"s{bi}.mQ{dc}.{tc}")
    for tc in range(NTC):
        def stats(e, tc=tc):
            e.tensor_scalar(s["mur"][:, 0, :], k.psum[f"pb{tc}"][:, :],
                            1.0 / 1024.0, None, ALU.mult)
            e.drain()
            e.tensor_scalar(s["rr"][:, 0, :], k.psum[f"pb{NTC + tc}"][:, :],
                            1.0 / 1024.0, None, ALU.mult)
            e.drain()
            e.scalar_tensor_tensor(s["rsc"][:, 0:512], s["mur"][:, 0, :], 1.0,
                                   s["mur"][:, 0, :], ALU.mult, ALU.mult)
            e.drain()
            return e.tensor_sub(s["rsc"][:, 0:512], s["rr"][:, 0, :], s["rsc"][:, 0:512])
        p.dve(stats, reads=(f"pt{tc}", f"pt{NTC + tc}"), writes=("rsc", "mur0", "rr0"),
              tag=f"s{bi}.st{tc}")

        def sqr(e):
            return e.activation(s["rsc"][:, 0:512], s["rsc"][:, 0:512], ACTF.Sqrt, bias=s["cst"][:, 0:1])
        p.act(sqr, reads=("rsc",), writes=("rsc",), tag=f"s{bi}.sd{tc}")

        def rcp(e):
            recip_acc(e, s["rr"][:, 1, :], s["rsc"][:, 0:512], s["rec"][:, 0:512])
            e.drain()
            return e.tensor_mul(s["mur"][:, 1, :], s["mur"][:, 0, :], s["rr"][:, 1, :])
        p.dve(rcp, reads=("rsc", "mur0"), writes=("rr1", "mur1", "rec"),
              tag=f"s{bi}.rc{tc}")
        for dc in range(NDC):
            sl = (tc * NDC + dc) % 2
            p.dma("sp_dma", s["fx"][:, sl, 0:512],
                  d["x_fm"][bi, dc, :, tc * 512:(tc + 1) * 512],
                  writes=(f"fx{sl}",), tag=f"s{bi}.x2{dc}.{tc}")

            def app(e, sl=sl, dc=dc):
                e.tensor_mul(s["ev"][:, 3, :], s["fx"][:, sl, 0:512], s["rr"][:, 1, :])
                e.drain()
                e.tensor_sub(s["fy"][:, sl, :], s["ev"][:, 3, :], s["mur"][:, 1, :])
                e.drain()
                e.tensor_reduce(s["small"][:, 0:1], s["fy"][:, sl, :], AX.X, ALU.max)
                e.drain()
                e.tensor_reduce(s["small"][:, 1:2], s["fy"][:, sl, :], AX.X, ALU.min)
                e.drain()
                e.tensor_tensor(st[:, K.S_MXC + dc:K.S_MXC + dc + 1],
                                st[:, K.S_MXC + dc:K.S_MXC + dc + 1],
                                s["small"][:, 0:1], ALU.max)
                e.drain()
                return e.tensor_tensor(st[:, K.S_MNC + dc:K.S_MNC + dc + 1],
                                       st[:, K.S_MNC + dc:K.S_MNC + dc + 1],
                                       s["small"][:, 1:2], ALU.min)
            p.dve(app, reads=(f"fx{sl}", "rr1", "mur1"),
                  writes=(f"fy{sl}", "chanstat", "ev3", "small"),
                  tag=f"s{bi}.ap{dc}.{tc}")
            p.dma("sp_dma", d["xn32"][bi, dc, :, tc * 512:(tc + 1) * 512],
                  s["fy"][:, sl, :], reads=(f"fy{sl}",),
                  writes=(f"xn32.{bi}.{dc}.{tc}",), tag=f"s{bi}.w{dc}.{tc}")


def emit_ar1(k):
    p, s, d = k.p, k.s, k.d
    st = s["stat"]

    def pack(e):
        e.tensor_copy(s["tmp8"][:, 0:8], st[:, K.S_MXC:K.S_MXC + 8])
        e.drain()
        return e.tensor_scalar(s["tmp8"][:, 8:16], st[:, K.S_MNC:K.S_MNC + 8],
                               -1.0, None, ALU.mult)
    p.dve(pack, reads=("chanstat",), writes=("tmp8a",), tag="ar1.pk")
    p.dma("pl_dma", d["ar1_in"][:, :], s["tmp8"][:, 0:16], reads=("tmp8a",),
          writes=("ar1_in",), tag="ar1.st")
    p.collective(ALU.max, [list(range(k.n_cores))],
                 d["ar1_in"].ap().opt(), d["ar1_out"].ap().opt(),
                 reads=("ar1_in",), writes=("ar1_out",), tag="ar1.cc")
    p.dma("pl_dma", s["tmp8"][:, 16:32], d["ar1_out"][:, :], reads=("ar1_out",),
          writes=("tmp8b",), tag="ar1.ld")

    def unpack(e):
        e.tensor_copy(st[:, K.S_MXC:K.S_MXC + 8], s["tmp8"][:, 16:24])
        e.drain()
        return e.tensor_scalar(st[:, K.S_MNC:K.S_MNC + 8], s["tmp8"][:, 24:32],
                               -1.0, None, ALU.mult)
    p.dve(unpack, reads=("tmp8b",), writes=("chanstat",), tag="ar1.un")
    qparams_perpart(k, st[:, K.S_MNC:K.S_MNC + 8], st[:, K.S_MXC:K.S_MXC + 8],
                    st[:, K.S_CINV:K.S_CINV + 8], st[:, K.S_CZP:K.S_CZP + 8],
                    st[:, K.S_CS:K.S_CS + 8], st[:, K.S_CNZS:K.S_CNZS + 8],
                    "chanpar", ("chanstat",), s["tmp8"][:, 88:96], tag="ar1.par")


def emit_ffn(k, bi):
    p, s, d = k.p, k.s, k.d
    T, NTC, LC = k.T, k.NTC, k.LC
    if bi == 1:
        p.fence(reads=(), writes=("wvb0", "w1t0", "w1t1", "w2t0", "w2t1"), tag="f.ffn1")
    for dc in range(NDC):
        p.dma("pl_dma", s["act_a"][:, dc, :], d["xn32"][bi, dc, :, :],
              reads=tuple(f"xn32.{bi}.{dc}.{t}" for t in range(NTC)),
              writes=(f"act_a.{dc}",), tag=f"fn{bi}.xn{dc}")
    for tcb in range(T // LC):
        for fc in range(NFC):
            wsl = fc % 2
            p.dma("pl_dma", s["w1t"][:, wsl, :, :],
                  d["w1b"][:, :, fc * 128:(fc + 1) * 128],
                  reads=tuple(f"w1b.{fc * 128 // k.T}.{dcc}"
                              for dcc in range(NDC)),
                  writes=(f"w1t{wsl}",), tag=f"fn{bi}.w1.{tcb}.{fc}")
            for h2 in range(LC // 512):
                pb = k.psum[f"pb{h2}"]
                tsl = slice(tcb * LC + h2 * 512, tcb * LC + (h2 + 1) * 512)
                for dc in range(NDC):
                    def mm(e, pb=pb, wsl=wsl, dc=dc, tsl=tsl):
                        return e.matmul(pb[:, :], s["w1t"][:, wsl, dc, :],
                                        s["act_a"][:, dc, tsl],
                                        start=(dc == 0), stop=(dc == NDC - 1))
                    p.pe(mm, reads=(f"w1t{wsl}", f"act_a.{dc}"), writes=(f"pt{h2}",),
                         tag=f"fn{bi}.h{tcb}.{fc}.{h2}.{dc}")

                def gelu(e, pb=pb, fc=fc, h2=h2):
                    return e.activation(s["hbuf"][:, fc, h2 * 512:(h2 + 1) * 512],
                                        pb[:, :], ACTF.Gelu_apprx_tanh,
                                        bias=s["bias"][:, K.B_B1 + fc:K.B_B1 + fc + 1])
                p.act(gelu, reads=(f"pt{h2}", "bias"), writes=(f"hbuf.{fc}",),
                      tag=f"fn{bi}.g{tcb}.{fc}.{h2}")
        for dc in range(NDC):
            for hf in range(2):
                p.dma("pl_dma" if hf else "sp_dma",
                      s["w2t"][:, hf * (NFC // 2):(hf + 1) * (NFC // 2), :],
                      d["w2b"][:, hf * (NFC // 2):(hf + 1) * (NFC // 2),
                               dc * 128:(dc + 1) * 128],
                      reads=tuple(f"w2b.{b}.{dc * 128 // min(k.T, D)}"
                                  for b in range(hf * (NFC // 2),
                                                 (hf + 1) * (NFC // 2))),
                      writes=(f"w2t{hf}",), tag=f"fn{bi}.w2.{tcb}.{dc}.{hf}")
            for h2 in range(LC // 512):
                pb = k.psum[f"pb{2 + h2}"]
                for fc in range(NFC):
                    def mm(e, pb=pb, fc=fc, h2=h2):
                        return e.matmul(pb[:, :], s["w2t"][:, fc, :],
                                        s["hbuf"][:, fc, h2 * 512:(h2 + 1) * 512],
                                        start=(fc == 0), stop=(fc == NFC - 1))
                    p.pe(mm, reads=(f"w2t{fc // (NFC // 2)}", f"hbuf.{fc}"),
                         writes=(f"pt{2 + h2}",),
                         tag=f"fn{bi}.y{tcb}.{dc}.{h2}.{fc}")

                def silu(e, pb=pb, dc=dc, h2=h2):
                    return e.activation(s["ev"][:, h2, :], pb[:, :], ACTF.Silu,
                                        bias=s["bias"][:, K.B_B2 + dc:K.B_B2 + dc + 1])
                p.act(silu, reads=(f"pt{2 + h2}", "bias"), writes=(f"ev{h2}",),
                      tag=f"fn{bi}.s{tcb}.{dc}.{h2}")
                tc = tcb * (LC // 512) + h2
                p.dma("sp_dma", d["g0"][bi, dc, :, tc * 512:(tc + 1) * 512],
                      s["ev"][:, h2, :], reads=(f"ev{h2}",),
                      writes=(f"g0.{bi}.{dc}.{tc}",), tag=f"fn{bi}.gs{tcb}.{dc}.{h2}")


def emit_xq_qkv(k, bi):
    p, s, d = k.p, k.s, k.d
    T, NTC = k.T, k.NTC
    st = s["stat"]
    for dc in range(NDC):
        for tc in range(NTC):
            sl = (dc * NTC + tc) % 2
            p.dma("sp_dma", s["fx"][:, sl, 0:512],
                  d["xn32"][bi, dc, :, tc * 512:(tc + 1) * 512],
                  reads=(f"xn32.{bi}.{dc}.{tc}",), writes=(f"fx{sl}",),
                  tag=f"xq{bi}.{dc}.{tc}")
            quant_chain(k, s["fx"][:, sl, 0:512],
                        s["act_a"][:, dc, tc * 512:(tc + 1) * 512],
                        st[:, K.S_CINV + dc:K.S_CINV + dc + 1],
                        st[:, K.S_CZP + dc:K.S_CZP + dc + 1],
                        st[:, K.S_CS + dc:K.S_CS + dc + 1],
                        st[:, K.S_CNZS + dc:K.S_CNZS + dc + 1],
                        reads=(f"fx{sl}", "chanpar"), writes=(f"act_a.{dc}",),
                        sl=sl, tag=f"xq{bi}.{dc}.{tc}")
    p.fence(reads=(), writes=("w1t0", "w1t1", "w2t0", "w2t1", "wvb0"),
            tag=f"f.v{bi}")
    for dc in range(NDC):
        p.dma("pl_dma", s["wvb0"][:, dc, :], d["wqT_v"][dc, :, :],
              reads=tuple(f"wv.{dc}.{j}" for j in range(8)),
              writes=("wvb0",), tag=f"qv{bi}.w{dc}")
    for fc in range(16):
        wsl = fc % 2
        p.dma("sp_dma", s["wtile"][:, wsl, :, :],
              d["wqT_qk"][fc, :, :, :].rearrange("a p j -> p a j"),
              reads=tuple(f"wqk.{fc}.{dcc}" for dcc in range(NDC)),
              writes=(f"wtile{wsl}",), tag=f"qk{bi}.w{fc}")
        for dc in range(NDC):
            for tc in range(NTC):
                def mm(e, wsl=wsl, dc=dc, tc=tc):
                    return e.matmul(k.psum[f"pb{tc}"][:, :], s["wtile"][:, wsl, dc, :],
                                    s["act_a"][:, dc, tc * 512:(tc + 1) * 512],
                                    start=(dc == 0), stop=(dc == NDC - 1))
                p.pe(mm, reads=(f"wtile{wsl}", f"act_a.{dc}"), writes=(f"pt{tc}",),
                     tag=f"qk{bi}.m{fc}.{dc}.{tc}")
        for tc in range(NTC):
            def evac(e, fc=fc, tc=tc):
                return e.activation(s["ev"][:, tc % 2, :], k.psum[f"pb{tc}"][:, :],
                                    ACTF.Identity,
                                    bias=s["bias"][:, K.B_L0QKV + fc:K.B_L0QKV + fc + 1])
            p.act(evac, reads=(f"pt{tc}", "bias"), writes=(f"ev{tc % 2}",),
                  tag=f"qk{bi}.e{fc}.{tc}")

            def mnmx(e, fc=fc, tc=tc):
                cx = K.S_RMX + (0 if fc < 8 else 1)
                cn = K.S_RMN + (0 if fc < 8 else 1)
                e.tensor_reduce(s["small"][:, 2:3], s["ev"][:, tc % 2, :], AX.X, ALU.max)
                e.drain()
                e.tensor_reduce(s["small"][:, 3:4], s["ev"][:, tc % 2, :], AX.X, ALU.min)
                e.drain()
                e.tensor_tensor(st[:, cx:cx + 1], st[:, cx:cx + 1],
                                s["small"][:, 2:3], ALU.max)
                e.drain()
                return e.tensor_tensor(st[:, cn:cn + 1], st[:, cn:cn + 1],
                                       s["small"][:, 3:4], ALU.min)
            p.dve(mnmx, reads=(f"ev{tc % 2}",), writes=("qkvstat", "small"),
                  tag=f"qk{bi}.n{fc}.{tc}")
            p.dma("sp_dma" if (fc + tc) % 2 == 0 else "pl_dma",
                  d["qkraw"][bi, fc, tc, :, :], s["ev"][:, tc % 2, :],
                  reads=(f"ev{tc % 2}",), writes=(f"qkraw.{bi}.{fc}.{tc}",),
                  tag=f"qk{bi}.s{fc}.{tc}")
    for t128 in range(k.NT128):
        for fv in range(2):
            pq = 4 + (2 * t128 + fv) % 2
            evs = (2 * t128 + fv) % 4
            pb = k.psum[f"pb{pq}"]
            for dc in range(NDC):
                def mm(e, pb=pb, t128=t128, fv=fv, dc=dc):
                    return e.matmul(pb[:, :],
                                    s["act_a"][:, dc, t128 * 128:(t128 + 1) * 128],
                                    s["wvb0"][:, dc, fv * 512:(fv + 1) * 512],
                                    start=(dc == 0), stop=False)
                p.pe(mm, reads=("wvb0", f"act_a.{dc}"), writes=(f"pt{pq}",),
                     tag=f"qv{bi}.m{t128}.{fv}.{dc}")

            def mmb(e, pb=pb, fv=fv):
                return e.matmul(pb[:, :], s["ones_b"][0:1, 0:128],
                                s["brow"][0:1, 0, fv * 512:(fv + 1) * 512],
                                start=False, stop=True)
            p.pe(mmb, reads=("brow0", "ones_b"), writes=(f"pt{pq}",),
                 tag=f"qv{bi}.b{t128}.{fv}")

            def evac(e, pb=pb, evs=evs):
                return e.tensor_copy(s["ev"][:, evs, :], pb[:, :])
            p.dve(evac, reads=(f"pt{pq}",), writes=(f"ev{evs}",),
                  tag=f"qv{bi}.e{t128}.{fv}")

            def mnmx(e, evs=evs):
                e.tensor_reduce(s["small"][:, 4:5], s["ev"][:, evs, :], AX.X, ALU.max)
                e.drain()
                e.tensor_reduce(s["small"][:, 5:6], s["ev"][:, evs, :], AX.X, ALU.min)
                e.drain()
                cx, cn = K.S_RMX + 2, K.S_RMN + 2
                e.tensor_tensor(st[:, cx:cx + 1], st[:, cx:cx + 1],
                                s["small"][:, 4:5], ALU.max)
                e.drain()
                return e.tensor_tensor(st[:, cn:cn + 1], st[:, cn:cn + 1],
                                       s["small"][:, 5:6], ALU.min)
            p.dve(mnmx, reads=(f"ev{evs}",), writes=("qkvstat", "small"),
                  tag=f"qv{bi}.n{t128}.{fv}")
            p.dma("sp_dma" if (t128 + fv) % 2 == 0 else "pl_dma",
                  d["vraw"][bi, t128, fv, :, :], s["ev"][:, evs, :],
                  reads=(f"ev{evs}",), writes=(f"vraw.{bi}.{t128}.{fv}",),
                  tag=f"qv{bi}.s{t128}.{fv}")


def bcast_params(k, pbname, src_res, dst_fn, tag):
    pass


def emit_ar2(k):
    p, s, d = k.p, k.s, k.d
    st = s["stat"]

    def pack(e):
        e.memset(s["tmp8"][:, 38:40], 0.0)
        e.drain()
        e.tensor_copy(s["tmp8"][:, 32:35], st[:, K.S_RMX:K.S_RMX + 3])
        e.drain()
        return e.tensor_scalar(s["tmp8"][:, 35:38], st[:, K.S_RMN:K.S_RMN + 3],
                               -1.0, None, ALU.mult)
    p.dve(pack, reads=("qkvstat",), writes=("tmp8c",), tag="ar2.pk")

    def tp(e):
        return e.transpose(k.psum["pb6"][0:8, 0:128], s["tmp8"][:, 32:40],
                           s["idn_f"][:, :])
    p.pe(tp, reads=("tmp8c", "idn_f"), writes=("pt6",), tag="ar2.tp")

    def red(e):
        return e.tensor_reduce(s["small"][0:8, 6:7], k.psum["pb6"][0:8, 0:128],
                               AX.X, ALU.max)
    p.dve(red, reads=("pt6",), writes=("small",), tag="ar2.rd")
    p.dma("pl_dma", d["ar2_in"][:, :], s["small"][0:8, 6:7], reads=("small",),
          writes=("ar2_in",), tag="ar2.st")
    p.collective(ALU.max, [list(range(k.n_cores))],
                 d["ar2_in"].ap().opt(), d["ar2_out"].ap().opt(),
                 reads=("ar2_in",), writes=("ar2_out",), tag="ar2.cc")
    p.dma("pl_dma", s["small"][0:8, 7:8], d["ar2_out"][:, :], reads=("ar2_out",),
          writes=("small78",), tag="ar2.ld")

    def tp2(e):
        return e.transpose(k.psum["pb7"][0:1, 0:8], s["small"][0:8, 7:8],
                           s["idn_f"][0:8, 0:8])
    p.pe(tp2, reads=("small78", "idn_f"), writes=("pt7",), tag="ar2.tp2")

    def ev7(e):
        return e.tensor_copy(s["small"][0:1, 8:16], k.psum["pb7"][0:1, 0:8])
    p.dve(ev7, reads=("pt7",), writes=("smallrow",), tag="ar2.e7")

    def bc(e):
        return e.matmul(k.psum["pb7"][:, 0:8], s["ones_f"][0:1, :],
                        s["small"][0:1, 8:16], start=True, stop=True)
    p.pe(bc, reads=("smallrow", "ones_f"), writes=("pt7",), tag="ar2.bc")

    def par(e):
        qp = s["qp"]
        pbv = k.psum["pb7"]
        sm = s["small"]
        e.tensor_scalar(sm[:, 9:12], pbv[:, 3:6], -1.0, None, ALU.mult)
        e.drain()
        e.tensor_scalar(sm[:, 9:12], sm[:, 9:12], 0.0, None, ALU.min)
        e.drain()
        e.tensor_scalar(sm[:, 12:15], pbv[:, 0:3], 0.0, None, ALU.max)
        e.drain()
        e.tensor_sub(sm[:, 15:18], sm[:, 12:15], sm[:, 9:12])
        e.drain()
        e.tensor_scalar(qp[:, K.QP_S:K.QP_S + 3], sm[:, 15:18], 1.0 / 255.0, 1e-8,
                        ALU.mult, ALU.max)
        e.drain()
        recip_acc(e, qp[:, K.QP_INV:K.QP_INV + 3],
                                     qp[:, K.QP_S:K.QP_S + 3], sm[:, 15:18])
        e.drain()
        e.tensor_mul(sm[:, 15:18], sm[:, 9:12], qp[:, K.QP_INV:K.QP_INV + 3])
        e.drain()
        e.tensor_scalar(s["qi"][:, 0, 0:3], sm[:, 15:18], -1.0, None, ALU.mult)
        e.drain()
        e.tensor_copy(qp[:, K.QP_ZP:K.QP_ZP + 3], s["qi"][:, 0, 0:3])
        e.drain()
        e.tensor_mul(sm[:, 15:18], qp[:, K.QP_ZP:K.QP_ZP + 3], qp[:, K.QP_S:K.QP_S + 3])
        e.drain()
        return e.tensor_scalar(qp[:, K.QP_NZS:K.QP_NZS + 3], sm[:, 15:18],
                               -1.0, None, ALU.mult)
    p.dve(par, reads=("pt7",), writes=("qpar", "small", "qi0"), tag="ar2.par")


def emit_l0_attention(k, bi):
    p, s, d = k.p, k.s, k.d
    T, NTC, NT128 = k.T, k.NTC, k.NT128
    qp = s["qp"]
    if bi == 0:
        p.fence(reads=(), writes=("wvb0", "w1t0", "w1t1", "w2t",
                       *(f"hbuf.{fc}" for fc in range(NFC)),
                       "vaug.ones", "nrmX"), tag="f.attn0")

        def vm(e):
            return e.memset(s["vaug"][:, :, :, :], 1.0)
        p.dve(vm, reads=("vaug.ones",), writes=("vaug.ones",), tag="vmemset")
    for sec, dst, base, ti in (("k", "act_a", 8, 1), ("q", "act_b", 0, 0)):
        for fcl in range(8):
            fc = base + fcl
            for tc in range(NTC):
                sl = (fcl * NTC + tc) % 2
                p.dma("sp_dma", s["fx"][:, sl, 0:512], d["qkraw"][bi, fc, tc, :, :],
                      reads=(f"qkraw.{bi}.{fc}.{tc}",), writes=(f"fx{sl}",),
                      tag=f"a{bi}.l{sec}{fcl}.{tc}")
                quant_chain(k, s["fx"][:, sl, 0:512],
                            s[dst][:, fcl, tc * 512:(tc + 1) * 512],
                            qp[:, K.QP_INV + ti:K.QP_INV + ti + 1],
                            qp[:, K.QP_ZP + ti:K.QP_ZP + ti + 1],
                            qp[:, K.QP_S + ti:K.QP_S + ti + 1],
                            qp[:, K.QP_NZS + ti:K.QP_NZS + ti + 1],
                            reads=(f"fx{sl}", "qpar"), writes=(f"{dst}.{fcl}",),
                            sl=sl, tag=f"a{bi}.q{sec}{fcl}.{tc}")
    for t128 in range(NT128):
        for fv in range(2):
            sl = (t128 * 2 + fv) % 2
            p.dma("sp_dma", s["fx"][:, sl, 0:512], d["vraw"][bi, t128, fv, :, :],
                  reads=(f"vraw.{bi}.{t128}.{fv}",), writes=(f"fx{sl}",),
                  tag=f"a{bi}.lv{t128}.{fv}")
            quant_chain(k, s["fx"][:, sl, 0:512],
                        s["vaug"][:, t128, fv * 8:(fv + 1) * 8, 0:64],
                        qp[:, K.QP_INV + 2:K.QP_INV + 3],
                        qp[:, K.QP_ZP + 2:K.QP_ZP + 3],
                        qp[:, K.QP_S + 2:K.QP_S + 3],
                        qp[:, K.QP_NZS + 2:K.QP_NZS + 3],
                        reads=(f"fx{sl}", "qpar"), writes=(f"vg.{t128 % 4}",),
                        sl=sl, tag=f"a{bi}.qv{t128}.{fv}")
    if bi == 0 and "dbg_ka" in K.DEBUG_OUT:
        for dc in range(NDC):
            p.dma("pl_dma", d["dbg_ka"][:, dc, :], s["act_a"][:, dc, :],
                  reads=(f"act_a.{dc}",), writes=("dbgka",), tag=f"dbg.ka{dc}")
            p.dma("pl_dma", d["dbg_qa"][:, dc, :], s["act_b"][:, dc, :],
                  reads=(f"act_b.{dc}",), writes=("dbgqa",), tag=f"dbg.qa{dc}")
    attention_core(k, bi, layer=0, Ks=s["act_a"], Kres="act_a",
                   Qs=s["act_b"], Qres="act_b", nrm=s["nrm0"], nres="nrmX")


def attention_core(k, bi, layer, Ks, Kres, Qs, Qres, nrm, nres):
    p, s, d = k.p, k.s, k.d
    T, NTC, NT128 = k.T, k.NTC, k.NT128
    st = s["stat"]
    for idx, (src, sres) in enumerate(((Qs, Qres), (Ks, Kres))):
        off = 0 if idx == 0 else T
        for tc in range(NTC):
            for dc in range(NDC):
                def sqr(e, src=src, dc=dc, tc=tc):
                    return e.activation(s["sq"][:, 0, 0:512],
                                        src[:, dc, tc * 512:(tc + 1) * 512],
                                        ACTF.Square)
                p.act(sqr, reads=(f"{sres}.{dc}",), writes=("sq0",),
                      tag=f"n{layer}{bi}.s{idx}.{tc}.{dc}")

                def mm(e, dc=dc):
                    return e.matmul(k.psum["pb6"][0:16, :], s["hind"][:, dc, :],
                                    s["sq"][:, 0, 0:512],
                                    start=(dc == 0), stop=(dc == NDC - 1))
                p.pe(mm, reads=("sq0", "hind"), writes=("pt6",),
                     tag=f"n{layer}{bi}.m{idx}.{tc}.{dc}")

            def rt(e, off=off, tc=tc):
                return e.activation(nrm[0:16, off + tc * 512:off + (tc + 1) * 512],
                                    k.psum["pb6"][0:16, :], ACTF.Sqrt)
            p.act(rt, reads=("pt6",), writes=(nres,), tag=f"n{layer}{bi}.r{idx}.{tc}")

    if layer == 0 and bi == 0 and "dbg_hind" in K.DEBUG_OUT:
        p.dma("pl_dma", d["dbg_hind"][:, :, :], s["hind"][:, :, :],
              reads=("hind",), writes=("hind",), tag="dbg.hind")
    if layer == 0 and bi == 0 and "dbg_nrm" in K.DEBUG_OUT:
        p.dma("pl_dma", d["dbg_nrm"][:, 0, 0:T], nrm[0:16, 0:T],
              reads=(nres,), writes=(nres,), tag="dbg.nrmq")
        p.dma("pl_dma", d["dbg_nrm"][:, 1, 0:T], nrm[0:16, T:2 * T],
              reads=(nres,), writes=(nres,), tag="dbg.nrmk")

    def cmax(e):
        e.tensor_reduce(s["small"][0:16, 20:21], nrm[0:16, 0:T], AX.X, ALU.max)
        e.drain()
        e.tensor_reduce(s["small"][0:16, 21:22], nrm[0:16, T:2 * T], AX.X, ALU.max)
        e.drain()
        e.tensor_mul(s["small"][0:16, 22:23], s["small"][0:16, 20:21],
                     s["small"][0:16, 21:22])
        e.drain()
        return e.tensor_scalar(s["small"][0:16, 23:24], s["small"][0:16, 22:23],
                               -0.125, None, ALU.mult)
    p.dve(cmax, reads=(nres,), writes=("small",), tag=f"c{layer}{bi}.mx")

    if layer == 0 and bi == 0 and "dbg_sm" in K.DEBUG_OUT:
        p.dma("sp_dma", d["dbg_sm"][:, :], s["small"][0:16, 20:24],
              reads=("small",), writes=("small",), tag="dbg.sm")

    def ctp(e):
        return e.transpose(k.psum["pb7"][0:1, 0:16], s["small"][0:16, 23:24],
                           s["idn_f"][0:16, 0:16])
    p.pe(ctp, reads=("small", "idn_f"), writes=("pt7",), tag=f"c{layer}{bi}.tp")

    def cev(e):
        return e.tensor_copy(s["small"][0:1, 32:48], k.psum["pb7"][0:1, 0:16])
    p.dve(cev, reads=("pt7",), writes=("small",), tag=f"c{layer}{bi}.ev")

    def cbc(e):
        return e.matmul(k.psum["pb7"][:, 0:16], s["ones_f"][0:1, :],
                        s["small"][0:1, 32:48], start=True, stop=True)
    p.pe(cbc, reads=("small", "ones_f"), writes=("pt7",), tag=f"c{layer}{bi}.bc")

    def cbe(e):
        return e.tensor_copy(s["cbias"][:, :], k.psum["pb7"][:, 0:16])
    p.dve(cbe, reads=("pt7",), writes=("cbias",), tag=f"c{layer}{bi}.eb")
    if layer == 0 and bi == 0 and "dbg_cb" in K.DEBUG_OUT:
        p.dma("sp_dma", d["dbg_cb"][:, :], s["cbias"][:, :],
              reads=("cbias",), writes=("cbias",), tag="dbg.cb")

    for hp in range(8):
        for qc in range(NTC):
            qsl = slice(qc * 512, (qc + 1) * 512)
            NJ = NT128 // 2

            def emit_S(j, sub):
                h = 2 * hp + sub
                po = 64 * sub
                for half in range(2):
                    kc = 2 * j + half

                    def mmS(e, hp=hp, po=po, kc=kc, qsl=qsl,
                            sub=sub, half=half):
                        return e.matmul(
                            k.psum[f"pb{2 + 2 * sub}{3 + 2 * sub}"][
                                :, 512 * half:512 * (half + 1)],
                            Ks[po:po + 64, hp, kc * 128:(kc + 1) * 128],
                            Qs[po:po + 64, hp, qsl],
                            start=True, stop=True)
                    p.pe(mmS, reads=(f"{Kres}.{hp}", f"{Qres}.{hp}"),
                         writes=(f"pt{2 + 2 * sub + half}",),
                         tag=f"A{layer}{bi}.S{hp}.{sub}.{qc}.{kc}")

            def emit_E(j, sub):
                h = 2 * hp + sub

                def expv(e, h=h, sub=sub):
                    return e.activation(s["ptile"][:, sub, :],
                                        k.psum[f"pb{2 + 2 * sub}{3 + 2 * sub}"][:, :],
                                        ACTF.Exp, scale=s["cst"][:, 1:2],
                                        bias=s["cbias"][:, h:h + 1])
                p.act(expv, reads=(f"pt{2 + 2 * sub}", f"pt{3 + 2 * sub}",
                                   "cbias"),
                      writes=(f"ptile{2 * sub}", f"ptile{2 * sub + 1}"),
                      tag=f"A{layer}{bi}.E{hp}.{sub}.{qc}.{j}")

            def emit_PV(j, sub):
                h = 2 * hp + sub
                pacc = k.psum[f"pb{sub}"]
                for half in range(2):
                    kc = 2 * j + half

                    def mmPV(e, h=h, kc=kc, sub=sub, half=half, pacc=pacc):
                        return e.matmul(pacc[0:66, :],
                                        s["vaug"][:, kc, h, 0:66],
                                        s["ptile"][:, sub, 512 * half:512 * (half + 1)],
                                        start=(kc == 0), stop=(kc == NT128 - 1))
                    p.pe(mmPV, reads=(f"ptile{2 * sub + half}", f"vg.{kc % 4}",
                                      "vaug.ones"),
                         writes=(f"pt{sub}",),
                         tag=f"A{layer}{bi}.P{hp}.{sub}.{qc}.{kc}")

            emit_S(0, 0)
            emit_S(0, 1)
            for j in range(NJ):
                for sub in range(2):
                    emit_E(j, sub)
                    if j + 1 < NJ:
                        emit_S(j + 1, sub)
                    emit_PV(j, sub)

            def dcp0(e):
                return e.tensor_copy(s["den"][64:65, :512], k.psum["pb0"][64:65, :])
            p.dve(dcp0, reads=("pt0",), writes=("den",),
                  tag=f"A{layer}{bi}.D{hp}.0.{qc}")

            def dcp1(e):
                return e.tensor_copy(s["rsc"][64:65, 0:512], k.psum["pb1"][64:65, :])
            p.dve(dcp1, reads=("pt1",), writes=("rsc",),
                  tag=f"A{layer}{bi}.D{hp}.1.{qc}")

            def dbc0(e):
                return e.matmul(k.psum["pb67"][0:64, 0:512],
                                s["onesf64"][64:65, 0:64],
                                s["den"][64:65, :512], start=True, stop=True)
            p.pe(dbc0, reads=("den", "onesf64"), writes=("pt6",),
                 tag=f"A{layer}{bi}.B{hp}.0.{qc}")

            def dbc1(e):
                return e.matmul(k.psum["pb67"][0:64, 512:1024],
                                s["onesf64"][64:65, 0:64],
                                s["rsc"][64:65, 0:512], start=True, stop=True)
            p.pe(dbc1, reads=("rsc", "onesf64"), writes=("pt7",),
                 tag=f"A{layer}{bi}.B{hp}.1.{qc}")

            def rcd(e):
                e.activation(s["ev"][0:64, 2:4, :], k.psum["pb67"][0:64, :],
                             ACTF.Ln)
                e.drain()
                e.activation(s["rec"][0:64, 0:512], s["ev"][0:64, 2, :],
                             ACTF.Exp, scale=s["cst"][0:64, 2:3],
                             bias=s["cst"][0:64, 3:4])
                return e.activation(s["rsc"][0:64, 0:512], s["ev"][0:64, 3, :],
                                    ACTF.Exp, scale=s["cst"][0:64, 2:3],
                                    bias=s["cst"][0:64, 3:4])
            p.act(rcd, reads=("pt6", "pt7", "cst"),
                  writes=("rec", "rsc", "ev2", "ev3"),
                  tag=f"A{layer}{bi}.R{hp}.{qc}")

            for sub in range(2):
                h = 2 * hp + sub
                po = 64 * sub
                pacc = k.psum[f"pb{sub}"]
                rtile = "rec" if sub == 0 else "rsc"
                if layer == 0:
                    def div(e, pacc=pacc, sub=sub, rtile=rtile):
                        return e.tensor_mul(s["ev"][0:64, sub, :], pacc[0:64, :],
                                            s[rtile][0:64, 0:512])
                    p.dve(div, reads=(f"pt{sub}", rtile), writes=(f"ev{sub}",),
                          tag=f"A{layer}{bi}.V{hp}.{sub}.{qc}")

                    def omn(e, sub=sub):
                        e.tensor_reduce(s["small"][0:64, 24:25], s["ev"][0:64, sub, :],
                                        AX.X, ALU.max)
                        e.drain()
                        e.tensor_reduce(s["small"][0:64, 25:26], s["ev"][0:64, sub, :],
                                        AX.X, ALU.min)
                        e.drain()
                        e.tensor_tensor(st[0:64, K.S_OMX:K.S_OMX + 1],
                                        st[0:64, K.S_OMX:K.S_OMX + 1],
                                        s["small"][0:64, 24:25], ALU.max)
                        e.drain()
                        return e.tensor_tensor(st[0:64, K.S_OMN:K.S_OMN + 1],
                                               st[0:64, K.S_OMN:K.S_OMN + 1],
                                               s["small"][0:64, 25:26], ALU.min)
                    p.dve(omn, reads=(f"ev{sub}",), writes=("ostat", "small"),
                          tag=f"A{layer}{bi}.O{hp}.{sub}.{qc}")
                    p.dma("sp_dma", d["oraw"][bi, h, qc, :, :], s["ev"][0:64, sub, :],
                          reads=(f"ev{sub}",), writes=(f"oraw.{bi}.{h}.{qc}",),
                          tag=f"A{layer}{bi}.W{hp}.{sub}.{qc}")
                else:
                    def divb(e, pacc=pacc, po=po, hp=hp, qsl=qsl, rtile=rtile):
                        return e.tensor_mul(s["act_a"][po:po + 64, hp, qsl],
                                            pacc[0:64, :], s[rtile][0:64, 0:512])
                    p.dve(divb, reads=(f"pt{sub}", rtile), writes=(f"act_a.{hp}",),
                          tag=f"A{layer}{bi}.V{hp}.{sub}.{qc}")


def emit_ar3(k):
    p, s, d = k.p, k.s, k.d
    st = s["stat"]

    def pack(e):
        e.tensor_copy(s["tmp8"][0:64, 40:41], st[0:64, K.S_OMX:K.S_OMX + 1])
        e.drain()
        return e.tensor_scalar(s["tmp8"][0:64, 41:42], st[0:64, K.S_OMN:K.S_OMN + 1],
                               -1.0, None, ALU.mult)
    p.dve(pack, reads=("ostat",), writes=("tmp8d",), tag="ar3.pk")

    def tp(e):
        return e.transpose(k.psum["pb6"][0:2, 0:64], s["tmp8"][0:64, 40:42],
                           s["idn_f"][0:64, 0:64])
    p.pe(tp, reads=("tmp8d", "idn_f"), writes=("pt6",), tag="ar3.tp")

    def red(e):
        return e.tensor_reduce(s["small"][0:2, 28:29], k.psum["pb6"][0:2, 0:64],
                               AX.X, ALU.max)
    p.dve(red, reads=("pt6",), writes=("small",), tag="ar3.rd")
    p.dma("pl_dma", d["ar3_in"][:, :], s["small"][0:2, 28:29], reads=("small",),
          writes=("ar3_in",), tag="ar3.st")
    p.collective(ALU.max, [list(range(k.n_cores))],
                 d["ar3_in"].ap().opt(), d["ar3_out"].ap().opt(),
                 reads=("ar3_in",), writes=("ar3_out",), tag="ar3.cc")
    p.dma("pl_dma", s["small"][0:2, 29:30], d["ar3_out"][:, :], reads=("ar3_out",),
          writes=("small29",), tag="ar3.ld")

    def tp2(e):
        return e.transpose(k.psum["pb7"][0:1, 0:2], s["small"][0:2, 29:30],
                           s["idn_f"][0:2, 0:2])
    p.pe(tp2, reads=("small29", "idn_f"), writes=("pt7",), tag="ar3.tp2")

    def ev(e):
        return e.tensor_copy(s["small"][0:1, 26:28], k.psum["pb7"][0:1, 0:2])
    p.dve(ev, reads=("pt7",), writes=("small",), tag="ar3.ev")

    def bc(e):
        return e.matmul(k.psum["pb7"][:, 0:2], s["ones_f"][0:1, :],
                        s["small"][0:1, 26:28], start=True, stop=True)
    p.pe(bc, reads=("small", "ones_f"), writes=("pt7",), tag="ar3.bc")

    def par(e):
        qp = s["qp"]
        pbv = k.psum["pb7"]
        sm = s["small"]
        e.tensor_scalar(sm[:, 9:10], pbv[:, 1:2], -1.0, None, ALU.mult)
        e.drain()
        e.tensor_scalar(sm[:, 9:10], sm[:, 9:10], 0.0, None, ALU.min)
        e.drain()
        e.tensor_scalar(sm[:, 10:11], pbv[:, 0:1], 0.0, None, ALU.max)
        e.drain()
        e.tensor_sub(sm[:, 11:12], sm[:, 10:11], sm[:, 9:10])
        e.drain()
        e.tensor_scalar(qp[:, K.QP_OS:K.QP_OS + 1], sm[:, 11:12], 1.0 / 255.0, 1e-8,
                        ALU.mult, ALU.max)
        e.drain()
        recip_acc(e, qp[:, K.QP_OINV:K.QP_OINV + 1],
                                     qp[:, K.QP_OS:K.QP_OS + 1], sm[:, 11:12])
        e.drain()
        e.tensor_mul(sm[:, 11:12], sm[:, 9:10], qp[:, K.QP_OINV:K.QP_OINV + 1])
        e.drain()
        e.tensor_scalar(s["qi"][:, 0, 0:1], sm[:, 11:12], -1.0, None, ALU.mult)
        e.drain()
        e.tensor_copy(qp[:, K.QP_OZP:K.QP_OZP + 1], s["qi"][:, 0, 0:1])
        e.drain()
        e.tensor_mul(sm[:, 11:12], qp[:, K.QP_OZP:K.QP_OZP + 1],
                     qp[:, K.QP_OS:K.QP_OS + 1])
        e.drain()
        return e.tensor_scalar(qp[:, K.QP_ONZS:K.QP_ONZS + 1], sm[:, 11:12],
                               -1.0, None, ALU.mult)
    p.dve(par, reads=("pt7",), writes=("opar", "small", "qi0"), tag="ar3.par")


def emit_l0_proj_update(k, bi):
    p, s, d = k.p, k.s, k.d
    T, NTC = k.T, k.NTC
    qp = s["qp"]
    ostg = [(s["fx"][0:64, 0, 0:512], "fx0"), (s["fx"][0:64, 1, 0:512], "fx1"),
            (s["den"][0:64, 0:512], "den"), (s["rec"][0:64, 0:512], "rec")]
    for hh in range(H):
        dc, po = hh // 2, 64 * (hh % 2)
        for qc in range(NTC):
            r4 = (hh * NTC + qc) % 4
            sl = r4 % 2
            sap, skey = ostg[r4]
            p.dma("sp_dma" if r4 % 2 == 0 else "pl_dma",
                  sap, d["oraw"][bi, hh, qc, :, :],
                  reads=(f"oraw.{bi}.{hh}.{qc}",), writes=(skey,),
                  tag=f"o{bi}.{hh}.{qc}")
            quant_chain(k, sap,
                        s["act_a"][po:po + 64, dc, qc * 512:(qc + 1) * 512],
                        qp[0:64, K.QP_OINV:K.QP_OINV + 1],
                        qp[0:64, K.QP_OZP:K.QP_OZP + 1],
                        qp[0:64, K.QP_OS:K.QP_OS + 1],
                        qp[0:64, K.QP_ONZS:K.QP_ONZS + 1],
                        reads=(skey, "opar"), writes=(f"act_a.{dc}",),
                        sl=sl, tag=f"o{bi}.{hh}.{qc}", np_=64)
    for fc in range(NDC):
        wsl = fc % 2
        p.dma("sp_dma", s["wtile"][:, wsl, :, :],
              d["pwT"][fc, :, :, :].rearrange("a p j -> p a j"),
              reads=tuple(f"pw.{fc}.{dcc}" for dcc in range(NDC)),
              writes=(f"wtile{wsl}",), tag=f"pj{bi}.w{fc}")
        for dc in range(NDC):
            for tc in range(NTC):
                def mm(e, wsl=wsl, dc=dc, tc=tc):
                    return e.matmul(k.psum[f"pb{tc}"][:, :], s["wtile"][:, wsl, dc, :],
                                    s["act_a"][:, dc, tc * 512:(tc + 1) * 512],
                                    start=(dc == 0), stop=(dc == NDC - 1))
                p.pe(mm, reads=(f"wtile{wsl}", f"act_a.{dc}"), writes=(f"pt{tc}",),
                     tag=f"pj{bi}.m{fc}.{dc}.{tc}")
        xstg = [(s["fx"][:, 0, 0:512], "fx0"), (s["fx"][:, 1, 0:512], "fx1"),
                (s["den"][:, 0:512], "den"), (s["rec"][:, 0:512], "rec")]
        gstg = [(s["fy"][:, 0, :], "fy0"), (s["fy"][:, 1, :], "fy1"),
                (s["rsc"][:, 0:512], "rsc")]
        for tc in range(NTC):
            def evac(e, fc=fc, tc=tc):
                return e.activation(s["ev"][:, tc % 2, :], k.psum[f"pb{tc}"][:, :],
                                    ACTF.Identity,
                                    bias=s["bias"][:, K.B_L0PROJ + fc:K.B_L0PROJ + fc + 1])
            p.act(evac, reads=(f"pt{tc}", "bias"), writes=(f"ev{tc % 2}",),
                  tag=f"pj{bi}.e{fc}.{tc}")
            sl = tc % 2
            r4 = (fc * NTC + tc) % 4
            r3 = (fc * NTC + tc) % 3
            xap, xkey = xstg[r4]
            gap_, gkey = gstg[r3]
            p.dma("sp_dma" if r4 % 2 == 0 else "pl_dma", xap,
                  d["x_fm"][bi, fc, :, tc * 512:(tc + 1) * 512],
                  writes=(xkey,), tag=f"pj{bi}.x{fc}.{tc}")
            p.dma("pl_dma" if r4 % 2 == 0 else "sp_dma", gap_,
                  d["g0"][bi, fc, :, tc * 512:(tc + 1) * 512],
                  reads=(f"g0.{bi}.{fc}.{tc}",), writes=(gkey,),
                  tag=f"pj{bi}.g{fc}.{tc}")

            def upd(e, sl=sl, xap=xap, gap_=gap_):
                e.tensor_mul(s["ev"][:, sl, :], s["ev"][:, sl, :], gap_)
                e.drain()
                return e.tensor_add(s["ev"][:, sl, :], s["ev"][:, sl, :], xap)
            p.dve(upd, reads=(f"ev{sl}", xkey, gkey), writes=(f"ev{sl}",),
                  tag=f"pj{bi}.u{fc}.{tc}")
            p.dma("sp_dma" if tc % 2 == 0 else "pl_dma",
                  d["xmid"][bi, fc, :, tc * 512:(tc + 1) * 512],
                  s["ev"][:, sl, :], reads=(f"ev{sl}",),
                  writes=(f"xmid.{bi}.{fc}.{tc}",), tag=f"pj{bi}.s{fc}.{tc}")


def emit_l1_stats(k, bi):
    p, s, d = k.p, k.s, k.d
    T, NTC, LC = k.T, k.NTC, k.LC
    nl = T // LC
    tstg = [(s["fx"][:, 0, 0:LC], "fx0"), (s["fx"][:, 1, 0:LC], "fx1"),
            (s["mur"][:, :, :].rearrange("p a b -> p (a b)")[:, 0:LC], "mur0")]
    for dc in range(NDC):
        for li in range(nl):
            sl = (dc * nl + li) % 2
            r3 = (dc * nl + li) % 3
            sap, skey = tstg[r3]
            p.dma("sp_dma" if r3 % 2 == 0 else "pl_dma", sap,
                  d["xmid"][bi, dc, :, li * LC:(li + 1) * LC],
                  reads=tuple(f"xmid.{bi}.{dc}.{li * (LC // 512) + j}"
                              for j in range(LC // 512)),
                  writes=(skey,), tag=f"t{bi}.x{dc}.{li}")

            def sqc(e, sl=sl, dc=dc, li=li, sap=sap):
                e.activation(s["sq"][:, sl, :], sap, ACTF.Square)
                e.drain()
                return e.activation(s["act_b"][:, dc, li * LC:(li + 1) * LC],
                                    sap, ACTF.Identity)
            p.act(sqc, reads=(skey,), writes=(f"sq{sl}", f"act_b.{dc}"),
                  tag=f"t{bi}.q{dc}.{li}")
            for q5 in range(LC // 512):
                tc = li * (LC // 512) + q5

                def mm1(e, dc=dc, li=li, q5=q5, tc=tc):
                    return e.matmul(k.psum[f"pb{tc}"][:, :], s["ones_b"][:, :],
                                    s["act_b"][:, dc, li * LC + q5 * 512:li * LC + (q5 + 1) * 512],
                                    start=(dc == 0), stop=(dc == NDC - 1))
                p.pe(mm1, reads=(f"act_b.{dc}", "ones_b"), writes=(f"pt{tc}",),
                     tag=f"t{bi}.mS{dc}.{tc}")

                def mm2(e, sl=sl, q5=q5, dc=dc, tc=tc):
                    return e.matmul(k.psum[f"pb{NTC + tc}"][:, :], s["ones_b"][:, :],
                                    s["sq"][:, sl, q5 * 512:(q5 + 1) * 512],
                                    start=(dc == 0), stop=(dc == NDC - 1))
                p.pe(mm2, reads=(f"sq{sl}", "ones_b"), writes=(f"pt{NTC + tc}",),
                     tag=f"t{bi}.mQ{dc}.{tc}")
    for tc in range(NTC):
        def stats(e, tc=tc):
            e.tensor_scalar(s["mur"][:, 0, :], k.psum[f"pb{tc}"][:, :],
                            1.0 / 1024.0, None, ALU.mult)
            e.drain()
            e.tensor_scalar(s["rr"][:, 0, :], k.psum[f"pb{NTC + tc}"][:, :],
                            1.0 / 1024.0, None, ALU.mult)
            e.drain()
            e.scalar_tensor_tensor(s["rsc"][:, 0:512], s["mur"][:, 0, :], 1.0,
                                   s["mur"][:, 0, :], ALU.mult, ALU.mult)
            e.drain()
            return e.tensor_sub(s["rsc"][:, 0:512], s["rr"][:, 0, :], s["rsc"][:, 0:512])
        p.dve(stats, reads=(f"pt{tc}", f"pt{NTC + tc}"), writes=("rsc", "mur0", "rr0"),
              tag=f"t{bi}.st{tc}")
        p.act(lambda e: e.activation(s["rsc"][:, 0:512], s["rsc"][:, 0:512], ACTF.Sqrt,
                                     bias=s["cst"][:, 0:1]),
              reads=("rsc", "cst"), writes=("rsc",), tag=f"t{bi}.sd{tc}")

        def rcp(e):
            recip_acc(e, s["rr"][:, 1, :], s["rsc"][:, 0:512], s["rec"][:, 0:512])
            e.drain()
            return e.tensor_mul(s["mur"][:, 1, :], s["mur"][:, 0, :], s["rr"][:, 1, :])
        p.dve(rcp, reads=("rsc", "mur0"), writes=("rr1", "mur1", "rec"),
              tag=f"t{bi}.rc{tc}")
        for dc in range(NDC):
            sl = (tc * NDC + dc) % 2
            p.dma("sp_dma", s["fx"][:, sl, 0:512],
                  d["xmid"][bi, dc, :, tc * 512:(tc + 1) * 512],
                  reads=(f"xmid.{bi}.{dc}.{tc}",), writes=(f"fx{sl}",),
                  tag=f"t{bi}.x2{dc}.{tc}")

            def app(e, sl=sl):
                e.tensor_mul(s["ev"][:, 3, :], s["fx"][:, sl, 0:512], s["rr"][:, 1, :])
                e.drain()
                return e.tensor_sub(s["fy"][:, sl, :], s["ev"][:, 3, :],
                                    s["mur"][:, 1, :])
            p.dve(app, reads=(f"fx{sl}", "rr1", "mur1"), writes=(f"fy{sl}", "ev3"),
                  tag=f"t{bi}.ap{dc}.{tc}")
            p.dma("sp_dma", d["xn132"][bi, dc, :, tc * 512:(tc + 1) * 512],
                  s["fy"][:, sl, :], reads=(f"fy{sl}",),
                  writes=(f"xn132.{bi}.{dc}.{tc}",), tag=f"t{bi}.w{dc}.{tc}")
    for dc in range(NDC):
        for li in range(nl):
            sl = (dc * nl + li) % 2
            p.dma("sp_dma", s["fx"][:, sl, 0:LC],
                  d["xn132"][bi, dc, :, li * LC:(li + 1) * LC],
                  reads=tuple(f"xn132.{bi}.{dc}.{li * (LC // 512) + j}"
                              for j in range(LC // 512)),
                  writes=(f"fx{sl}",), tag=f"u{bi}.x{dc}.{li}")

            def sqc(e, sl=sl, dc=dc, li=li):
                e.activation(s["sq"][:, sl, :], s["fx"][:, sl, 0:LC], ACTF.Square)
                e.drain()
                return e.activation(s["act_b"][:, dc, li * LC:(li + 1) * LC],
                                    s["fx"][:, sl, 0:LC], ACTF.Identity)
            p.act(sqc, reads=(f"fx{sl}",), writes=(f"sq{sl}", f"act_b.{dc}"),
                  tag=f"u{bi}.q{dc}.{li}")
            for q5 in range(LC // 512):
                tc = li * (LC // 512) + q5

                def mm1(e, dc=dc, li=li, q5=q5, tc=tc):
                    return e.matmul(k.psum[f"pb{tc}"][:, :], s["ones_b"][:, :],
                                    s["act_b"][:, dc, li * LC + q5 * 512:li * LC + (q5 + 1) * 512],
                                    start=(dc == 0), stop=(dc == NDC - 1))
                p.pe(mm1, reads=(f"act_b.{dc}", "ones_b"), writes=(f"pt{tc}",),
                     tag=f"u{bi}.mS{dc}.{tc}")

                def mm2(e, sl=sl, q5=q5, dc=dc, tc=tc):
                    return e.matmul(k.psum[f"pb{NTC + tc}"][:, :], s["ones_b"][:, :],
                                    s["sq"][:, sl, q5 * 512:(q5 + 1) * 512],
                                    start=(dc == 0), stop=(dc == NDC - 1))
                p.pe(mm2, reads=(f"sq{sl}", "ones_b"), writes=(f"pt{NTC + tc}",),
                     tag=f"u{bi}.mQ{dc}.{tc}")
    for tc in range(NTC):
        def stats(e, tc=tc):
            e.tensor_scalar(s["mur"][:, 0, :], k.psum[f"pb{tc}"][:, :],
                            1.0 / 1024.0, None, ALU.mult)
            e.drain()
            e.tensor_scalar(s["rr"][:, 0, :], k.psum[f"pb{NTC + tc}"][:, :],
                            1.0 / 1024.0, None, ALU.mult)
            e.drain()
            e.scalar_tensor_tensor(s["rsc"][:, 0:512], s["mur"][:, 0, :], 1.0,
                                   s["mur"][:, 0, :], ALU.mult, ALU.mult)
            e.drain()
            return e.tensor_sub(s["rsc"][:, 0:512], s["rr"][:, 0, :], s["rsc"][:, 0:512])
        p.dve(stats, reads=(f"pt{tc}", f"pt{NTC + tc}"), writes=("rsc", "mur0", "rr0"),
              tag=f"u{bi}.st{tc}")
        p.act(lambda e: e.activation(s["rsc"][:, 0:512], s["rsc"][:, 0:512], ACTF.Sqrt,
                                     bias=s["cst"][:, 0:1]),
              reads=("rsc", "cst"), writes=("rsc",), tag=f"u{bi}.sd{tc}")

        def rcp(e):
            recip_acc(e, s["rr"][:, 1, :], s["rsc"][:, 0:512], s["rec"][:, 0:512])
            e.drain()
            return e.tensor_mul(s["mur"][:, 1, :], s["mur"][:, 0, :], s["rr"][:, 1, :])
        p.dve(rcp, reads=("rsc", "mur0"), writes=("rr1", "mur1", "rec"),
              tag=f"u{bi}.rc{tc}")
        for dc in range(NDC):
            sl = (tc * NDC + dc) % 2
            p.dma("sp_dma", s["fx"][:, sl, 0:512],
                  d["xn132"][bi, dc, :, tc * 512:(tc + 1) * 512],
                  reads=(f"xn132.{bi}.{dc}.{tc}",), writes=(f"fx{sl}",),
                  tag=f"u{bi}.x2{dc}.{tc}")

            def app(e, sl=sl):
                e.tensor_mul(s["ev"][:, 3, :], s["fx"][:, sl, 0:512], s["rr"][:, 1, :])
                e.drain()
                return e.tensor_sub(s["ev"][:, 3, :], s["ev"][:, 3, :],
                                    s["mur"][:, 1, :])
            p.dve(app, reads=(f"fx{sl}", "rr1", "mur1"), writes=("ev3",),
                  tag=f"u{bi}.a2{dc}.{tc}")

            def aff(e, dc=dc, tc=tc):
                return e.activation(s["act_a"][:, dc, tc * 512:(tc + 1) * 512],
                                    s["ev"][:, 3, :], ACTF.Identity,
                                    bias=s["bias"][:, K.B_LNB + dc:K.B_LNB + dc + 1],
                                    scale=s["bias"][:, K.B_LNW + dc:K.B_LNW + dc + 1])
            p.act(aff, reads=("ev3", "bias"), writes=(f"act_a.{dc}",),
                  tag=f"u{bi}.af{dc}.{tc}")


def l1w_view(k, name):
    return k.d[name][:, :].rearrange("(a p) (b j) -> p a b j", p=128, j=128)


def emit_l1_qkv(k, bi):
    p, s, d = k.p, k.s, k.d
    T, NTC, NT128 = k.T, k.NTC, k.NT128
    if bi == 0:
        p.fence(reads=(), writes=("nrmX",) + tuple(f"kst.{j}" for j in range(NDC)),
                tag="f.kst")
    p.fence(reads=(), writes=("lgt", "wrt", "axw", "nrmY", "wvb"), tag=f"f.wvb{bi}")
    for fc in range(16):
        wsl = fc % 2
        p.dma("pl_dma", s["wtile"][:, wsl, :, :], l1w_view(k, "l1_qkv_w")[:, :, fc, :],
              writes=(f"wtile{wsl}",), tag=f"q1{bi}.w{fc}")
        for dc in range(NDC):
            for tc in range(NTC):
                def mm(e, wsl=wsl, dc=dc, tc=tc):
                    return e.matmul(k.psum[f"pb{tc}"][:, :], s["wtile"][:, wsl, dc, :],
                                    s["act_a"][:, dc, tc * 512:(tc + 1) * 512],
                                    start=(dc == 0), stop=(dc == NDC - 1))
                p.pe(mm, reads=(f"wtile{wsl}", f"act_a.{dc}"), writes=(f"pt{tc}",),
                     tag=f"q1{bi}.m{fc}.{dc}.{tc}")
        for tc in range(NTC):
            if fc < 8:
                dst = s["act_b"][:, fc, tc * 512:(tc + 1) * 512]
                res = f"act_b.{fc}"
            else:
                dst = s["kst"][:, fc - 8, tc * 512:(tc + 1) * 512]
                res = f"kst.{fc - 8}"

            def evac(e, fc=fc, tc=tc, dst=dst):
                return e.activation(dst, k.psum[f"pb{tc}"][:, :], ACTF.Identity,
                                    bias=s["bias"][:, K.B_L1QKV + fc:K.B_L1QKV + fc + 1])
            p.act(evac, reads=(f"pt{tc}", "bias"), writes=(res,),
                  tag=f"q1{bi}.e{fc}.{tc}")
    for fv in range(2):
        p.dma("pl_dma", s["wvb"][:, :, :],
              d["l1_qkv_w"][:, 2048 + fv * 512:2048 + (fv + 1) * 512]
              .rearrange("(a p) j -> p a j", p=128),
              writes=("wvb",), tag=f"v1{bi}.w{fv}")
        for t128 in range(NT128):
            pq = 4 + t128 % 2
            pb = k.psum[f"pb{pq}"]
            for dc in range(NDC):
                def mm(e, pb=pb, t128=t128, dc=dc):
                    return e.matmul(pb[:, :],
                                    s["act_a"][:, dc, t128 * 128:(t128 + 1) * 128],
                                    s["wvb"][:, dc, :],
                                    start=(dc == 0), stop=False)
                p.pe(mm, reads=("wvb", f"act_a.{dc}"), writes=(f"pt{pq}",),
                     tag=f"v1{bi}.m{fv}.{t128}.{dc}")

            def mmb(e, pb=pb, fv=fv):
                return e.matmul(pb[:, :], s["ones_b"][0:1, 0:128],
                                s["brow"][0:1, 1, fv * 512:(fv + 1) * 512],
                                start=False, stop=True)
            p.pe(mmb, reads=("brow1", "ones_b"), writes=(f"pt{pq}",),
                 tag=f"v1{bi}.b{fv}.{t128}")

            def evac(e, pb=pb, t128=t128, fv=fv):
                return e.tensor_copy(s["vaug"][:, t128, fv * 8:(fv + 1) * 8, 0:64],
                                     pb[:, :])
            p.dve(evac, reads=(f"pt{pq}", "vaug.ones"), writes=(f"vg.{t128 % 4}",),
                  tag=f"v1{bi}.e{fv}.{t128}")


def emit_l1_ffn(k, bi):
    p, s, d = k.p, k.s, k.d
    T, NTC, NT128 = k.T, k.NTC, k.NT128
    p.fence(reads=(), writes=("wvb", "lgt", "wrt", "axw"), tag=f"f.lgt{bi}")
    for dc in range(NDC):
        p.dma("pl_dma", s["act_a"][:, dc, :], d["xn132"][bi, dc, :, :],
              reads=tuple(f"xn132.{bi}.{dc}.{t}" for t in range(NTC)),
              writes=(f"act_a.{dc}",), tag=f"ff{bi}.xn{dc}")
    for tc in range(NTC):
        for dc in range(NDC):
            def mm(e, dc=dc, tc=tc):
                return e.matmul(k.psum["pb6"][0:8, :], s["lR"][:, dc, :],
                                s["act_a"][:, dc, tc * 512:(tc + 1) * 512],
                                start=(dc == 0), stop=(dc == NDC - 1))
            p.pe(mm, reads=("lR", f"act_a.{dc}"), writes=("pt6",),
                 tag=f"rt{bi}.m{tc}.{dc}")

        def evac(e, tc=tc):
            return e.activation(s["lgt"][0:8, tc * 512:(tc + 1) * 512],
                                k.psum["pb6"][0:8, :], ACTF.Identity,
                                bias=s["small"][0:8, 30:31])
        p.act(evac, reads=("pt6", "rb"), writes=("lgt",), tag=f"rt{bi}.e{tc}")
    if "dbg_lgt" in K.DEBUG_OUT:
        p.dma("sp_dma", d["dbg_lgt"][bi, :, :], s["lgt"][0:8, :],
              reads=("lgt",), writes=("dbgl",), tag=f"dbg.lgt{bi}")
    for t128 in range(NT128):
        def tp(e, t128=t128):
            return e.transpose(k.psum["pb7"][0:128, 0:8],
                               s["lgt"][0:8, t128 * 128:(t128 + 1) * 128],
                               s["idn_f"][0:8, 0:8])
        p.pe(tp, reads=("lgt", "idn_f"), writes=("pt7",), tag=f"rt{bi}.t{t128}")

        def chain(e):
            t8 = s["tmp8"]
            e.tensor_copy(t8[:, 50:58], k.psum["pb7"][0:128, 0:8])
            e.drain()
            e.tensor_reduce(t8[:, 58:59], t8[:, 50:58], AX.X, ALU.max)
            e.drain()
            e.tensor_scalar(t8[:, 59:60], t8[:, 58:59], -1.0, None, ALU.mult)
            e.drain()
            e.tensor_scalar(t8[:, 64:72], t8[:, 50:58], t8[:, 58:59], None,
                            ALU.is_equal)
            e.drain()
            e.scalar_tensor_tensor(t8[:, 64:72], t8[:, 64:72], -1e30, t8[:, 50:58],
                                   ALU.mult, ALU.add)
            e.drain()
            e.tensor_reduce(t8[:, 60:61], t8[:, 64:72], AX.X, ALU.max)
            e.drain()
            return e.tensor_scalar(t8[:, 64:72], t8[:, 50:58], t8[:, 60:61], None,
                                   ALU.is_ge)
        p.dve(chain, reads=("pt7",), writes=("t8r",), tag=f"rt{bi}.c{t128}")

        def ex(e):
            return e.activation(s["tmp8"][:, 72:80], s["tmp8"][:, 50:58], ACTF.Exp,
                                bias=s["tmp8"][:, 59:60])
        p.act(ex, reads=("t8r",), writes=("t8x",), tag=f"rt{bi}.x{t128}")

        def fin(e):
            t8 = s["tmp8"]
            e.tensor_mul(t8[:, 72:80], t8[:, 72:80], t8[:, 64:72])
            e.drain()
            e.tensor_reduce(t8[:, 61:62], t8[:, 72:80], AX.X, ALU.add)
            e.drain()
            recip_acc(e, t8[:, 62:63], t8[:, 61:62], t8[:, 63:64])
            e.drain()
            return e.tensor_scalar(t8[:, 72:80], t8[:, 72:80], t8[:, 62:63], None,
                                   ALU.mult)
        p.dve(fin, reads=("t8x", "t8r"), writes=("t8x",), tag=f"rt{bi}.f{t128}")

        def tpb(e):
            return e.transpose(k.psum["pb7"][0:8, 0:128], s["tmp8"][:, 72:80],
                               s["idn_f"][:, :])
        p.pe(tpb, reads=("t8x", "idn_f"), writes=("pt7",), tag=f"rt{bi}.b{t128}")

        def wev(e, t128=t128):
            return e.tensor_copy(s["wrt"][0:8, t128 * 128:(t128 + 1) * 128],
                                 k.psum["pb7"][0:8, 0:128])
        p.dve(wev, reads=("pt7",), writes=("wrt",), tag=f"rt{bi}.w{t128}")
    if "dbg_wrt" in K.DEBUG_OUT:
        def wrtc(e):
            return e.tensor_copy(s["rr"][:, 0, 0:8].rearrange("p a -> p a") if False else s["mur"][0:8, 0, 0:512], s["wrt"][0:8, 0:512])
        p.dma("pl_dma", d["dbg_wrt"][bi, :, :], s["wrt"][0:8, :],
              reads=("wrt",), writes=("dbgw",), tag=f"dbg.wrt{bi}")
    p.fence(reads=(), writes=("lgt", "axw"), tag=f"f.axw{bi}")
    for tc in range(NTC):
        pb2 = k.psum[f"pb{tc % 2}"]
        for dc in range(NDC):
            def mm(e, pb2=pb2, dc=dc, tc=tc):
                return e.matmul(pb2[:, :], s["lA"][:, dc, :],
                                s["act_a"][:, dc, tc * 512:(tc + 1) * 512],
                                start=(dc == 0), stop=(dc == NDC - 1))
            p.pe(mm, reads=("lA", f"act_a.{dc}"), writes=(f"pt{tc % 2}",),
                 tag=f"ax{bi}.m{tc}.{dc}")
        pb3 = k.psum[f"pb{2 + tc % 2}"]

        def mmw(e, pb3=pb3, tc=tc):
            return e.matmul(pb3[:, :], s["eind"][:, :],
                            s["wrt"][0:8, tc * 512:(tc + 1) * 512],
                            start=True, stop=True)
        p.pe(mmw, reads=("eind", "wrt"), writes=(f"pt{2 + tc % 2}",),
             tag=f"ax{bi}.w{tc}")

        def axw(e, pb2=pb2, pb3=pb3, tc=tc):
            e.tensor_copy(s["rsc"][:, 0:512], pb3[:, :])
            e.drain()
            return e.tensor_mul(s["axw"][:, tc * 512:(tc + 1) * 512],
                                pb2[:, :], s["rsc"][:, 0:512])
        p.dve(axw, reads=(f"pt{tc % 2}", f"pt{2 + tc % 2}"), writes=("axw", "rsc"),
              tag=f"ax{bi}.s{tc}")
    if "dbg_axw" in K.DEBUG_OUT:
        p.dma("pl_dma", d["dbg_axw"][bi, :, :], s["axw"][:, :],
              reads=("axw",), writes=("dbga",), tag=f"dbg.axw{bi}")
    for fc in range(NDC):
        wsl = fc % 2
        p.dma("pl_dma", s["wtile"][:, wsl, :, :], l1w_view(k, "l1_base_wT")[:, :, fc, :],
              writes=(f"wtile{wsl}",), tag=f"bs{bi}.w{fc}")
        if fc == 0 and bi == 0 and "dbg_wt" in K.DEBUG_OUT:
            p.dma("pl_dma", d["dbg_wt"][:, :, :], s["wtile"][:, wsl, :, :],
                  reads=(f"wtile{wsl}",), writes=("dbgwt",), tag="dbg.wt")
        for dc in range(NDC):
            for tc in range(NTC):
                pq = 4 + tc

                def mm(e, wsl=wsl, dc=dc, tc=tc, pq=pq):
                    return e.matmul(k.psum[f"pb{pq}"][:, :], s["wtile"][:, wsl, dc, :],
                                    s["act_a"][:, dc, tc * 512:(tc + 1) * 512],
                                    start=(dc == 0), stop=False)
                p.pe(mm, reads=(f"wtile{wsl}", f"act_a.{dc}"), writes=(f"pt{pq}",),
                     tag=f"bs{bi}.m{fc}.{dc}.{tc}")
        for tc in range(NTC):
            pq = 4 + tc

            def mml(e, fc=fc, tc=tc, pq=pq):
                return e.matmul(k.psum[f"pb{pq}"][:, :],
                                s["lB"][:, fc * 128:(fc + 1) * 128],
                                s["axw"][:, tc * 512:(tc + 1) * 512],
                                start=False, stop=True)
            p.pe(mml, reads=("lB", "axw"), writes=(f"pt{pq}",),
                 tag=f"bs{bi}.l{fc}.{tc}")

            def silu(e, fc=fc, tc=tc, pq=pq):
                return e.activation(s["ev"][:, tc % 2, :], k.psum[f"pb{pq}"][:, :],
                                    ACTF.Silu,
                                    bias=s["bias"][:, K.B_BASE + fc:K.B_BASE + fc + 1])
            p.act(silu, reads=(f"pt{pq}", "bias"), writes=(f"ev{tc % 2}",),
                  tag=f"bs{bi}.s{fc}.{tc}")
            p.dma("sp_dma", d["g1"][bi, fc, :, tc * 512:(tc + 1) * 512],
                  s["ev"][:, tc % 2, :], reads=(f"ev{tc % 2}",),
                  writes=(f"g1.{bi}.{fc}.{tc}",), tag=f"bs{bi}.g{fc}.{tc}")


def emit_l1_attention(k, bi):
    p, s = k.p, k.s
    p.fence(reads=(), writes=("lgt", "wrt", "axw", "wvb", "nrmY"), tag=f"f.n1{bi}")
    attention_core(k, bi, layer=1, Ks=s["kst"], Kres="kst",
                   Qs=s["act_b"], Qres="act_b", nrm=s["nrm1"], nres="nrmY")


def emit_l1_proj_out(k, bi):
    p, s, d = k.p, k.s, k.d
    T, NTC = k.T, k.NTC
    for fc in range(NDC):
        wsl = fc % 2
        p.dma("pl_dma", s["wtile"][:, wsl, :, :], l1w_view(k, "l1_proj_w")[:, :, fc, :],
              writes=(f"wtile{wsl}",), tag=f"p1{bi}.w{fc}")
        for dc in range(NDC):
            for tc in range(NTC):
                def mm(e, wsl=wsl, dc=dc, tc=tc):
                    return e.matmul(k.psum[f"pb{tc}"][:, :], s["wtile"][:, wsl, dc, :],
                                    s["act_a"][:, dc, tc * 512:(tc + 1) * 512],
                                    start=(dc == 0), stop=(dc == NDC - 1))
                p.pe(mm, reads=(f"wtile{wsl}", f"act_a.{dc}"), writes=(f"pt{tc}",),
                     tag=f"p1{bi}.m{fc}.{dc}.{tc}")
        for tc in range(NTC):
            def evac(e, fc=fc, tc=tc):
                return e.activation(s["ev"][:, 2 + tc % 2, :], k.psum[f"pb{tc}"][:, :],
                                    ACTF.Identity,
                                    bias=s["bias"][:, K.B_L1PROJ + fc:K.B_L1PROJ + fc + 1])
            p.act(evac, reads=(f"pt{tc}", "bias"), writes=(f"ev{2 + tc % 2}",),
                  tag=f"p1{bi}.e{fc}.{tc}")
            sl = tc % 2
            p.dma("sp_dma", s["fx"][:, sl, 0:512],
                  d["xn132"][bi, fc, :, tc * 512:(tc + 1) * 512],
                  reads=(f"xn132.{bi}.{fc}.{tc}",), writes=(f"fx{sl}",),
                  tag=f"p1{bi}.n{fc}.{tc}")
            p.dma("sp_dma", s["fy"][:, sl, :],
                  d["g1"][bi, fc, :, tc * 512:(tc + 1) * 512],
                  reads=(f"g1.{bi}.{fc}.{tc}",), writes=(f"fy{sl}",),
                  tag=f"p1{bi}.g{fc}.{tc}")
            p.dma("sp_dma", s["mur"][:, sl, :],
                  d["xmid"][bi, fc, :, tc * 512:(tc + 1) * 512],
                  reads=(f"xmid.{bi}.{fc}.{tc}",), writes=(f"mur{sl}",),
                  tag=f"p1{bi}.xm{fc}.{tc}")

            def upd(e, sl=sl, tc=tc):
                e.tensor_add(s["ev"][:, 2 + sl, :], s["ev"][:, 2 + sl, :],
                             s["fx"][:, sl, 0:512])
                e.drain()
                e.tensor_mul(s["ev"][:, 2 + sl, :], s["ev"][:, 2 + sl, :],
                             s["fy"][:, sl, :])
                e.drain()
                return e.tensor_add(s["ev"][:, 2 + sl, :], s["ev"][:, 2 + sl, :],
                                    s["mur"][:, sl, :])
            p.dve(upd, reads=(f"ev{2 + sl}", f"fx{sl}", f"fy{sl}", f"mur{sl}"),
                  writes=(f"ev{2 + sl}",), tag=f"p1{bi}.u{fc}.{tc}")
            p.dma("sp_dma", d["out_fm"][bi, fc, :, tc * 512:(tc + 1) * 512],
                  s["ev"][:, 2 + sl, :], reads=(f"ev{2 + sl}",),
                  writes=(f"OUT.{bi}.{fc}.{tc}",), tag=f"p1{bi}.o{fc}.{tc}")


def emit_final(k):
    outs = tuple(f"OUT.{bi}.{fc}.{tc}" for bi in range(NB) for fc in range(NDC)
                 for tc in range(k.NTC))
    k.p.fence(reads=outs, writes=(), tag="final")


def build(T=2048, n_cores=8, sim_strict=False, debug_out=()):
    K.DEBUG_OUT = tuple(debug_out)
    k = K(T=T, n_cores=n_cores, sim_strict=sim_strict)
    emit_setup(k)
    emit_wq_preproc(k, "l0_qkv_w", 24)
    emit_wq_preproc(k, "l0_proj_w", 8)
    emit_wcast(k)
    for bi in range(NB):
        emit_l0_stats(k, bi)
    emit_ar1(k)
    for bi in range(NB):
        emit_ffn(k, bi)
        emit_xq_qkv(k, bi)
    emit_ar2(k)
    for bi in range(NB):
        emit_l0_attention(k, bi)
    emit_ar3(k)
    for bi in range(NB):
        emit_l0_proj_update(k, bi)
    for bi in range(NB):
        emit_l1_stats(k, bi)
        emit_l1_qkv(k, bi)
        emit_l1_ffn(k, bi)
        emit_l1_attention(k, bi)
        emit_l1_proj_out(k, bi)
    emit_final(k)
    k.p.emit()
    return k




def make_inmaps(inputs, n_cores=8, T=2048):
    x = np.asarray(inputs["L0_attn_x"], np.float32)
    Bt, N, C = x.shape
    f32 = lambda a: np.ascontiguousarray(np.asarray(a, np.float32))

    def col(a, w):
        return np.ascontiguousarray(np.asarray(a, np.float32).reshape(w, 128).T)

    qkvb = np.asarray(inputs["L0_attn_qkv_b"], np.float32)
    l1qkvb = np.asarray(inputs["L1_attn_b_qkv"], np.float32)
    hi = np.zeros((128, NDC, 16), np.float32)
    for dc in range(NDC):
        for pp in range(128):
            hi[pp, dc, (dc * 128 + pp) // 64] = 1.0
    ei = np.zeros((8, 128), np.float32)
    for e in range(8):
        ei[e, e * 16:(e + 1) * 16] = 2.0

    shared = {
        "l0_qkv_w": f32(inputs["L0_attn_qkv_w"]),
        "l0_qkv_b": col(qkvb, 24),
        "l0_qkv_bv_row": f32(qkvb[2048:3072]).reshape(1, D),
        "l0_proj_w": f32(inputs["L0_attn_proj_w"]),
        "l0_proj_b": col(inputs["L0_attn_proj_b"], 8),
        "l1_ln_w": col(inputs["L1_attn_ln_w"], 8),
        "l1_ln_b": col(inputs["L1_attn_ln_b"], 8),
        "l1_qkv_w": f32(inputs["L1_attn_W_qkv"]),
        "l1_qkv_b": col(l1qkvb, 24),
        "l1_qkv_bv_row": f32(l1qkvb[2048:3072]).reshape(1, D),
        "l1_proj_w": f32(inputs["L1_attn_W_proj"]),
        "l1_proj_b": col(inputs["L1_attn_b_proj"], 8),
        "l1_base_wT": f32(np.asarray(inputs["L1_ffn_base_w"], np.float32).T),
        "l1_base_b": col(inputs["L1_ffn_base_b"], 8),
        "l1_lora_A": f32(np.concatenate(
            [np.asarray(inputs["L1_ffn_lora_A"], np.float32)[e] for e in range(8)],
            axis=1)),
        "l1_lora_B": f32(np.asarray(inputs["L1_ffn_lora_B"], np.float32)
                         .reshape(128, D)),
        "l1_router_wT": f32(np.asarray(inputs["L1_ffn_router_w"], np.float32).T),
        "l1_router_b": f32(np.asarray(inputs["L1_ffn_router_b"], np.float32)
                           .reshape(8, 1)),
        "c_idn": np.eye(128, dtype=np.float32),
        "c_headind": hi,
        "c_expind2": ei,
    }
    w1 = np.asarray(inputs["L0_ffn_w1"], np.float32)
    b1 = np.asarray(inputs["L0_ffn_b1"], np.float32)
    w2 = np.asarray(inputs["L0_ffn_w2"], np.float32)
    b2 = np.asarray(inputs["L0_ffn_b2"], np.float32)
    in_maps = []
    for c in range(n_cores):
        m = dict(shared)
        xs = x[2 * c:2 * c + 2]
        m["x_fm"] = np.ascontiguousarray(
            xs.reshape(2, N, NDC, 128).transpose(0, 2, 3, 1))
        m["l0_w1"] = f32(w1[c])
        m["l0_b1"] = col(b1[c], NFC)
        m["l0_w2"] = f32(w2[c])
        m["l0_b2"] = col(b2[c], NDC)
        in_maps.append(m)
    return in_maps


def kernel(**inputs):
    n_cores, T = 8, 2048
    k = build(T=T, n_cores=n_cores, sim_strict=True)
    in_maps = make_inmaps(inputs, n_cores=n_cores, T=T)
    res = run_bass_kernel_spmd(k.nc, in_maps, core_ids=list(range(n_cores)))
    outs = []
    for c in range(n_cores):
        o = np.asarray(res.results[c]["out_fm"], np.float32)
        outs.append(o.transpose(0, 3, 1, 2).reshape(2, T, D))
    return np.concatenate(outs, axis=0).astype(np.float32)
